# version 05
# baseline (speedup 1.0000x reference)
import numpy as np
import ml_dtypes
import concourse.bass as bass
import concourse.mybir as mybir
from concourse.bass_utils import run_bass_kernel_spmd
from contextlib import ExitStack

F32 = mybir.dt.float32
BF16 = mybir.dt.bfloat16
FP8 = mybir.dt.float8e4
AF = mybir.ActivationFunctionType
ALU = mybir.AluOpType
AX = mybir.AxisListType

NCORES = 8
L = 8192
DM = 2048
T = 1024
NT = 8
D_IN = 4096
HEADS = 64
NG = 67
CC = 16
IC = 48
DC = 32
EPS = 1e-5
TW = 1027


class Sem:
    __slots__ = ('h', 'v')

    def __init__(self, h):
        self.h = h
        self.v = 0

    def inc(self, instr, n=1):
        instr.then_inc(self.h, n)
        self.v += n
        return self.v


def build_nc(debug_taps=(), max_phase='G'):
    nc = bass.Bass('TRN2', target_bir_lowering=False)

    x_main = nc.dram_tensor('x_main', [T, DM], F32, kind='ExternalInput').ap()
    x_halo = nc.dram_tensor('x_halo', [128, DM], F32, kind='ExternalInput').ap()
    w_inT = nc.dram_tensor('w_inT', [NG, 128, CC, 128], F32, kind='ExternalInput').ap()
    w_g = nc.dram_tensor('w_g', [IC, 128, CC, 128], F32, kind='ExternalInput').ap()
    w_u = nc.dram_tensor('w_u', [IC, 128, CC, 128], F32, kind='ExternalInput').ap()
    w_d = nc.dram_tensor('w_d', [IC, 128, DM], F32, kind='ExternalInput').ap()
    w_oT = nc.dram_tensor('w_oT', [4, DC, 128, 512], F32, kind='ExternalInput').ap()
    conv_w_t = nc.dram_tensor('conv_w_t', [128, 136], F32, kind='ExternalInput').ap()
    conv_b_t = nc.dram_tensor('conv_b_t', [128, 34], F32, kind='ExternalInput').ap()
    dt_bias = nc.dram_tensor('dt_bias', [64, 1], F32, kind='ExternalInput').ap()
    a_log = nc.dram_tensor('a_log', [64, 1], F32, kind='ExternalInput').ap()
    d_rep = nc.dram_tensor('d_rep', [128, 64], F32, kind='ExternalInput').ap()
    norm_w_t = nc.dram_tensor('norm_w_t', [128, 32], F32, kind='ExternalInput').ap()
    gs_t = nc.dram_tensor('gs_t', [128, IC], F32, kind='ExternalInput').ap()
    us_t = nc.dram_tensor('us_t', [128, IC], F32, kind='ExternalInput').ap()
    ds_b = nc.dram_tensor('ds_b', [128, DM], F32, kind='ExternalInput').ap()
    ident_bf = nc.dram_tensor('ident_bf', [128, 128], BF16, kind='ExternalInput').ap()
    ident_f = nc.dram_tensor('ident_f', [128, 128], F32, kind='ExternalInput').ap()
    mask_ge = nc.dram_tensor('mask_ge', [128, 128], BF16, kind='ExternalInput').ap()
    ones_bf = nc.dram_tensor('ones_bf', [128, 1], BF16, kind='ExternalInput').ap()
    ones_f = nc.dram_tensor('ones_f', [1, 128], F32, kind='ExternalInput').ap()
    mask_prev = nc.dram_tensor('mask_prev', [128, 8], F32, kind='ExternalInput').ap()
    out_hbm = nc.dram_tensor('out', [T, DM], F32, kind='ExternalOutput').ap()

    g_stash = nc.dram_tensor('g_stash', [32, 128, T], BF16)
    xT_stash = nc.dram_tensor('xT_stash', [NT, 128, D_IN], BF16)
    hgn_stash = nc.dram_tensor('hgn_stash', [DC, 128, T], BF16)
    fuse_stash = nc.dram_tensor('fuse_stash', [IC, 128, T], BF16)
    ffn_stash = nc.dram_tensor('ffn_stash', [NT, 128, DM], BF16)
    cc_in = nc.dram_tensor('cc_in', [128, 4097], F32)
    cc_out = nc.dram_tensor('cc_out', [NCORES * 128, 4097], F32)

    dbg_shapes = {
        'xn_fm': [128, CC * TW], 'xbcpre': [128, TW],
        'bfm': [128, T], 'cfm': [128, T], 'dtf': [64, T],
        'grel': [64, T], 'xT': [NT * 128, D_IN],
        'slocal': [128, D_IN], 'sin': [128, D_IN],
        'ytok': [NT * 128, D_IN], 'hgnz': [NT * 128, DC * 128],
        'gsilz': [NT * 128, DC * 128],
    }
    dbg = {}
    for name in debug_taps:
        dbg[name] = nc.dram_tensor('dbg_' + name, dbg_shapes[name], F32,
                                   kind='ExternalOutput').ap()

    es = ExitStack()

    def sb(name, shape, dtype):
        return es.enter_context(nc.sbuf_tensor(name, shape, dtype))

    def sem(name):
        return Sem(es.enter_context(nc.semaphore(name)))

    xn_fm = sb('xn_fm', [128, CC * TW], BF16)
    B_fm = sb('B_fm', [128, T], BF16)
    C_fm = sb('C_fm', [128, T], BF16)
    B_T = sb('B_T', [128, NT * 128], BF16)
    dt_fm = sb('dt_fm', [64, T], F32)
    ln_or_erel = sb('ln_or_erel', [64, T], F32)
    grel = sb('grel', [64, T], F32)
    sgrel = sb('sgrel', [64, T], F32)
    est_or_dA = sb('est_or_dA', [64, T], F32)
    grel_T = sb('grel_T', [128, NT * 64], F32)
    sgrel_T = sb('sgrel_T', [128, NT * 64], F32)
    dtE_T = sb('dtE_T', [128, NT * 64], F32)
    erel_T = sb('erel_T', [128, NT * 64], F32)
    exp_ctot = sb('exp_ctot', [64, 8], F32)
    ctot_sum = sb('ctot_sum', [64, 1], F32)
    dec_row = sb('dec_row', [128, 64], F32)
    S_run = sb('S_run', [128, D_IN], F32)

    idbf_sb = sb('idbf_sb', [128, 128], BF16)
    idf_sb = sb('idf_sb', [128, 128], F32)
    cvals = sb('cvals', [128, 4], F32)

    T1 = es.enter_context(nc.psum_tensor('T1', [128, 1536], F32))
    T2 = es.enter_context(nc.psum_tensor('T2', [128, 1024], F32))
    T3 = es.enter_context(nc.psum_tensor('T3', [128, 1024], BF16))
    T4 = es.enter_context(nc.psum_tensor('T4', [128, 1024], BF16))
    T5 = es.enter_context(nc.psum_tensor('T5', [128, 512], F32))

    def bfbank(k):
        return T3 if k % 2 == 0 else T4

    def fpbank(k):
        return T2[:, 512:1024] if k % 2 == 0 else T5[:, 0:512]

    def ybanks(cg):
        if cg % 2 == 0:
            return T1[:, 0:512], T1[:, 512:1024]
        return T1[:, 1024:1536], T2[:, 0:512]

    gB = [0]
    gF = [0]
    evGB = {}
    evGBe = {}
    evGF = {}
    evGFe = {}


    s_const = sem('s_const')
    s_pe = sem('s_pe')
    s_ve = sem('s_ve')
    s_sc = sem('s_sc')
    s_gp = sem('s_gp')
    s_cc = sem('s_cc')
    s_out = sem('s_out')
    s_dbg = sem('s_dbg')
    s_x = [sem('s_x0'), sem('s_x1')]
    s_bc = sem('s_bc')
    s_wb = [sem('s_wb0'), sem('s_wb1')]
    s_gst = [sem('s_gst0'), sem('s_gst1')]
    s_xtst = sem('s_xtst')
    s_exp = sem('s_exp')
    s_dc = sem('s_dc')
    s_wgu = [sem('s_wgu0'), sem('s_wgu1')]
    s_fst = [sem('s_fst0'), sem('s_fst1')]
    s_wd = [sem('s_wd0'), sem('s_wd1')]
    s_fu = [sem('s_fu0'), sem('s_fu1')]
    s_dst = sem('s_dst')
    s_hst = sem('s_hst')
    s_ej = sem('s_ej')
    s_dr = sem('s_dr')
    s_ec = sem('s_ec')
    s_ez = [sem('s_ez0'), sem('s_ez1')]
    s_wo = sem('s_wo')
    s_gl = [sem('s_gl0'), sem('s_gl1')]

    block = es.enter_context(nc.Block())

    def SY(f):
        @block.sync
        def _(e):
            f(e)

    def VE(f):
        @block.vector
        def _(e):
            e.drain()
            f(e)

    def SC(f):
        @block.scalar
        def _(e):
            e.drain()
            f(e)

    def GP(f):
        @block.gpsimd
        def _(e):
            e.drain()
            f(e)

    def TE(f):
        @block.tensor
        def _(e):
            f(e)


    def _early_exit(wait_list, stacks=()):
        def fin(e):
            for sh, vv in wait_list:
                e.wait_ge(sh, vv)
            if s_dbg.v:
                e.wait_ge(s_dbg.h, s_dbg.v)
        SY(fin)
        for st_ in reversed(stacks):
            st_.close()
        es.close()
        return nc

    def const_loads(e):
        for dst, src in [(idbf_sb, ident_bf), (idf_sb, ident_f)]:
            s_const.inc(e.dma_start(out=dst[:], in_=src), 16)
    SY(const_loads)
    LD_CONST = s_const.v

    VE_SETUP = [0]

    def ve_setup(v):
        v.memset(cvals[:, 0:1], 1e-38)
        v.memset(cvals[:, 1:2], EPS)
        v.memset(cvals[:, 2:3], 1.0 / D_IN)
        VE_SETUP[0] = s_ve.inc(v.drain())
    VE(ve_setup)

    SC_SETUP = [0]

    PHA = [None]
    esWb = ExitStack()
    cw_sb = esWb.enter_context(nc.sbuf_tensor('cw_sb', [128, 136], F32))
    cb_sb = esWb.enter_context(nc.sbuf_tensor('cb_sb', [128, 34], F32))
    dtb_sb = esWb.enter_context(nc.sbuf_tensor('dtb_sb', [64, 1], F32))
    alog_sb = esWb.enter_context(nc.sbuf_tensor('alog_sb', [64, 1], F32))
    nA_sb = esWb.enter_context(nc.sbuf_tensor('nA_sb', [64, 1], F32))
    zeros64 = esWb.enter_context(nc.sbuf_tensor('zeros64', [64, 128], F32))
    Wb = [esWb.enter_context(nc.sbuf_tensor(f'Wb{i}', [128, CC * 128], BF16))
          for i in range(2)]

    B_CONST = [0]

    def b_consts(e):
        for dst, srcap in [(cw_sb, conv_w_t), (cb_sb, conv_b_t),
                           (dtb_sb, dt_bias), (alog_sb, a_log)]:
            s_bc.inc(e.dma_start(out=dst[:], in_=srcap), 16)
        B_CONST[0] = s_bc.v
    SY(b_consts)

    def b_setup_ve(v):
        v.wait_ge(s_bc.h, B_CONST[0])
        v.memset(zeros64[:], 0.0)
        BZERO[0] = s_ve.inc(v.drain())
    BZERO = [0]
    VE(b_setup_ve)

    def b_setup_sc(sc):
        sc.wait_ge(s_bc.h, B_CONST[0])
        sc.activation(out=nA_sb[:], in_=alog_sb[:], func=AF.Exp)
        sc.drain()
        sc.activation(out=nA_sb[:], in_=nA_sb[:], func=AF.Copy, scale=-1.0)
        SC_SETUP[0] = s_sc.inc(sc.drain())
    SC(b_setup_sc)

    evB_w = {}

    def b_w01(gp):
        for g in range(2):
            evB_w[g] = (s_wb[g % 2].h, s_wb[g % 2].inc(
                gp.dma_start(out=Wb[g % 2][:],
                             in_=w_inT[g].rearrange('p c j -> p (c j)')), 16))
    GP(b_w01)

    esA = ExitStack()

    def sbA(name, shape, dtype):
        return esA.enter_context(nc.sbuf_tensor(name, shape, dtype))

    x_t = [sbA(f'x_t{i}', [128, DM], F32) for i in range(2)]
    ln_sq = sbA('ln_sq', [128, DM], F32)
    xn_t = [sbA(f'xn_t{i}', [128, DM], BF16) for i in range(2)]
    ln_m = [sbA(f'ln_m{i}', [128, 1], F32) for i in range(2)]
    ln_v = [sbA(f'ln_v{i}', [128, 1], F32) for i in range(2)]

    evA_load = {}
    evA_stats = {}
    evA_apply = {}
    evA_tq = {}
    evA_eq = {}

    for i in range(9):
        def a_load(e, i=i):
            if i >= 2:
                e.wait_ge(s_sc.h, evA_apply[i - 2])
            src = x_halo if i == 0 else x_main[(i - 1) * 128:i * 128, :]
            sx = s_x[i % 2]
            evA_load[i] = (sx.h, sx.inc(e.dma_start(out=x_t[i % 2][:], in_=src), 16))
        SY(a_load)

        def a_stats(v, i=i):
            b = i % 2
            v.wait_ge(*evA_load[i])
            v.reduce_sum(out=ln_m[b][:], in_=x_t[b][:], axis=AX.X)
            v.drain()
            v.tensor_scalar_mul(out=ln_m[b][:], in0=ln_m[b][:], scalar1=1.0 / DM)
            v.drain()
            v.tensor_scalar_sub(out=x_t[b][:], in0=x_t[b][:], scalar1=ln_m[b][:])
            v.drain()
            v.tensor_tensor(out=ln_sq[:], in0=x_t[b][:], in1=x_t[b][:], op=ALU.mult)
            v.drain()
            v.reduce_sum(out=ln_v[b][:], in_=ln_sq[:], axis=AX.X)
            v.drain()
            v.tensor_scalar(out=ln_v[b][:], in0=ln_v[b][:], scalar1=1.0 / DM,
                            scalar2=EPS, op0=ALU.mult, op1=ALU.add)
            v.drain()
            v.reciprocal(out=ln_v[b][:], in_=ln_v[b][:])
            evA_stats[i] = s_ve.inc(v.drain())
        VE(a_stats)

        def a_apply(sc, i=i):
            b = i % 2
            sc.wait_ge(s_ve.h, evA_stats[i])
            if i >= 2:
                sc.wait_ge(s_pe.h, evGB[(i - 2) * 4 + 3])
            sc.activation(out=ln_v[b][:], in_=ln_v[b][:], func=AF.Sqrt)
            sc.drain()
            sc.activation(out=xn_t[b][:], in_=x_t[b][:], func=AF.Copy,
                          scale=ln_v[b][:])
            evA_apply[i] = s_sc.inc(sc.drain())
        SC(a_apply)

        for quad in range(4):
            def a_tr(te, i=i, quad=quad):
                k = gB[0]
                if i == 0 and quad == 0:
                    te.wait_ge(s_const.h, LD_CONST)
                if quad == 0:
                    te.wait_ge(s_sc.h, evA_apply[i])
                if k >= 2:
                    te.wait_ge(s_ve.h, evGBe[k - 2])
                bank = bfbank(k)
                for j in range(4):
                    cc = quad * 4 + j
                    te.transpose(bank[:, j * 128:j * 128 + 128],
                                 xn_t[i % 2][:, cc * 128:(cc + 1) * 128], idbf_sb[:])
                evGB[k] = s_pe.inc(te.drain())
                gB[0] += 1
            TE(a_tr)

            def a_ev(v, i=i, quad=quad):
                k = gB[0] - 1
                v.wait_ge(s_pe.h, evGB[k])
                bank = bfbank(k)
                cc0 = quad * 4
                if i == 0:
                    ins = v.tensor_copy(
                        out=xn_fm[:].rearrange('p (c w) -> p c w', c=CC)[
                            :, cc0:cc0 + 4, 0:3],
                        in_=bank[:, 0:512].rearrange('p (j w) -> p j w', j=4)[
                            :, :, 125:128])
                else:
                    off = 3 + (i - 1) * 128
                    ins = v.tensor_copy(
                        out=xn_fm[:].rearrange('p (c w) -> p c w', c=CC)[
                            :, cc0:cc0 + 4, off:off + 128],
                        in_=bank[:, 0:512].rearrange('p (j w) -> p j w', j=4))
                evGBe[k] = s_ve.inc(ins)
            VE(a_ev)

    A_XN = evGBe[35]
    if max_phase == 'A':
        return _early_exit([(s_ve.h, A_XN)], (esWb, esA))
    esA.close()

    if 'xn_fm' in dbg:
        def d_xn(e):
            e.wait_ge(s_ve.h, A_XN)
            s_dbg.inc(e.dma_start(out=dbg['xn_fm'], in_=xn_fm[:]), 16)
        SY(d_xn)

    esB2 = ExitStack()
    x_T = esB2.enter_context(nc.sbuf_tensor('x_T', [128, NT * D_IN], BF16))
    esB = ExitStack()

    def sbB(name, shape, dtype):
        return esB.enter_context(nc.sbuf_tensor(name, shape, dtype))

    gtile = [sbB(f'gtile{i}', [128, T], BF16) for i in range(2)]
    xbc32 = [sbB(f'xbc32{i}', [128, TW], F32) for i in range(2)]
    cacc = [sbB(f'cacc{i}', [128, T], F32) for i in range(2)]
    sil_t = [sbB(f'sil_t{i}', [128, T], BF16) for i in range(3)]
    dt_raw = sbB('dt_raw', [64, T], F32)
    sp_t1 = sbB('sp_t1', [64, T], F32)

    evB_gdone = {}
    evB_mm = {}
    evB_gsil = {}
    evB_gst = {}
    evB_cevac = {}
    evB_fma = {}
    evB_sil = {}
    evB_tq = {}
    evB_teq = {}
    evB_xtev = {}
    gate_cons = [None, None]

    def lhsT(g, cc, rows=128):
        return Wb[g % 2][:, cc * 128:cc * 128 + rows]

    def emit_wload(g):
        def gp_f(gp):
            gp.wait_ge(s_pe.h, evB_gdone[g - 2])
            evB_w[g] = (s_wb[g % 2].h, s_wb[g % 2].inc(
                gp.dma_start(out=Wb[g % 2][:],
                             in_=w_inT[g].rearrange('p c j -> p (c j)')), 16))
        GP(gp_f)

    def emit_xt_transposes(src_buf, cidx):
        for half in range(2):
            def te_f(te, half=half):
                if half == 0:
                    te.wait_ge(s_sc.h, evB_sil[cidx])
                k = gB[0]
                if k >= 2:
                    te.wait_ge(s_ve.h, evGBe[k - 2])
                bank = bfbank(k)
                for j in range(4):
                    z = half * 4 + j
                    te.transpose(bank[:, j * 128:j * 128 + 128],
                                 src_buf[:, z * 128:(z + 1) * 128], idbf_sb[:])
                evGB[k] = s_pe.inc(te.drain())
                gB[0] += 1
            TE(te_f)

            def ve_f(v, half=half):
                k = gB[0] - 1
                v.wait_ge(s_pe.h, evGB[k])
                bank = bfbank(k)
                if cidx < 32:
                    dst = x_T[:].rearrange('p (z w) -> p z w', z=NT)[
                        :, half * 4:half * 4 + 4,
                        cidx * 128:cidx * 128 + 128]
                else:
                    dst = B_T[:].rearrange('p (z w) -> p z w', z=NT)[
                        :, half * 4:half * 4 + 4, :]
                ins = v.tensor_copy(
                    out=dst, in_=bank[:, 0:512].rearrange('p (j w) -> p j w', j=4))
                evGBe[k] = s_ve.inc(ins)
                if half == 1:
                    evB_xtev[cidx] = evGBe[k]
            VE(ve_f)

    for g in range(NG):
        rc = g
        if g < 32:
            def te_f(te, g=g, rc=rc):
                te.wait_ge(*evB_w[g])
                if rc == 0:
                    te.wait_ge(s_ve.h, A_XN)
                for si, (t0, t1) in enumerate([(3, 515), (515, 1027)]):
                    if gate_cons[si] is not None:
                        te.wait_ge(s_sc.h, gate_cons[si])
                    dst = T1[:, si * 512:si * 512 + 512]
                    ins = None
                    for cc in range(CC):
                        ins = nc.tensor.matmul(
                            dst, lhsT(g, cc),
                            xn_fm[:, cc * TW + t0:cc * TW + t1],
                            start=(cc == 0), stop=(cc == CC - 1))
                    evB_mm[(rc, si)] = s_pe.inc(ins)
                evB_gdone[g] = s_pe.v
            TE(te_f)

            def sc_f(sc, rc=rc):
                for si in range(2):
                    sc.wait_ge(s_pe.h, evB_mm[(rc, si)])
                    if rc >= 2 and si == 0:
                        sc.wait_ge(*evB_gst[rc - 2])
                    src_ = T1[:, si * 512:si * 512 + 512]
                    ins = sc.activation(
                        out=gtile[rc % 2][:, si * 512:(si + 1) * 512],
                        in_=src_, func=AF.Silu)
                    gate_cons[si] = s_sc.inc(ins)
                evB_gsil[rc] = s_sc.v
            SC(sc_f)

            def sy_f(e, rc=rc):
                e.wait_ge(s_sc.h, evB_gsil[rc])
                sg_ = s_gst[rc % 2]
                evB_gst[rc] = (sg_.h, sg_.inc(
                    e.dma_start(out=g_stash.ap()[rc], in_=gtile[rc % 2][:]), 16))
            SY(sy_f)
        elif g < 66:
            cidx = g - 32
            par = cidx % 2
            spar = cidx % 3

            def te_f(te, g=g, cidx=cidx):
                te.wait_ge(*evB_w[g])
                if cidx == 0:
                    te.wait_ge(s_sc.h, gate_cons[0])
                    te.wait_ge(s_sc.h, gate_cons[1])
                if cidx > 0:
                    vv = evB_cevac[cidx - 1]
                    te.wait_ge(s_ve.h, vv[0])
                    te.wait_ge(s_sc.h, vv[1])
                ins = None
                for (t0, t1) in [(0, 512), (512, 1024), (1024, TW)]:
                    for cc in range(CC):
                        ins = nc.tensor.matmul(
                            T1[:, t0:t1], lhsT(g, cc),
                            xn_fm[:, cc * TW + t0:cc * TW + t1],
                            start=(cc == 0), stop=(cc == CC - 1))
                evB_mm[('c', cidx)] = s_pe.inc(ins)
                evB_gdone[g] = s_pe.v
            TE(te_f)

            def ve_e(v, cidx=cidx, par=par):
                v.wait_ge(s_pe.h, evB_mm[('c', cidx)])
                if cidx >= 2:
                    v.wait_ge(s_ve.h, evB_fma[cidx - 2])
                ins = v.tensor_copy(out=xbc32[par][:, 0:640], in_=T1[:, 0:640])
                evB_cevac.setdefault(cidx, [0, 0])[0] = s_ve.inc(ins)
            VE(ve_e)

            def sc_e(sc, cidx=cidx, par=par):
                sc.wait_ge(s_pe.h, evB_mm[('c', cidx)])
                if cidx >= 2:
                    sc.wait_ge(s_ve.h, evB_fma[cidx - 2])
                ins = sc.activation(out=xbc32[par][:, 640:TW],
                                    in_=T1[:, 640:TW], func=AF.Copy)
                evB_cevac[cidx][1] = s_sc.inc(ins)
            SC(sc_e)

            if cidx == 0 and 'xbcpre' in dbg:
                def d_xbc(e, cidx=cidx, par=par):
                    e.wait_ge(s_ve.h, evB_cevac[cidx][0])
                    e.wait_ge(s_sc.h, evB_cevac[cidx][1])
                    s_dbg.inc(e.dma_start(out=dbg['xbcpre'],
                                          in_=xbc32[par][:]), 16)
                SY(d_xbc)

            def ve_c(v, cidx=cidx, par=par):
                v.wait_ge(s_sc.h, evB_cevac[cidx][1])
                ins = None
                for colh in range(2):
                    c0 = colh * 512
                    for i in range(4):
                        if i == 0:
                            ins = v.tensor_scalar(
                                out=cacc[par][:, c0:c0 + 512],
                                in0=xbc32[par][:, c0:c0 + 512],
                                scalar1=cw_sb[:, cidx * 4:cidx * 4 + 1],
                                scalar2=None, op0=ALU.mult)
                        else:
                            ins = v.scalar_tensor_tensor(
                                out=cacc[par][:, c0:c0 + 512],
                                in0=xbc32[par][:, c0 + i:c0 + 512 + i],
                                scalar=cw_sb[:, cidx * 4 + i:cidx * 4 + i + 1],
                                in1=cacc[par][:, c0:c0 + 512],
                                op0=ALU.mult, op1=ALU.add)
                        v.drain()
                evB_fma[cidx] = s_ve.inc(v.drain())
            VE(ve_c)

            def sc_s(sc, cidx=cidx, par=par, spar=spar):
                sc.wait_ge(s_ve.h, evB_fma[cidx])
                if cidx < 32:
                    if cidx >= 3:
                        sc.wait_ge(s_ve.h, evB_xtev[cidx - 3])
                    dst = sil_t[spar][:]
                elif cidx == 32:
                    dst = B_fm[:]
                else:
                    dst = C_fm[:]
                ins = sc.activation(out=dst, in_=cacc[par][:, 0:T],
                                    func=AF.Silu, bias=cb_sb[:, cidx:cidx + 1])
                evB_sil[cidx] = s_sc.inc(ins)
            SC(sc_s)

            if cidx < 32:
                emit_xt_transposes(sil_t[spar], cidx)
            elif cidx == 32:
                emit_xt_transposes(B_fm, cidx)
        else:
            def te_f(te, g=g):
                te.wait_ge(*evB_w[g])
                vv = evB_cevac[33]
                te.wait_ge(s_ve.h, vv[0])
                te.wait_ge(s_sc.h, vv[1])
                for si, (t0, t1) in enumerate([(3, 515), (515, 1027)]):
                    if gate_cons[si] is not None:
                        te.wait_ge(s_sc.h, gate_cons[si])
                    dst = T1[0:64, si * 512:si * 512 + 512]
                    ins = None
                    for cc in range(CC):
                        ins = nc.tensor.matmul(
                            dst, lhsT(g, cc, rows=64),
                            xn_fm[:, cc * TW + t0:cc * TW + t1],
                            start=(cc == 0), stop=(cc == CC - 1))
                    evB_mm[('dt', si)] = s_pe.inc(ins)
                evB_gdone[g] = s_pe.v
            TE(te_f)

            def sc_f(sc):
                sc.wait_ge(s_pe.h, evB_mm[('dt', 0)])
                sc.wait_ge(s_pe.h, evB_mm[('dt', 1)])
                for si in range(2):
                    src_ = T1[0:64, si * 512:si * 512 + 512]
                    d0 = slice(si * 512, (si + 1) * 512)
                    sc.activation(out=sp_t1[:, d0], in_=src_, func=AF.Abs,
                                  bias=dtb_sb[:])
                    sc.drain()
                    sc.activation(out=sp_t1[:, d0], in_=sp_t1[:, d0],
                                  func=AF.Exp, scale=-1.0)
                    sc.drain()
                    sc.activation(out=sp_t1[:, d0], in_=sp_t1[:, d0],
                                  func=AF.Ln, bias=1.0)
                    sc.drain()
                    sc.activation(out=dt_raw[:, d0], in_=src_, func=AF.Relu,
                                  bias=dtb_sb[:])
                    sc.drain()
                evB_mm['dt_sc'] = s_sc.inc(sc.drain())
            SC(sc_f)

            def ve_f(v):
                v.wait_ge(s_sc.h, evB_mm['dt_sc'])
                v.wait_ge(s_sc.h, SC_SETUP[0])
                v.wait_ge(s_ve.h, BZERO[0])
                v.tensor_tensor(out=dt_fm[:], in0=dt_raw[:], in1=sp_t1[:],
                                op=ALU.add)
                v.drain()
                v.tensor_scalar_mul(out=est_or_dA[:], in0=dt_fm[:],
                                    scalar1=nA_sb[:])
                v.drain()
                for z in range(NT):
                    sl = slice(z * 128, (z + 1) * 128)
                    v.tensor_tensor_scan(out=grel[:, sl], data0=est_or_dA[:, sl],
                                         data1=zeros64[:], initial=0.0,
                                         op0=ALU.add, op1=ALU.add)
                    v.drain()
                evB_mm['dt_ve'] = s_ve.inc(v.drain())
            VE(ve_f)

            def sc_f2(sc):
                sc.wait_ge(s_ve.h, evB_mm['dt_ve'])
                sc.wait_ge(s_ve.h, VE_SETUP[0])
                sc.activation(out=ln_or_erel[:], in_=dt_fm[:], func=AF.Ln,
                              bias=cvals[0:64, 0:1])
                evB_mm['dt_sc2'] = s_sc.inc(sc.drain())
            SC(sc_f2)

            def ve_f2(v):
                v.wait_ge(s_sc.h, evB_mm['dt_sc2'])
                v.tensor_sub(out=sgrel[:], in0=grel[:], in1=ln_or_erel[:])
                v.drain()
                for z in range(NT):
                    sl = slice(z * 128, (z + 1) * 128)
                    v.tensor_scalar_sub(
                        out=est_or_dA[:, sl], in0=grel[:, sl],
                        scalar1=grel[:, z * 128 + 127:z * 128 + 128])
                    v.drain()
                v.reduce_sum(out=ctot_sum[:], in_=grel[:, 127::128], axis=AX.X)
                evB_mm['dt_ve2'] = s_ve.inc(v.drain())
            VE(ve_f2)

            def sc_f3(sc):
                sc.wait_ge(s_ve.h, evB_mm['dt_ve2'])
                sc.activation(out=est_or_dA[:], in_=est_or_dA[:], func=AF.Exp,
                              scale=-1.0)
                sc.drain()
                sc.activation(out=exp_ctot[:], in_=grel[:, 127::128],
                              func=AF.Exp)
                sc.drain()
                sc.activation(out=ln_or_erel[:], in_=grel[:], func=AF.Exp)
                evB_mm['dt_sc3'] = s_sc.inc(sc.drain())
            SC(sc_f3)

            def ve_f3(v):
                v.wait_ge(s_sc.h, evB_mm['dt_sc3'])
                v.tensor_tensor(out=dt_fm[:], in0=dt_fm[:], in1=est_or_dA[:],
                                op=ALU.mult)
                evB_mm['dt_ve3'] = s_ve.inc(v.drain())
            VE(ve_f3)

            for z in range(NT):
                def te_tt(te, z=z):
                    if z == 0:
                        te.wait_ge(s_ve.h, evB_mm['dt_ve3'])
                    k = gF[0]
                    if k >= 2:
                        te.wait_ge(s_ve.h, evGFe[k - 2])
                    bank = fpbank(k)
                    for jj, srcbuf in enumerate((grel, sgrel, dt_fm, ln_or_erel)):
                        te.transpose(
                            bank[:, jj * 128:jj * 128 + 64],
                            srcbuf[0:64, z * 128:(z + 1) * 128],
                            idf_sb[0:64, 0:64])
                    evGF[k] = s_pe.inc(te.drain())
                    gF[0] += 1
                TE(te_tt)

                def ve_tt(v, z=z):
                    k = gF[0] - 1
                    v.wait_ge(s_pe.h, evGF[k])
                    bank = fpbank(k)
                    ins = None
                    for jj, dstbuf in enumerate((grel_T, sgrel_T, dtE_T, erel_T)):
                        ins = v.tensor_copy(
                            out=dstbuf[:, z * 64:(z + 1) * 64],
                            in_=bank[:, jj * 128:jj * 128 + 64])
                        v.drain()
                    evGFe[k] = s_ve.inc(v.drain())
                    evB_mm['dect'] = s_ve.v
                VE(ve_tt)
        if g + 2 < NG:
            emit_wload(g + 2)

    B_XT = evB_xtev[32]
    B_DECT = evB_mm['dect']

    if 'bfm' in dbg:
        def d_bfm(e):
            e.wait_ge(s_sc.h, evB_sil[32])
            s_dbg.inc(e.dma_start(out=dbg['bfm'], in_=B_fm[:]), 16)
            e.wait_ge(s_sc.h, evB_sil[33])
            s_dbg.inc(e.dma_start(out=dbg['cfm'], in_=C_fm[:]), 16)
        SY(d_bfm)
    if 'dtf' in dbg:
        def d_dtf(e):
            e.wait_ge(s_ve.h, evB_mm['dt_ve'])
            s_dbg.inc(e.dma_start(out=dbg['dtf'], in_=dt_raw[:]), 16)
            s_dbg.inc(e.dma_start(out=dbg['grel'], in_=grel[:]), 16)
        SY(d_dtf)
    if 'xT' in dbg:
        def d_xt(e):
            e.wait_ge(s_ve.h, B_XT)
            for z in range(NT):
                s_dbg.inc(e.dma_start(out=dbg['xT'][z * 128:(z + 1) * 128, :],
                                      in_=x_T[:, z * D_IN:(z + 1) * D_IN]), 16)
        SY(d_xt)

    evB_xtst = {}

    def b_xt_stash(e):
        e.wait_ge(s_ve.h, B_XT)
        for z in range(NT):
            evB_xtst[z] = s_xtst.inc(
                e.dma_start(out=xT_stash.ap()[z],
                            in_=x_T[:, z * D_IN:(z + 1) * D_IN]), 16)
    SY(b_xt_stash)

    esB3 = ExitStack()
    xd_buf = esB3.enter_context(nc.sbuf_tensor('xd_buf', [128, D_IN], BF16))
    SW = {}

    def b_walk_init(v):
        v.wait_ge(s_ve.h, B_XT)
        v.wait_ge(s_ve.h, B_DECT)
        v.memset(S_run[:], 0.0)
        SW['init'] = s_ve.inc(v.drain())
    VE(b_walk_init)

    for z in range(NT):
        def ve_xd(v, z=z):
            if z == 0:
                v.wait_ge(s_ve.h, SW['init'])
            else:
                v.wait_ge(s_pe.h, SW[('mm', z - 1, 7)])
            ins = v.tensor_tensor(
                out=xd_buf[:].rearrange('p (h q) -> p h q', h=64),
                in0=x_T[:, z * D_IN:(z + 1) * D_IN].rearrange(
                    'p (h q) -> p h q', h=64),
                in1=dtE_T[:, z * 64:(z + 1) * 64].unsqueeze(2).broadcast_to(
                    [128, 64, 64]),
                op=ALU.mult)
            SW[('xd', z)] = s_ve.inc(ins)
        VE(ve_xd)

        def te_dec(te, z=z):
            te.wait_ge(s_ve.h, SW[('xd', z)])
            k = gF[0]
            if k >= 2:
                te.wait_ge(s_ve.h, evGFe[k - 2])
            te.transpose(fpbank(k)[:, 0:64],
                         exp_ctot[0:64, z:z + 1].broadcast_to([64, 128]),
                         idf_sb[0:64, 0:64])
            evGF[k] = s_pe.inc(te.drain())
            gF[0] += 1
        TE(te_dec)

        def ve_dec(v, z=z):
            k = gF[0] - 1
            v.wait_ge(s_pe.h, evGF[k])
            evGFe[k] = s_ve.inc(v.tensor_copy(
                out=dec_row[:, 0:64], in_=fpbank(k)[:, 0:64]))
            SW[('dec', z)] = evGFe[k]
        VE(ve_dec)

        for cg in range(8):
            def te_st(te, z=z, cg=cg):
                te.wait_ge(s_ve.h, SW[('xd', z)])
                if z == 0 and cg < 2:
                    te.wait_ge(s_sc.h, evB_mm['dt_sc'])
                if cg >= 2:
                    te.wait_ge(s_ve.h, SW[('upd', z, cg - 2)])
                dst = T1[:, (cg % 2) * 512:(cg % 2) * 512 + 512]
                ins = nc.tensor.matmul(
                    dst, B_T[:, z * 128:(z + 1) * 128],
                    xd_buf[:, cg * 512:(cg + 1) * 512], start=True, stop=True)
                SW[('mm', z, cg)] = s_pe.inc(ins)
            TE(te_st)

            def ve_upd(v, z=z, cg=cg):
                v.wait_ge(s_pe.h, SW[('mm', z, cg)])
                v.wait_ge(s_ve.h, SW[('dec', z)])
                sl = slice(cg * 512, (cg + 1) * 512)
                v.tensor_tensor(
                    out=S_run[:, sl].rearrange('p (h q) -> p h q', h=8),
                    in0=S_run[:, sl].rearrange('p (h q) -> p h q', h=8),
                    in1=dec_row[:, cg * 8:(cg + 1) * 8].unsqueeze(2).broadcast_to(
                        [128, 8, 64]),
                    op=ALU.mult)
                v.drain()
                ins = v.tensor_tensor(
                    out=S_run[:, sl], in0=S_run[:, sl],
                    in1=T1[:, (cg % 2) * 512:(cg % 2) * 512 + 512], op=ALU.add)
                SW[('upd', z, cg)] = s_ve.inc(ins)
            VE(ve_upd)

    B_SLOCAL = SW[('upd', NT - 1, 7)]

    if 'slocal' in dbg:
        def d_sl(e):
            e.wait_ge(s_ve.h, B_SLOCAL)
            s_dbg.inc(e.dma_start(out=dbg['slocal'], in_=S_run[:]), 16)
        SY(d_sl)

    evB_exp = {}

    def b_export(e):
        e.wait_ge(s_ve.h, B_SLOCAL)
        e.wait_ge(s_ve.h, BZERO[0])
        s_exp.inc(e.dma_start(out=cc_in.ap()[:, 0:D_IN], in_=S_run[:]), 16)
        s_exp.inc(
            e.dma_start(out=cc_in.ap()[0:64, D_IN:D_IN + 1], in_=ctot_sum[:],
                        allow_slow_non_contiguous=True), 16)
        evB_exp['done'] = s_exp.inc(
            e.dma_start(out=cc_in.ap()[64:128, D_IN:D_IN + 1],
                        in_=zeros64[0:64, 0:1],
                        allow_slow_non_contiguous=True), 16)
    SY(b_export)

    if max_phase == 'B':
        return _early_exit([(s_exp.h, evB_exp['done']), (s_xtst.h, s_xtst.v),
                            (s_gst[0].h, s_gst[0].v), (s_gst[1].h, s_gst[1].v)],
                           (esWb, esB2, esB, esB3))
    esB3.close()
    esB.close()
    esB2.close()
    esWb.close()

    esD = ExitStack()

    def sbD(name, shape, dtype):
        return esD.enter_context(nc.sbuf_tensor(name, shape, dtype))

    gs_sb = sbD('gs_sb', [128, IC], F32)
    us_sb = sbD('us_sb', [128, IC], F32)
    ds_sb = sbD('ds_sb', [128, DM], F32)
    Wg8 = [sbD(f'Wg8{i}', [128, CC * 128], FP8) for i in range(2)]
    Wu8 = [sbD(f'Wu8{i}', [128, CC * 128], FP8) for i in range(2)]
    gsil_t = sbD('gsil_t', [128, 512], F32)
    fuse_t = [sbD(f'fuse_t{i}', [128, T], BF16) for i in range(2)]

    evD_w = {}
    evD_done = {}
    evD_mm = {}
    evD_sil = {}
    evD_fuse = {}
    evD_fst = {}
    last_fuse = [None]

    D_CONST = [0]

    def d_consts(e):
        e.wait_ge(s_xtst.h, s_xtst.v)
        e.wait_ge(s_ve.h, B_SLOCAL)
        e.wait_ge(s_gst[0].h, s_gst[0].v)
        e.wait_ge(s_gst[1].h, s_gst[1].v)
        e.wait_ge(s_sc.h, evB_mm['dt_sc3'])
        e.wait_ge(s_exp.h, evB_exp['done'])
        for dst, srcap in [(gs_sb, gs_t), (us_sb, us_t), (ds_sb, ds_b)]:
            s_dc.inc(e.dma_start(out=dst[:], in_=srcap), 16)
        D_CONST[0] = s_dc.v
    SY(d_consts)

    def d_w01(gp):
        gp.wait_ge(s_xtst.h, s_xtst.v)
        gp.wait_ge(s_ve.h, B_SLOCAL)
        gp.wait_ge(s_exp.h, evB_exp['done'])
        for ic in range(2):
            sw = s_wgu[ic % 2]
            sw.inc(gp.dma_start(out=Wg8[ic % 2][:],
                                in_=w_g[ic].rearrange('p c j -> p (c j)')), 16)
            evD_w[ic] = (sw.h, sw.inc(
                gp.dma_start(out=Wu8[ic % 2][:],
                             in_=w_u[ic].rearrange('p c j -> p (c j)')), 16))
    GP(d_w01)

    for ic in range(IC):
        if ic >= 2:
            def gp_w(gp, ic=ic):
                gp.wait_ge(s_pe.h, evD_done[ic - 2])
                sw = s_wgu[ic % 2]
                sw.inc(gp.dma_start(
                    out=Wg8[ic % 2][:],
                    in_=w_g[ic].rearrange('p c j -> p (c j)')), 16)
                evD_w[ic] = (sw.h, sw.inc(gp.dma_start(
                    out=Wu8[ic % 2][:],
                    in_=w_u[ic].rearrange('p c j -> p (c j)')), 16))
            GP(gp_w)

        for th in range(2):
            def te_f(te, ic=ic, th=th):
                if ic == 0 and th == 0:
                    te.wait_ge(s_ve.h, A_XN)
                    te.wait_ge(s_sc.h, evB_mm['dt_sc'])
                    te.wait_ge(s_sc.h, evB_gsil[31])
                    te.wait_ge(s_ve.h, B_SLOCAL)
                if th == 0:
                    te.wait_ge(*evD_w[ic])
                if ic >= 1:
                    te.wait_ge(s_ve.h, evD_fuse[(ic - 1, th)])
                t0 = 3 + th * 512
                regs = ([(T1, 0), (T1, 512)] if th == 0
                        else [(T1, 1024), (T2, 0)])
                ins = None
                for Wt, (reg, off) in [(Wg8, regs[0]), (Wu8, regs[1])]:
                    dst = reg[:, off:off + 512]
                    for cc in range(CC):
                        ins = nc.tensor.matmul(
                            dst, Wt[ic % 2][:, cc * 128:(cc + 1) * 128],
                            xn_fm[:, cc * TW + t0:cc * TW + t0 + 512],
                            start=(cc == 0), stop=(cc == CC - 1))
                evD_mm[(ic, th)] = s_pe.inc(ins)
                if th == 1:
                    evD_done[ic] = s_pe.v
            TE(te_f)

            def sc_f(sc, ic=ic, th=th):
                sc.wait_ge(s_pe.h, evD_mm[(ic, th)])
                if ic == 0 and th == 0:
                    sc.wait_ge(s_dc.h, D_CONST[0])
                if last_fuse[0] is not None:
                    sc.wait_ge(s_ve.h, last_fuse[0])
                greg, goff = (T1, 0) if th == 0 else (T1, 1024)
                ins = sc.activation(out=gsil_t[:], in_=greg[:, goff:goff + 512],
                                    func=AF.Silu, scale=gs_sb[:, ic:ic + 1])
                evD_sil[(ic, th)] = s_sc.inc(ins)
            SC(sc_f)

            def ve_f(v, ic=ic, th=th):
                v.wait_ge(s_sc.h, evD_sil[(ic, th)])
                if ic >= 2:
                    v.wait_ge(*evD_fst[ic - 2])
                ureg, uoff = (T1, 512) if th == 0 else (T2, 0)
                ins = v.scalar_tensor_tensor(
                    out=fuse_t[ic % 2][:, th * 512:(th + 1) * 512],
                    in0=ureg[:, uoff:uoff + 512],
                    scalar=us_sb[:, ic:ic + 1],
                    in1=gsil_t[:], op0=ALU.mult, op1=ALU.mult)
                evD_fuse[(ic, th)] = s_ve.inc(ins)
                last_fuse[0] = evD_fuse[(ic, th)]
            VE(ve_f)

        def sy_f(e, ic=ic):
            e.wait_ge(s_ve.h, evD_fuse[(ic, 1)])
            sf = s_fst[ic % 2]
            evD_fst[ic] = (sf.h, sf.inc(
                e.dma_start(out=fuse_stash.ap()[ic], in_=fuse_t[ic % 2][:]), 16))
        SY(sy_f)

    Wd8 = [sbD(f'Wd8{i}', [128, 8 * 512], FP8) for i in range(2)]
    fuT = [sbD(f'fuT{i}', [128, 8 * 128], BF16) for i in range(2)]
    dn_t = sbD('dn_t', [128, 512], BF16)

    evD_dw = {}
    evD_dfu = {}
    evD_dmm = {}
    evD_dev = {}
    evD_dst = {}
    dn_pass = [(tbh, osq) for tbh in range(2) for osq in range(4)]
    fu_k = [0]
    dn_prev = [None]

    def dreg(tb4):
        return (T1[:, tb4 * 512:tb4 * 512 + 512] if tb4 < 3
                else T2[:, 0:512])

    def d_dw01(gp):
        for k12 in range(2):
            pi, sg = divmod(k12, 6)
            tbh, osq = dn_pass[pi]
            sw = s_wd[k12 % 2]
            evD_dw[k12] = (sw.h, sw.inc(gp.dma_start(
                out=Wd8[k12 % 2][:].rearrange('p (a b) -> p a b', a=8),
                in_=w_d[sg * 8:(sg + 1) * 8, :,
                        osq * 512:(osq + 1) * 512].transpose([1, 0, 2])), 16))
    GP(d_dw01)

    for pi, (tbh, osq) in enumerate(dn_pass):
        for sg in range(6):
            k12 = pi * 6 + sg
            if k12 >= 2:
                def gp_w(gp, k12=k12):
                    pi2, sg2 = divmod(k12, 6)
                    tbh2, osq2 = dn_pass[pi2]
                    gp.wait_ge(s_pe.h, evD_dmm[('sgdone', k12 - 2)])
                    sw = s_wd[k12 % 2]
                    evD_dw[k12] = (sw.h, sw.inc(gp.dma_start(
                        out=Wd8[k12 % 2][:].rearrange('p (a b) -> p a b', a=8),
                        in_=w_d[sg2 * 8:(sg2 + 1) * 8, :,
                                osq2 * 512:(osq2 + 1) * 512].transpose(
                            [1, 0, 2])), 16))
                GP(gp_w)

            for tb4 in range(4):
                tb = tbh * 4 + tb4
                k = fu_k[0]

                def sy_fu(e, pi=pi, sg=sg, tb4=tb4, tb=tb, k=k):
                    if k >= 2:
                        e.wait_ge(s_pe.h, evD_dmm[('fu', k - 2)])
                    if k == 0:
                        e.wait_ge(s_fst[0].h, s_fst[0].v)
                        e.wait_ge(s_fst[1].h, s_fst[1].v)
                    sf = s_fu[k % 2]
                    evD_dfu[k] = (sf.h, sf.inc(e.dma_start(
                        out=fuT[k % 2][:].rearrange('p (a b) -> p a b', a=8),
                        in_=fuse_stash.ap()[sg * 8:(sg + 1) * 8, :,
                                            tb * 128:(tb + 1) * 128].transpose(
                            [1, 0, 2])), 16))
                SY(sy_fu)

                def te_f(te, pi=pi, sg=sg, tb4=tb4, k=k, k12=k12):
                    te.wait_ge(*evD_dfu[k])
                    if tb4 == 0:
                        te.wait_ge(*evD_dw[k12])
                    if pi == 0 and sg == 0 and tb4 == 0:
                        te.wait_ge(s_ve.h, last_fuse[0])
                        te.wait_ge(s_ve.h, B_SLOCAL)
                    if pi >= 1 and sg == 0:
                        te.wait_ge(s_dst.h, evD_dst[(pi - 1, tb4)])
                    dst = dreg(tb4)
                    ins = None
                    for j in range(8):
                        ins = nc.tensor.matmul(
                            dst, fuT[k % 2][:, j * 128:(j + 1) * 128],
                            Wd8[k12 % 2][:, j * 512:(j + 1) * 512],
                            start=(sg == 0 and j == 0),
                            stop=(sg == 5 and j == 7))
                    evD_dmm[(pi, tb4)] = s_pe.inc(ins)
                    evD_dmm[('fu', k)] = s_pe.v
                    if tb4 == 3:
                        evD_dmm[('sgdone', k12)] = s_pe.v
                TE(te_f)
                fu_k[0] += 1

        for tb4 in range(4):
            tb = tbh * 4 + tb4

            def ve_f(v, pi=pi, osq=osq, tb4=tb4, tb=tb):
                v.wait_ge(s_pe.h, evD_dmm[(pi, tb4)])
                v.wait_ge(s_dc.h, D_CONST[0])
                if dn_prev[0] is not None:
                    v.wait_ge(s_dst.h, evD_dst[dn_prev[0]])
                o0 = osq * 512
                ins = v.tensor_tensor(out=dn_t[:], in0=dreg(tb4),
                                      in1=ds_sb[:, o0:o0 + 512], op=ALU.mult)
                evD_dev[(pi, tb4)] = s_ve.inc(ins)
            VE(ve_f)

            def sy_s(e, pi=pi, osq=osq, tb4=tb4, tb=tb):
                e.wait_ge(s_ve.h, evD_dev[(pi, tb4)])
                o0 = osq * 512
                evD_dst[(pi, tb4)] = s_dst.inc(e.dma_start(
                    out=ffn_stash.ap()[tb][:, o0:o0 + 512], in_=dn_t[:]), 16)
            SY(sy_s)
            dn_prev[0] = (pi, tb4)

    D_EV_ALL = [None]

    def d_fin(v):
        v.wait_ge(s_dst.h, evD_dst[(7, 3)])
        D_EV_ALL[0] = s_ve.inc(v.drain())
    VE(d_fin)

    def gp_cc(gp):
        gp.wait_ge(s_exp.h, evB_exp['done'])
        s_cc.inc(gp.collective_compute(
            'AllGather', ALU.bypass, replica_groups=[list(range(NCORES))],
            ins=[cc_in.ap().opt()], outs=[cc_out.ap().opt()]))
    GP(gp_cc)
    CC_DONE = s_cc.v

    if max_phase == 'D':
        return _early_exit([(s_ve.h, D_EV_ALL[0]), (s_cc.h, CC_DONE)], (esD,))
    esD.close()

    esE = ExitStack()

    def sbE(name, shape, dtype):
        return esE.enter_context(nc.sbuf_tensor(name, shape, dtype))

    mask_sb = sbE('mask_sb', [128, 128], BF16)
    drep_bf = sbE('drep_bf', [128, 64], BF16)
    nw_sb = sbE('nw_sb', [128, 32], F32)
    onebf_sb = sbE('onebf_sb', [128, 1], BF16)
    onef_sb = sbE('onef_sb', [1, 128], F32)
    mprev_sb = sbE('mprev_sb', [128, 8], F32)
    Sj = [sbE('Sj0', [128, D_IN], F32)] * 2
    dAj = sbE('dAj', [64, 1], F32)
    dj = sbE('dj', [64, 1], F32)
    S_run_bf = sbE('S_run_bf', [128, D_IN], BF16)

    evE = {}

    E_CONST = [0]

    def e_consts(e):
        e.wait_ge(s_ve.h, D_EV_ALL[0])
        for dst, srcap in [(mask_sb, mask_ge), (nw_sb, norm_w_t),
                           (onebf_sb, ones_bf), (onef_sb, ones_f),
                           (mprev_sb, mask_prev)]:
            s_ec.inc(e.dma_start(out=dst[:], in_=srcap), 16)
        E_CONST[0] = s_ec.v
    SY(e_consts)

    E_DREP = [0]

    def e_drep(gp):
        gp.wait_ge(s_ve.h, D_EV_ALL[0])
        E_DREP[0] = s_dr.inc(gp.dma_start(out=drep_bf[:], in_=d_rep), 16)
    GP(e_drep)

    def e_sin_zero(v):
        v.wait_ge(s_exp.h, evB_exp['done'])
        v.memset(S_run[:], 0.0)
        evE['zero'] = s_ve.inc(v.drain())
    VE(e_sin_zero)

    for j in range(NCORES):
        def e_loads(e, j=j):
            if j == 0:
                e.wait_ge(s_cc.h, CC_DONE)
                e.wait_ge(s_ve.h, D_EV_ALL[0])
            if j >= 1:
                e.wait_ge(s_ve.h, evE[('sacc', j - 1)])
            s_ej.inc(
                e.dma_start(out=Sj[j % 2][:],
                            in_=cc_out.ap()[j * 128:(j + 1) * 128, 0:D_IN]), 16)
            both = s_ej.inc(
                e.dma_start(out=dAj[:],
                            in_=cc_out.ap()[j * 128:j * 128 + 64,
                                            D_IN:D_IN + 1],
                            allow_slow_non_contiguous=True), 16)
            evE[('sj', j)] = (s_ej.h, both)
            evE[('dAj', j)] = (s_ej.h, both)
        SY(e_loads)

        def e_dj(sc, j=j):
            sc.wait_ge(*evE[('dAj', j)])
            if j == 0:
                sc.wait_ge(s_ec.h, E_CONST[0])
            if j >= 1:
                sc.wait_ge(s_ve.h, evE[('sacc', j - 1)])
            ins = sc.activation(out=dj[:], in_=dAj[:], func=AF.Exp,
                                scale=mprev_sb[0:64, j:j + 1])
            evE[('dj', j)] = s_sc.inc(ins)
        SC(e_dj)

        def te_dj(te, j=j):
            te.wait_ge(s_sc.h, evE[('dj', j)])
            if j == 0:
                te.wait_ge(s_ve.h, D_EV_ALL[0])
            k = gF[0]
            if k >= 2:
                te.wait_ge(s_ve.h, evGFe[k - 2])
            te.transpose(fpbank(k)[:, 0:64],
                         dj[0:64, 0:1].broadcast_to([64, 128]),
                         idf_sb[0:64, 0:64])
            evGF[k] = s_pe.inc(te.drain())
            gF[0] += 1
        TE(te_dj)

        def ve_dj(v, j=j):
            k = gF[0] - 1
            if j == 0:
                v.wait_ge(s_ec.h, E_CONST[0])
            v.wait_ge(s_pe.h, evGF[k])
            evGFe[k] = s_ve.inc(v.tensor_copy(
                out=dec_row[:, 0:64], in_=fpbank(k)[:, 0:64]))
            v.drain()
            v.wait_ge(*evE[('sj', j)])
            if j == 0:
                v.wait_ge(s_ve.h, evE['zero'])
            ins = None
            for cg in range(8):
                sl = slice(cg * 512, (cg + 1) * 512)
                v.tensor_tensor(
                    out=S_run[:, sl].rearrange('p (h q) -> p h q', h=8),
                    in0=S_run[:, sl].rearrange('p (h q) -> p h q', h=8),
                    in1=dec_row[:, cg * 8:(cg + 1) * 8].unsqueeze(2).broadcast_to(
                        [128, 8, 64]),
                    op=ALU.mult)
                v.drain()
                ins = v.scalar_tensor_tensor(
                    out=S_run[:, sl], in0=Sj[j % 2][:, sl],
                    scalar=mprev_sb[:, j:j + 1], in1=S_run[:, sl],
                    op0=ALU.mult, op1=ALU.add)
                v.drain()
            evE[('sacc', j)] = s_ve.inc(v.drain())
        VE(ve_dj)

    E_SIN = evE[('sacc', NCORES - 1)]

    if 'sin' in dbg:
        def d_sin(e):
            e.wait_ge(s_ve.h, E_SIN)
            s_dbg.inc(e.dma_start(out=dbg['sin'], in_=S_run[:]), 16)
        SY(d_sin)

    def e_sbf(v):
        v.wait_ge(s_ve.h, E_SIN)
        evE['sbf'] = s_ve.inc(v.tensor_copy(out=S_run_bf[:], in_=S_run[:]))
    VE(e_sbf)

    xTc = [sbE(f'xTc{i}', [128, D_IN], BF16) for i in range(2)]
    gsz = [sbE(f'gsz{i}', [128, DC * 128], BF16) for i in range(2)]
    G_m = sbE('G_m', [128, 128], BF16)
    diffc = sbE('diffc', [128, 8 * 128], F32)
    diffe = sbE('diffe', [128, 8 * 128], BF16)
    M_sb = sbE('M_sb', [128, HEADS * 128], BF16)
    ytmp = sbE('ytmp', [128, 512], F32)
    dterm = sbE('dterm', [128, D_IN], BF16)
    y_tok = sbE('y_tok', [128, D_IN], BF16)
    y_fm = sbE('y_fm', [128, DC * 128], BF16)
    hgn_c = sbE('hgn_c', [128, DC * 128], BF16)
    rstd = sbE('rstd', [1, 128], F32)
    xd_E = sbE('xd_E', [128, D_IN], BF16)

    evF = {}

    def e_pre_loads(e):
        e.wait_ge(s_ve.h, D_EV_ALL[0])
        e.wait_ge(s_xtst.h, s_xtst.v)
        e.wait_ge(s_gst[0].h, s_gst[0].v)
        e.wait_ge(s_gst[1].h, s_gst[1].v)
        s_ez[0].inc(e.dma_start(out=xTc[0][:], in_=xT_stash.ap()[0]), 16)
        both = s_ez[0].inc(
            e.dma_start(out=gsz[0][:].rearrange('p (a b) -> p a b', a=DC),
                        in_=g_stash.ap()[:, :, 0:128].transpose([1, 0, 2])), 16)
        evF[('xt', 0)] = (s_ez[0].h, both)
        evF[('gs', 0)] = (s_ez[0].h, both)
    SY(e_pre_loads)

    for z in range(NT):
        def e_next_loads(e, z=z):
            if z + 1 < NT:
                if z >= 1:
                    e.wait_ge(s_ve.h, evF[('hg', z - 1)])
                    e.wait_ge(s_pe.h, evF[('ydiag', z - 1, 7)])
                sz_ = s_ez[(z + 1) % 2]
                sz_.inc(e.dma_start(out=xTc[(z + 1) % 2][:],
                                    in_=xT_stash.ap()[z + 1]), 16)
                both = sz_.inc(
                    e.dma_start(
                        out=gsz[(z + 1) % 2][:].rearrange('p (a b) -> p a b', a=DC),
                        in_=g_stash.ap()[:, :, (z + 1) * 128:(z + 2) * 128].transpose(
                            [1, 0, 2])), 16)
                evF[('xt', z + 1)] = (sz_.h, both)
                evF[('gs', z + 1)] = (sz_.h, both)
        SY(e_next_loads)

        def gp_dt(gp, z=z):
            gp.wait_ge(*evF[('xt', z)])
            if z == 0:
                gp.wait_ge(s_dr.h, E_DREP[0])
            if z >= 1:
                gp.wait_ge(s_ve.h, evF[('yfin', z - 1, 7)])
            ins = gp.tensor_tensor(
                out=dterm[:].rearrange('p (h q) -> p h q', h=64),
                in0=xTc[z % 2][:].rearrange('p (h q) -> p h q', h=64),
                in1=drep_bf[0:128, 0:64].unsqueeze(2).broadcast_to([128, 64, 64]),
                op=ALU.mult)
            evF[('dterm', z)] = s_gp.inc(ins)
        GP(gp_dt)

        def te_g(te, z=z):
            if z == 0:
                te.wait_ge(s_ve.h, evE['sbf'])
                te.wait_ge(s_ve.h, D_EV_ALL[0])
            else:
                te.wait_ge(s_ve.h, evF[('gm', z - 1)])
                te.wait_ge(s_ve.h, evF[('yoffs', z - 1, 7)])
            ins = nc.tensor.matmul(
                T1[:, 0:128], B_fm[:, z * 128:(z + 1) * 128],
                C_fm[:, z * 128:(z + 1) * 128], start=True, stop=True)
            evF[('gmm', z)] = s_pe.inc(ins)
        TE(te_g)

        def ve_g(v, z=z):
            v.wait_ge(s_pe.h, evF[('gmm', z)])
            if z == 0:
                v.wait_ge(s_ec.h, E_CONST[0])
            if z >= 1:
                v.wait_ge(s_ve.h, evF[('m', z - 1, 7)])
            ins = v.tensor_tensor(out=G_m[:], in0=T1[:, 0:128],
                                  in1=mask_sb[:], op=ALU.mult)
            evF[('gm', z)] = s_ve.inc(ins)
        VE(ve_g)

        for hb in range(8):
            for sub in range(2):
                def te_d(te, z=z, hb=hb, sub=sub):
                    if z == 0 and hb == 0 and sub == 0:
                        te.wait_ge(s_ve.h, D_EV_ALL[0])
                    k = gF[0]
                    if k >= 2:
                        te.wait_ge(s_ve.h, evGFe[k - 2])
                    bank = fpbank(k)
                    for jj in range(4):
                        j = sub * 4 + jj
                        h = hb * 8 + j
                        te.transpose(
                            bank[:, jj * 128:jj * 128 + 128],
                            grel_T[:, z * 64 + h:z * 64 + h + 1].broadcast_to(
                                [128, 128]),
                            idf_sb[:])
                    evGF[k] = s_pe.inc(te.drain())
                    gF[0] += 1
                TE(te_d)

                def ve_d(v, z=z, hb=hb, sub=sub):
                    k = gF[0] - 1
                    if sub == 0:
                        prev = evF.get(('diffe', z, hb - 1)) if hb > 0 else (
                            evF.get(('diffe', z - 1, 7)) if z > 0 else None)
                        if prev is not None:
                            v.wait_ge(s_sc.h, prev)
                    v.wait_ge(s_pe.h, evGF[k])
                    bank = fpbank(k)
                    ins = None
                    for jj in range(4):
                        j = sub * 4 + jj
                        h = hb * 8 + j
                        ins = v.tensor_scalar(
                            out=diffc[:, j * 128:(j + 1) * 128],
                            in0=bank[:, jj * 128:jj * 128 + 128],
                            scalar1=sgrel_T[:, z * 64 + h:z * 64 + h + 1],
                            scalar2=30.0, op0=ALU.subtract, op1=ALU.min)
                        v.drain()
                    evGFe[k] = s_ve.inc(v.drain())
                    evF[('diffc', z, hb)] = s_ve.v
                VE(ve_d)

            def sc_d(sc, z=z, hb=hb):
                sc.wait_ge(s_ve.h, evF[('diffc', z, hb)])
                prevm = evF.get(('m', z, hb - 1)) if hb > 0 else (
                    evF.get(('m', z - 1, 7)) if z > 0 else None)
                if prevm is not None:
                    sc.wait_ge(s_ve.h, prevm)
                ins = sc.activation(out=diffe[:], in_=diffc[:], func=AF.Exp)
                evF[('diffe', z, hb)] = s_sc.inc(ins)
            SC(sc_d)

            def ve_m(v, z=z, hb=hb):
                v.wait_ge(s_sc.h, evF[('diffe', z, hb)])
                v.wait_ge(s_ve.h, evF[('gm', z)])
                if z >= 1 and hb == 0:
                    v.wait_ge(s_pe.h, evF[('ydiag', z - 1, 7)])
                ins = v.tensor_tensor(
                    out=M_sb[:, hb * 1024:(hb + 1) * 1024].rearrange(
                        'p (a b) -> p a b', a=8),
                    in0=diffe[:].rearrange('p (a b) -> p a b', a=8),
                    in1=G_m[:].unsqueeze(1).broadcast_to([128, 8, 128]),
                    op=ALU.mult)
                evF[('m', z, hb)] = s_ve.inc(ins)
            VE(ve_m)

        for cg in range(8):
            def te_y(te, z=z, cg=cg):
                offb, diagb = ybanks(cg)
                if cg == 0:
                    te.wait_ge(*evF[('xt', z)])
                    te.wait_ge(s_ve.h, evF[('gm', z)])
                    if z >= 1:
                        te.wait_ge(s_ve.h, evF[('supd', z - 1, 7)])
                        te.wait_ge(s_ve.h, evF[('hgn', z - 1)])
                if cg == 1 and z >= 1:
                    te.wait_ge(s_ve.h, evF[('supd', z - 1, 7)])
                te.wait_ge(s_ve.h, evF[('m', z, cg)])
                if cg >= 2:
                    te.wait_ge(s_ve.h, evF[('yoffs', z, cg - 2)])
                    te.wait_ge(s_ve.h, evF[('yfin', z, cg - 2)])
                ins = nc.tensor.matmul(
                    offb, C_fm[:, z * 128:(z + 1) * 128],
                    S_run_bf[:, cg * 512:(cg + 1) * 512], start=True, stop=True)
                evF[('yoff', z, cg)] = s_pe.inc(ins)
                ins = None
                for j in range(8):
                    h = cg * 8 + j
                    ins = nc.tensor.matmul(
                        diagb[:, j * 64:j * 64 + 64],
                        M_sb[:, h * 128:(h + 1) * 128],
                        xTc[z % 2][:, h * 64:(h + 1) * 64],
                        start=True, stop=True)
                    if j == 0 and cg <= 1:
                        pass
                evF[('ydiag', z, cg)] = s_pe.inc(ins)
            TE(te_y)

            def ve_y(v, z=z, cg=cg):
                offb, diagb = ybanks(cg)
                v.wait_ge(s_pe.h, evF[('yoff', z, cg)])
                if cg == 0:
                    v.wait_ge(s_gp.h, evF[('dterm', z)])
                v.tensor_tensor(
                    out=ytmp[:].rearrange('p (a b) -> p a b', a=8),
                    in0=offb.rearrange('p (a b) -> p a b', a=8),
                    in1=erel_T[:, z * 64 + cg * 8:z * 64 + (cg + 1) * 8].unsqueeze(
                        2).broadcast_to([128, 8, 64]),
                    op=ALU.mult)
                evF[('yoffs', z, cg)] = s_ve.inc(v.drain())
                v.tensor_tensor(out=ytmp[:], in0=ytmp[:],
                                in1=dterm[:, cg * 512:(cg + 1) * 512], op=ALU.add)
                v.drain()
                v.wait_ge(s_pe.h, evF[('ydiag', z, cg)])
                ins = v.tensor_tensor(out=y_tok[:, cg * 512:(cg + 1) * 512],
                                      in0=diagb, in1=ytmp[:],
                                      op=ALU.add)
                evF[('yfin', z, cg)] = s_ve.inc(ins)
            VE(ve_y)

        def gp_xd2(gp, z=z):
            gp.wait_ge(s_ve.h, evF[('yfin', z, 7)])
            ins = gp.tensor_tensor(
                out=xd_E[:].rearrange('p (h q) -> p h q', h=64),
                in0=xTc[z % 2][:].rearrange('p (h q) -> p h q', h=64),
                in1=dtE_T[:, z * 64:(z + 1) * 64].unsqueeze(2).broadcast_to(
                    [128, 64, 64]),
                op=ALU.mult)
            evF[('xd', z)] = s_gp.inc(ins)
        GP(gp_xd2)

        def te_dec2(te, z=z):
            te.wait_ge(s_gp.h, evF[('xd', z)])
            k = gF[0]
            if k >= 2:
                te.wait_ge(s_ve.h, evGFe[k - 2])
            te.transpose(fpbank(k)[:, 0:64],
                         exp_ctot[0:64, z:z + 1].broadcast_to([64, 128]),
                         idf_sb[0:64, 0:64])
            evGF[k] = s_pe.inc(te.drain())
            gF[0] += 1
        TE(te_dec2)

        def ve_dec2(v, z=z):
            k = gF[0] - 1
            v.wait_ge(s_pe.h, evGF[k])
            evGFe[k] = s_ve.inc(v.tensor_copy(
                out=dec_row[:, 0:64], in_=fpbank(k)[:, 0:64]))
            evF[('dec', z)] = evGFe[k]
        VE(ve_dec2)

        for cg in range(8):
            def te_st2(te, z=z, cg=cg):
                te.wait_ge(s_gp.h, evF[('xd', z)])
                if cg >= 2:
                    te.wait_ge(s_ve.h, evF[('supd', z, cg - 2)])
                dst = (T1[:, 1024:1536] if cg % 2 == 0 else T2[:, 0:512])
                ins = nc.tensor.matmul(
                    dst, B_T[:, z * 128:(z + 1) * 128],
                    xd_E[:, cg * 512:(cg + 1) * 512], start=True, stop=True)
                evF[('smm', z, cg)] = s_pe.inc(ins)
            TE(te_st2)

            def ve_upd2(v, z=z, cg=cg):
                v.wait_ge(s_pe.h, evF[('smm', z, cg)])
                v.wait_ge(s_ve.h, evF[('dec', z)])
                sl = slice(cg * 512, (cg + 1) * 512)
                v.tensor_tensor(
                    out=S_run[:, sl].rearrange('p (h q) -> p h q', h=8),
                    in0=S_run[:, sl].rearrange('p (h q) -> p h q', h=8),
                    in1=dec_row[:, cg * 8:(cg + 1) * 8].unsqueeze(2).broadcast_to(
                        [128, 8, 64]),
                    op=ALU.mult)
                v.drain()
                v.tensor_tensor(
                    out=S_run[:, sl], in0=S_run[:, sl],
                    in1=(T1[:, 1024:1536] if cg % 2 == 0 else T2[:, 0:512]),
                    op=ALU.add)
                v.drain()
                ins = v.tensor_copy(out=S_run_bf[:, sl], in_=S_run[:, sl])
                evF[('supd', z, cg)] = s_ve.inc(ins)
            VE(ve_upd2)

        for qd in range(8):
            def te_yt(te, z=z, qd=qd):
                if qd == 0:
                    te.wait_ge(s_ve.h, evF[('yfin', z, 7)])
                k = gB[0]
                if k >= 2:
                    te.wait_ge(s_ve.h, evGBe[k - 2])
                bank = bfbank(k)
                for j in range(4):
                    dc = qd * 4 + j
                    te.transpose(bank[:, j * 128:j * 128 + 128],
                                 y_tok[:, dc * 128:(dc + 1) * 128], idbf_sb[:])
                evGB[k] = s_pe.inc(te.drain())
                gB[0] += 1
            TE(te_yt)

            def ve_yt(v, z=z, qd=qd):
                k = gB[0] - 1
                v.wait_ge(s_pe.h, evGB[k])
                bank = bfbank(k)
                ins = v.tensor_copy(
                    out=y_fm[:].rearrange('p (d w) -> p d w', d=DC)[
                        :, qd * 4:qd * 4 + 4, :],
                    in_=bank[:, 0:512].rearrange('p (j w) -> p j w', j=4))
                evGBe[k] = s_ve.inc(ins)
                evF[('yt', z, qd)] = s_ve.v
            VE(ve_yt)
        evF[('ydone', z)] = evF[('yt', z, 7)]

        if 'ytok' in dbg:
            def d_yt(e, z=z):
                e.wait_ge(s_ve.h, evF[('ydone', z)])
                s_dbg.inc(e.dma_start(out=dbg['ytok'][z * 128:(z + 1) * 128, :],
                                      in_=y_tok[:]), 16)
            SY(d_yt)

        def ve_hg(v, z=z):
            v.wait_ge(s_ve.h, evF[('ydone', z)])
            v.wait_ge(*evF[('gs', z)])
            if z >= 1:
                v.wait_ge(s_pe.h, evF[('var', z - 1)])
                v.wait_ge(s_hst.h, evF[('hgnst', z - 1)])
            v.tensor_tensor(out=y_fm[:], in0=y_fm[:], in1=gsz[z % 2][:],
                            op=ALU.mult)
            v.drain()
            ins = v.tensor_tensor(out=hgn_c[:], in0=y_fm[:], in1=y_fm[:],
                                  op=ALU.mult)
            evF[('hg', z)] = s_ve.inc(ins)
        VE(ve_hg)

        def te_var(te, z=z):
            te.wait_ge(s_ve.h, evF[('hg', z)])
            if z >= 1:
                te.wait_ge(s_sc.h, evF[('sqv', z - 1)])
            ins = None
            for dc in range(DC):
                ins = nc.tensor.matmul(
                    T1[0:1, 512:640], onebf_sb[:],
                    hgn_c[:, dc * 128:(dc + 1) * 128],
                    start=(dc == 0), stop=(dc == DC - 1))
            evF[('var', z)] = s_pe.inc(ins)
        TE(te_var)

        def sc_rstd(sc, z=z):
            sc.wait_ge(s_pe.h, evF[('var', z)])
            ins = sc.activation(out=rstd[:], in_=T1[0:1, 512:640],
                                func=AF.Sqrt, scale=cvals[0:1, 2:3],
                                bias=cvals[0:1, 1:2])
            evF[('sqv', z)] = s_sc.inc(ins)
        SC(sc_rstd)

        def ve_rstd(v, z=z):
            v.wait_ge(s_sc.h, evF[('sqv', z)])
            ins = v.reciprocal(out=rstd[:], in_=rstd[:])
            evF[('rstd', z)] = s_ve.inc(ins)
        VE(ve_rstd)

        def te_rb(te, z=z):
            te.wait_ge(s_ve.h, evF[('rstd', z)])
            if z >= 1:
                te.wait_ge(s_ve.h, evF[('hgn', z - 1)])
            ins = nc.tensor.matmul(T1[:, 640:768], onef_sb[:], rstd[:],
                                   start=True, stop=True)
            evF[('rb', z)] = s_pe.inc(ins)
        TE(te_rb)

        def ve_hgn(v, z=z):
            v.wait_ge(s_pe.h, evF[('rb', z)])
            if z >= 1:
                v.wait_ge(s_hst.h, evF[('hgnst', z - 1)])
            ins = None
            for dc in range(DC):
                ins = v.scalar_tensor_tensor(
                    out=hgn_c[:, dc * 128:(dc + 1) * 128],
                    in0=y_fm[:, dc * 128:(dc + 1) * 128],
                    scalar=nw_sb[:, dc:dc + 1], in1=T1[:, 640:768],
                    op0=ALU.mult, op1=ALU.mult)
                v.drain()
            evF[('hgn', z)] = s_ve.inc(ins)
        VE(ve_hgn)

        def sy_hgn(e, z=z):
            e.wait_ge(s_ve.h, evF[('hgn', z)])
            evF[('hgnst', z)] = s_hst.inc(
                e.dma_start(
                    out=hgn_stash.ap()[:, :, z * 128:(z + 1) * 128].transpose(
                        [1, 0, 2]),
                    in_=hgn_c[:].rearrange('p (a b) -> p a b', a=DC)), 16)
        SY(sy_hgn)

        if 'hgnz' in dbg:
            def d_hgn(e, z=z):
                e.wait_ge(s_ve.h, evF[('hgn', z)])
                s_dbg.inc(e.dma_start(out=dbg['hgnz'][z * 128:(z + 1) * 128, :],
                                      in_=hgn_c[:]), 16)
            SY(d_hgn)
        if 'gsilz' in dbg:
            def d_gsl(e, z=z):
                e.wait_ge(*evF[('gs', z)])
                s_dbg.inc(e.dma_start(out=dbg['gsilz'][z * 128:(z + 1) * 128, :],
                                      in_=gsz[z % 2][:]), 16)
            SY(d_gsl)

    if max_phase == 'F':
        return _early_exit([(s_hst.h, evF[('hgnst', NT - 1)]),
                            (s_ve.h, evF[('hgn', NT - 1)])], (esE,))
    esE.close()
    esG = ExitStack()

    def sbG(name, shape, dtype):
        return esG.enter_context(nc.sbuf_tensor(name, shape, dtype))

    Wo_bf = sbG('Wo_bf', [128, DC * 512], BF16)
    hgnT = [sbG(f'hgnT{i}', [128, DC * 128], BF16) for i in range(2)]
    xq_t = [sbG(f'xq_t{i}', [128, 512], F32) for i in range(2)]
    fq_t = [sbG(f'fq_t{i}', [128, 512], BF16) for i in range(2)]
    o_t = sbG('o_t', [128, 512], F32)

    evG = {}

    for q in range(4):
        def g_wload(gp, q=q):
            if q == 0:
                gp.wait_ge(s_ve.h, evF[('hgn', NT - 1)])
                gp.wait_ge(s_pe.h, evF[('rb', NT - 1)])
                gp.wait_ge(s_sc.h, evF[('sqv', NT - 1)])
                gp.wait_ge(s_hst.h, evF[('hgnst', NT - 1)])
            if q >= 1:
                gp.wait_ge(s_pe.h, evG[('mmlast', q - 1)])
            evG[('w', q)] = s_wo.inc(
                gp.dma_start(out=Wo_bf[:].rearrange('p (a b) -> p a b', a=DC),
                             in_=w_oT[q].transpose([1, 0, 2])), 16)
        GP(g_wload)

        for z in range(NT):
            def g_loads(e, q=q, z=z):
                kz = q * NT + z
                if kz >= 2:
                    e.wait_ge(s_ve.h, evG[('fin', (kz - 2) // NT, (kz - 2) % NT)])
                e.wait_ge(s_hst.h, evF[('hgnst', z)])
                if kz == 0:
                    e.wait_ge(s_dst.h, s_dst.v)
                    e.wait_ge(s_ve.h, evF[('hgn', NT - 1)])
                    e.wait_ge(s_pe.h, evF[('rb', NT - 1)])
                    e.wait_ge(s_sc.h, evF[('sqv', NT - 1)])
                sg2 = s_gl[kz % 2]
                sg2.inc(
                    e.dma_start(
                        out=hgnT[kz % 2][:].rearrange('p (a b) -> p a b', a=DC),
                        in_=hgn_stash.ap()[:, :, z * 128:(z + 1) * 128].transpose(
                            [1, 0, 2])), 16)
                sg2.inc(
                    e.dma_start(out=xq_t[kz % 2][:],
                                in_=x_main[z * 128:(z + 1) * 128,
                                           q * 512:(q + 1) * 512]), 16)
                all3 = sg2.inc(
                    e.dma_start(out=fq_t[kz % 2][:],
                                in_=ffn_stash.ap()[z][:, q * 512:(q + 1) * 512]), 16)
                evG[('h', q, z)] = (sg2.h, all3)
                evG[('x', q, z)] = (sg2.h, all3)
                evG[('f', q, z)] = (sg2.h, all3)
            SY(g_loads)

            def g_mm(te, q=q, z=z):
                kz = q * NT + z
                te.wait_ge(*evG[('h', q, z)])
                if z == 0:
                    te.wait_ge(s_wo.h, evG[('w', q)])
                if kz == 0:
                    te.wait_ge(s_ve.h, evF[('yfin', NT - 1, 7)])
                if kz >= 2:
                    te.wait_ge(s_ve.h, evG[('ev', (kz - 2) // NT, (kz - 2) % NT)])
                dst = T1[:, (kz % 2) * 512:(kz % 2) * 512 + 512]
                ins = None
                for dc in range(DC):
                    ins = nc.tensor.matmul(
                        dst, hgnT[kz % 2][:, dc * 128:(dc + 1) * 128],
                        Wo_bf[:, dc * 512:(dc + 1) * 512],
                        start=(dc == 0), stop=(dc == DC - 1))
                evG[('mm', q, z)] = s_pe.inc(ins)
                if z == NT - 1:
                    evG[('mmlast', q)] = s_pe.v
            TE(g_mm)

            def g_fin(v, q=q, z=z):
                kz = q * NT + z
                v.wait_ge(s_pe.h, evG[('mm', q, z)])
                v.wait_ge(*evG[('x', q, z)])
                if kz >= 1:
                    pq, pz = (kz - 1) // NT, (kz - 1) % NT
                    v.wait_ge(s_out.h, evG[('st', pq, pz)])
                dst = T1[:, (kz % 2) * 512:(kz % 2) * 512 + 512]
                v.tensor_tensor(out=o_t[:], in0=dst, in1=xq_t[kz % 2][:],
                                op=ALU.add)
                evG[('ev', q, z)] = s_ve.inc(v.drain())
                ins = v.tensor_tensor(out=o_t[:], in0=o_t[:], in1=fq_t[kz % 2][:],
                                      op=ALU.add)
                evG[('fin', q, z)] = s_ve.inc(ins)
            VE(g_fin)

            def g_store(e, q=q, z=z):
                e.wait_ge(s_ve.h, evG[('fin', q, z)])
                evG[('st', q, z)] = s_out.inc(
                    e.dma_start(out=out_hbm[z * 128:(z + 1) * 128,
                                            q * 512:(q + 1) * 512],
                                in_=o_t[:]), 16)
            SY(g_store)

    def final_wait(e):
        e.wait_ge(s_out.h, s_out.v)
    SY(final_wait)

    return nc


def _prep_shared(inputs):
    f32 = np.float32
    W = np.ascontiguousarray(inputs['L0_attn_in_proj_w'], dtype=f32)
    Wpad = np.zeros((NG * 128, DM), f32)
    Wpad[:8512] = W
    w_inT = np.ascontiguousarray(
        Wpad.reshape(NG, 128, CC, 128).transpose(0, 3, 2, 1))
    gw = np.ascontiguousarray(inputs['L0_ffn_gate_w'], f32)
    uw = np.ascontiguousarray(inputs['L0_ffn_up_w'], f32)
    w_g = np.ascontiguousarray(gw.reshape(CC, 128, IC, 128).transpose(2, 1, 0, 3))
    w_u = np.ascontiguousarray(uw.reshape(CC, 128, IC, 128).transpose(2, 1, 0, 3))
    w_d = np.ascontiguousarray(inputs['L0_ffn_down_w'], f32).reshape(IC, 128, DM)
    Wo = np.ascontiguousarray(inputs['L0_attn_out_proj_w'], f32)
    w_oT = np.ascontiguousarray(Wo.reshape(4, 512, DC, 128).transpose(0, 2, 3, 1))
    conv_w = np.ascontiguousarray(inputs['L0_attn_conv_w'], f32)
    conv_w_t = np.ascontiguousarray(
        conv_w.reshape(34, 128, 4).transpose(1, 0, 2).reshape(128, 136))
    conv_b_t = np.ascontiguousarray(
        inputs['L0_attn_conv_b'].astype(f32).reshape(34, 128).T)
    dt_bias = inputs['L0_attn_dt_bias'].astype(f32).reshape(64, 1)
    a_log = inputs['L0_attn_A_log'].astype(f32).reshape(64, 1)
    d_rep = np.ascontiguousarray(
        np.broadcast_to(inputs['L0_attn_D'].astype(f32)[None, :], (128, 64)))
    norm_w_t = np.ascontiguousarray(
        inputs['L0_attn_norm_w'].astype(f32).reshape(32, 128).T)
    gs_t = np.ascontiguousarray(
        inputs['L0_ffn_gate_scale'].astype(f32).reshape(IC, 128).T)
    us_t = np.ascontiguousarray(
        inputs['L0_ffn_up_scale'].astype(f32).reshape(IC, 128).T)
    ds_b = np.ascontiguousarray(
        np.broadcast_to(inputs['L0_ffn_down_scale'].astype(f32)[None, :],
                        (128, DM)))
    ident_bf = np.eye(128, dtype=ml_dtypes.bfloat16)
    ident_f = np.eye(128, dtype=f32)
    s_idx = np.arange(128)[:, None]
    t_idx = np.arange(128)[None, :]
    mask_ge = (t_idx >= s_idx).astype(ml_dtypes.bfloat16)
    ones_bfv = np.ones((128, 1), ml_dtypes.bfloat16)
    ones_fv = np.ones((1, 128), f32)
    return dict(w_inT=w_inT, w_g=w_g, w_u=w_u, w_d=w_d, w_oT=w_oT,
                conv_w_t=conv_w_t, conv_b_t=conv_b_t, dt_bias=dt_bias,
                a_log=a_log, d_rep=d_rep, norm_w_t=norm_w_t, gs_t=gs_t,
                us_t=us_t, ds_b=ds_b, ident_bf=ident_bf, ident_f=ident_f,
                mask_ge=mask_ge, ones_bf=ones_bfv, ones_f=ones_fv)


def make_in_maps(inputs):
    shared = _prep_shared(inputs)
    x = np.ascontiguousarray(inputs['L0_attn_hidden_states'],
                             np.float32).reshape(L, DM)
    in_maps = []
    for k in range(NCORES):
        m = dict(shared)
        m['x_main'] = np.ascontiguousarray(x[k * T:(k + 1) * T])
        if k == 0:
            m['x_halo'] = np.zeros((128, DM), np.float32)
        else:
            m['x_halo'] = np.ascontiguousarray(x[k * T - 128:k * T])
        mp = np.zeros((128, 8), np.float32)
        mp[:, :k] = 1.0
        m['mask_prev'] = mp
        in_maps.append(m)
    return in_maps


def kernel(**inputs):
    nc = build_nc()
    in_maps = make_in_maps(inputs)
    res = run_bass_kernel_spmd(nc, in_maps, core_ids=list(range(NCORES)))
    out = np.concatenate([np.asarray(r['out']) for r in res.results], axis=0)
    return out.reshape(1, L, DM).astype(np.float32)


# version 06
# speedup vs baseline: 1.0276x; 1.0276x over previous
import numpy as np
import ml_dtypes
import concourse.bass as bass
import concourse.mybir as mybir
from concourse.bass_utils import run_bass_kernel_spmd
from contextlib import ExitStack

F32 = mybir.dt.float32
BF16 = mybir.dt.bfloat16
FP8 = mybir.dt.float8e4
AF = mybir.ActivationFunctionType
ALU = mybir.AluOpType
AX = mybir.AxisListType

NCORES = 8
L = 8192
DM = 2048
T = 1024
NT = 8
D_IN = 4096
HEADS = 64
NG = 67
CC = 16
IC = 48
DC = 32
EPS = 1e-5
TW = 1027


class Sem:
    __slots__ = ('h', 'v')

    def __init__(self, h):
        self.h = h
        self.v = 0

    def inc(self, instr, n=1):
        instr.then_inc(self.h, n)
        self.v += n
        return self.v


def build_nc(debug_taps=(), max_phase='G'):
    nc = bass.Bass('TRN2', target_bir_lowering=False)

    x_main = nc.dram_tensor('x_main', [T, DM], F32, kind='ExternalInput').ap()
    x_halo = nc.dram_tensor('x_halo', [128, DM], F32, kind='ExternalInput').ap()
    w_inT = nc.dram_tensor('w_inT', [NG, 128, CC, 128], F32, kind='ExternalInput').ap()
    w_g = nc.dram_tensor('w_g', [IC, 128, CC, 128], F32, kind='ExternalInput').ap()
    w_u = nc.dram_tensor('w_u', [IC, 128, CC, 128], F32, kind='ExternalInput').ap()
    w_d = nc.dram_tensor('w_d', [IC, 128, DM], F32, kind='ExternalInput').ap()
    w_oT = nc.dram_tensor('w_oT', [4, DC, 128, 512], F32, kind='ExternalInput').ap()
    conv_w_t = nc.dram_tensor('conv_w_t', [128, 136], F32, kind='ExternalInput').ap()
    conv_b_t = nc.dram_tensor('conv_b_t', [128, 34], F32, kind='ExternalInput').ap()
    dt_bias = nc.dram_tensor('dt_bias', [64, 1], F32, kind='ExternalInput').ap()
    a_log = nc.dram_tensor('a_log', [64, 1], F32, kind='ExternalInput').ap()
    d_rep = nc.dram_tensor('d_rep', [128, 64], F32, kind='ExternalInput').ap()
    norm_w_t = nc.dram_tensor('norm_w_t', [128, 32], F32, kind='ExternalInput').ap()
    gs_t = nc.dram_tensor('gs_t', [128, IC], F32, kind='ExternalInput').ap()
    us_t = nc.dram_tensor('us_t', [128, IC], F32, kind='ExternalInput').ap()
    ds_b = nc.dram_tensor('ds_b', [128, DM], F32, kind='ExternalInput').ap()
    ident_bf = nc.dram_tensor('ident_bf', [128, 128], BF16, kind='ExternalInput').ap()
    ident_f = nc.dram_tensor('ident_f', [128, 128], F32, kind='ExternalInput').ap()
    mask_ge = nc.dram_tensor('mask_ge', [128, 128], BF16, kind='ExternalInput').ap()
    ones_bf = nc.dram_tensor('ones_bf', [128, 1], BF16, kind='ExternalInput').ap()
    ones_f = nc.dram_tensor('ones_f', [1, 128], F32, kind='ExternalInput').ap()
    mask_prev = nc.dram_tensor('mask_prev', [128, 8], F32, kind='ExternalInput').ap()
    out_hbm = nc.dram_tensor('out', [T, DM], F32, kind='ExternalOutput').ap()

    g_stash = nc.dram_tensor('g_stash', [32, 128, T], BF16)
    xT_stash = nc.dram_tensor('xT_stash', [NT, 128, D_IN], BF16)
    hgn_stash = nc.dram_tensor('hgn_stash', [DC, 128, T], BF16)
    fuse_stash = nc.dram_tensor('fuse_stash', [IC, 128, T], BF16)
    ffn_stash = nc.dram_tensor('ffn_stash', [NT, 128, DM], BF16)
    cc_in = nc.dram_tensor('cc_in', [128, 4097], F32)
    cc_out = nc.dram_tensor('cc_out', [NCORES * 128, 4097], F32)

    dbg_shapes = {
        'xn_fm': [128, CC * TW], 'xbcpre': [128, TW],
        'bfm': [128, T], 'cfm': [128, T], 'dtf': [64, T],
        'grel': [64, T], 'xT': [NT * 128, D_IN],
        'slocal': [128, D_IN], 'sin': [128, D_IN],
        'ytok': [NT * 128, D_IN], 'hgnz': [NT * 128, DC * 128],
        'gsilz': [NT * 128, DC * 128],
    }
    dbg = {}
    for name in debug_taps:
        dbg[name] = nc.dram_tensor('dbg_' + name, dbg_shapes[name], F32,
                                   kind='ExternalOutput').ap()

    es = ExitStack()

    def sb(name, shape, dtype):
        return es.enter_context(nc.sbuf_tensor(name, shape, dtype))

    def sem(name):
        return Sem(es.enter_context(nc.semaphore(name)))

    xn_fm = sb('xn_fm', [128, CC * TW], BF16)
    B_fm = sb('B_fm', [128, T], BF16)
    C_fm = sb('C_fm', [128, T], BF16)
    B_T = sb('B_T', [128, NT * 128], BF16)
    dt_fm = sb('dt_fm', [64, T], F32)
    ln_or_erel = sb('ln_or_erel', [64, T], F32)
    grel = sb('grel', [64, T], F32)
    sgrel = sb('sgrel', [64, T], F32)
    est_or_dA = sb('est_or_dA', [64, T], F32)
    grel_T = sb('grel_T', [128, NT * 64], F32)
    sgrel_T = sb('sgrel_T', [128, NT * 64], F32)
    dtE_T = sb('dtE_T', [128, NT * 64], F32)
    erel_T = sb('erel_T', [128, NT * 64], F32)
    exp_ctot = sb('exp_ctot', [64, 8], F32)
    ctot_sum = sb('ctot_sum', [64, 1], F32)
    dec_row = sb('dec_row', [128, 64], F32)
    S_run = sb('S_run', [128, D_IN], F32)

    idbf_sb = sb('idbf_sb', [128, 128], BF16)
    idf_sb = sb('idf_sb', [128, 128], F32)
    cvals = sb('cvals', [128, 4], F32)

    T1 = es.enter_context(nc.psum_tensor('T1', [128, 1536], F32))
    T2 = es.enter_context(nc.psum_tensor('T2', [128, 1024], F32))
    T3 = es.enter_context(nc.psum_tensor('T3', [128, 1024], BF16))
    T4 = es.enter_context(nc.psum_tensor('T4', [128, 1024], BF16))
    T5 = es.enter_context(nc.psum_tensor('T5', [128, 512], F32))

    def bfbank(k):
        return T3 if k % 2 == 0 else T4

    def fpbank(k):
        return T2[:, 512:1024] if k % 2 == 0 else T5[:, 0:512]

    def ybanks(cg):
        if cg % 2 == 0:
            return T1[:, 0:512], T1[:, 512:1024]
        return T1[:, 1024:1536], T2[:, 0:512]

    gB = [0]
    gF = [0]
    evGB = {}
    evGBe = {}
    evGF = {}
    evGFe = {}


    s_const = sem('s_const')
    s_pe = sem('s_pe')
    s_ve = sem('s_ve')
    s_sc = sem('s_sc')
    s_gp = sem('s_gp')
    s_cc = sem('s_cc')
    s_out = sem('s_out')
    s_dbg = sem('s_dbg')
    s_x = [sem('s_x0'), sem('s_x1')]
    s_bc = sem('s_bc')
    s_wb = [sem('s_wb0'), sem('s_wb1')]
    s_gst = [sem('s_gst0'), sem('s_gst1')]
    s_xtst = sem('s_xtst')
    s_exp = sem('s_exp')
    s_dc = sem('s_dc')
    s_wgu = [sem('s_wgu0'), sem('s_wgu1')]
    s_fst = [sem('s_fst0'), sem('s_fst1')]
    s_wd = [sem('s_wd0'), sem('s_wd1')]
    s_fu = [sem('s_fu0'), sem('s_fu1')]
    s_dst = sem('s_dst')
    s_hst = sem('s_hst')
    s_ej = sem('s_ej')
    s_dr = sem('s_dr')
    s_ec = sem('s_ec')
    s_ez = [sem('s_ez0'), sem('s_ez1')]
    s_wo = sem('s_wo')
    s_gl = [sem('s_gl0'), sem('s_gl1')]

    block = es.enter_context(nc.Block())

    def SY(f):
        @block.sync
        def _(e):
            f(e)

    def VE(f):
        @block.vector
        def _(e):
            e.drain()
            f(e)

    def SC(f):
        @block.scalar
        def _(e):
            e.drain()
            f(e)

    def GP(f):
        @block.gpsimd
        def _(e):
            e.drain()
            f(e)

    def TE(f):
        @block.tensor
        def _(e):
            f(e)


    def _early_exit(wait_list, stacks=()):
        def fin(e):
            for sh, vv in wait_list:
                e.wait_ge(sh, vv)
            if s_dbg.v:
                e.wait_ge(s_dbg.h, s_dbg.v)
        SY(fin)
        for st_ in reversed(stacks):
            st_.close()
        es.close()
        return nc

    def const_loads(e):
        for dst, src in [(idbf_sb, ident_bf), (idf_sb, ident_f)]:
            s_const.inc(e.dma_start(out=dst[:], in_=src), 16)
    SY(const_loads)
    LD_CONST = s_const.v

    VE_SETUP = [0]

    def ve_setup(v):
        v.memset(cvals[:, 0:1], 1e-38)
        v.memset(cvals[:, 1:2], EPS)
        v.memset(cvals[:, 2:3], 1.0 / D_IN)
        VE_SETUP[0] = s_ve.inc(v.drain())
    VE(ve_setup)

    SC_SETUP = [0]

    PHA = [None]
    esWb = ExitStack()
    cw_sb = esWb.enter_context(nc.sbuf_tensor('cw_sb', [128, 136], F32))
    cb_sb = esWb.enter_context(nc.sbuf_tensor('cb_sb', [128, 34], F32))
    dtb_sb = esWb.enter_context(nc.sbuf_tensor('dtb_sb', [64, 1], F32))
    alog_sb = esWb.enter_context(nc.sbuf_tensor('alog_sb', [64, 1], F32))
    nA_sb = esWb.enter_context(nc.sbuf_tensor('nA_sb', [64, 1], F32))
    zeros64 = esWb.enter_context(nc.sbuf_tensor('zeros64', [64, 128], F32))
    Wb = [esWb.enter_context(nc.sbuf_tensor(f'Wb{i}', [128, CC * 128], BF16))
          for i in range(2)]

    B_CONST = [0]

    def b_consts(e):
        for dst, srcap in [(cw_sb, conv_w_t), (cb_sb, conv_b_t),
                           (dtb_sb, dt_bias), (alog_sb, a_log)]:
            s_bc.inc(e.dma_start(out=dst[:], in_=srcap), 16)
        B_CONST[0] = s_bc.v
    SY(b_consts)

    def b_setup_ve(v):
        v.wait_ge(s_bc.h, B_CONST[0])
        v.memset(zeros64[:], 0.0)
        BZERO[0] = s_ve.inc(v.drain())
    BZERO = [0]
    VE(b_setup_ve)

    def b_setup_sc(sc):
        sc.wait_ge(s_bc.h, B_CONST[0])
        sc.activation(out=nA_sb[:], in_=alog_sb[:], func=AF.Exp)
        sc.drain()
        sc.activation(out=nA_sb[:], in_=nA_sb[:], func=AF.Copy, scale=-1.0)
        SC_SETUP[0] = s_sc.inc(sc.drain())
    SC(b_setup_sc)

    evB_w = {}

    def b_w01(gp):
        for g in range(2):
            evB_w[g] = (s_wb[g % 2].h, s_wb[g % 2].inc(
                gp.dma_start(out=Wb[g % 2][:],
                             in_=w_inT[g].rearrange('p c j -> p (c j)')), 16))
    GP(b_w01)

    esA = ExitStack()

    def sbA(name, shape, dtype):
        return esA.enter_context(nc.sbuf_tensor(name, shape, dtype))

    x_t = [sbA(f'x_t{i}', [128, DM], F32) for i in range(2)]
    ln_sq = sbA('ln_sq', [128, DM], F32)
    xn_t = [sbA(f'xn_t{i}', [128, DM], BF16) for i in range(2)]
    ln_m = [sbA(f'ln_m{i}', [128, 1], F32) for i in range(2)]
    ln_v = [sbA(f'ln_v{i}', [128, 1], F32) for i in range(2)]

    evA_load = {}
    evA_stats = {}
    evA_apply = {}
    evA_tq = {}
    evA_eq = {}

    for i in range(9):
        def a_load(e, i=i):
            if i >= 2:
                e.wait_ge(s_sc.h, evA_apply[i - 2])
            src = x_halo if i == 0 else x_main[(i - 1) * 128:i * 128, :]
            sx = s_x[i % 2]
            evA_load[i] = (sx.h, sx.inc(e.dma_start(out=x_t[i % 2][:], in_=src), 16))
        SY(a_load)

        def a_stats(v, i=i):
            b = i % 2
            v.wait_ge(*evA_load[i])
            v.reduce_sum(out=ln_m[b][:], in_=x_t[b][:], axis=AX.X)
            v.drain()
            v.tensor_scalar_mul(out=ln_m[b][:], in0=ln_m[b][:], scalar1=1.0 / DM)
            v.drain()
            v.tensor_scalar_sub(out=x_t[b][:], in0=x_t[b][:], scalar1=ln_m[b][:])
            v.drain()
            v.tensor_tensor(out=ln_sq[:], in0=x_t[b][:], in1=x_t[b][:], op=ALU.mult)
            v.drain()
            v.reduce_sum(out=ln_v[b][:], in_=ln_sq[:], axis=AX.X)
            v.drain()
            v.tensor_scalar(out=ln_v[b][:], in0=ln_v[b][:], scalar1=1.0 / DM,
                            scalar2=EPS, op0=ALU.mult, op1=ALU.add)
            v.drain()
            v.reciprocal(out=ln_v[b][:], in_=ln_v[b][:])
            evA_stats[i] = s_ve.inc(v.drain())
        VE(a_stats)

        def a_apply(sc, i=i):
            b = i % 2
            sc.wait_ge(s_ve.h, evA_stats[i])
            if i >= 2:
                sc.wait_ge(s_pe.h, evGB[(i - 2) * 4 + 3])
            sc.activation(out=ln_v[b][:], in_=ln_v[b][:], func=AF.Sqrt)
            sc.drain()
            sc.activation(out=xn_t[b][:], in_=x_t[b][:], func=AF.Copy,
                          scale=ln_v[b][:])
            evA_apply[i] = s_sc.inc(sc.drain())
        SC(a_apply)

        for quad in range(4):
            def a_tr(te, i=i, quad=quad):
                k = gB[0]
                if i == 0 and quad == 0:
                    te.wait_ge(s_const.h, LD_CONST)
                if quad == 0:
                    te.wait_ge(s_sc.h, evA_apply[i])
                if k >= 2:
                    te.wait_ge(s_ve.h, evGBe[k - 2])
                bank = bfbank(k)
                for j in range(4):
                    cc = quad * 4 + j
                    te.transpose(bank[:, j * 128:j * 128 + 128],
                                 xn_t[i % 2][:, cc * 128:(cc + 1) * 128], idbf_sb[:])
                evGB[k] = s_pe.inc(te.drain())
                gB[0] += 1
            TE(a_tr)

            def a_ev(v, i=i, quad=quad):
                k = gB[0] - 1
                v.wait_ge(s_pe.h, evGB[k])
                bank = bfbank(k)
                cc0 = quad * 4
                if i == 0:
                    ins = v.tensor_copy(
                        out=xn_fm[:].rearrange('p (c w) -> p c w', c=CC)[
                            :, cc0:cc0 + 4, 0:3],
                        in_=bank[:, 0:512].rearrange('p (j w) -> p j w', j=4)[
                            :, :, 125:128])
                else:
                    off = 3 + (i - 1) * 128
                    ins = v.tensor_copy(
                        out=xn_fm[:].rearrange('p (c w) -> p c w', c=CC)[
                            :, cc0:cc0 + 4, off:off + 128],
                        in_=bank[:, 0:512].rearrange('p (j w) -> p j w', j=4))
                evGBe[k] = s_ve.inc(ins)
            VE(a_ev)

    A_XN = evGBe[35]
    if max_phase == 'A':
        return _early_exit([(s_ve.h, A_XN)], (esWb, esA))
    esA.close()

    if 'xn_fm' in dbg:
        def d_xn(e):
            e.wait_ge(s_ve.h, A_XN)
            s_dbg.inc(e.dma_start(out=dbg['xn_fm'], in_=xn_fm[:]), 16)
        SY(d_xn)

    esB2 = ExitStack()
    x_T = esB2.enter_context(nc.sbuf_tensor('x_T', [128, NT * D_IN], BF16))
    esB = ExitStack()

    def sbB(name, shape, dtype):
        return esB.enter_context(nc.sbuf_tensor(name, shape, dtype))

    gtile = [sbB(f'gtile{i}', [128, T], BF16) for i in range(2)]
    xbc32 = [sbB(f'xbc32{i}', [128, TW], F32) for i in range(2)]
    cacc = [sbB(f'cacc{i}', [128, T], F32) for i in range(2)]
    sil_t = [sbB(f'sil_t{i}', [128, T], BF16) for i in range(3)]
    dt_raw = sbB('dt_raw', [64, T], F32)
    sp_t1 = sbB('sp_t1', [64, T], F32)

    evB_gdone = {}
    evB_mm = {}
    evB_gsil = {}
    evB_gst = {}
    evB_cevac = {}
    evB_fma = {}
    evB_sil = {}
    evB_tq = {}
    evB_teq = {}
    evB_xtev = {}
    gate_cons = [None, None]

    def lhsT(g, cc, rows=128):
        return Wb[g % 2][:, cc * 128:cc * 128 + rows]

    def emit_wload(g):
        def gp_f(gp):
            gp.wait_ge(s_pe.h, evB_gdone[g - 2])
            evB_w[g] = (s_wb[g % 2].h, s_wb[g % 2].inc(
                gp.dma_start(out=Wb[g % 2][:],
                             in_=w_inT[g].rearrange('p c j -> p (c j)')), 16))
        GP(gp_f)

    def emit_xt_transposes(src_buf, cidx):
        for half in range(2):
            def te_f(te, half=half):
                if half == 0:
                    te.wait_ge(s_sc.h, evB_sil[cidx])
                k = gB[0]
                if k >= 2:
                    te.wait_ge(s_ve.h, evGBe[k - 2])
                bank = bfbank(k)
                for j in range(4):
                    z = half * 4 + j
                    te.transpose(bank[:, j * 128:j * 128 + 128],
                                 src_buf[:, z * 128:(z + 1) * 128], idbf_sb[:])
                evGB[k] = s_pe.inc(te.drain())
                gB[0] += 1
            TE(te_f)

            def ve_f(v, half=half):
                k = gB[0] - 1
                v.wait_ge(s_pe.h, evGB[k])
                bank = bfbank(k)
                if cidx < 32:
                    dst = x_T[:].rearrange('p (z w) -> p z w', z=NT)[
                        :, half * 4:half * 4 + 4,
                        cidx * 128:cidx * 128 + 128]
                else:
                    dst = B_T[:].rearrange('p (z w) -> p z w', z=NT)[
                        :, half * 4:half * 4 + 4, :]
                ins = v.tensor_copy(
                    out=dst, in_=bank[:, 0:512].rearrange('p (j w) -> p j w', j=4))
                evGBe[k] = s_ve.inc(ins)
                if half == 1:
                    evB_xtev[cidx] = evGBe[k]
            VE(ve_f)

    for g in range(NG):
        rc = g
        if g < 32:
            def te_f(te, g=g, rc=rc):
                te.wait_ge(*evB_w[g])
                if rc == 0:
                    te.wait_ge(s_ve.h, A_XN)
                for si, (t0, t1) in enumerate([(3, 515), (515, 1027)]):
                    if gate_cons[si] is not None:
                        te.wait_ge(s_sc.h, gate_cons[si])
                    dst = T1[:, si * 512:si * 512 + 512]
                    ins = None
                    for cc in range(CC):
                        ins = nc.tensor.matmul(
                            dst, lhsT(g, cc),
                            xn_fm[:, cc * TW + t0:cc * TW + t1],
                            start=(cc == 0), stop=(cc == CC - 1))
                    evB_mm[(rc, si)] = s_pe.inc(ins)
                evB_gdone[g] = s_pe.v
            TE(te_f)

            def sc_f(sc, rc=rc):
                for si in range(2):
                    sc.wait_ge(s_pe.h, evB_mm[(rc, si)])
                    if rc >= 2 and si == 0:
                        sc.wait_ge(*evB_gst[rc - 2])
                    src_ = T1[:, si * 512:si * 512 + 512]
                    ins = sc.activation(
                        out=gtile[rc % 2][:, si * 512:(si + 1) * 512],
                        in_=src_, func=AF.Silu)
                    gate_cons[si] = s_sc.inc(ins)
                evB_gsil[rc] = s_sc.v
            SC(sc_f)

            def sy_f(e, rc=rc):
                e.wait_ge(s_sc.h, evB_gsil[rc])
                sg_ = s_gst[rc % 2]
                evB_gst[rc] = (sg_.h, sg_.inc(
                    e.dma_start(out=g_stash.ap()[rc], in_=gtile[rc % 2][:]), 16))
            SY(sy_f)
        elif g < 66:
            cidx = g - 32
            par = cidx % 2
            spar = cidx % 3

            def te_f(te, g=g, cidx=cidx):
                te.wait_ge(*evB_w[g])
                if cidx == 0:
                    te.wait_ge(s_sc.h, gate_cons[0])
                    te.wait_ge(s_sc.h, gate_cons[1])
                if cidx > 0:
                    vv = evB_cevac[cidx - 1]
                    te.wait_ge(s_ve.h, vv[0])
                    te.wait_ge(s_sc.h, vv[1])
                ins = None
                for (t0, t1) in [(0, 512), (512, 1024), (1024, TW)]:
                    for cc in range(CC):
                        ins = nc.tensor.matmul(
                            T1[:, t0:t1], lhsT(g, cc),
                            xn_fm[:, cc * TW + t0:cc * TW + t1],
                            start=(cc == 0), stop=(cc == CC - 1))
                evB_mm[('c', cidx)] = s_pe.inc(ins)
                evB_gdone[g] = s_pe.v
            TE(te_f)

            def ve_e(v, cidx=cidx, par=par):
                v.wait_ge(s_pe.h, evB_mm[('c', cidx)])
                if cidx >= 2:
                    v.wait_ge(s_ve.h, evB_fma[cidx - 2])
                ins = v.tensor_copy(out=xbc32[par][:, 0:640], in_=T1[:, 0:640])
                evB_cevac.setdefault(cidx, [0, 0])[0] = s_ve.inc(ins)
            VE(ve_e)

            def sc_e(sc, cidx=cidx, par=par):
                sc.wait_ge(s_pe.h, evB_mm[('c', cidx)])
                if cidx >= 2:
                    sc.wait_ge(s_ve.h, evB_fma[cidx - 2])
                ins = sc.activation(out=xbc32[par][:, 640:TW],
                                    in_=T1[:, 640:TW], func=AF.Copy)
                evB_cevac[cidx][1] = s_sc.inc(ins)
            SC(sc_e)

            if cidx == 0 and 'xbcpre' in dbg:
                def d_xbc(e, cidx=cidx, par=par):
                    e.wait_ge(s_ve.h, evB_cevac[cidx][0])
                    e.wait_ge(s_sc.h, evB_cevac[cidx][1])
                    s_dbg.inc(e.dma_start(out=dbg['xbcpre'],
                                          in_=xbc32[par][:]), 16)
                SY(d_xbc)

            def ve_c(v, cidx=cidx, par=par):
                v.wait_ge(s_sc.h, evB_cevac[cidx][1])
                ins = None
                for colh in range(2):
                    c0 = colh * 512
                    for i in range(4):
                        if i == 0:
                            ins = v.tensor_scalar(
                                out=cacc[par][:, c0:c0 + 512],
                                in0=xbc32[par][:, c0:c0 + 512],
                                scalar1=cw_sb[:, cidx * 4:cidx * 4 + 1],
                                scalar2=None, op0=ALU.mult)
                        else:
                            ins = v.scalar_tensor_tensor(
                                out=cacc[par][:, c0:c0 + 512],
                                in0=xbc32[par][:, c0 + i:c0 + 512 + i],
                                scalar=cw_sb[:, cidx * 4 + i:cidx * 4 + i + 1],
                                in1=cacc[par][:, c0:c0 + 512],
                                op0=ALU.mult, op1=ALU.add)
                        v.drain()
                evB_fma[cidx] = s_ve.inc(v.drain())
            VE(ve_c)

            def sc_s(sc, cidx=cidx, par=par, spar=spar):
                sc.wait_ge(s_ve.h, evB_fma[cidx])
                if cidx < 32:
                    if cidx >= 3:
                        sc.wait_ge(s_ve.h, evB_xtev[cidx - 3])
                    dst = sil_t[spar][:]
                elif cidx == 32:
                    dst = B_fm[:]
                else:
                    dst = C_fm[:]
                ins = sc.activation(out=dst, in_=cacc[par][:, 0:T],
                                    func=AF.Silu, bias=cb_sb[:, cidx:cidx + 1])
                evB_sil[cidx] = s_sc.inc(ins)
            SC(sc_s)

            if cidx < 32:
                emit_xt_transposes(sil_t[spar], cidx)
            elif cidx == 32:
                emit_xt_transposes(B_fm, cidx)
        else:
            def te_f(te, g=g):
                te.wait_ge(*evB_w[g])
                vv = evB_cevac[33]
                te.wait_ge(s_ve.h, vv[0])
                te.wait_ge(s_sc.h, vv[1])
                for si, (t0, t1) in enumerate([(3, 515), (515, 1027)]):
                    if gate_cons[si] is not None:
                        te.wait_ge(s_sc.h, gate_cons[si])
                    dst = T1[0:64, si * 512:si * 512 + 512]
                    ins = None
                    for cc in range(CC):
                        ins = nc.tensor.matmul(
                            dst, lhsT(g, cc, rows=64),
                            xn_fm[:, cc * TW + t0:cc * TW + t1],
                            start=(cc == 0), stop=(cc == CC - 1))
                    evB_mm[('dt', si)] = s_pe.inc(ins)
                evB_gdone[g] = s_pe.v
            TE(te_f)

            def sc_f(sc):
                sc.wait_ge(s_pe.h, evB_mm[('dt', 0)])
                sc.wait_ge(s_pe.h, evB_mm[('dt', 1)])
                for si in range(2):
                    src_ = T1[0:64, si * 512:si * 512 + 512]
                    d0 = slice(si * 512, (si + 1) * 512)
                    sc.activation(out=sp_t1[:, d0], in_=src_, func=AF.Abs,
                                  bias=dtb_sb[:])
                    sc.drain()
                    sc.activation(out=sp_t1[:, d0], in_=sp_t1[:, d0],
                                  func=AF.Exp, scale=-1.0)
                    sc.drain()
                    sc.activation(out=sp_t1[:, d0], in_=sp_t1[:, d0],
                                  func=AF.Ln, bias=1.0)
                    sc.drain()
                    sc.activation(out=dt_raw[:, d0], in_=src_, func=AF.Relu,
                                  bias=dtb_sb[:])
                    sc.drain()
                evB_mm['dt_sc'] = s_sc.inc(sc.drain())
            SC(sc_f)

            def ve_f(v):
                v.wait_ge(s_sc.h, evB_mm['dt_sc'])
                v.wait_ge(s_sc.h, SC_SETUP[0])
                v.wait_ge(s_ve.h, BZERO[0])
                v.tensor_tensor(out=dt_fm[:], in0=dt_raw[:], in1=sp_t1[:],
                                op=ALU.add)
                v.drain()
                v.tensor_scalar_mul(out=est_or_dA[:], in0=dt_fm[:],
                                    scalar1=nA_sb[:])
                v.drain()
                for z in range(NT):
                    sl = slice(z * 128, (z + 1) * 128)
                    v.tensor_tensor_scan(out=grel[:, sl], data0=est_or_dA[:, sl],
                                         data1=zeros64[:], initial=0.0,
                                         op0=ALU.add, op1=ALU.add)
                    v.drain()
                evB_mm['dt_ve'] = s_ve.inc(v.drain())
            VE(ve_f)

            def sc_f2(sc):
                sc.wait_ge(s_ve.h, evB_mm['dt_ve'])
                sc.wait_ge(s_ve.h, VE_SETUP[0])
                sc.activation(out=ln_or_erel[:], in_=dt_fm[:], func=AF.Ln,
                              bias=cvals[0:64, 0:1])
                evB_mm['dt_sc2'] = s_sc.inc(sc.drain())
            SC(sc_f2)

            def ve_f2(v):
                v.wait_ge(s_sc.h, evB_mm['dt_sc2'])
                v.tensor_sub(out=sgrel[:], in0=grel[:], in1=ln_or_erel[:])
                v.drain()
                for z in range(NT):
                    sl = slice(z * 128, (z + 1) * 128)
                    v.tensor_scalar_sub(
                        out=est_or_dA[:, sl], in0=grel[:, sl],
                        scalar1=grel[:, z * 128 + 127:z * 128 + 128])
                    v.drain()
                v.reduce_sum(out=ctot_sum[:], in_=grel[:, 127::128], axis=AX.X)
                evB_mm['dt_ve2'] = s_ve.inc(v.drain())
            VE(ve_f2)

            def sc_f3(sc):
                sc.wait_ge(s_ve.h, evB_mm['dt_ve2'])
                sc.activation(out=est_or_dA[:], in_=est_or_dA[:], func=AF.Exp,
                              scale=-1.0)
                sc.drain()
                sc.activation(out=exp_ctot[:], in_=grel[:, 127::128],
                              func=AF.Exp)
                sc.drain()
                sc.activation(out=ln_or_erel[:], in_=grel[:], func=AF.Exp)
                evB_mm['dt_sc3'] = s_sc.inc(sc.drain())
            SC(sc_f3)

            def ve_f3(v):
                v.wait_ge(s_sc.h, evB_mm['dt_sc3'])
                v.tensor_tensor(out=dt_fm[:], in0=dt_fm[:], in1=est_or_dA[:],
                                op=ALU.mult)
                evB_mm['dt_ve3'] = s_ve.inc(v.drain())
            VE(ve_f3)

            for z in range(NT):
                def te_tt(te, z=z):
                    if z == 0:
                        te.wait_ge(s_ve.h, evB_mm['dt_ve3'])
                    k = gF[0]
                    if k >= 2:
                        te.wait_ge(s_ve.h, evGFe[k - 2])
                    bank = fpbank(k)
                    for jj, srcbuf in enumerate((grel, sgrel, dt_fm, ln_or_erel)):
                        te.transpose(
                            bank[:, jj * 128:jj * 128 + 64],
                            srcbuf[0:64, z * 128:(z + 1) * 128],
                            idf_sb[0:64, 0:64])
                    evGF[k] = s_pe.inc(te.drain())
                    gF[0] += 1
                TE(te_tt)

                def ve_tt(v, z=z):
                    k = gF[0] - 1
                    v.wait_ge(s_pe.h, evGF[k])
                    bank = fpbank(k)
                    ins = None
                    for jj, dstbuf in enumerate((grel_T, sgrel_T, dtE_T, erel_T)):
                        ins = v.tensor_copy(
                            out=dstbuf[:, z * 64:(z + 1) * 64],
                            in_=bank[:, jj * 128:jj * 128 + 64])
                        v.drain()
                    evGFe[k] = s_ve.inc(v.drain())
                    evB_mm['dect'] = s_ve.v
                VE(ve_tt)
        if g + 2 < NG:
            emit_wload(g + 2)

    B_XT = evB_xtev[32]
    B_DECT = evB_mm['dect']

    if 'bfm' in dbg:
        def d_bfm(e):
            e.wait_ge(s_sc.h, evB_sil[32])
            s_dbg.inc(e.dma_start(out=dbg['bfm'], in_=B_fm[:]), 16)
            e.wait_ge(s_sc.h, evB_sil[33])
            s_dbg.inc(e.dma_start(out=dbg['cfm'], in_=C_fm[:]), 16)
        SY(d_bfm)
    if 'dtf' in dbg:
        def d_dtf(e):
            e.wait_ge(s_ve.h, evB_mm['dt_ve'])
            s_dbg.inc(e.dma_start(out=dbg['dtf'], in_=dt_raw[:]), 16)
            s_dbg.inc(e.dma_start(out=dbg['grel'], in_=grel[:]), 16)
        SY(d_dtf)
    if 'xT' in dbg:
        def d_xt(e):
            e.wait_ge(s_ve.h, B_XT)
            for z in range(NT):
                s_dbg.inc(e.dma_start(out=dbg['xT'][z * 128:(z + 1) * 128, :],
                                      in_=x_T[:, z * D_IN:(z + 1) * D_IN]), 16)
        SY(d_xt)

    evB_xtst = {}

    def b_xt_stash(e):
        e.wait_ge(s_ve.h, B_XT)
        for z in range(NT):
            evB_xtst[z] = s_xtst.inc(
                e.dma_start(out=xT_stash.ap()[z],
                            in_=x_T[:, z * D_IN:(z + 1) * D_IN]), 16)
    SY(b_xt_stash)

    esB3 = ExitStack()
    xd_buf = esB3.enter_context(nc.sbuf_tensor('xd_buf', [128, D_IN], BF16))
    SW = {}

    def b_walk_init(v):
        v.wait_ge(s_ve.h, B_XT)
        v.wait_ge(s_ve.h, B_DECT)
        v.memset(S_run[:], 0.0)
        SW['init'] = s_ve.inc(v.drain())
    VE(b_walk_init)

    for z in range(NT):
        def ve_xd(v, z=z):
            if z == 0:
                v.wait_ge(s_ve.h, SW['init'])
            else:
                v.wait_ge(s_pe.h, SW[('mm', z - 1, 7)])
            ins = v.tensor_tensor(
                out=xd_buf[:].rearrange('p (h q) -> p h q', h=64),
                in0=x_T[:, z * D_IN:(z + 1) * D_IN].rearrange(
                    'p (h q) -> p h q', h=64),
                in1=dtE_T[:, z * 64:(z + 1) * 64].unsqueeze(2).broadcast_to(
                    [128, 64, 64]),
                op=ALU.mult)
            SW[('xd', z)] = s_ve.inc(ins)
        VE(ve_xd)

        def te_dec(te, z=z):
            te.wait_ge(s_ve.h, SW[('xd', z)])
            k = gF[0]
            if k >= 2:
                te.wait_ge(s_ve.h, evGFe[k - 2])
            te.transpose(fpbank(k)[:, 0:64],
                         exp_ctot[0:64, z:z + 1].broadcast_to([64, 128]),
                         idf_sb[0:64, 0:64])
            evGF[k] = s_pe.inc(te.drain())
            gF[0] += 1
        TE(te_dec)

        def ve_dec(v, z=z):
            k = gF[0] - 1
            v.wait_ge(s_pe.h, evGF[k])
            evGFe[k] = s_ve.inc(v.tensor_copy(
                out=dec_row[:, 0:64], in_=fpbank(k)[:, 0:64]))
            SW[('dec', z)] = evGFe[k]
        VE(ve_dec)

        for cg in range(8):
            def te_st(te, z=z, cg=cg):
                te.wait_ge(s_ve.h, SW[('xd', z)])
                if z == 0 and cg < 2:
                    te.wait_ge(s_sc.h, evB_mm['dt_sc'])
                if cg >= 2:
                    te.wait_ge(s_ve.h, SW[('upd', z, cg - 2)])
                dst = T1[:, (cg % 2) * 512:(cg % 2) * 512 + 512]
                ins = nc.tensor.matmul(
                    dst, B_T[:, z * 128:(z + 1) * 128],
                    xd_buf[:, cg * 512:(cg + 1) * 512], start=True, stop=True)
                SW[('mm', z, cg)] = s_pe.inc(ins)
            TE(te_st)

            def ve_upd(v, z=z, cg=cg):
                v.wait_ge(s_pe.h, SW[('mm', z, cg)])
                v.wait_ge(s_ve.h, SW[('dec', z)])
                sl = slice(cg * 512, (cg + 1) * 512)
                v.tensor_tensor(
                    out=S_run[:, sl].rearrange('p (h q) -> p h q', h=8),
                    in0=S_run[:, sl].rearrange('p (h q) -> p h q', h=8),
                    in1=dec_row[:, cg * 8:(cg + 1) * 8].unsqueeze(2).broadcast_to(
                        [128, 8, 64]),
                    op=ALU.mult)
                v.drain()
                ins = v.tensor_tensor(
                    out=S_run[:, sl], in0=S_run[:, sl],
                    in1=T1[:, (cg % 2) * 512:(cg % 2) * 512 + 512], op=ALU.add)
                SW[('upd', z, cg)] = s_ve.inc(ins)
            VE(ve_upd)

    B_SLOCAL = SW[('upd', NT - 1, 7)]

    if 'slocal' in dbg:
        def d_sl(e):
            e.wait_ge(s_ve.h, B_SLOCAL)
            s_dbg.inc(e.dma_start(out=dbg['slocal'], in_=S_run[:]), 16)
        SY(d_sl)

    evB_exp = {}

    def b_export(e):
        e.wait_ge(s_ve.h, B_SLOCAL)
        e.wait_ge(s_ve.h, BZERO[0])
        s_exp.inc(e.dma_start(out=cc_in.ap()[:, 0:D_IN], in_=S_run[:]), 16)
        s_exp.inc(
            e.dma_start(out=cc_in.ap()[0:64, D_IN:D_IN + 1], in_=ctot_sum[:],
                        allow_slow_non_contiguous=True), 16)
        evB_exp['done'] = s_exp.inc(
            e.dma_start(out=cc_in.ap()[64:128, D_IN:D_IN + 1],
                        in_=zeros64[0:64, 0:1],
                        allow_slow_non_contiguous=True), 16)
    SY(b_export)

    if max_phase == 'B':
        return _early_exit([(s_exp.h, evB_exp['done']), (s_xtst.h, s_xtst.v),
                            (s_gst[0].h, s_gst[0].v), (s_gst[1].h, s_gst[1].v)],
                           (esWb, esB2, esB, esB3))
    esB3.close()
    esB.close()
    esB2.close()
    esWb.close()

    esD = ExitStack()

    def sbD(name, shape, dtype):
        return esD.enter_context(nc.sbuf_tensor(name, shape, dtype))

    gs_sb = sbD('gs_sb', [128, IC], F32)
    us_sb = sbD('us_sb', [128, IC], F32)
    ds_sb = sbD('ds_sb', [128, DM], F32)
    Wg8 = [sbD(f'Wg8{i}', [128, CC * 128], FP8) for i in range(2)]
    Wu8 = [sbD(f'Wu8{i}', [128, CC * 128], FP8) for i in range(2)]
    gsil_t = sbD('gsil_t', [128, 512], F32)
    fuse_t = [sbD(f'fuse_t{i}', [128, T], BF16) for i in range(2)]

    evD_w = {}
    evD_done = {}
    evD_mm = {}
    evD_sil = {}
    evD_fuse = {}
    evD_fst = {}
    last_fuse = [None]

    D_CONST = [0]

    def d_consts(e):
        e.wait_ge(s_xtst.h, s_xtst.v)
        e.wait_ge(s_ve.h, B_SLOCAL)
        e.wait_ge(s_gst[0].h, s_gst[0].v)
        e.wait_ge(s_gst[1].h, s_gst[1].v)
        e.wait_ge(s_sc.h, evB_mm['dt_sc3'])
        e.wait_ge(s_exp.h, evB_exp['done'])
        for dst, srcap in [(gs_sb, gs_t), (us_sb, us_t), (ds_sb, ds_b)]:
            s_dc.inc(e.dma_start(out=dst[:], in_=srcap), 16)
        D_CONST[0] = s_dc.v
    SY(d_consts)

    def d_w01(gp):
        gp.wait_ge(s_xtst.h, s_xtst.v)
        gp.wait_ge(s_ve.h, B_SLOCAL)
        gp.wait_ge(s_exp.h, evB_exp['done'])
        for ic in range(2):
            sw = s_wgu[ic % 2]
            sw.inc(gp.dma_start(out=Wg8[ic % 2][:],
                                in_=w_g[ic].rearrange('p c j -> p (c j)')), 16)
            evD_w[ic] = (sw.h, sw.inc(
                gp.dma_start(out=Wu8[ic % 2][:],
                             in_=w_u[ic].rearrange('p c j -> p (c j)')), 16))
    GP(d_w01)

    for ic in range(IC):
        if ic >= 2:
            def gp_w(gp, ic=ic):
                gp.wait_ge(s_pe.h, evD_done[ic - 2])
                sw = s_wgu[ic % 2]
                sw.inc(gp.dma_start(
                    out=Wg8[ic % 2][:],
                    in_=w_g[ic].rearrange('p c j -> p (c j)')), 16)
                evD_w[ic] = (sw.h, sw.inc(gp.dma_start(
                    out=Wu8[ic % 2][:],
                    in_=w_u[ic].rearrange('p c j -> p (c j)')), 16))
            GP(gp_w)

        for th in range(2):
            def te_f(te, ic=ic, th=th):
                if ic == 0 and th == 0:
                    te.wait_ge(s_ve.h, A_XN)
                    te.wait_ge(s_sc.h, evB_mm['dt_sc'])
                    te.wait_ge(s_sc.h, evB_gsil[31])
                    te.wait_ge(s_ve.h, B_SLOCAL)
                if th == 0:
                    te.wait_ge(*evD_w[ic])
                if ic >= 1:
                    te.wait_ge(s_ve.h, evD_fuse[(ic - 1, th)])
                t0 = 3 + th * 512
                regs = ([(T1, 0), (T1, 512)] if th == 0
                        else [(T1, 1024), (T2, 0)])
                ins = None
                for Wt, (reg, off) in [(Wg8, regs[0]), (Wu8, regs[1])]:
                    dst = reg[:, off:off + 512]
                    for cc in range(CC):
                        ins = nc.tensor.matmul(
                            dst, Wt[ic % 2][:, cc * 128:(cc + 1) * 128],
                            xn_fm[:, cc * TW + t0:cc * TW + t0 + 512],
                            start=(cc == 0), stop=(cc == CC - 1))
                evD_mm[(ic, th)] = s_pe.inc(ins)
                if th == 1:
                    evD_done[ic] = s_pe.v
            TE(te_f)

            def sc_f(sc, ic=ic, th=th):
                sc.wait_ge(s_pe.h, evD_mm[(ic, th)])
                if ic == 0 and th == 0:
                    sc.wait_ge(s_dc.h, D_CONST[0])
                if last_fuse[0] is not None:
                    sc.wait_ge(s_ve.h, last_fuse[0])
                greg, goff = (T1, 0) if th == 0 else (T1, 1024)
                ins = sc.activation(out=gsil_t[:], in_=greg[:, goff:goff + 512],
                                    func=AF.Silu, scale=gs_sb[:, ic:ic + 1])
                evD_sil[(ic, th)] = s_sc.inc(ins)
            SC(sc_f)

            def ve_f(v, ic=ic, th=th):
                v.wait_ge(s_sc.h, evD_sil[(ic, th)])
                if ic >= 2:
                    v.wait_ge(*evD_fst[ic - 2])
                ureg, uoff = (T1, 512) if th == 0 else (T2, 0)
                ins = v.scalar_tensor_tensor(
                    out=fuse_t[ic % 2][:, th * 512:(th + 1) * 512],
                    in0=ureg[:, uoff:uoff + 512],
                    scalar=us_sb[:, ic:ic + 1],
                    in1=gsil_t[:], op0=ALU.mult, op1=ALU.mult)
                evD_fuse[(ic, th)] = s_ve.inc(ins)
                last_fuse[0] = evD_fuse[(ic, th)]
            VE(ve_f)

        def sy_f(e, ic=ic):
            e.wait_ge(s_ve.h, evD_fuse[(ic, 1)])
            sf = s_fst[ic % 2]
            evD_fst[ic] = (sf.h, sf.inc(
                e.dma_start(out=fuse_stash.ap()[ic], in_=fuse_t[ic % 2][:]), 16))
        SY(sy_f)

    Wd8 = [sbD(f'Wd8{i}', [128, 8 * 512], FP8) for i in range(2)]
    fuT = [sbD(f'fuT{i}', [128, 8 * 128], BF16) for i in range(2)]
    dn_t = sbD('dn_t', [128, 512], BF16)

    evD_dw = {}
    evD_dfu = {}
    evD_dmm = {}
    evD_dev = {}
    evD_dst = {}
    dn_pass = [(tbh, osq) for tbh in range(2) for osq in range(4)]
    fu_k = [0]
    dn_prev = [None]

    def dreg(tb4):
        return (T1[:, tb4 * 512:tb4 * 512 + 512] if tb4 < 3
                else T2[:, 0:512])

    def d_dw01(gp):
        for k12 in range(2):
            pi, sg = divmod(k12, 6)
            tbh, osq = dn_pass[pi]
            sw = s_wd[k12 % 2]
            evD_dw[k12] = (sw.h, sw.inc(gp.dma_start(
                out=Wd8[k12 % 2][:].rearrange('p (a b) -> p a b', a=8),
                in_=w_d[sg * 8:(sg + 1) * 8, :,
                        osq * 512:(osq + 1) * 512].transpose([1, 0, 2])), 16))
    GP(d_dw01)

    for pi, (tbh, osq) in enumerate(dn_pass):
        for sg in range(6):
            k12 = pi * 6 + sg
            if k12 >= 2:
                def gp_w(gp, k12=k12):
                    pi2, sg2 = divmod(k12, 6)
                    tbh2, osq2 = dn_pass[pi2]
                    gp.wait_ge(s_pe.h, evD_dmm[('sgdone', k12 - 2)])
                    sw = s_wd[k12 % 2]
                    evD_dw[k12] = (sw.h, sw.inc(gp.dma_start(
                        out=Wd8[k12 % 2][:].rearrange('p (a b) -> p a b', a=8),
                        in_=w_d[sg2 * 8:(sg2 + 1) * 8, :,
                                osq2 * 512:(osq2 + 1) * 512].transpose(
                            [1, 0, 2])), 16))
                GP(gp_w)

            for tb4 in range(4):
                tb = tbh * 4 + tb4
                k = fu_k[0]

                def sy_fu(e, pi=pi, sg=sg, tb4=tb4, tb=tb, k=k):
                    if k >= 2:
                        e.wait_ge(s_pe.h, evD_dmm[('fu', k - 2)])
                    if k == 0:
                        e.wait_ge(s_fst[0].h, s_fst[0].v)
                        e.wait_ge(s_fst[1].h, s_fst[1].v)
                    sf = s_fu[k % 2]
                    evD_dfu[k] = (sf.h, sf.inc(e.dma_start(
                        out=fuT[k % 2][:].rearrange('p (a b) -> p a b', a=8),
                        in_=fuse_stash.ap()[sg * 8:(sg + 1) * 8, :,
                                            tb * 128:(tb + 1) * 128].transpose(
                            [1, 0, 2])), 16))
                SY(sy_fu)

                def te_f(te, pi=pi, sg=sg, tb4=tb4, k=k, k12=k12):
                    te.wait_ge(*evD_dfu[k])
                    if tb4 == 0:
                        te.wait_ge(*evD_dw[k12])
                    if pi == 0 and sg == 0 and tb4 == 0:
                        te.wait_ge(s_ve.h, last_fuse[0])
                        te.wait_ge(s_ve.h, B_SLOCAL)
                    if pi >= 1 and sg == 0:
                        te.wait_ge(s_dst.h, evD_dst[(pi - 1, tb4)])
                    dst = dreg(tb4)
                    ins = None
                    for j in range(8):
                        ins = nc.tensor.matmul(
                            dst, fuT[k % 2][:, j * 128:(j + 1) * 128],
                            Wd8[k12 % 2][:, j * 512:(j + 1) * 512],
                            start=(sg == 0 and j == 0),
                            stop=(sg == 5 and j == 7))
                    evD_dmm[(pi, tb4)] = s_pe.inc(ins)
                    evD_dmm[('fu', k)] = s_pe.v
                    if tb4 == 3:
                        evD_dmm[('sgdone', k12)] = s_pe.v
                TE(te_f)
                fu_k[0] += 1

        for tb4 in range(4):
            tb = tbh * 4 + tb4

            def ve_f(v, pi=pi, osq=osq, tb4=tb4, tb=tb):
                v.wait_ge(s_pe.h, evD_dmm[(pi, tb4)])
                v.wait_ge(s_dc.h, D_CONST[0])
                if dn_prev[0] is not None:
                    v.wait_ge(s_dst.h, evD_dst[dn_prev[0]])
                o0 = osq * 512
                ins = v.tensor_tensor(out=dn_t[:], in0=dreg(tb4),
                                      in1=ds_sb[:, o0:o0 + 512], op=ALU.mult)
                evD_dev[(pi, tb4)] = s_ve.inc(ins)
            VE(ve_f)

            def sy_s(e, pi=pi, osq=osq, tb4=tb4, tb=tb):
                e.wait_ge(s_ve.h, evD_dev[(pi, tb4)])
                o0 = osq * 512
                evD_dst[(pi, tb4)] = s_dst.inc(e.dma_start(
                    out=ffn_stash.ap()[tb][:, o0:o0 + 512], in_=dn_t[:]), 16)
            SY(sy_s)
            dn_prev[0] = (pi, tb4)

    D_EV_ALL = [None]

    def d_fin(v):
        v.wait_ge(s_dst.h, evD_dst[(7, 3)])
        D_EV_ALL[0] = s_ve.inc(v.drain())
    VE(d_fin)

    def gp_cc(gp):
        gp.wait_ge(s_exp.h, evB_exp['done'])
        s_cc.inc(gp.collective_compute(
            'AllGather', ALU.bypass, replica_groups=[list(range(NCORES))],
            ins=[cc_in.ap().opt()], outs=[cc_out.ap().opt()]))
    GP(gp_cc)
    CC_DONE = s_cc.v

    if max_phase == 'D':
        return _early_exit([(s_ve.h, D_EV_ALL[0]), (s_cc.h, CC_DONE)], (esD,))
    esD.close()

    esE = ExitStack()

    def sbE(name, shape, dtype):
        return esE.enter_context(nc.sbuf_tensor(name, shape, dtype))

    mask_sb = sbE('mask_sb', [128, 128], BF16)
    drep_bf = sbE('drep_bf', [128, 64], BF16)
    nw_sb = sbE('nw_sb', [128, 32], F32)
    onebf_sb = sbE('onebf_sb', [128, 1], BF16)
    onef_sb = sbE('onef_sb', [1, 128], F32)
    mprev_sb = sbE('mprev_sb', [128, 8], F32)
    Sj = [sbE('Sj0', [128, D_IN], F32)] * 2
    dAj = sbE('dAj', [64, 1], F32)
    dj = sbE('dj', [64, 1], F32)
    S_run_bf = sbE('S_run_bf', [128, D_IN], BF16)

    evE = {}

    E_CONST = [0]

    def e_consts(e):
        e.wait_ge(s_ve.h, D_EV_ALL[0])
        for dst, srcap in [(mask_sb, mask_ge), (nw_sb, norm_w_t),
                           (onebf_sb, ones_bf), (onef_sb, ones_f),
                           (mprev_sb, mask_prev)]:
            s_ec.inc(e.dma_start(out=dst[:], in_=srcap), 16)
        E_CONST[0] = s_ec.v
    SY(e_consts)

    E_DREP = [0]

    def e_drep(gp):
        gp.wait_ge(s_ve.h, D_EV_ALL[0])
        E_DREP[0] = s_dr.inc(gp.dma_start(out=drep_bf[:], in_=d_rep), 16)
    GP(e_drep)

    def e_sin_zero(v):
        v.wait_ge(s_exp.h, evB_exp['done'])
        v.memset(S_run[:], 0.0)
        evE['zero'] = s_ve.inc(v.drain())
    VE(e_sin_zero)

    for j in range(NCORES):
        def e_loads(e, j=j):
            if j == 0:
                e.wait_ge(s_cc.h, CC_DONE)
                e.wait_ge(s_ve.h, D_EV_ALL[0])
            if j >= 1:
                e.wait_ge(s_ve.h, evE[('sacc', j - 1)])
            s_ej.inc(
                e.dma_start(out=Sj[j % 2][:],
                            in_=cc_out.ap()[j * 128:(j + 1) * 128, 0:D_IN]), 16)
            both = s_ej.inc(
                e.dma_start(out=dAj[:],
                            in_=cc_out.ap()[j * 128:j * 128 + 64,
                                            D_IN:D_IN + 1],
                            allow_slow_non_contiguous=True), 16)
            evE[('sj', j)] = (s_ej.h, both)
            evE[('dAj', j)] = (s_ej.h, both)
        SY(e_loads)

        def e_dj(sc, j=j):
            sc.wait_ge(*evE[('dAj', j)])
            if j == 0:
                sc.wait_ge(s_ec.h, E_CONST[0])
            if j >= 1:
                sc.wait_ge(s_ve.h, evE[('sacc', j - 1)])
            ins = sc.activation(out=dj[:], in_=dAj[:], func=AF.Exp,
                                scale=mprev_sb[0:64, j:j + 1])
            evE[('dj', j)] = s_sc.inc(ins)
        SC(e_dj)

        def te_dj(te, j=j):
            te.wait_ge(s_sc.h, evE[('dj', j)])
            if j == 0:
                te.wait_ge(s_ve.h, D_EV_ALL[0])
            k = gF[0]
            if k >= 2:
                te.wait_ge(s_ve.h, evGFe[k - 2])
            te.transpose(fpbank(k)[:, 0:64],
                         dj[0:64, 0:1].broadcast_to([64, 128]),
                         idf_sb[0:64, 0:64])
            evGF[k] = s_pe.inc(te.drain())
            gF[0] += 1
        TE(te_dj)

        def ve_dj(v, j=j):
            k = gF[0] - 1
            if j == 0:
                v.wait_ge(s_ec.h, E_CONST[0])
            v.wait_ge(s_pe.h, evGF[k])
            evGFe[k] = s_ve.inc(v.tensor_copy(
                out=dec_row[:, 0:64], in_=fpbank(k)[:, 0:64]))
            v.drain()
            v.wait_ge(*evE[('sj', j)])
            if j == 0:
                v.wait_ge(s_ve.h, evE['zero'])
            ins = None
            for cg in range(8):
                sl = slice(cg * 512, (cg + 1) * 512)
                v.tensor_tensor(
                    out=S_run[:, sl].rearrange('p (h q) -> p h q', h=8),
                    in0=S_run[:, sl].rearrange('p (h q) -> p h q', h=8),
                    in1=dec_row[:, cg * 8:(cg + 1) * 8].unsqueeze(2).broadcast_to(
                        [128, 8, 64]),
                    op=ALU.mult)
                v.drain()
                ins = v.scalar_tensor_tensor(
                    out=S_run[:, sl], in0=Sj[j % 2][:, sl],
                    scalar=mprev_sb[:, j:j + 1], in1=S_run[:, sl],
                    op0=ALU.mult, op1=ALU.add)
                v.drain()
            evE[('sacc', j)] = s_ve.inc(v.drain())
        VE(ve_dj)

    E_SIN = evE[('sacc', NCORES - 1)]

    if 'sin' in dbg:
        def d_sin(e):
            e.wait_ge(s_ve.h, E_SIN)
            s_dbg.inc(e.dma_start(out=dbg['sin'], in_=S_run[:]), 16)
        SY(d_sin)

    def e_sbf(v):
        v.wait_ge(s_ve.h, E_SIN)
        evE['sbf'] = s_ve.inc(v.tensor_copy(out=S_run_bf[:], in_=S_run[:]))
    VE(e_sbf)

    xTc = [sbE(f'xTc{i}', [128, D_IN], BF16) for i in range(2)]
    gsz = [sbE(f'gsz{i}', [128, DC * 128], BF16) for i in range(2)]
    G_m = sbE('G_m', [128, 128], BF16)
    diffc = sbE('diffc', [128, 8 * 128], F32)
    diffe = sbE('diffe', [128, 8 * 128], BF16)
    M_sb = sbE('M_sb', [128, HEADS * 128], BF16)
    ytmp = sbE('ytmp', [128, 512], F32)
    dterm = sbE('dterm', [128, D_IN], BF16)
    y_tok = sbE('y_tok', [128, D_IN], BF16)
    y_fm = sbE('y_fm', [128, DC * 128], BF16)
    hgn_c = sbE('hgn_c', [128, DC * 128], BF16)
    rstd = sbE('rstd', [1, 128], F32)
    xd_E = sbE('xd_E', [128, D_IN], BF16)

    evF = {}

    def e_pre_loads(e):
        e.wait_ge(s_ve.h, D_EV_ALL[0])
        e.wait_ge(s_xtst.h, s_xtst.v)
        e.wait_ge(s_gst[0].h, s_gst[0].v)
        e.wait_ge(s_gst[1].h, s_gst[1].v)
        s_ez[0].inc(e.dma_start(out=xTc[0][:], in_=xT_stash.ap()[0]), 16)
        both = s_ez[0].inc(
            e.dma_start(out=gsz[0][:].rearrange('p (a b) -> p a b', a=DC),
                        in_=g_stash.ap()[:, :, 0:128].transpose([1, 0, 2])), 16)
        evF[('xt', 0)] = (s_ez[0].h, both)
        evF[('gs', 0)] = (s_ez[0].h, both)
    SY(e_pre_loads)

    for z in range(NT):
        def e_next_loads(e, z=z):
            if z + 1 < NT:
                if z >= 1:
                    e.wait_ge(s_ve.h, evF[('hg', z - 1)])
                    e.wait_ge(s_pe.h, evF[('ydiag', z - 1, 7)])
                sz_ = s_ez[(z + 1) % 2]
                sz_.inc(e.dma_start(out=xTc[(z + 1) % 2][:],
                                    in_=xT_stash.ap()[z + 1]), 16)
                both = sz_.inc(
                    e.dma_start(
                        out=gsz[(z + 1) % 2][:].rearrange('p (a b) -> p a b', a=DC),
                        in_=g_stash.ap()[:, :, (z + 1) * 128:(z + 2) * 128].transpose(
                            [1, 0, 2])), 16)
                evF[('xt', z + 1)] = (sz_.h, both)
                evF[('gs', z + 1)] = (sz_.h, both)
        SY(e_next_loads)

        def te_g(te, z=z):
            if z == 0:
                te.wait_ge(s_ve.h, evE['sbf'])
                te.wait_ge(s_ve.h, D_EV_ALL[0])
            else:
                te.wait_ge(s_ve.h, evF[('gm', z - 1)])
                te.wait_ge(s_ve.h, evF[('yoffs', z - 1, 7)])
            ins = nc.tensor.matmul(
                T1[:, 0:128], B_fm[:, z * 128:(z + 1) * 128],
                C_fm[:, z * 128:(z + 1) * 128], start=True, stop=True)
            evF[('gmm', z)] = s_pe.inc(ins)
        TE(te_g)

        def ve_g(v, z=z):
            v.wait_ge(s_pe.h, evF[('gmm', z)])
            if z == 0:
                v.wait_ge(s_ec.h, E_CONST[0])
            if z >= 1:
                v.wait_ge(s_ve.h, evF[('m', z - 1, 7)])
            ins = v.tensor_tensor(out=G_m[:], in0=T1[:, 0:128],
                                  in1=mask_sb[:], op=ALU.mult)
            evF[('gm', z)] = s_ve.inc(ins)
        VE(ve_g)

        for hb in range(8):
            for sub in range(2):
                def te_d(te, z=z, hb=hb, sub=sub):
                    if z == 0 and hb == 0 and sub == 0:
                        te.wait_ge(s_ve.h, D_EV_ALL[0])
                    k = gF[0]
                    if k >= 2:
                        te.wait_ge(s_ve.h, evGFe[k - 2])
                    bank = fpbank(k)
                    for jj in range(4):
                        j = sub * 4 + jj
                        h = hb * 8 + j
                        te.transpose(
                            bank[:, jj * 128:jj * 128 + 128],
                            grel_T[:, z * 64 + h:z * 64 + h + 1].broadcast_to(
                                [128, 128]),
                            idf_sb[:])
                    evGF[k] = s_pe.inc(te.drain())
                    gF[0] += 1
                TE(te_d)

                def ve_d(v, z=z, hb=hb, sub=sub):
                    k = gF[0] - 1
                    if sub == 0:
                        prev = evF.get(('diffe', z, hb - 1)) if hb > 0 else (
                            evF.get(('diffe', z - 1, 7)) if z > 0 else None)
                        if prev is not None:
                            v.wait_ge(s_sc.h, prev)
                    v.wait_ge(s_pe.h, evGF[k])
                    bank = fpbank(k)
                    ins = None
                    for jj in range(4):
                        j = sub * 4 + jj
                        h = hb * 8 + j
                        ins = v.tensor_scalar(
                            out=diffc[:, j * 128:(j + 1) * 128],
                            in0=bank[:, jj * 128:jj * 128 + 128],
                            scalar1=sgrel_T[:, z * 64 + h:z * 64 + h + 1],
                            scalar2=30.0, op0=ALU.subtract, op1=ALU.min)
                        v.drain()
                    evGFe[k] = s_ve.inc(v.drain())
                    evF[('diffc', z, hb)] = s_ve.v
                VE(ve_d)

            def sc_d(sc, z=z, hb=hb):
                sc.wait_ge(s_ve.h, evF[('diffc', z, hb)])
                prevm = evF.get(('m', z, hb - 1)) if hb > 0 else (
                    evF.get(('m', z - 1, 7)) if z > 0 else None)
                if prevm is not None:
                    sc.wait_ge(s_ve.h, prevm)
                ins = sc.activation(out=diffe[:], in_=diffc[:], func=AF.Exp)
                evF[('diffe', z, hb)] = s_sc.inc(ins)
            SC(sc_d)

            def ve_m(v, z=z, hb=hb):
                v.wait_ge(s_sc.h, evF[('diffe', z, hb)])
                v.wait_ge(s_ve.h, evF[('gm', z)])
                if z >= 1 and hb == 0:
                    v.wait_ge(s_pe.h, evF[('ydiag', z - 1, 7)])
                ins = v.tensor_tensor(
                    out=M_sb[:, hb * 1024:(hb + 1) * 1024].rearrange(
                        'p (a b) -> p a b', a=8),
                    in0=diffe[:].rearrange('p (a b) -> p a b', a=8),
                    in1=G_m[:].unsqueeze(1).broadcast_to([128, 8, 128]),
                    op=ALU.mult)
                evF[('m', z, hb)] = s_ve.inc(ins)
            VE(ve_m)

        for cg in range(8):
            def te_y(te, z=z, cg=cg):
                offb, diagb = ybanks(cg)
                if cg == 0:
                    te.wait_ge(*evF[('xt', z)])
                    te.wait_ge(s_ve.h, evF[('gm', z)])
                    if z >= 1:
                        te.wait_ge(s_ve.h, evF[('supd', z - 1, 7)])
                        te.wait_ge(s_ve.h, evF[('hgn', z - 1)])
                if cg == 1 and z >= 1:
                    te.wait_ge(s_ve.h, evF[('supd', z - 1, 7)])
                te.wait_ge(s_ve.h, evF[('m', z, cg)])
                if cg >= 2:
                    te.wait_ge(s_ve.h, evF[('yoffs', z, cg - 2)])
                    te.wait_ge(s_ve.h, evF[('yfin', z, cg - 2)])
                ins = nc.tensor.matmul(
                    offb, C_fm[:, z * 128:(z + 1) * 128],
                    S_run_bf[:, cg * 512:(cg + 1) * 512], start=True, stop=True)
                evF[('yoff', z, cg)] = s_pe.inc(ins)
                ins = None
                for j in range(8):
                    h = cg * 8 + j
                    ins = nc.tensor.matmul(
                        diagb[:, j * 64:j * 64 + 64],
                        M_sb[:, h * 128:(h + 1) * 128],
                        xTc[z % 2][:, h * 64:(h + 1) * 64],
                        start=True, stop=True)
                    if j == 0 and cg <= 1:
                        pass
                evF[('ydiag', z, cg)] = s_pe.inc(ins)
            TE(te_y)

            def ve_y(v, z=z, cg=cg):
                offb, diagb = ybanks(cg)
                v.wait_ge(s_pe.h, evF[('yoff', z, cg)])
                if z == 0 and cg == 0:
                    v.wait_ge(s_dr.h, E_DREP[0])
                v.tensor_tensor(
                    out=ytmp[:].rearrange('p (a b) -> p a b', a=8),
                    in0=offb.rearrange('p (a b) -> p a b', a=8),
                    in1=erel_T[:, z * 64 + cg * 8:z * 64 + (cg + 1) * 8].unsqueeze(
                        2).broadcast_to([128, 8, 64]),
                    op=ALU.mult)
                evF[('yoffs', z, cg)] = s_ve.inc(v.drain())
                v.tensor_tensor(
                    out=dterm[:, 0:512].rearrange('p (a b) -> p a b', a=8),
                    in0=xTc[z % 2][:, cg * 512:(cg + 1) * 512].rearrange(
                        'p (a b) -> p a b', a=8),
                    in1=drep_bf[:, cg * 8:(cg + 1) * 8].unsqueeze(2).broadcast_to(
                        [128, 8, 64]),
                    op=ALU.mult)
                v.drain()
                v.tensor_tensor(out=ytmp[:], in0=ytmp[:],
                                in1=dterm[:, 0:512], op=ALU.add)
                v.drain()
                v.wait_ge(s_pe.h, evF[('ydiag', z, cg)])
                ins = v.tensor_tensor(out=y_tok[:, cg * 512:(cg + 1) * 512],
                                      in0=diagb, in1=ytmp[:],
                                      op=ALU.add)
                evF[('yfin', z, cg)] = s_ve.inc(ins)
            VE(ve_y)

        def ve_xd2(v, z=z):
            v.wait_ge(s_ve.h, evF[('yfin', z, 7)])
            ins = v.tensor_tensor(
                out=xd_E[:].rearrange('p (h q) -> p h q', h=64),
                in0=xTc[z % 2][:].rearrange('p (h q) -> p h q', h=64),
                in1=dtE_T[:, z * 64:(z + 1) * 64].unsqueeze(2).broadcast_to(
                    [128, 64, 64]),
                op=ALU.mult)
            evF[('xd', z)] = s_ve.inc(ins)
        VE(ve_xd2)

        def te_dec2(te, z=z):
            te.wait_ge(s_ve.h, evF[('xd', z)])
            k = gF[0]
            if k >= 2:
                te.wait_ge(s_ve.h, evGFe[k - 2])
            te.transpose(fpbank(k)[:, 0:64],
                         exp_ctot[0:64, z:z + 1].broadcast_to([64, 128]),
                         idf_sb[0:64, 0:64])
            evGF[k] = s_pe.inc(te.drain())
            gF[0] += 1
        TE(te_dec2)

        def ve_dec2(v, z=z):
            k = gF[0] - 1
            v.wait_ge(s_pe.h, evGF[k])
            evGFe[k] = s_ve.inc(v.tensor_copy(
                out=dec_row[:, 0:64], in_=fpbank(k)[:, 0:64]))
            evF[('dec', z)] = evGFe[k]
        VE(ve_dec2)

        for cg in range(8):
            def te_st2(te, z=z, cg=cg):
                te.wait_ge(s_ve.h, evF[('xd', z)])
                if cg >= 2:
                    te.wait_ge(s_ve.h, evF[('supd', z, cg - 2)])
                dst = (T1[:, 1024:1536] if cg % 2 == 0 else T2[:, 0:512])
                ins = nc.tensor.matmul(
                    dst, B_T[:, z * 128:(z + 1) * 128],
                    xd_E[:, cg * 512:(cg + 1) * 512], start=True, stop=True)
                evF[('smm', z, cg)] = s_pe.inc(ins)
            TE(te_st2)

            def ve_upd2(v, z=z, cg=cg):
                v.wait_ge(s_pe.h, evF[('smm', z, cg)])
                v.wait_ge(s_ve.h, evF[('dec', z)])
                sl = slice(cg * 512, (cg + 1) * 512)
                v.tensor_tensor(
                    out=S_run[:, sl].rearrange('p (h q) -> p h q', h=8),
                    in0=S_run[:, sl].rearrange('p (h q) -> p h q', h=8),
                    in1=dec_row[:, cg * 8:(cg + 1) * 8].unsqueeze(2).broadcast_to(
                        [128, 8, 64]),
                    op=ALU.mult)
                v.drain()
                v.tensor_tensor(
                    out=S_run[:, sl], in0=S_run[:, sl],
                    in1=(T1[:, 1024:1536] if cg % 2 == 0 else T2[:, 0:512]),
                    op=ALU.add)
                v.drain()
                ins = v.tensor_copy(out=S_run_bf[:, sl], in_=S_run[:, sl])
                evF[('supd', z, cg)] = s_ve.inc(ins)
            VE(ve_upd2)

        for qd in range(8):
            def te_yt(te, z=z, qd=qd):
                if qd == 0:
                    te.wait_ge(s_ve.h, evF[('yfin', z, 7)])
                k = gB[0]
                if k >= 2:
                    te.wait_ge(s_ve.h, evGBe[k - 2])
                bank = bfbank(k)
                for j in range(4):
                    dc = qd * 4 + j
                    te.transpose(bank[:, j * 128:j * 128 + 128],
                                 y_tok[:, dc * 128:(dc + 1) * 128], idbf_sb[:])
                evGB[k] = s_pe.inc(te.drain())
                gB[0] += 1
            TE(te_yt)

            def ve_yt(v, z=z, qd=qd):
                k = gB[0] - 1
                v.wait_ge(s_pe.h, evGB[k])
                bank = bfbank(k)
                ins = v.tensor_copy(
                    out=y_fm[:].rearrange('p (d w) -> p d w', d=DC)[
                        :, qd * 4:qd * 4 + 4, :],
                    in_=bank[:, 0:512].rearrange('p (j w) -> p j w', j=4))
                evGBe[k] = s_ve.inc(ins)
                evF[('yt', z, qd)] = s_ve.v
            VE(ve_yt)
        evF[('ydone', z)] = evF[('yt', z, 7)]

        if 'ytok' in dbg:
            def d_yt(e, z=z):
                e.wait_ge(s_ve.h, evF[('ydone', z)])
                s_dbg.inc(e.dma_start(out=dbg['ytok'][z * 128:(z + 1) * 128, :],
                                      in_=y_tok[:]), 16)
            SY(d_yt)

        def ve_hg(v, z=z):
            v.wait_ge(s_ve.h, evF[('ydone', z)])
            v.wait_ge(*evF[('gs', z)])
            if z >= 1:
                v.wait_ge(s_pe.h, evF[('var', z - 1)])
                v.wait_ge(s_hst.h, evF[('hgnst', z - 1)])
            v.tensor_tensor(out=y_fm[:], in0=y_fm[:], in1=gsz[z % 2][:],
                            op=ALU.mult)
            v.drain()
            ins = v.tensor_tensor(out=hgn_c[:], in0=y_fm[:], in1=y_fm[:],
                                  op=ALU.mult)
            evF[('hg', z)] = s_ve.inc(ins)
        VE(ve_hg)

        def te_var(te, z=z):
            te.wait_ge(s_ve.h, evF[('hg', z)])
            if z >= 1:
                te.wait_ge(s_sc.h, evF[('sqv', z - 1)])
            ins = None
            for dc in range(DC):
                ins = nc.tensor.matmul(
                    T1[0:1, 512:640], onebf_sb[:],
                    hgn_c[:, dc * 128:(dc + 1) * 128],
                    start=(dc == 0), stop=(dc == DC - 1))
            evF[('var', z)] = s_pe.inc(ins)
        TE(te_var)

        def sc_rstd(sc, z=z):
            sc.wait_ge(s_pe.h, evF[('var', z)])
            ins = sc.activation(out=rstd[:], in_=T1[0:1, 512:640],
                                func=AF.Sqrt, scale=cvals[0:1, 2:3],
                                bias=cvals[0:1, 1:2])
            evF[('sqv', z)] = s_sc.inc(ins)
        SC(sc_rstd)

        def ve_rstd(v, z=z):
            v.wait_ge(s_sc.h, evF[('sqv', z)])
            ins = v.reciprocal(out=rstd[:], in_=rstd[:])
            evF[('rstd', z)] = s_ve.inc(ins)
        VE(ve_rstd)

        def te_rb(te, z=z):
            te.wait_ge(s_ve.h, evF[('rstd', z)])
            if z >= 1:
                te.wait_ge(s_ve.h, evF[('hgn', z - 1)])
            ins = nc.tensor.matmul(T1[:, 640:768], onef_sb[:], rstd[:],
                                   start=True, stop=True)
            evF[('rb', z)] = s_pe.inc(ins)
        TE(te_rb)

        def ve_hgn(v, z=z):
            v.wait_ge(s_pe.h, evF[('rb', z)])
            if z >= 1:
                v.wait_ge(s_hst.h, evF[('hgnst', z - 1)])
            ins = None
            for dc in range(DC):
                ins = v.scalar_tensor_tensor(
                    out=hgn_c[:, dc * 128:(dc + 1) * 128],
                    in0=y_fm[:, dc * 128:(dc + 1) * 128],
                    scalar=nw_sb[:, dc:dc + 1], in1=T1[:, 640:768],
                    op0=ALU.mult, op1=ALU.mult)
                v.drain()
            evF[('hgn', z)] = s_ve.inc(ins)
        VE(ve_hgn)

        def sy_hgn(e, z=z):
            e.wait_ge(s_ve.h, evF[('hgn', z)])
            evF[('hgnst', z)] = s_hst.inc(
                e.dma_start(
                    out=hgn_stash.ap()[:, :, z * 128:(z + 1) * 128].transpose(
                        [1, 0, 2]),
                    in_=hgn_c[:].rearrange('p (a b) -> p a b', a=DC)), 16)
        SY(sy_hgn)

        if 'hgnz' in dbg:
            def d_hgn(e, z=z):
                e.wait_ge(s_ve.h, evF[('hgn', z)])
                s_dbg.inc(e.dma_start(out=dbg['hgnz'][z * 128:(z + 1) * 128, :],
                                      in_=hgn_c[:]), 16)
            SY(d_hgn)
        if 'gsilz' in dbg:
            def d_gsl(e, z=z):
                e.wait_ge(*evF[('gs', z)])
                s_dbg.inc(e.dma_start(out=dbg['gsilz'][z * 128:(z + 1) * 128, :],
                                      in_=gsz[z % 2][:]), 16)
            SY(d_gsl)

    if max_phase == 'F':
        return _early_exit([(s_hst.h, evF[('hgnst', NT - 1)]),
                            (s_ve.h, evF[('hgn', NT - 1)])], (esE,))
    esE.close()
    esG = ExitStack()

    def sbG(name, shape, dtype):
        return esG.enter_context(nc.sbuf_tensor(name, shape, dtype))

    Wo_bf = sbG('Wo_bf', [128, DC * 512], BF16)
    hgnT = [sbG(f'hgnT{i}', [128, DC * 128], BF16) for i in range(2)]
    xq_t = [sbG(f'xq_t{i}', [128, 512], F32) for i in range(2)]
    fq_t = [sbG(f'fq_t{i}', [128, 512], BF16) for i in range(2)]
    o_t = sbG('o_t', [128, 512], F32)

    evG = {}

    for q in range(4):
        def g_wload(gp, q=q):
            if q == 0:
                gp.wait_ge(s_ve.h, evF[('hgn', NT - 1)])
                gp.wait_ge(s_pe.h, evF[('rb', NT - 1)])
                gp.wait_ge(s_sc.h, evF[('sqv', NT - 1)])
                gp.wait_ge(s_hst.h, evF[('hgnst', NT - 1)])
            if q >= 1:
                gp.wait_ge(s_pe.h, evG[('mmlast', q - 1)])
            evG[('w', q)] = s_wo.inc(
                gp.dma_start(out=Wo_bf[:].rearrange('p (a b) -> p a b', a=DC),
                             in_=w_oT[q].transpose([1, 0, 2])), 16)
        GP(g_wload)

        for z in range(NT):
            def g_loads(e, q=q, z=z):
                kz = q * NT + z
                if kz >= 2:
                    e.wait_ge(s_ve.h, evG[('fin', (kz - 2) // NT, (kz - 2) % NT)])
                e.wait_ge(s_hst.h, evF[('hgnst', z)])
                if kz == 0:
                    e.wait_ge(s_dst.h, s_dst.v)
                    e.wait_ge(s_ve.h, evF[('hgn', NT - 1)])
                    e.wait_ge(s_pe.h, evF[('rb', NT - 1)])
                    e.wait_ge(s_sc.h, evF[('sqv', NT - 1)])
                sg2 = s_gl[kz % 2]
                sg2.inc(
                    e.dma_start(
                        out=hgnT[kz % 2][:].rearrange('p (a b) -> p a b', a=DC),
                        in_=hgn_stash.ap()[:, :, z * 128:(z + 1) * 128].transpose(
                            [1, 0, 2])), 16)
                sg2.inc(
                    e.dma_start(out=xq_t[kz % 2][:],
                                in_=x_main[z * 128:(z + 1) * 128,
                                           q * 512:(q + 1) * 512]), 16)
                all3 = sg2.inc(
                    e.dma_start(out=fq_t[kz % 2][:],
                                in_=ffn_stash.ap()[z][:, q * 512:(q + 1) * 512]), 16)
                evG[('h', q, z)] = (sg2.h, all3)
                evG[('x', q, z)] = (sg2.h, all3)
                evG[('f', q, z)] = (sg2.h, all3)
            SY(g_loads)

            def g_mm(te, q=q, z=z):
                kz = q * NT + z
                te.wait_ge(*evG[('h', q, z)])
                if z == 0:
                    te.wait_ge(s_wo.h, evG[('w', q)])
                if kz == 0:
                    te.wait_ge(s_ve.h, evF[('yfin', NT - 1, 7)])
                if kz >= 2:
                    te.wait_ge(s_ve.h, evG[('ev', (kz - 2) // NT, (kz - 2) % NT)])
                dst = T1[:, (kz % 2) * 512:(kz % 2) * 512 + 512]
                ins = None
                for dc in range(DC):
                    ins = nc.tensor.matmul(
                        dst, hgnT[kz % 2][:, dc * 128:(dc + 1) * 128],
                        Wo_bf[:, dc * 512:(dc + 1) * 512],
                        start=(dc == 0), stop=(dc == DC - 1))
                evG[('mm', q, z)] = s_pe.inc(ins)
                if z == NT - 1:
                    evG[('mmlast', q)] = s_pe.v
            TE(g_mm)

            def g_fin(v, q=q, z=z):
                kz = q * NT + z
                v.wait_ge(s_pe.h, evG[('mm', q, z)])
                v.wait_ge(*evG[('x', q, z)])
                if kz >= 1:
                    pq, pz = (kz - 1) // NT, (kz - 1) % NT
                    v.wait_ge(s_out.h, evG[('st', pq, pz)])
                dst = T1[:, (kz % 2) * 512:(kz % 2) * 512 + 512]
                v.tensor_tensor(out=o_t[:], in0=dst, in1=xq_t[kz % 2][:],
                                op=ALU.add)
                evG[('ev', q, z)] = s_ve.inc(v.drain())
                ins = v.tensor_tensor(out=o_t[:], in0=o_t[:], in1=fq_t[kz % 2][:],
                                      op=ALU.add)
                evG[('fin', q, z)] = s_ve.inc(ins)
            VE(g_fin)

            def g_store(e, q=q, z=z):
                e.wait_ge(s_ve.h, evG[('fin', q, z)])
                evG[('st', q, z)] = s_out.inc(
                    e.dma_start(out=out_hbm[z * 128:(z + 1) * 128,
                                            q * 512:(q + 1) * 512],
                                in_=o_t[:]), 16)
            SY(g_store)

    def final_wait(e):
        e.wait_ge(s_out.h, s_out.v)
    SY(final_wait)

    return nc


def _prep_shared(inputs):
    f32 = np.float32
    W = np.ascontiguousarray(inputs['L0_attn_in_proj_w'], dtype=f32)
    Wpad = np.zeros((NG * 128, DM), f32)
    Wpad[:8512] = W
    w_inT = np.ascontiguousarray(
        Wpad.reshape(NG, 128, CC, 128).transpose(0, 3, 2, 1))
    gw = np.ascontiguousarray(inputs['L0_ffn_gate_w'], f32)
    uw = np.ascontiguousarray(inputs['L0_ffn_up_w'], f32)
    w_g = np.ascontiguousarray(gw.reshape(CC, 128, IC, 128).transpose(2, 1, 0, 3))
    w_u = np.ascontiguousarray(uw.reshape(CC, 128, IC, 128).transpose(2, 1, 0, 3))
    w_d = np.ascontiguousarray(inputs['L0_ffn_down_w'], f32).reshape(IC, 128, DM)
    Wo = np.ascontiguousarray(inputs['L0_attn_out_proj_w'], f32)
    w_oT = np.ascontiguousarray(Wo.reshape(4, 512, DC, 128).transpose(0, 2, 3, 1))
    conv_w = np.ascontiguousarray(inputs['L0_attn_conv_w'], f32)
    conv_w_t = np.ascontiguousarray(
        conv_w.reshape(34, 128, 4).transpose(1, 0, 2).reshape(128, 136))
    conv_b_t = np.ascontiguousarray(
        inputs['L0_attn_conv_b'].astype(f32).reshape(34, 128).T)
    dt_bias = inputs['L0_attn_dt_bias'].astype(f32).reshape(64, 1)
    a_log = inputs['L0_attn_A_log'].astype(f32).reshape(64, 1)
    d_rep = np.ascontiguousarray(
        np.broadcast_to(inputs['L0_attn_D'].astype(f32)[None, :], (128, 64)))
    norm_w_t = np.ascontiguousarray(
        inputs['L0_attn_norm_w'].astype(f32).reshape(32, 128).T)
    gs_t = np.ascontiguousarray(
        inputs['L0_ffn_gate_scale'].astype(f32).reshape(IC, 128).T)
    us_t = np.ascontiguousarray(
        inputs['L0_ffn_up_scale'].astype(f32).reshape(IC, 128).T)
    ds_b = np.ascontiguousarray(
        np.broadcast_to(inputs['L0_ffn_down_scale'].astype(f32)[None, :],
                        (128, DM)))
    ident_bf = np.eye(128, dtype=ml_dtypes.bfloat16)
    ident_f = np.eye(128, dtype=f32)
    s_idx = np.arange(128)[:, None]
    t_idx = np.arange(128)[None, :]
    mask_ge = (t_idx >= s_idx).astype(ml_dtypes.bfloat16)
    ones_bfv = np.ones((128, 1), ml_dtypes.bfloat16)
    ones_fv = np.ones((1, 128), f32)
    return dict(w_inT=w_inT, w_g=w_g, w_u=w_u, w_d=w_d, w_oT=w_oT,
                conv_w_t=conv_w_t, conv_b_t=conv_b_t, dt_bias=dt_bias,
                a_log=a_log, d_rep=d_rep, norm_w_t=norm_w_t, gs_t=gs_t,
                us_t=us_t, ds_b=ds_b, ident_bf=ident_bf, ident_f=ident_f,
                mask_ge=mask_ge, ones_bf=ones_bfv, ones_f=ones_fv)


def make_in_maps(inputs):
    shared = _prep_shared(inputs)
    x = np.ascontiguousarray(inputs['L0_attn_hidden_states'],
                             np.float32).reshape(L, DM)
    in_maps = []
    for k in range(NCORES):
        m = dict(shared)
        m['x_main'] = np.ascontiguousarray(x[k * T:(k + 1) * T])
        if k == 0:
            m['x_halo'] = np.zeros((128, DM), np.float32)
        else:
            m['x_halo'] = np.ascontiguousarray(x[k * T - 128:k * T])
        mp = np.zeros((128, 8), np.float32)
        mp[:, :k] = 1.0
        m['mask_prev'] = mp
        in_maps.append(m)
    return in_maps


def kernel(**inputs):
    nc = build_nc()
    in_maps = make_in_maps(inputs)
    res = run_bass_kernel_spmd(nc, in_maps, core_ids=list(range(NCORES)))
    out = np.concatenate([np.asarray(r['out']) for r in res.results], axis=0)
    return out.reshape(1, L, DM).astype(np.float32)


# version 07
# speedup vs baseline: 1.0434x; 1.0153x over previous
import numpy as np
import ml_dtypes
import concourse.bass as bass
import concourse.mybir as mybir
from concourse.bass_utils import run_bass_kernel_spmd
from contextlib import ExitStack

F32 = mybir.dt.float32
BF16 = mybir.dt.bfloat16
FP8 = mybir.dt.float8e4
AF = mybir.ActivationFunctionType
ALU = mybir.AluOpType
AX = mybir.AxisListType

NCORES = 8
L = 8192
DM = 2048
T = 1024
NT = 8
D_IN = 4096
HEADS = 64
NG = 67
CC = 16
IC = 48
DC = 32
EPS = 1e-5
TW = 1027


class Sem:
    __slots__ = ('h', 'v')

    def __init__(self, h):
        self.h = h
        self.v = 0

    def inc(self, instr, n=1):
        instr.then_inc(self.h, n)
        self.v += n
        return self.v


def build_nc(debug_taps=(), max_phase='G'):
    nc = bass.Bass('TRN2', target_bir_lowering=False)

    x_main = nc.dram_tensor('x_main', [T, DM], F32, kind='ExternalInput').ap()
    x_halo = nc.dram_tensor('x_halo', [128, DM], F32, kind='ExternalInput').ap()
    w_inT = nc.dram_tensor('w_inT', [NG, 128, CC, 128], F32, kind='ExternalInput').ap()
    w_g = nc.dram_tensor('w_g', [IC, 128, CC, 128], F32, kind='ExternalInput').ap()
    w_u = nc.dram_tensor('w_u', [IC, 128, CC, 128], F32, kind='ExternalInput').ap()
    w_d = nc.dram_tensor('w_d', [IC, 128, DM], F32, kind='ExternalInput').ap()
    w_oT = nc.dram_tensor('w_oT', [4, DC, 128, 512], F32, kind='ExternalInput').ap()
    conv_w_t = nc.dram_tensor('conv_w_t', [128, 136], F32, kind='ExternalInput').ap()
    conv_b_t = nc.dram_tensor('conv_b_t', [128, 34], F32, kind='ExternalInput').ap()
    dt_bias = nc.dram_tensor('dt_bias', [64, 1], F32, kind='ExternalInput').ap()
    a_log = nc.dram_tensor('a_log', [64, 1], F32, kind='ExternalInput').ap()
    d_rep = nc.dram_tensor('d_rep', [128, 64], F32, kind='ExternalInput').ap()
    norm_w_t = nc.dram_tensor('norm_w_t', [128, 32], F32, kind='ExternalInput').ap()
    gs_t = nc.dram_tensor('gs_t', [128, IC], F32, kind='ExternalInput').ap()
    us_t = nc.dram_tensor('us_t', [128, IC], F32, kind='ExternalInput').ap()
    ds_b = nc.dram_tensor('ds_b', [128, DM], F32, kind='ExternalInput').ap()
    ident_bf = nc.dram_tensor('ident_bf', [128, 128], BF16, kind='ExternalInput').ap()
    ident_f = nc.dram_tensor('ident_f', [128, 128], F32, kind='ExternalInput').ap()
    mask_ge = nc.dram_tensor('mask_ge', [128, 128], BF16, kind='ExternalInput').ap()
    ones_bf = nc.dram_tensor('ones_bf', [128, 1], BF16, kind='ExternalInput').ap()
    ones_f = nc.dram_tensor('ones_f', [1, 128], F32, kind='ExternalInput').ap()
    mask_prev = nc.dram_tensor('mask_prev', [128, 8], F32, kind='ExternalInput').ap()
    out_hbm = nc.dram_tensor('out', [T, DM], F32, kind='ExternalOutput').ap()

    g_stash = nc.dram_tensor('g_stash', [32, 128, T], BF16)
    xT_stash = nc.dram_tensor('xT_stash', [NT, 128, D_IN], BF16)
    hgn_stash = nc.dram_tensor('hgn_stash', [DC, 128, T], BF16)
    fuse_stash = nc.dram_tensor('fuse_stash', [IC, 128, T], BF16)
    ffn_stash = nc.dram_tensor('ffn_stash', [NT, 128, DM], BF16)
    cc_in = nc.dram_tensor('cc_in', [128, 4097], F32)
    cc_out = nc.dram_tensor('cc_out', [NCORES * 128, 4097], F32)

    dbg_shapes = {
        'xn_fm': [128, CC * TW], 'xbcpre': [128, TW],
        'bfm': [128, T], 'cfm': [128, T], 'dtf': [64, T],
        'grel': [64, T], 'xT': [NT * 128, D_IN],
        'slocal': [128, D_IN], 'sin': [128, D_IN],
        'ytok': [NT * 128, D_IN], 'hgnz': [NT * 128, DC * 128],
        'gsilz': [NT * 128, DC * 128],
    }
    dbg = {}
    for name in debug_taps:
        dbg[name] = nc.dram_tensor('dbg_' + name, dbg_shapes[name], F32,
                                   kind='ExternalOutput').ap()

    es = ExitStack()

    def sb(name, shape, dtype):
        return es.enter_context(nc.sbuf_tensor(name, shape, dtype))

    def sem(name):
        return Sem(es.enter_context(nc.semaphore(name)))

    xn_fm = sb('xn_fm', [128, CC * TW], BF16)
    B_fm = sb('B_fm', [128, T], BF16)
    C_fm = sb('C_fm', [128, T], BF16)
    B_T = sb('B_T', [128, NT * 128], BF16)
    dt_fm = sb('dt_fm', [64, T], F32)
    ln_or_erel = sb('ln_or_erel', [64, T], F32)
    grel = sb('grel', [64, T], F32)
    sgrel = sb('sgrel', [64, T], F32)
    est_or_dA = sb('est_or_dA', [64, T], F32)
    grel_T = sb('grel_T', [128, NT * 64], F32)
    sgrel_T = sb('sgrel_T', [128, NT * 64], F32)
    dtE_T = sb('dtE_T', [128, NT * 64], F32)
    erel_T = sb('erel_T', [128, NT * 64], F32)
    exp_ctot = sb('exp_ctot', [64, 8], F32)
    ctot_sum = sb('ctot_sum', [64, 1], F32)
    dec_row = sb('dec_row', [128, 64], F32)
    S_run = sb('S_run', [128, D_IN], F32)

    idbf_sb = sb('idbf_sb', [128, 128], BF16)
    idf_sb = sb('idf_sb', [128, 128], F32)
    cvals = sb('cvals', [128, 4], F32)

    T1 = es.enter_context(nc.psum_tensor('T1', [128, 1536], F32))
    T2 = es.enter_context(nc.psum_tensor('T2', [128, 1024], F32))
    T3 = es.enter_context(nc.psum_tensor('T3', [128, 1024], BF16))
    T4 = es.enter_context(nc.psum_tensor('T4', [128, 1024], BF16))
    T5 = es.enter_context(nc.psum_tensor('T5', [128, 512], F32))

    def bfbank(k):
        return T3 if k % 2 == 0 else T4

    def fpbank(k):
        return T2[:, 512:1024] if k % 2 == 0 else T5[:, 0:512]

    def ybanks(cg):
        if cg % 2 == 0:
            return T1[:, 0:512], T1[:, 512:1024]
        return T1[:, 1024:1536], T2[:, 0:512]

    gB = [0]
    gF = [0]
    evGB = {}
    evGBe = {}
    evGF = {}
    evGFe = {}


    s_const = sem('s_const')
    s_pe = sem('s_pe')
    s_ve = sem('s_ve')
    s_sc = sem('s_sc')
    s_gp = sem('s_gp')
    s_cc = sem('s_cc')
    s_out = sem('s_out')
    s_dbg = sem('s_dbg')
    s_x = [sem('s_x0'), sem('s_x1')]
    s_bc = sem('s_bc')
    s_wb = [sem('s_wb0'), sem('s_wb1')]
    s_gst = [sem('s_gst0'), sem('s_gst1')]
    s_xtst = sem('s_xtst')
    s_exp = sem('s_exp')
    s_dc = sem('s_dc')
    s_wgu = [sem('s_wgu0'), sem('s_wgu1')]
    s_fst = [sem('s_fst0'), sem('s_fst1')]
    s_wd = [sem('s_wd0'), sem('s_wd1')]
    s_fu = [sem('s_fu0'), sem('s_fu1')]
    s_dst = sem('s_dst')
    s_hst = sem('s_hst')
    s_ej = sem('s_ej')
    s_dr = sem('s_dr')
    s_ec = sem('s_ec')
    s_ez = [sem('s_ez0'), sem('s_ez1')]
    s_wo = sem('s_wo')
    s_gl = [sem('s_gl0'), sem('s_gl1')]

    block = es.enter_context(nc.Block())

    def SY(f):
        @block.sync
        def _(e):
            f(e)

    def VE(f):
        @block.vector
        def _(e):
            f(e)

    def SC(f):
        @block.scalar
        def _(e):
            f(e)

    def GP(f):
        @block.gpsimd
        def _(e):
            f(e)

    def TE(f):
        @block.tensor
        def _(e):
            f(e)


    def _early_exit(wait_list, stacks=()):
        def fin(e):
            for sh, vv in wait_list:
                e.wait_ge(sh, vv)
            if s_dbg.v:
                e.wait_ge(s_dbg.h, s_dbg.v)
        SY(fin)
        for st_ in reversed(stacks):
            st_.close()
        es.close()
        return nc

    def const_loads(e):
        for dst, src in [(idbf_sb, ident_bf), (idf_sb, ident_f)]:
            s_const.inc(e.dma_start(out=dst[:], in_=src), 16)
    SY(const_loads)
    LD_CONST = s_const.v

    VE_SETUP = [0]

    def ve_setup(v):
        v.memset(cvals[:, 0:1], 1e-38)
        v.memset(cvals[:, 1:2], EPS)
        v.memset(cvals[:, 2:3], 1.0 / D_IN)
        VE_SETUP[0] = s_ve.inc(v.drain())
    VE(ve_setup)

    SC_SETUP = [0]

    PHA = [None]
    esWb = ExitStack()
    cw_sb = esWb.enter_context(nc.sbuf_tensor('cw_sb', [128, 136], F32))
    cb_sb = esWb.enter_context(nc.sbuf_tensor('cb_sb', [128, 34], F32))
    dtb_sb = esWb.enter_context(nc.sbuf_tensor('dtb_sb', [64, 1], F32))
    alog_sb = esWb.enter_context(nc.sbuf_tensor('alog_sb', [64, 1], F32))
    nA_sb = esWb.enter_context(nc.sbuf_tensor('nA_sb', [64, 1], F32))
    zeros64 = esWb.enter_context(nc.sbuf_tensor('zeros64', [64, 128], F32))
    Wb = [esWb.enter_context(nc.sbuf_tensor(f'Wb{i}', [128, CC * 128], BF16))
          for i in range(2)]

    B_CONST = [0]

    def b_consts(e):
        for dst, srcap in [(cw_sb, conv_w_t), (cb_sb, conv_b_t),
                           (dtb_sb, dt_bias), (alog_sb, a_log)]:
            s_bc.inc(e.dma_start(out=dst[:], in_=srcap), 16)
        B_CONST[0] = s_bc.v
    SY(b_consts)

    def b_setup_ve(v):
        v.wait_ge(s_bc.h, B_CONST[0])
        v.memset(zeros64[:], 0.0)
        BZERO[0] = s_ve.inc(v.drain())
    BZERO = [0]
    VE(b_setup_ve)

    def b_setup_sc(sc):
        sc.wait_ge(s_bc.h, B_CONST[0])
        sc.activation(out=nA_sb[:], in_=alog_sb[:], func=AF.Exp)
        sc.drain()
        sc.activation(out=nA_sb[:], in_=nA_sb[:], func=AF.Copy, scale=-1.0)
        SC_SETUP[0] = s_sc.inc(sc.drain())
    SC(b_setup_sc)

    evB_w = {}

    def b_w01(gp):
        for g in range(2):
            evB_w[g] = (s_wb[g % 2].h, s_wb[g % 2].inc(
                gp.dma_start(out=Wb[g % 2][:],
                             in_=w_inT[g].rearrange('p c j -> p (c j)')), 16))
    GP(b_w01)

    esA = ExitStack()

    def sbA(name, shape, dtype):
        return esA.enter_context(nc.sbuf_tensor(name, shape, dtype))

    x_t = [sbA(f'x_t{i}', [128, DM], F32) for i in range(2)]
    ln_sq = sbA('ln_sq', [128, DM], F32)
    xn_t = [sbA(f'xn_t{i}', [128, DM], BF16) for i in range(2)]
    ln_m = [sbA(f'ln_m{i}', [128, 1], F32) for i in range(2)]
    ln_v = [sbA(f'ln_v{i}', [128, 1], F32) for i in range(2)]

    evA_load = {}
    evA_stats = {}
    evA_apply = {}
    evA_tq = {}
    evA_eq = {}

    for i in range(9):
        def a_load(e, i=i):
            if i >= 2:
                e.wait_ge(s_sc.h, evA_apply[i - 2])
            src = x_halo if i == 0 else x_main[(i - 1) * 128:i * 128, :]
            sx = s_x[i % 2]
            evA_load[i] = (sx.h, sx.inc(e.dma_start(out=x_t[i % 2][:], in_=src), 16))
        SY(a_load)

        def a_stats(v, i=i):
            b = i % 2
            v.wait_ge(*evA_load[i])
            v.reduce_sum(out=ln_m[b][:], in_=x_t[b][:], axis=AX.X)
            v.drain()
            v.tensor_scalar_mul(out=ln_m[b][:], in0=ln_m[b][:], scalar1=1.0 / DM)
            v.drain()
            v.tensor_scalar_sub(out=x_t[b][:], in0=x_t[b][:], scalar1=ln_m[b][:])
            v.drain()
            v.tensor_tensor(out=ln_sq[:], in0=x_t[b][:], in1=x_t[b][:], op=ALU.mult)
            v.drain()
            v.reduce_sum(out=ln_v[b][:], in_=ln_sq[:], axis=AX.X)
            v.drain()
            v.tensor_scalar(out=ln_v[b][:], in0=ln_v[b][:], scalar1=1.0 / DM,
                            scalar2=EPS, op0=ALU.mult, op1=ALU.add)
            evA_stats[i] = s_ve.inc(v.drain())
        VE(a_stats)

        def a_apply(sc, i=i):
            b = i % 2
            sc.wait_ge(s_ve.h, evA_stats[i])
            if i >= 2:
                sc.wait_ge(s_pe.h, evGB[(i - 2) * 4 + 3])
            sc.activation(out=ln_v[b][:], in_=ln_v[b][:], func=AF.Ln)
            sc.drain()
            sc.activation(out=ln_v[b][:], in_=ln_v[b][:], func=AF.Exp,
                          scale=-0.5)
            sc.drain()
            sc.activation(out=xn_t[b][:], in_=x_t[b][:], func=AF.Copy,
                          scale=ln_v[b][:])
            evA_apply[i] = s_sc.inc(sc.drain())
        SC(a_apply)

        for quad in range(4):
            def a_tr(te, i=i, quad=quad):
                k = gB[0]
                if i == 0 and quad == 0:
                    te.wait_ge(s_const.h, LD_CONST)
                if quad == 0:
                    te.wait_ge(s_sc.h, evA_apply[i])
                if k >= 2:
                    te.wait_ge(s_ve.h, evGBe[k - 2])
                bank = bfbank(k)
                for j in range(4):
                    cc = quad * 4 + j
                    te.transpose(bank[:, j * 128:j * 128 + 128],
                                 xn_t[i % 2][:, cc * 128:(cc + 1) * 128], idbf_sb[:])
                evGB[k] = s_pe.inc(te.drain())
                gB[0] += 1
            TE(a_tr)

            def a_ev(v, i=i, quad=quad):
                k = gB[0] - 1
                v.wait_ge(s_pe.h, evGB[k])
                bank = bfbank(k)
                cc0 = quad * 4
                if i == 0:
                    ins = v.tensor_copy(
                        out=xn_fm[:].rearrange('p (c w) -> p c w', c=CC)[
                            :, cc0:cc0 + 4, 0:3],
                        in_=bank[:, 0:512].rearrange('p (j w) -> p j w', j=4)[
                            :, :, 125:128])
                else:
                    off = 3 + (i - 1) * 128
                    ins = v.tensor_copy(
                        out=xn_fm[:].rearrange('p (c w) -> p c w', c=CC)[
                            :, cc0:cc0 + 4, off:off + 128],
                        in_=bank[:, 0:512].rearrange('p (j w) -> p j w', j=4))
                evGBe[k] = s_ve.inc(ins)
            VE(a_ev)

    A_XN = evGBe[35]
    if max_phase == 'A':
        return _early_exit([(s_ve.h, A_XN)], (esWb, esA))
    esA.close()

    if 'xn_fm' in dbg:
        def d_xn(e):
            e.wait_ge(s_ve.h, A_XN)
            s_dbg.inc(e.dma_start(out=dbg['xn_fm'], in_=xn_fm[:]), 16)
        SY(d_xn)

    esB2 = ExitStack()
    x_T = esB2.enter_context(nc.sbuf_tensor('x_T', [128, NT * D_IN], BF16))
    esB = ExitStack()

    def sbB(name, shape, dtype):
        return esB.enter_context(nc.sbuf_tensor(name, shape, dtype))

    gtile = [sbB(f'gtile{i}', [128, T], BF16) for i in range(2)]
    xbc32 = [sbB(f'xbc32{i}', [128, TW], F32) for i in range(2)]
    cacc = [sbB(f'cacc{i}', [128, T], F32) for i in range(2)]
    sil_t = [sbB(f'sil_t{i}', [128, T], BF16) for i in range(3)]
    dt_raw = sbB('dt_raw', [64, T], F32)
    sp_t1 = sbB('sp_t1', [64, T], F32)

    evB_gdone = {}
    evB_mm = {}
    evB_gsil = {}
    evB_gst = {}
    evB_cevac = {}
    evB_fma = {}
    evB_sil = {}
    evB_tq = {}
    evB_teq = {}
    evB_xtev = {}
    gate_cons = [None, None]

    def lhsT(g, cc, rows=128):
        return Wb[g % 2][:, cc * 128:cc * 128 + rows]

    def emit_wload(g):
        def gp_f(gp):
            gp.wait_ge(s_pe.h, evB_gdone[g - 2])
            evB_w[g] = (s_wb[g % 2].h, s_wb[g % 2].inc(
                gp.dma_start(out=Wb[g % 2][:],
                             in_=w_inT[g].rearrange('p c j -> p (c j)')), 16))
        GP(gp_f)

    def emit_xt_transposes(src_buf, cidx):
        for half in range(2):
            def te_f(te, half=half):
                if half == 0:
                    te.wait_ge(s_sc.h, evB_sil[cidx])
                k = gB[0]
                if k >= 2:
                    te.wait_ge(s_ve.h, evGBe[k - 2])
                bank = bfbank(k)
                for j in range(4):
                    z = half * 4 + j
                    te.transpose(bank[:, j * 128:j * 128 + 128],
                                 src_buf[:, z * 128:(z + 1) * 128], idbf_sb[:])
                evGB[k] = s_pe.inc(te.drain())
                gB[0] += 1
            TE(te_f)

            def ve_f(v, half=half):
                k = gB[0] - 1
                v.wait_ge(s_pe.h, evGB[k])
                bank = bfbank(k)
                if cidx < 32:
                    dst = x_T[:].rearrange('p (z w) -> p z w', z=NT)[
                        :, half * 4:half * 4 + 4,
                        cidx * 128:cidx * 128 + 128]
                else:
                    dst = B_T[:].rearrange('p (z w) -> p z w', z=NT)[
                        :, half * 4:half * 4 + 4, :]
                ins = v.tensor_copy(
                    out=dst, in_=bank[:, 0:512].rearrange('p (j w) -> p j w', j=4))
                evGBe[k] = s_ve.inc(ins)
                if half == 1:
                    evB_xtev[cidx] = evGBe[k]
            VE(ve_f)

    for g in range(NG):
        rc = g
        if g < 32:
            def te_f(te, g=g, rc=rc):
                te.wait_ge(*evB_w[g])
                if rc == 0:
                    te.wait_ge(s_ve.h, A_XN)
                for si, (t0, t1) in enumerate([(3, 515), (515, 1027)]):
                    if gate_cons[si] is not None:
                        te.wait_ge(s_sc.h, gate_cons[si])
                    dst = T1[:, si * 512:si * 512 + 512]
                    ins = None
                    for cc in range(CC):
                        ins = nc.tensor.matmul(
                            dst, lhsT(g, cc),
                            xn_fm[:, cc * TW + t0:cc * TW + t1],
                            start=(cc == 0), stop=(cc == CC - 1))
                    evB_mm[(rc, si)] = s_pe.inc(ins)
                evB_gdone[g] = s_pe.v
            TE(te_f)

            def sc_f(sc, rc=rc):
                for si in range(2):
                    sc.wait_ge(s_pe.h, evB_mm[(rc, si)])
                    if rc >= 2 and si == 0:
                        sc.wait_ge(*evB_gst[rc - 2])
                    src_ = T1[:, si * 512:si * 512 + 512]
                    ins = sc.activation(
                        out=gtile[rc % 2][:, si * 512:(si + 1) * 512],
                        in_=src_, func=AF.Silu)
                    gate_cons[si] = s_sc.inc(ins)
                evB_gsil[rc] = s_sc.v
            SC(sc_f)

            def sy_f(e, rc=rc):
                e.wait_ge(s_sc.h, evB_gsil[rc])
                sg_ = s_gst[rc % 2]
                evB_gst[rc] = (sg_.h, sg_.inc(
                    e.dma_start(out=g_stash.ap()[rc], in_=gtile[rc % 2][:]), 16))
            SY(sy_f)
        elif g < 66:
            cidx = g - 32
            par = cidx % 2
            spar = cidx % 3

            def te_f(te, g=g, cidx=cidx):
                te.wait_ge(*evB_w[g])
                if cidx == 0:
                    te.wait_ge(s_sc.h, gate_cons[0])
                    te.wait_ge(s_sc.h, gate_cons[1])
                if cidx > 0:
                    vv = evB_cevac[cidx - 1]
                    te.wait_ge(s_ve.h, vv[0])
                    te.wait_ge(s_sc.h, vv[1])
                ins = None
                for (t0, t1) in [(0, 512), (512, 1024), (1024, TW)]:
                    for cc in range(CC):
                        ins = nc.tensor.matmul(
                            T1[:, t0:t1], lhsT(g, cc),
                            xn_fm[:, cc * TW + t0:cc * TW + t1],
                            start=(cc == 0), stop=(cc == CC - 1))
                evB_mm[('c', cidx)] = s_pe.inc(ins)
                evB_gdone[g] = s_pe.v
            TE(te_f)

            def ve_e(v, cidx=cidx, par=par):
                v.wait_ge(s_pe.h, evB_mm[('c', cidx)])
                if cidx >= 2:
                    v.wait_ge(s_ve.h, evB_fma[cidx - 2])
                ins = v.tensor_copy(out=xbc32[par][:, 0:640], in_=T1[:, 0:640])
                evB_cevac.setdefault(cidx, [0, 0])[0] = s_ve.inc(ins)
            VE(ve_e)

            def sc_e(sc, cidx=cidx, par=par):
                sc.wait_ge(s_pe.h, evB_mm[('c', cidx)])
                if cidx >= 2:
                    sc.wait_ge(s_ve.h, evB_fma[cidx - 2])
                ins = sc.activation(out=xbc32[par][:, 640:TW],
                                    in_=T1[:, 640:TW], func=AF.Copy)
                evB_cevac[cidx][1] = s_sc.inc(ins)
            SC(sc_e)

            if cidx == 0 and 'xbcpre' in dbg:
                def d_xbc(e, cidx=cidx, par=par):
                    e.wait_ge(s_ve.h, evB_cevac[cidx][0])
                    e.wait_ge(s_sc.h, evB_cevac[cidx][1])
                    s_dbg.inc(e.dma_start(out=dbg['xbcpre'],
                                          in_=xbc32[par][:]), 16)
                SY(d_xbc)

            def ve_c(v, cidx=cidx, par=par):
                v.wait_ge(s_ve.h, evB_cevac[cidx][0])
                v.wait_ge(s_sc.h, evB_cevac[cidx][1])
                ins = None
                for colh in range(2):
                    c0 = colh * 512
                    for i in range(4):
                        if i == 0:
                            ins = v.tensor_scalar(
                                out=cacc[par][:, c0:c0 + 512],
                                in0=xbc32[par][:, c0:c0 + 512],
                                scalar1=cw_sb[:, cidx * 4:cidx * 4 + 1],
                                scalar2=None, op0=ALU.mult)
                        else:
                            ins = v.scalar_tensor_tensor(
                                out=cacc[par][:, c0:c0 + 512],
                                in0=xbc32[par][:, c0 + i:c0 + 512 + i],
                                scalar=cw_sb[:, cidx * 4 + i:cidx * 4 + i + 1],
                                in1=cacc[par][:, c0:c0 + 512],
                                op0=ALU.mult, op1=ALU.add)
                        v.drain()
                evB_fma[cidx] = s_ve.inc(v.drain())
            VE(ve_c)

            def sc_s(sc, cidx=cidx, par=par, spar=spar):
                sc.wait_ge(s_ve.h, evB_fma[cidx])
                if cidx < 32:
                    if cidx >= 3:
                        sc.wait_ge(s_ve.h, evB_xtev[cidx - 3])
                    dst = sil_t[spar][:]
                elif cidx == 32:
                    dst = B_fm[:]
                else:
                    dst = C_fm[:]
                ins = sc.activation(out=dst, in_=cacc[par][:, 0:T],
                                    func=AF.Silu, bias=cb_sb[:, cidx:cidx + 1])
                evB_sil[cidx] = s_sc.inc(ins)
            SC(sc_s)

            if cidx < 32:
                emit_xt_transposes(sil_t[spar], cidx)
            elif cidx == 32:
                emit_xt_transposes(B_fm, cidx)
        else:
            def te_f(te, g=g):
                te.wait_ge(*evB_w[g])
                vv = evB_cevac[33]
                te.wait_ge(s_ve.h, vv[0])
                te.wait_ge(s_sc.h, vv[1])
                for si, (t0, t1) in enumerate([(3, 515), (515, 1027)]):
                    if gate_cons[si] is not None:
                        te.wait_ge(s_sc.h, gate_cons[si])
                    dst = T1[0:64, si * 512:si * 512 + 512]
                    ins = None
                    for cc in range(CC):
                        ins = nc.tensor.matmul(
                            dst, lhsT(g, cc, rows=64),
                            xn_fm[:, cc * TW + t0:cc * TW + t1],
                            start=(cc == 0), stop=(cc == CC - 1))
                    evB_mm[('dt', si)] = s_pe.inc(ins)
                evB_gdone[g] = s_pe.v
            TE(te_f)

            def sc_f(sc):
                sc.wait_ge(s_pe.h, evB_mm[('dt', 0)])
                sc.wait_ge(s_pe.h, evB_mm[('dt', 1)])
                for si in range(2):
                    src_ = T1[0:64, si * 512:si * 512 + 512]
                    d0 = slice(si * 512, (si + 1) * 512)
                    sc.activation(out=sp_t1[:, d0], in_=src_, func=AF.Abs,
                                  bias=dtb_sb[:])
                    sc.drain()
                    sc.activation(out=sp_t1[:, d0], in_=sp_t1[:, d0],
                                  func=AF.Exp, scale=-1.0)
                    sc.drain()
                    sc.activation(out=sp_t1[:, d0], in_=sp_t1[:, d0],
                                  func=AF.Ln, bias=1.0)
                    sc.drain()
                    sc.activation(out=dt_raw[:, d0], in_=src_, func=AF.Relu,
                                  bias=dtb_sb[:])
                    sc.drain()
                evB_mm['dt_sc'] = s_sc.inc(sc.drain())
            SC(sc_f)

            def ve_f(v):
                v.wait_ge(s_sc.h, evB_mm['dt_sc'])
                v.wait_ge(s_sc.h, SC_SETUP[0])
                v.wait_ge(s_ve.h, BZERO[0])
                v.tensor_tensor(out=dt_fm[:], in0=dt_raw[:], in1=sp_t1[:],
                                op=ALU.add)
                v.drain()
                v.tensor_scalar_mul(out=est_or_dA[:], in0=dt_fm[:],
                                    scalar1=nA_sb[:])
                v.drain()
                for z in range(NT):
                    sl = slice(z * 128, (z + 1) * 128)
                    v.tensor_tensor_scan(out=grel[:, sl], data0=est_or_dA[:, sl],
                                         data1=zeros64[:], initial=0.0,
                                         op0=ALU.add, op1=ALU.add)
                    v.drain()
                evB_mm['dt_ve'] = s_ve.inc(v.drain())
            VE(ve_f)

            def sc_f2(sc):
                sc.wait_ge(s_ve.h, evB_mm['dt_ve'])
                sc.wait_ge(s_ve.h, VE_SETUP[0])
                sc.activation(out=ln_or_erel[:], in_=dt_fm[:], func=AF.Ln,
                              bias=cvals[0:64, 0:1])
                evB_mm['dt_sc2'] = s_sc.inc(sc.drain())
            SC(sc_f2)

            def ve_f2(v):
                v.wait_ge(s_sc.h, evB_mm['dt_sc2'])
                v.tensor_sub(out=sgrel[:], in0=grel[:], in1=ln_or_erel[:])
                v.drain()
                for z in range(NT):
                    sl = slice(z * 128, (z + 1) * 128)
                    v.tensor_scalar_sub(
                        out=est_or_dA[:, sl], in0=grel[:, sl],
                        scalar1=grel[:, z * 128 + 127:z * 128 + 128])
                    v.drain()
                v.reduce_sum(out=ctot_sum[:], in_=grel[:, 127::128], axis=AX.X)
                evB_mm['dt_ve2'] = s_ve.inc(v.drain())
            VE(ve_f2)

            def sc_f3(sc):
                sc.wait_ge(s_ve.h, evB_mm['dt_ve2'])
                sc.activation(out=est_or_dA[:], in_=est_or_dA[:], func=AF.Exp,
                              scale=-1.0)
                sc.drain()
                sc.activation(out=exp_ctot[:], in_=grel[:, 127::128],
                              func=AF.Exp)
                sc.drain()
                sc.activation(out=ln_or_erel[:], in_=grel[:], func=AF.Exp)
                evB_mm['dt_sc3'] = s_sc.inc(sc.drain())
            SC(sc_f3)

            def ve_f3(v):
                v.wait_ge(s_sc.h, evB_mm['dt_sc3'])
                v.tensor_tensor(out=dt_fm[:], in0=dt_fm[:], in1=est_or_dA[:],
                                op=ALU.mult)
                evB_mm['dt_ve3'] = s_ve.inc(v.drain())
            VE(ve_f3)

            for z in range(NT):
                def te_tt(te, z=z):
                    if z == 0:
                        te.wait_ge(s_ve.h, evB_mm['dt_ve3'])
                    k = gF[0]
                    if k >= 2:
                        te.wait_ge(s_ve.h, evGFe[k - 2])
                    bank = fpbank(k)
                    for jj, srcbuf in enumerate((grel, sgrel, dt_fm, ln_or_erel)):
                        te.transpose(
                            bank[:, jj * 128:jj * 128 + 64],
                            srcbuf[0:64, z * 128:(z + 1) * 128],
                            idf_sb[0:64, 0:64])
                    evGF[k] = s_pe.inc(te.drain())
                    gF[0] += 1
                TE(te_tt)

                def ve_tt(v, z=z):
                    k = gF[0] - 1
                    v.wait_ge(s_pe.h, evGF[k])
                    bank = fpbank(k)
                    ins = None
                    for jj, dstbuf in enumerate((grel_T, sgrel_T, dtE_T, erel_T)):
                        ins = v.tensor_copy(
                            out=dstbuf[:, z * 64:(z + 1) * 64],
                            in_=bank[:, jj * 128:jj * 128 + 64])
                        v.drain()
                    evGFe[k] = s_ve.inc(v.drain())
                    evB_mm['dect'] = s_ve.v
                VE(ve_tt)
        if g + 2 < NG:
            emit_wload(g + 2)

    B_XT = evB_xtev[32]
    B_DECT = evB_mm['dect']

    if 'bfm' in dbg:
        def d_bfm(e):
            e.wait_ge(s_sc.h, evB_sil[32])
            s_dbg.inc(e.dma_start(out=dbg['bfm'], in_=B_fm[:]), 16)
            e.wait_ge(s_sc.h, evB_sil[33])
            s_dbg.inc(e.dma_start(out=dbg['cfm'], in_=C_fm[:]), 16)
        SY(d_bfm)
    if 'dtf' in dbg:
        def d_dtf(e):
            e.wait_ge(s_ve.h, evB_mm['dt_ve'])
            s_dbg.inc(e.dma_start(out=dbg['dtf'], in_=dt_raw[:]), 16)
            s_dbg.inc(e.dma_start(out=dbg['grel'], in_=grel[:]), 16)
        SY(d_dtf)
    if 'xT' in dbg:
        def d_xt(e):
            e.wait_ge(s_ve.h, B_XT)
            for z in range(NT):
                s_dbg.inc(e.dma_start(out=dbg['xT'][z * 128:(z + 1) * 128, :],
                                      in_=x_T[:, z * D_IN:(z + 1) * D_IN]), 16)
        SY(d_xt)

    evB_xtst = {}

    def b_xt_stash(e):
        e.wait_ge(s_ve.h, B_XT)
        for z in range(NT):
            evB_xtst[z] = s_xtst.inc(
                e.dma_start(out=xT_stash.ap()[z],
                            in_=x_T[:, z * D_IN:(z + 1) * D_IN]), 16)
    SY(b_xt_stash)

    esB3 = ExitStack()
    xd_buf = esB3.enter_context(nc.sbuf_tensor('xd_buf', [128, D_IN], BF16))
    SW = {}

    def b_walk_init(v):
        v.wait_ge(s_ve.h, B_XT)
        v.wait_ge(s_ve.h, B_DECT)
        v.memset(S_run[:], 0.0)
        SW['init'] = s_ve.inc(v.drain())
    VE(b_walk_init)

    for z in range(NT):
        def ve_xd(v, z=z):
            if z == 0:
                v.wait_ge(s_ve.h, SW['init'])
            else:
                v.wait_ge(s_pe.h, SW[('mm', z - 1, 7)])
            ins = v.tensor_tensor(
                out=xd_buf[:].rearrange('p (h q) -> p h q', h=64),
                in0=x_T[:, z * D_IN:(z + 1) * D_IN].rearrange(
                    'p (h q) -> p h q', h=64),
                in1=dtE_T[:, z * 64:(z + 1) * 64].unsqueeze(2).broadcast_to(
                    [128, 64, 64]),
                op=ALU.mult)
            SW[('xd', z)] = s_ve.inc(ins)
        VE(ve_xd)

        def te_dec(te, z=z):
            te.wait_ge(s_ve.h, SW[('xd', z)])
            k = gF[0]
            if k >= 2:
                te.wait_ge(s_ve.h, evGFe[k - 2])
            te.transpose(fpbank(k)[:, 0:64],
                         exp_ctot[0:64, z:z + 1].broadcast_to([64, 128]),
                         idf_sb[0:64, 0:64])
            evGF[k] = s_pe.inc(te.drain())
            gF[0] += 1
        TE(te_dec)

        def ve_dec(v, z=z):
            k = gF[0] - 1
            v.wait_ge(s_pe.h, evGF[k])
            evGFe[k] = s_ve.inc(v.tensor_copy(
                out=dec_row[:, 0:64], in_=fpbank(k)[:, 0:64]))
            SW[('dec', z)] = evGFe[k]
        VE(ve_dec)

        for cg in range(8):
            def te_st(te, z=z, cg=cg):
                te.wait_ge(s_ve.h, SW[('xd', z)])
                if z == 0 and cg < 2:
                    te.wait_ge(s_sc.h, evB_mm['dt_sc'])
                if cg >= 2:
                    te.wait_ge(s_ve.h, SW[('upd', z, cg - 2)])
                dst = T1[:, (cg % 2) * 512:(cg % 2) * 512 + 512]
                ins = nc.tensor.matmul(
                    dst, B_T[:, z * 128:(z + 1) * 128],
                    xd_buf[:, cg * 512:(cg + 1) * 512], start=True, stop=True)
                SW[('mm', z, cg)] = s_pe.inc(ins)
            TE(te_st)

            def ve_upd(v, z=z, cg=cg):
                v.wait_ge(s_pe.h, SW[('mm', z, cg)])
                v.wait_ge(s_ve.h, SW[('dec', z)])
                sl = slice(cg * 512, (cg + 1) * 512)
                v.tensor_tensor(
                    out=S_run[:, sl].rearrange('p (h q) -> p h q', h=8),
                    in0=S_run[:, sl].rearrange('p (h q) -> p h q', h=8),
                    in1=dec_row[:, cg * 8:(cg + 1) * 8].unsqueeze(2).broadcast_to(
                        [128, 8, 64]),
                    op=ALU.mult)
                v.drain()
                ins = v.tensor_tensor(
                    out=S_run[:, sl], in0=S_run[:, sl],
                    in1=T1[:, (cg % 2) * 512:(cg % 2) * 512 + 512], op=ALU.add)
                SW[('upd', z, cg)] = s_ve.inc(ins)
            VE(ve_upd)

    B_SLOCAL = SW[('upd', NT - 1, 7)]

    if 'slocal' in dbg:
        def d_sl(e):
            e.wait_ge(s_ve.h, B_SLOCAL)
            s_dbg.inc(e.dma_start(out=dbg['slocal'], in_=S_run[:]), 16)
        SY(d_sl)

    evB_exp = {}

    def b_export(e):
        e.wait_ge(s_ve.h, B_SLOCAL)
        e.wait_ge(s_ve.h, BZERO[0])
        s_exp.inc(e.dma_start(out=cc_in.ap()[:, 0:D_IN], in_=S_run[:]), 16)
        s_exp.inc(
            e.dma_start(out=cc_in.ap()[0:64, D_IN:D_IN + 1], in_=ctot_sum[:],
                        allow_slow_non_contiguous=True), 16)
        evB_exp['done'] = s_exp.inc(
            e.dma_start(out=cc_in.ap()[64:128, D_IN:D_IN + 1],
                        in_=zeros64[0:64, 0:1],
                        allow_slow_non_contiguous=True), 16)
    SY(b_export)

    if max_phase == 'B':
        return _early_exit([(s_exp.h, evB_exp['done']), (s_xtst.h, s_xtst.v),
                            (s_gst[0].h, s_gst[0].v), (s_gst[1].h, s_gst[1].v)],
                           (esWb, esB2, esB, esB3))
    esB3.close()
    esB.close()
    esB2.close()
    esWb.close()

    esD = ExitStack()

    def sbD(name, shape, dtype):
        return esD.enter_context(nc.sbuf_tensor(name, shape, dtype))

    gs_sb = sbD('gs_sb', [128, IC], F32)
    us_sb = sbD('us_sb', [128, IC], F32)
    ds_sb = sbD('ds_sb', [128, DM], F32)
    Wg8 = [sbD(f'Wg8{i}', [128, CC * 128], FP8) for i in range(2)]
    Wu8 = [sbD(f'Wu8{i}', [128, CC * 128], FP8) for i in range(2)]
    gsil_t = sbD('gsil_t', [128, 512], F32)
    fuse_t = [sbD(f'fuse_t{i}', [128, T], BF16) for i in range(2)]

    evD_w = {}
    evD_done = {}
    evD_mm = {}
    evD_sil = {}
    evD_fuse = {}
    evD_fst = {}
    last_fuse = [None]

    D_CONST = [0]

    def d_consts(e):
        e.wait_ge(s_xtst.h, s_xtst.v)
        e.wait_ge(s_ve.h, B_SLOCAL)
        e.wait_ge(s_gst[0].h, s_gst[0].v)
        e.wait_ge(s_gst[1].h, s_gst[1].v)
        e.wait_ge(s_sc.h, evB_mm['dt_sc3'])
        e.wait_ge(s_exp.h, evB_exp['done'])
        for dst, srcap in [(gs_sb, gs_t), (us_sb, us_t), (ds_sb, ds_b)]:
            s_dc.inc(e.dma_start(out=dst[:], in_=srcap), 16)
        D_CONST[0] = s_dc.v
    SY(d_consts)

    def d_w01(gp):
        gp.wait_ge(s_xtst.h, s_xtst.v)
        gp.wait_ge(s_ve.h, B_SLOCAL)
        gp.wait_ge(s_exp.h, evB_exp['done'])
        for ic in range(2):
            sw = s_wgu[ic % 2]
            sw.inc(gp.dma_start(out=Wg8[ic % 2][:],
                                in_=w_g[ic].rearrange('p c j -> p (c j)')), 16)
            evD_w[ic] = (sw.h, sw.inc(
                gp.dma_start(out=Wu8[ic % 2][:],
                             in_=w_u[ic].rearrange('p c j -> p (c j)')), 16))
    GP(d_w01)

    for ic in range(IC):
        if ic >= 2:
            def gp_w(gp, ic=ic):
                gp.wait_ge(s_pe.h, evD_done[ic - 2])
                sw = s_wgu[ic % 2]
                sw.inc(gp.dma_start(
                    out=Wg8[ic % 2][:],
                    in_=w_g[ic].rearrange('p c j -> p (c j)')), 16)
                evD_w[ic] = (sw.h, sw.inc(gp.dma_start(
                    out=Wu8[ic % 2][:],
                    in_=w_u[ic].rearrange('p c j -> p (c j)')), 16))
            GP(gp_w)

        for th in range(2):
            def te_f(te, ic=ic, th=th):
                if ic == 0 and th == 0:
                    te.wait_ge(s_ve.h, A_XN)
                    te.wait_ge(s_sc.h, evB_mm['dt_sc'])
                    te.wait_ge(s_sc.h, evB_gsil[31])
                    te.wait_ge(s_ve.h, B_SLOCAL)
                if th == 0:
                    te.wait_ge(*evD_w[ic])
                if ic >= 1:
                    te.wait_ge(s_ve.h, evD_fuse[(ic - 1, th)])
                t0 = 3 + th * 512
                regs = ([(T1, 0), (T1, 512)] if th == 0
                        else [(T1, 1024), (T2, 0)])
                ins = None
                for Wt, (reg, off) in [(Wg8, regs[0]), (Wu8, regs[1])]:
                    dst = reg[:, off:off + 512]
                    for cc in range(CC):
                        ins = nc.tensor.matmul(
                            dst, Wt[ic % 2][:, cc * 128:(cc + 1) * 128],
                            xn_fm[:, cc * TW + t0:cc * TW + t0 + 512],
                            start=(cc == 0), stop=(cc == CC - 1))
                evD_mm[(ic, th)] = s_pe.inc(ins)
                if th == 1:
                    evD_done[ic] = s_pe.v
            TE(te_f)

            def sc_f(sc, ic=ic, th=th):
                sc.wait_ge(s_pe.h, evD_mm[(ic, th)])
                if ic == 0 and th == 0:
                    sc.wait_ge(s_dc.h, D_CONST[0])
                if last_fuse[0] is not None:
                    sc.wait_ge(s_ve.h, last_fuse[0])
                greg, goff = (T1, 0) if th == 0 else (T1, 1024)
                ins = sc.activation(out=gsil_t[:], in_=greg[:, goff:goff + 512],
                                    func=AF.Silu, scale=gs_sb[:, ic:ic + 1])
                evD_sil[(ic, th)] = s_sc.inc(ins)
            SC(sc_f)

            def ve_f(v, ic=ic, th=th):
                v.wait_ge(s_sc.h, evD_sil[(ic, th)])
                if ic >= 2:
                    v.wait_ge(*evD_fst[ic - 2])
                ureg, uoff = (T1, 512) if th == 0 else (T2, 0)
                ins = v.scalar_tensor_tensor(
                    out=fuse_t[ic % 2][:, th * 512:(th + 1) * 512],
                    in0=ureg[:, uoff:uoff + 512],
                    scalar=us_sb[:, ic:ic + 1],
                    in1=gsil_t[:], op0=ALU.mult, op1=ALU.mult)
                evD_fuse[(ic, th)] = s_ve.inc(ins)
                last_fuse[0] = evD_fuse[(ic, th)]
            VE(ve_f)

        def sy_f(e, ic=ic):
            e.wait_ge(s_ve.h, evD_fuse[(ic, 1)])
            sf = s_fst[ic % 2]
            evD_fst[ic] = (sf.h, sf.inc(
                e.dma_start(out=fuse_stash.ap()[ic], in_=fuse_t[ic % 2][:]), 16))
        SY(sy_f)

    Wd8 = [sbD(f'Wd8{i}', [128, 8 * 512], FP8) for i in range(2)]
    fuT = [sbD(f'fuT{i}', [128, 8 * 128], BF16) for i in range(2)]
    dn_t = sbD('dn_t', [128, 512], BF16)

    evD_dw = {}
    evD_dfu = {}
    evD_dmm = {}
    evD_dev = {}
    evD_dst = {}
    dn_pass = [(tbh, osq) for tbh in range(2) for osq in range(4)]
    fu_k = [0]
    dn_prev = [None]

    def dreg(tb4):
        return (T1[:, tb4 * 512:tb4 * 512 + 512] if tb4 < 3
                else T2[:, 0:512])

    def d_dw01(gp):
        for k12 in range(2):
            pi, sg = divmod(k12, 6)
            tbh, osq = dn_pass[pi]
            sw = s_wd[k12 % 2]
            evD_dw[k12] = (sw.h, sw.inc(gp.dma_start(
                out=Wd8[k12 % 2][:].rearrange('p (a b) -> p a b', a=8),
                in_=w_d[sg * 8:(sg + 1) * 8, :,
                        osq * 512:(osq + 1) * 512].transpose([1, 0, 2])), 16))
    GP(d_dw01)

    for pi, (tbh, osq) in enumerate(dn_pass):
        for sg in range(6):
            k12 = pi * 6 + sg
            if k12 >= 2:
                def gp_w(gp, k12=k12):
                    pi2, sg2 = divmod(k12, 6)
                    tbh2, osq2 = dn_pass[pi2]
                    gp.wait_ge(s_pe.h, evD_dmm[('sgdone', k12 - 2)])
                    sw = s_wd[k12 % 2]
                    evD_dw[k12] = (sw.h, sw.inc(gp.dma_start(
                        out=Wd8[k12 % 2][:].rearrange('p (a b) -> p a b', a=8),
                        in_=w_d[sg2 * 8:(sg2 + 1) * 8, :,
                                osq2 * 512:(osq2 + 1) * 512].transpose(
                            [1, 0, 2])), 16))
                GP(gp_w)

            for tb4 in range(4):
                tb = tbh * 4 + tb4
                k = fu_k[0]

                def sy_fu(e, pi=pi, sg=sg, tb4=tb4, tb=tb, k=k):
                    if k >= 2:
                        e.wait_ge(s_pe.h, evD_dmm[('fu', k - 2)])
                    if k == 0:
                        e.wait_ge(s_fst[0].h, s_fst[0].v)
                        e.wait_ge(s_fst[1].h, s_fst[1].v)
                    sf = s_fu[k % 2]
                    evD_dfu[k] = (sf.h, sf.inc(e.dma_start(
                        out=fuT[k % 2][:].rearrange('p (a b) -> p a b', a=8),
                        in_=fuse_stash.ap()[sg * 8:(sg + 1) * 8, :,
                                            tb * 128:(tb + 1) * 128].transpose(
                            [1, 0, 2])), 16))
                SY(sy_fu)

                def te_f(te, pi=pi, sg=sg, tb4=tb4, k=k, k12=k12):
                    te.wait_ge(*evD_dfu[k])
                    if tb4 == 0:
                        te.wait_ge(*evD_dw[k12])
                    if pi == 0 and sg == 0 and tb4 == 0:
                        te.wait_ge(s_ve.h, last_fuse[0])
                        te.wait_ge(s_ve.h, B_SLOCAL)
                    if pi >= 1 and sg == 0:
                        te.wait_ge(s_dst.h, evD_dst[(pi - 1, tb4)])
                    dst = dreg(tb4)
                    ins = None
                    for j in range(8):
                        ins = nc.tensor.matmul(
                            dst, fuT[k % 2][:, j * 128:(j + 1) * 128],
                            Wd8[k12 % 2][:, j * 512:(j + 1) * 512],
                            start=(sg == 0 and j == 0),
                            stop=(sg == 5 and j == 7))
                    evD_dmm[(pi, tb4)] = s_pe.inc(ins)
                    evD_dmm[('fu', k)] = s_pe.v
                    if tb4 == 3:
                        evD_dmm[('sgdone', k12)] = s_pe.v
                TE(te_f)
                fu_k[0] += 1

        for tb4 in range(4):
            tb = tbh * 4 + tb4

            def ve_f(v, pi=pi, osq=osq, tb4=tb4, tb=tb):
                v.wait_ge(s_pe.h, evD_dmm[(pi, tb4)])
                v.wait_ge(s_dc.h, D_CONST[0])
                if dn_prev[0] is not None:
                    v.wait_ge(s_dst.h, evD_dst[dn_prev[0]])
                o0 = osq * 512
                ins = v.tensor_tensor(out=dn_t[:], in0=dreg(tb4),
                                      in1=ds_sb[:, o0:o0 + 512], op=ALU.mult)
                evD_dev[(pi, tb4)] = s_ve.inc(ins)
            VE(ve_f)

            def sy_s(e, pi=pi, osq=osq, tb4=tb4, tb=tb):
                e.wait_ge(s_ve.h, evD_dev[(pi, tb4)])
                o0 = osq * 512
                evD_dst[(pi, tb4)] = s_dst.inc(e.dma_start(
                    out=ffn_stash.ap()[tb][:, o0:o0 + 512], in_=dn_t[:]), 16)
            SY(sy_s)
            dn_prev[0] = (pi, tb4)

    D_EV_ALL = [None]

    def d_fin(v):
        v.wait_ge(s_dst.h, evD_dst[(7, 3)])
        D_EV_ALL[0] = s_ve.inc(v.drain())
    VE(d_fin)

    def gp_cc(gp):
        gp.wait_ge(s_exp.h, evB_exp['done'])
        s_cc.inc(gp.collective_compute(
            'AllGather', ALU.bypass, replica_groups=[list(range(NCORES))],
            ins=[cc_in.ap().opt()], outs=[cc_out.ap().opt()]))
    GP(gp_cc)
    CC_DONE = s_cc.v

    if max_phase == 'D':
        return _early_exit([(s_ve.h, D_EV_ALL[0]), (s_cc.h, CC_DONE)], (esD,))
    esD.close()

    esE = ExitStack()

    def sbE(name, shape, dtype):
        return esE.enter_context(nc.sbuf_tensor(name, shape, dtype))

    mask_sb = sbE('mask_sb', [128, 128], BF16)
    drep_bf = sbE('drep_bf', [128, 64], BF16)
    nw_sb = sbE('nw_sb', [128, 32], F32)
    onebf_sb = sbE('onebf_sb', [128, 1], BF16)
    onef_sb = sbE('onef_sb', [1, 128], F32)
    mprev_sb = sbE('mprev_sb', [128, 8], F32)
    Sj = [sbE('Sj0', [128, D_IN], F32)] * 2
    dAj = sbE('dAj', [64, 1], F32)
    dj = sbE('dj', [64, 1], F32)
    S_run_bf = sbE('S_run_bf', [128, D_IN], BF16)

    evE = {}

    E_CONST = [0]

    def e_consts(e):
        e.wait_ge(s_ve.h, D_EV_ALL[0])
        for dst, srcap in [(mask_sb, mask_ge), (nw_sb, norm_w_t),
                           (onebf_sb, ones_bf), (onef_sb, ones_f),
                           (mprev_sb, mask_prev)]:
            s_ec.inc(e.dma_start(out=dst[:], in_=srcap), 16)
        E_CONST[0] = s_ec.v
    SY(e_consts)

    E_DREP = [0]

    def e_drep(gp):
        gp.wait_ge(s_ve.h, D_EV_ALL[0])
        E_DREP[0] = s_dr.inc(gp.dma_start(out=drep_bf[:], in_=d_rep), 16)
    GP(e_drep)

    def e_sin_zero(v):
        v.wait_ge(s_exp.h, evB_exp['done'])
        v.memset(S_run[:], 0.0)
        evE['zero'] = s_ve.inc(v.drain())
    VE(e_sin_zero)

    for j in range(NCORES):
        def e_loads(e, j=j):
            if j == 0:
                e.wait_ge(s_cc.h, CC_DONE)
                e.wait_ge(s_ve.h, D_EV_ALL[0])
            if j >= 1:
                e.wait_ge(s_ve.h, evE[('sacc', j - 1)])
            s_ej.inc(
                e.dma_start(out=Sj[j % 2][:],
                            in_=cc_out.ap()[j * 128:(j + 1) * 128, 0:D_IN]), 16)
            both = s_ej.inc(
                e.dma_start(out=dAj[:],
                            in_=cc_out.ap()[j * 128:j * 128 + 64,
                                            D_IN:D_IN + 1],
                            allow_slow_non_contiguous=True), 16)
            evE[('sj', j)] = (s_ej.h, both)
            evE[('dAj', j)] = (s_ej.h, both)
        SY(e_loads)

        def e_dj(sc, j=j):
            sc.wait_ge(*evE[('dAj', j)])
            if j == 0:
                sc.wait_ge(s_ec.h, E_CONST[0])
            if j >= 1:
                sc.wait_ge(s_ve.h, evE[('sacc', j - 1)])
            ins = sc.activation(out=dj[:], in_=dAj[:], func=AF.Exp,
                                scale=mprev_sb[0:64, j:j + 1])
            evE[('dj', j)] = s_sc.inc(ins)
        SC(e_dj)

        def te_dj(te, j=j):
            te.wait_ge(s_sc.h, evE[('dj', j)])
            if j == 0:
                te.wait_ge(s_ve.h, D_EV_ALL[0])
            k = gF[0]
            if k >= 2:
                te.wait_ge(s_ve.h, evGFe[k - 2])
            te.transpose(fpbank(k)[:, 0:64],
                         dj[0:64, 0:1].broadcast_to([64, 128]),
                         idf_sb[0:64, 0:64])
            evGF[k] = s_pe.inc(te.drain())
            gF[0] += 1
        TE(te_dj)

        def ve_dj(v, j=j):
            k = gF[0] - 1
            if j == 0:
                v.wait_ge(s_ec.h, E_CONST[0])
            v.wait_ge(s_pe.h, evGF[k])
            evGFe[k] = s_ve.inc(v.tensor_copy(
                out=dec_row[:, 0:64], in_=fpbank(k)[:, 0:64]))
            v.drain()
            v.wait_ge(*evE[('sj', j)])
            if j == 0:
                v.wait_ge(s_ve.h, evE['zero'])
            ins = None
            for cg in range(8):
                sl = slice(cg * 512, (cg + 1) * 512)
                v.tensor_tensor(
                    out=S_run[:, sl].rearrange('p (h q) -> p h q', h=8),
                    in0=S_run[:, sl].rearrange('p (h q) -> p h q', h=8),
                    in1=dec_row[:, cg * 8:(cg + 1) * 8].unsqueeze(2).broadcast_to(
                        [128, 8, 64]),
                    op=ALU.mult)
                v.drain()
                ins = v.scalar_tensor_tensor(
                    out=S_run[:, sl], in0=Sj[j % 2][:, sl],
                    scalar=mprev_sb[:, j:j + 1], in1=S_run[:, sl],
                    op0=ALU.mult, op1=ALU.add)
                v.drain()
            evE[('sacc', j)] = s_ve.inc(v.drain())
        VE(ve_dj)

    E_SIN = evE[('sacc', NCORES - 1)]

    if 'sin' in dbg:
        def d_sin(e):
            e.wait_ge(s_ve.h, E_SIN)
            s_dbg.inc(e.dma_start(out=dbg['sin'], in_=S_run[:]), 16)
        SY(d_sin)

    def e_sbf(v):
        v.wait_ge(s_ve.h, E_SIN)
        evE['sbf'] = s_ve.inc(v.tensor_copy(out=S_run_bf[:], in_=S_run[:]))
    VE(e_sbf)

    xTc = [sbE(f'xTc{i}', [128, D_IN], BF16) for i in range(2)]
    gsz = [sbE(f'gsz{i}', [128, DC * 128], BF16) for i in range(2)]
    G_m = sbE('G_m', [128, 128], BF16)
    diffc = sbE('diffc', [128, 8 * 128], F32)
    diffe = sbE('diffe', [128, 8 * 128], BF16)
    M_sb = sbE('M_sb', [128, HEADS * 128], BF16)
    ytmp = sbE('ytmp', [128, 512], F32)
    dterm = sbE('dterm', [128, D_IN], BF16)
    y_tok = sbE('y_tok', [128, D_IN], BF16)
    y_fm = sbE('y_fm', [128, DC * 128], BF16)
    hgn_c = sbE('hgn_c', [128, DC * 128], BF16)
    rstd = sbE('rstd', [1, 128], F32)
    xd_E = sbE('xd_E', [128, D_IN], BF16)

    evF = {}

    def e_pre_loads(e):
        e.wait_ge(s_ve.h, D_EV_ALL[0])
        e.wait_ge(s_xtst.h, s_xtst.v)
        e.wait_ge(s_gst[0].h, s_gst[0].v)
        e.wait_ge(s_gst[1].h, s_gst[1].v)
        s_ez[0].inc(e.dma_start(out=xTc[0][:], in_=xT_stash.ap()[0]), 16)
        both = s_ez[0].inc(
            e.dma_start(out=gsz[0][:].rearrange('p (a b) -> p a b', a=DC),
                        in_=g_stash.ap()[:, :, 0:128].transpose([1, 0, 2])), 16)
        evF[('xt', 0)] = (s_ez[0].h, both)
        evF[('gs', 0)] = (s_ez[0].h, both)
    SY(e_pre_loads)

    for z in range(NT):
        def e_next_loads(e, z=z):
            if z + 1 < NT:
                if z >= 1:
                    e.wait_ge(s_ve.h, evF[('hg', z - 1)])
                    e.wait_ge(s_pe.h, evF[('ydiag', z - 1, 7)])
                sz_ = s_ez[(z + 1) % 2]
                sz_.inc(e.dma_start(out=xTc[(z + 1) % 2][:],
                                    in_=xT_stash.ap()[z + 1]), 16)
                both = sz_.inc(
                    e.dma_start(
                        out=gsz[(z + 1) % 2][:].rearrange('p (a b) -> p a b', a=DC),
                        in_=g_stash.ap()[:, :, (z + 1) * 128:(z + 2) * 128].transpose(
                            [1, 0, 2])), 16)
                evF[('xt', z + 1)] = (sz_.h, both)
                evF[('gs', z + 1)] = (sz_.h, both)
        SY(e_next_loads)

        def te_g(te, z=z):
            if z == 0:
                te.wait_ge(s_ve.h, evE['sbf'])
                te.wait_ge(s_ve.h, D_EV_ALL[0])
            else:
                te.wait_ge(s_ve.h, evF[('gm', z - 1)])
                te.wait_ge(s_ve.h, evF[('yoffs', z - 1, 7)])
            ins = nc.tensor.matmul(
                T1[:, 0:128], B_fm[:, z * 128:(z + 1) * 128],
                C_fm[:, z * 128:(z + 1) * 128], start=True, stop=True)
            evF[('gmm', z)] = s_pe.inc(ins)
        TE(te_g)

        def ve_g(v, z=z):
            v.wait_ge(s_pe.h, evF[('gmm', z)])
            if z == 0:
                v.wait_ge(s_ec.h, E_CONST[0])
            if z >= 1:
                v.wait_ge(s_ve.h, evF[('m', z - 1, 7)])
            ins = v.tensor_tensor(out=G_m[:], in0=T1[:, 0:128],
                                  in1=mask_sb[:], op=ALU.mult)
            evF[('gm', z)] = s_ve.inc(ins)
        VE(ve_g)

        for hb in range(8):
            for sub in range(2):
                def te_d(te, z=z, hb=hb, sub=sub):
                    if z == 0 and hb == 0 and sub == 0:
                        te.wait_ge(s_ve.h, D_EV_ALL[0])
                    k = gF[0]
                    if k >= 2:
                        te.wait_ge(s_ve.h, evGFe[k - 2])
                    bank = fpbank(k)
                    for jj in range(4):
                        j = sub * 4 + jj
                        h = hb * 8 + j
                        te.transpose(
                            bank[:, jj * 128:jj * 128 + 128],
                            grel_T[:, z * 64 + h:z * 64 + h + 1].broadcast_to(
                                [128, 128]),
                            idf_sb[:])
                    evGF[k] = s_pe.inc(te.drain())
                    gF[0] += 1
                TE(te_d)

                def ve_d(v, z=z, hb=hb, sub=sub):
                    k = gF[0] - 1
                    if sub == 0:
                        prev = evF.get(('diffe', z, hb - 1)) if hb > 0 else (
                            evF.get(('diffe', z - 1, 7)) if z > 0 else None)
                        if prev is not None:
                            v.wait_ge(s_sc.h, prev)
                    v.wait_ge(s_pe.h, evGF[k])
                    bank = fpbank(k)
                    ins = None
                    for jj in range(4):
                        j = sub * 4 + jj
                        h = hb * 8 + j
                        ins = v.tensor_scalar(
                            out=diffc[:, j * 128:(j + 1) * 128],
                            in0=bank[:, jj * 128:jj * 128 + 128],
                            scalar1=sgrel_T[:, z * 64 + h:z * 64 + h + 1],
                            scalar2=30.0, op0=ALU.subtract, op1=ALU.min)
                        v.drain()
                    evGFe[k] = s_ve.inc(v.drain())
                    evF[('diffc', z, hb)] = s_ve.v
                VE(ve_d)

            def sc_d(sc, z=z, hb=hb):
                sc.wait_ge(s_ve.h, evF[('diffc', z, hb)])
                prevm = evF.get(('m', z, hb - 1)) if hb > 0 else (
                    evF.get(('m', z - 1, 7)) if z > 0 else None)
                if prevm is not None:
                    sc.wait_ge(s_ve.h, prevm)
                ins = sc.activation(out=diffe[:], in_=diffc[:], func=AF.Exp)
                evF[('diffe', z, hb)] = s_sc.inc(ins)
            SC(sc_d)

            def ve_m(v, z=z, hb=hb):
                v.wait_ge(s_sc.h, evF[('diffe', z, hb)])
                v.wait_ge(s_ve.h, evF[('gm', z)])
                if z >= 1 and hb == 0:
                    v.wait_ge(s_pe.h, evF[('ydiag', z - 1, 7)])
                ins = v.tensor_tensor(
                    out=M_sb[:, hb * 1024:(hb + 1) * 1024].rearrange(
                        'p (a b) -> p a b', a=8),
                    in0=diffe[:].rearrange('p (a b) -> p a b', a=8),
                    in1=G_m[:].unsqueeze(1).broadcast_to([128, 8, 128]),
                    op=ALU.mult)
                evF[('m', z, hb)] = s_ve.inc(ins)
            VE(ve_m)

        for cg in range(8):
            def te_y(te, z=z, cg=cg):
                offb, diagb = ybanks(cg)
                if cg == 0:
                    te.wait_ge(*evF[('xt', z)])
                    te.wait_ge(s_ve.h, evF[('gm', z)])
                    if z >= 1:
                        te.wait_ge(s_ve.h, evF[('supd', z - 1, 7)])
                        te.wait_ge(s_ve.h, evF[('hgn', z - 1)])
                if cg == 1 and z >= 1:
                    te.wait_ge(s_ve.h, evF[('supd', z - 1, 7)])
                te.wait_ge(s_ve.h, evF[('m', z, cg)])
                if cg >= 2:
                    te.wait_ge(s_ve.h, evF[('yoffs', z, cg - 2)])
                    te.wait_ge(s_ve.h, evF[('yfin', z, cg - 2)])
                ins = nc.tensor.matmul(
                    offb, C_fm[:, z * 128:(z + 1) * 128],
                    S_run_bf[:, cg * 512:(cg + 1) * 512], start=True, stop=True)
                evF[('yoff', z, cg)] = s_pe.inc(ins)
                ins = None
                for j in range(8):
                    h = cg * 8 + j
                    ins = nc.tensor.matmul(
                        diagb[:, j * 64:j * 64 + 64],
                        M_sb[:, h * 128:(h + 1) * 128],
                        xTc[z % 2][:, h * 64:(h + 1) * 64],
                        start=True, stop=True)
                    if j == 0 and cg <= 1:
                        pass
                evF[('ydiag', z, cg)] = s_pe.inc(ins)
            TE(te_y)

            def ve_y(v, z=z, cg=cg):
                offb, diagb = ybanks(cg)
                if cg >= 1:
                    v.wait_ge(s_ve.h, evF[('yfin', z, cg - 1)])
                v.wait_ge(s_pe.h, evF[('yoff', z, cg)])
                if z == 0 and cg == 0:
                    v.wait_ge(s_dr.h, E_DREP[0])
                v.tensor_tensor(
                    out=ytmp[:].rearrange('p (a b) -> p a b', a=8),
                    in0=offb.rearrange('p (a b) -> p a b', a=8),
                    in1=erel_T[:, z * 64 + cg * 8:z * 64 + (cg + 1) * 8].unsqueeze(
                        2).broadcast_to([128, 8, 64]),
                    op=ALU.mult)
                evF[('yoffs', z, cg)] = s_ve.inc(v.drain())
                v.tensor_tensor(
                    out=dterm[:, 0:512].rearrange('p (a b) -> p a b', a=8),
                    in0=xTc[z % 2][:, cg * 512:(cg + 1) * 512].rearrange(
                        'p (a b) -> p a b', a=8),
                    in1=drep_bf[:, cg * 8:(cg + 1) * 8].unsqueeze(2).broadcast_to(
                        [128, 8, 64]),
                    op=ALU.mult)
                v.drain()
                v.tensor_tensor(out=ytmp[:], in0=ytmp[:],
                                in1=dterm[:, 0:512], op=ALU.add)
                v.drain()
                v.wait_ge(s_pe.h, evF[('ydiag', z, cg)])
                ins = v.tensor_tensor(out=y_tok[:, cg * 512:(cg + 1) * 512],
                                      in0=diagb, in1=ytmp[:],
                                      op=ALU.add)
                evF[('yfin', z, cg)] = s_ve.inc(ins)
            VE(ve_y)

        def ve_xd2(v, z=z):
            v.wait_ge(s_ve.h, evF[('yfin', z, 7)])
            ins = v.tensor_tensor(
                out=xd_E[:].rearrange('p (h q) -> p h q', h=64),
                in0=xTc[z % 2][:].rearrange('p (h q) -> p h q', h=64),
                in1=dtE_T[:, z * 64:(z + 1) * 64].unsqueeze(2).broadcast_to(
                    [128, 64, 64]),
                op=ALU.mult)
            evF[('xd', z)] = s_ve.inc(ins)
        VE(ve_xd2)

        def te_dec2(te, z=z):
            te.wait_ge(s_ve.h, evF[('xd', z)])
            k = gF[0]
            if k >= 2:
                te.wait_ge(s_ve.h, evGFe[k - 2])
            te.transpose(fpbank(k)[:, 0:64],
                         exp_ctot[0:64, z:z + 1].broadcast_to([64, 128]),
                         idf_sb[0:64, 0:64])
            evGF[k] = s_pe.inc(te.drain())
            gF[0] += 1
        TE(te_dec2)

        def ve_dec2(v, z=z):
            k = gF[0] - 1
            v.wait_ge(s_pe.h, evGF[k])
            evGFe[k] = s_ve.inc(v.tensor_copy(
                out=dec_row[:, 0:64], in_=fpbank(k)[:, 0:64]))
            evF[('dec', z)] = evGFe[k]
        VE(ve_dec2)

        for cg in range(8):
            def te_st2(te, z=z, cg=cg):
                te.wait_ge(s_ve.h, evF[('xd', z)])
                if cg >= 2:
                    te.wait_ge(s_ve.h, evF[('supd', z, cg - 2)])
                dst = (T1[:, 1024:1536] if cg % 2 == 0 else T2[:, 0:512])
                ins = nc.tensor.matmul(
                    dst, B_T[:, z * 128:(z + 1) * 128],
                    xd_E[:, cg * 512:(cg + 1) * 512], start=True, stop=True)
                evF[('smm', z, cg)] = s_pe.inc(ins)
            TE(te_st2)

            def ve_upd2(v, z=z, cg=cg):
                v.wait_ge(s_pe.h, evF[('smm', z, cg)])
                v.wait_ge(s_ve.h, evF[('dec', z)])
                sl = slice(cg * 512, (cg + 1) * 512)
                v.tensor_tensor(
                    out=S_run[:, sl].rearrange('p (h q) -> p h q', h=8),
                    in0=S_run[:, sl].rearrange('p (h q) -> p h q', h=8),
                    in1=dec_row[:, cg * 8:(cg + 1) * 8].unsqueeze(2).broadcast_to(
                        [128, 8, 64]),
                    op=ALU.mult)
                v.drain()
                v.tensor_tensor(
                    out=S_run[:, sl], in0=S_run[:, sl],
                    in1=(T1[:, 1024:1536] if cg % 2 == 0 else T2[:, 0:512]),
                    op=ALU.add)
                v.drain()
                ins = v.tensor_copy(out=S_run_bf[:, sl], in_=S_run[:, sl])
                evF[('supd', z, cg)] = s_ve.inc(ins)
            VE(ve_upd2)

        for qd in range(8):
            def te_yt(te, z=z, qd=qd):
                if qd == 0:
                    te.wait_ge(s_ve.h, evF[('yfin', z, 7)])
                k = gB[0]
                if k >= 2:
                    te.wait_ge(s_ve.h, evGBe[k - 2])
                bank = bfbank(k)
                for j in range(4):
                    dc = qd * 4 + j
                    te.transpose(bank[:, j * 128:j * 128 + 128],
                                 y_tok[:, dc * 128:(dc + 1) * 128], idbf_sb[:])
                evGB[k] = s_pe.inc(te.drain())
                gB[0] += 1
            TE(te_yt)

            def ve_yt(v, z=z, qd=qd):
                k = gB[0] - 1
                v.wait_ge(s_pe.h, evGB[k])
                bank = bfbank(k)
                ins = v.tensor_copy(
                    out=y_fm[:].rearrange('p (d w) -> p d w', d=DC)[
                        :, qd * 4:qd * 4 + 4, :],
                    in_=bank[:, 0:512].rearrange('p (j w) -> p j w', j=4))
                evGBe[k] = s_ve.inc(ins)
                evF[('yt', z, qd)] = s_ve.v
            VE(ve_yt)
        evF[('ydone', z)] = evF[('yt', z, 7)]

        if 'ytok' in dbg:
            def d_yt(e, z=z):
                e.wait_ge(s_ve.h, evF[('ydone', z)])
                s_dbg.inc(e.dma_start(out=dbg['ytok'][z * 128:(z + 1) * 128, :],
                                      in_=y_tok[:]), 16)
            SY(d_yt)

        def ve_hg(v, z=z):
            v.wait_ge(s_ve.h, evF[('ydone', z)])
            v.wait_ge(*evF[('gs', z)])
            if z >= 1:
                v.wait_ge(s_pe.h, evF[('var', z - 1)])
                v.wait_ge(s_hst.h, evF[('hgnst', z - 1)])
            v.tensor_tensor(out=y_fm[:], in0=y_fm[:], in1=gsz[z % 2][:],
                            op=ALU.mult)
            v.drain()
            ins = v.tensor_tensor(out=hgn_c[:], in0=y_fm[:], in1=y_fm[:],
                                  op=ALU.mult)
            evF[('hg', z)] = s_ve.inc(ins)
        VE(ve_hg)

        def te_var(te, z=z):
            te.wait_ge(s_ve.h, evF[('hg', z)])
            if z >= 1:
                te.wait_ge(s_sc.h, evF[('sqv', z - 1)])
            ins = None
            for dc in range(DC):
                ins = nc.tensor.matmul(
                    T1[0:1, 512:640], onebf_sb[:],
                    hgn_c[:, dc * 128:(dc + 1) * 128],
                    start=(dc == 0), stop=(dc == DC - 1))
            evF[('var', z)] = s_pe.inc(ins)
        TE(te_var)

        def sc_rstd(sc, z=z):
            sc.wait_ge(s_pe.h, evF[('var', z)])
            sc.activation(out=rstd[:], in_=T1[0:1, 512:640],
                          func=AF.Ln, scale=cvals[0:1, 2:3],
                          bias=cvals[0:1, 1:2])
            sc.drain()
            ins = sc.activation(out=rstd[:], in_=rstd[:], func=AF.Exp,
                                scale=-0.5)
            evF[('sqv', z)] = s_sc.inc(ins)
        SC(sc_rstd)

        def ve_rstd(v, z=z):
            v.wait_ge(s_sc.h, evF[('sqv', z)])
            ins = v.tensor_copy(out=rstd[:], in_=rstd[:])
            evF[('rstd', z)] = s_ve.inc(ins)
        VE(ve_rstd)

        def te_rb(te, z=z):
            te.wait_ge(s_ve.h, evF[('rstd', z)])
            if z >= 1:
                te.wait_ge(s_ve.h, evF[('hgn', z - 1)])
            ins = nc.tensor.matmul(T1[:, 640:768], onef_sb[:], rstd[:],
                                   start=True, stop=True)
            evF[('rb', z)] = s_pe.inc(ins)
        TE(te_rb)

        def ve_hgn(v, z=z):
            v.wait_ge(s_pe.h, evF[('rb', z)])
            if z >= 1:
                v.wait_ge(s_hst.h, evF[('hgnst', z - 1)])
            ins = None
            for dc in range(DC):
                ins = v.scalar_tensor_tensor(
                    out=hgn_c[:, dc * 128:(dc + 1) * 128],
                    in0=y_fm[:, dc * 128:(dc + 1) * 128],
                    scalar=nw_sb[:, dc:dc + 1], in1=T1[:, 640:768],
                    op0=ALU.mult, op1=ALU.mult)
                v.drain()
            evF[('hgn', z)] = s_ve.inc(ins)
        VE(ve_hgn)

        def sy_hgn(e, z=z):
            e.wait_ge(s_ve.h, evF[('hgn', z)])
            evF[('hgnst', z)] = s_hst.inc(
                e.dma_start(
                    out=hgn_stash.ap()[:, :, z * 128:(z + 1) * 128].transpose(
                        [1, 0, 2]),
                    in_=hgn_c[:].rearrange('p (a b) -> p a b', a=DC)), 16)
        SY(sy_hgn)

        if 'hgnz' in dbg:
            def d_hgn(e, z=z):
                e.wait_ge(s_ve.h, evF[('hgn', z)])
                s_dbg.inc(e.dma_start(out=dbg['hgnz'][z * 128:(z + 1) * 128, :],
                                      in_=hgn_c[:]), 16)
            SY(d_hgn)
        if 'gsilz' in dbg:
            def d_gsl(e, z=z):
                e.wait_ge(*evF[('gs', z)])
                s_dbg.inc(e.dma_start(out=dbg['gsilz'][z * 128:(z + 1) * 128, :],
                                      in_=gsz[z % 2][:]), 16)
            SY(d_gsl)

    if max_phase == 'F':
        return _early_exit([(s_hst.h, evF[('hgnst', NT - 1)]),
                            (s_ve.h, evF[('hgn', NT - 1)])], (esE,))
    esE.close()
    esG = ExitStack()

    def sbG(name, shape, dtype):
        return esG.enter_context(nc.sbuf_tensor(name, shape, dtype))

    Wo_bf = sbG('Wo_bf', [128, DC * 512], BF16)
    hgnT = [sbG(f'hgnT{i}', [128, DC * 128], BF16) for i in range(2)]
    xq_t = [sbG(f'xq_t{i}', [128, 512], F32) for i in range(2)]
    fq_t = [sbG(f'fq_t{i}', [128, 512], BF16) for i in range(2)]
    o_t = sbG('o_t', [128, 512], F32)

    evG = {}

    for q in range(4):
        def g_wload(gp, q=q):
            if q == 0:
                gp.wait_ge(s_ve.h, evF[('hgn', NT - 1)])
                gp.wait_ge(s_pe.h, evF[('rb', NT - 1)])
                gp.wait_ge(s_sc.h, evF[('sqv', NT - 1)])
                gp.wait_ge(s_hst.h, evF[('hgnst', NT - 1)])
            if q >= 1:
                gp.wait_ge(s_pe.h, evG[('mmlast', q - 1)])
            evG[('w', q)] = s_wo.inc(
                gp.dma_start(out=Wo_bf[:].rearrange('p (a b) -> p a b', a=DC),
                             in_=w_oT[q].transpose([1, 0, 2])), 16)
        GP(g_wload)

        for z in range(NT):
            def g_loads(e, q=q, z=z):
                kz = q * NT + z
                if kz >= 2:
                    e.wait_ge(s_ve.h, evG[('fin', (kz - 2) // NT, (kz - 2) % NT)])
                e.wait_ge(s_hst.h, evF[('hgnst', z)])
                if kz == 0:
                    e.wait_ge(s_dst.h, s_dst.v)
                    e.wait_ge(s_ve.h, evF[('hgn', NT - 1)])
                    e.wait_ge(s_pe.h, evF[('rb', NT - 1)])
                    e.wait_ge(s_sc.h, evF[('sqv', NT - 1)])
                sg2 = s_gl[kz % 2]
                sg2.inc(
                    e.dma_start(
                        out=hgnT[kz % 2][:].rearrange('p (a b) -> p a b', a=DC),
                        in_=hgn_stash.ap()[:, :, z * 128:(z + 1) * 128].transpose(
                            [1, 0, 2])), 16)
                sg2.inc(
                    e.dma_start(out=xq_t[kz % 2][:],
                                in_=x_main[z * 128:(z + 1) * 128,
                                           q * 512:(q + 1) * 512]), 16)
                all3 = sg2.inc(
                    e.dma_start(out=fq_t[kz % 2][:],
                                in_=ffn_stash.ap()[z][:, q * 512:(q + 1) * 512]), 16)
                evG[('h', q, z)] = (sg2.h, all3)
                evG[('x', q, z)] = (sg2.h, all3)
                evG[('f', q, z)] = (sg2.h, all3)
            SY(g_loads)

            def g_mm(te, q=q, z=z):
                kz = q * NT + z
                te.wait_ge(*evG[('h', q, z)])
                if z == 0:
                    te.wait_ge(s_wo.h, evG[('w', q)])
                if kz == 0:
                    te.wait_ge(s_ve.h, evF[('yfin', NT - 1, 7)])
                if kz >= 2:
                    te.wait_ge(s_ve.h, evG[('ev', (kz - 2) // NT, (kz - 2) % NT)])
                dst = T1[:, (kz % 2) * 512:(kz % 2) * 512 + 512]
                ins = None
                for dc in range(DC):
                    ins = nc.tensor.matmul(
                        dst, hgnT[kz % 2][:, dc * 128:(dc + 1) * 128],
                        Wo_bf[:, dc * 512:(dc + 1) * 512],
                        start=(dc == 0), stop=(dc == DC - 1))
                evG[('mm', q, z)] = s_pe.inc(ins)
                if z == NT - 1:
                    evG[('mmlast', q)] = s_pe.v
            TE(g_mm)

            def g_fin(v, q=q, z=z):
                kz = q * NT + z
                v.wait_ge(s_pe.h, evG[('mm', q, z)])
                v.wait_ge(*evG[('x', q, z)])
                if kz >= 1:
                    pq, pz = (kz - 1) // NT, (kz - 1) % NT
                    v.wait_ge(s_out.h, evG[('st', pq, pz)])
                dst = T1[:, (kz % 2) * 512:(kz % 2) * 512 + 512]
                v.tensor_tensor(out=o_t[:], in0=dst, in1=xq_t[kz % 2][:],
                                op=ALU.add)
                evG[('ev', q, z)] = s_ve.inc(v.drain())
                ins = v.tensor_tensor(out=o_t[:], in0=o_t[:], in1=fq_t[kz % 2][:],
                                      op=ALU.add)
                evG[('fin', q, z)] = s_ve.inc(ins)
            VE(g_fin)

            def g_store(e, q=q, z=z):
                e.wait_ge(s_ve.h, evG[('fin', q, z)])
                evG[('st', q, z)] = s_out.inc(
                    e.dma_start(out=out_hbm[z * 128:(z + 1) * 128,
                                            q * 512:(q + 1) * 512],
                                in_=o_t[:]), 16)
            SY(g_store)

    def final_wait(e):
        e.wait_ge(s_out.h, s_out.v)
    SY(final_wait)

    return nc


def _prep_shared(inputs):
    f32 = np.float32
    W = np.ascontiguousarray(inputs['L0_attn_in_proj_w'], dtype=f32)
    Wpad = np.zeros((NG * 128, DM), f32)
    Wpad[:8512] = W
    w_inT = np.ascontiguousarray(
        Wpad.reshape(NG, 128, CC, 128).transpose(0, 3, 2, 1))
    gw = np.ascontiguousarray(inputs['L0_ffn_gate_w'], f32)
    uw = np.ascontiguousarray(inputs['L0_ffn_up_w'], f32)
    w_g = np.ascontiguousarray(gw.reshape(CC, 128, IC, 128).transpose(2, 1, 0, 3))
    w_u = np.ascontiguousarray(uw.reshape(CC, 128, IC, 128).transpose(2, 1, 0, 3))
    w_d = np.ascontiguousarray(inputs['L0_ffn_down_w'], f32).reshape(IC, 128, DM)
    Wo = np.ascontiguousarray(inputs['L0_attn_out_proj_w'], f32)
    w_oT = np.ascontiguousarray(Wo.reshape(4, 512, DC, 128).transpose(0, 2, 3, 1))
    conv_w = np.ascontiguousarray(inputs['L0_attn_conv_w'], f32)
    conv_w_t = np.ascontiguousarray(
        conv_w.reshape(34, 128, 4).transpose(1, 0, 2).reshape(128, 136))
    conv_b_t = np.ascontiguousarray(
        inputs['L0_attn_conv_b'].astype(f32).reshape(34, 128).T)
    dt_bias = inputs['L0_attn_dt_bias'].astype(f32).reshape(64, 1)
    a_log = inputs['L0_attn_A_log'].astype(f32).reshape(64, 1)
    d_rep = np.ascontiguousarray(
        np.broadcast_to(inputs['L0_attn_D'].astype(f32)[None, :], (128, 64)))
    norm_w_t = np.ascontiguousarray(
        inputs['L0_attn_norm_w'].astype(f32).reshape(32, 128).T)
    gs_t = np.ascontiguousarray(
        inputs['L0_ffn_gate_scale'].astype(f32).reshape(IC, 128).T)
    us_t = np.ascontiguousarray(
        inputs['L0_ffn_up_scale'].astype(f32).reshape(IC, 128).T)
    ds_b = np.ascontiguousarray(
        np.broadcast_to(inputs['L0_ffn_down_scale'].astype(f32)[None, :],
                        (128, DM)))
    ident_bf = np.eye(128, dtype=ml_dtypes.bfloat16)
    ident_f = np.eye(128, dtype=f32)
    s_idx = np.arange(128)[:, None]
    t_idx = np.arange(128)[None, :]
    mask_ge = (t_idx >= s_idx).astype(ml_dtypes.bfloat16)
    ones_bfv = np.ones((128, 1), ml_dtypes.bfloat16)
    ones_fv = np.ones((1, 128), f32)
    return dict(w_inT=w_inT, w_g=w_g, w_u=w_u, w_d=w_d, w_oT=w_oT,
                conv_w_t=conv_w_t, conv_b_t=conv_b_t, dt_bias=dt_bias,
                a_log=a_log, d_rep=d_rep, norm_w_t=norm_w_t, gs_t=gs_t,
                us_t=us_t, ds_b=ds_b, ident_bf=ident_bf, ident_f=ident_f,
                mask_ge=mask_ge, ones_bf=ones_bfv, ones_f=ones_fv)


def make_in_maps(inputs):
    shared = _prep_shared(inputs)
    x = np.ascontiguousarray(inputs['L0_attn_hidden_states'],
                             np.float32).reshape(L, DM)
    in_maps = []
    for k in range(NCORES):
        m = dict(shared)
        m['x_main'] = np.ascontiguousarray(x[k * T:(k + 1) * T])
        if k == 0:
            m['x_halo'] = np.zeros((128, DM), np.float32)
        else:
            m['x_halo'] = np.ascontiguousarray(x[k * T - 128:k * T])
        mp = np.zeros((128, 8), np.float32)
        mp[:, :k] = 1.0
        m['mask_prev'] = mp
        in_maps.append(m)
    return in_maps


def kernel(**inputs):
    nc = build_nc()
    in_maps = make_in_maps(inputs)
    res = run_bass_kernel_spmd(nc, in_maps, core_ids=list(range(NCORES)))
    out = np.concatenate([np.asarray(r['out']) for r in res.results], axis=0)
    return out.reshape(1, L, DM).astype(np.float32)


# version 08
# speedup vs baseline: 1.0543x; 1.0104x over previous
import numpy as np
import ml_dtypes
import concourse.bass as bass
import concourse.mybir as mybir
from concourse.bass_utils import run_bass_kernel_spmd
from contextlib import ExitStack

F32 = mybir.dt.float32
BF16 = mybir.dt.bfloat16
FP8 = mybir.dt.float8e4
AF = mybir.ActivationFunctionType
ALU = mybir.AluOpType
AX = mybir.AxisListType

NCORES = 8
L = 8192
DM = 2048
T = 1024
NT = 8
D_IN = 4096
HEADS = 64
NG = 67
CC = 16
IC = 48
DC = 32
EPS = 1e-5
TW = 1027


class Sem:
    __slots__ = ('h', 'v')

    def __init__(self, h):
        self.h = h
        self.v = 0

    def inc(self, instr, n=1):
        instr.then_inc(self.h, n)
        self.v += n
        return self.v


def build_nc(debug_taps=(), max_phase='G'):
    nc = bass.Bass('TRN2', target_bir_lowering=False)

    x_main = nc.dram_tensor('x_main', [T, DM], F32, kind='ExternalInput').ap()
    x_halo = nc.dram_tensor('x_halo', [128, DM], F32, kind='ExternalInput').ap()
    w_inT = nc.dram_tensor('w_inT', [NG, 128, CC, 128], F32, kind='ExternalInput').ap()
    w_g = nc.dram_tensor('w_g', [IC, 128, CC, 128], F32, kind='ExternalInput').ap()
    w_u = nc.dram_tensor('w_u', [IC, 128, CC, 128], F32, kind='ExternalInput').ap()
    w_d = nc.dram_tensor('w_d', [IC, 128, DM], F32, kind='ExternalInput').ap()
    w_oT = nc.dram_tensor('w_oT', [4, DC, 128, 512], F32, kind='ExternalInput').ap()
    conv_w_t = nc.dram_tensor('conv_w_t', [128, 136], F32, kind='ExternalInput').ap()
    conv_b_t = nc.dram_tensor('conv_b_t', [128, 34], F32, kind='ExternalInput').ap()
    dt_bias = nc.dram_tensor('dt_bias', [64, 1], F32, kind='ExternalInput').ap()
    a_log = nc.dram_tensor('a_log', [64, 1], F32, kind='ExternalInput').ap()
    d_rep = nc.dram_tensor('d_rep', [128, 64], F32, kind='ExternalInput').ap()
    norm_w_t = nc.dram_tensor('norm_w_t', [128, 32], F32, kind='ExternalInput').ap()
    gs_t = nc.dram_tensor('gs_t', [128, IC], F32, kind='ExternalInput').ap()
    us_t = nc.dram_tensor('us_t', [128, IC], F32, kind='ExternalInput').ap()
    ds_b = nc.dram_tensor('ds_b', [128, DM], F32, kind='ExternalInput').ap()
    ident_bf = nc.dram_tensor('ident_bf', [128, 128], BF16, kind='ExternalInput').ap()
    ident_f = nc.dram_tensor('ident_f', [128, 128], F32, kind='ExternalInput').ap()
    mask_ge = nc.dram_tensor('mask_ge', [128, 128], BF16, kind='ExternalInput').ap()
    ones_bf = nc.dram_tensor('ones_bf', [128, 1], BF16, kind='ExternalInput').ap()
    ones_f = nc.dram_tensor('ones_f', [1, 128], F32, kind='ExternalInput').ap()
    mask_prev = nc.dram_tensor('mask_prev', [128, 8], F32, kind='ExternalInput').ap()
    out_hbm = nc.dram_tensor('out', [T, DM], F32, kind='ExternalOutput').ap()

    g_stash = nc.dram_tensor('g_stash', [32, 128, T], BF16)
    xT_stash = nc.dram_tensor('xT_stash', [NT, 128, D_IN], BF16)
    hgn_stash = nc.dram_tensor('hgn_stash', [NT, 128, DC * 128], BF16)
    fuse_stash = nc.dram_tensor('fuse_stash', [IC, 128, T], BF16)
    ffn_stash = nc.dram_tensor('ffn_stash', [NT, 128, DM], BF16)
    cc_in = nc.dram_tensor('cc_in', [128, 4097], F32)
    cc_out = nc.dram_tensor('cc_out', [NCORES * 128, 4097], F32)

    dbg_shapes = {
        'xn_fm': [128, CC * TW], 'xbcpre': [128, TW],
        'bfm': [128, T], 'cfm': [128, T], 'dtf': [64, T],
        'grel': [64, T], 'xT': [NT * 128, D_IN],
        'slocal': [128, D_IN], 'sin': [128, D_IN],
        'ytok': [NT * 128, D_IN], 'hgnz': [NT * 128, DC * 128],
        'gsilz': [NT * 128, DC * 128],
    }
    dbg = {}
    for name in debug_taps:
        dbg[name] = nc.dram_tensor('dbg_' + name, dbg_shapes[name], F32,
                                   kind='ExternalOutput').ap()

    es = ExitStack()

    def sb(name, shape, dtype):
        return es.enter_context(nc.sbuf_tensor(name, shape, dtype))

    def sem(name):
        return Sem(es.enter_context(nc.semaphore(name)))

    xn_fm = sb('xn_fm', [128, CC * TW], BF16)
    B_fm = sb('B_fm', [128, T], BF16)
    C_fm = sb('C_fm', [128, T], BF16)
    B_T = sb('B_T', [128, NT * 128], BF16)
    dt_fm = sb('dt_fm', [64, T], F32)
    ln_or_erel = sb('ln_or_erel', [64, T], F32)
    grel = sb('grel', [64, T], F32)
    sgrel = sb('sgrel', [64, T], F32)
    est_or_dA = sb('est_or_dA', [64, T], F32)
    grel_T = sb('grel_T', [128, NT * 64], F32)
    sgrel_T = sb('sgrel_T', [128, NT * 64], F32)
    dtE_T = sb('dtE_T', [128, NT * 64], F32)
    erel_T = sb('erel_T', [128, NT * 64], F32)
    exp_ctot = sb('exp_ctot', [64, 8], F32)
    ctot_sum = sb('ctot_sum', [64, 1], F32)
    dec_row = sb('dec_row', [128, 64], F32)
    S_run = sb('S_run', [128, D_IN], F32)

    idbf_sb = sb('idbf_sb', [128, 128], BF16)
    idf_sb = sb('idf_sb', [128, 128], F32)
    cvals = sb('cvals', [128, 4], F32)

    T1 = es.enter_context(nc.psum_tensor('T1', [128, 1536], F32))
    T2 = es.enter_context(nc.psum_tensor('T2', [128, 1024], F32))
    T3 = es.enter_context(nc.psum_tensor('T3', [128, 1024], BF16))
    T4 = es.enter_context(nc.psum_tensor('T4', [128, 1024], BF16))
    T5 = es.enter_context(nc.psum_tensor('T5', [128, 512], F32))

    def bfbank(k):
        return T3 if k % 2 == 0 else T4

    def fpbank(k):
        return T2[:, 512:1024] if k % 2 == 0 else T5[:, 0:512]

    def ybanks(cg):
        if cg % 2 == 0:
            return T1[:, 0:512], T1[:, 512:1024]
        return T1[:, 1024:1536], T2[:, 0:512]

    gB = [0]
    gF = [0]
    evGB = {}
    evGBe = {}
    evGF = {}
    evGFe = {}


    s_const = sem('s_const')
    s_pe = sem('s_pe')
    s_ve = sem('s_ve')
    s_sc = sem('s_sc')
    s_gp = sem('s_gp')
    s_cc = sem('s_cc')
    s_out = sem('s_out')
    s_dbg = sem('s_dbg')
    s_x = [sem('s_x0'), sem('s_x1')]
    s_bc = sem('s_bc')
    s_wb = [sem('s_wb0'), sem('s_wb1')]
    s_gst = [sem('s_gst0'), sem('s_gst1')]
    s_xtst = sem('s_xtst')
    s_exp = sem('s_exp')
    s_dc = sem('s_dc')
    s_wgu = [sem('s_wgu0'), sem('s_wgu1')]
    s_fst = [sem('s_fst0'), sem('s_fst1')]
    s_wd = [sem('s_wd0'), sem('s_wd1')]
    s_fu = [sem('s_fu0'), sem('s_fu1')]
    s_dst = sem('s_dst')
    s_hst = sem('s_hst')
    s_ej = sem('s_ej')
    s_dr = sem('s_dr')
    s_ec = sem('s_ec')
    s_ez = [sem('s_ez0'), sem('s_ez1')]
    s_wo = sem('s_wo')
    s_gl = [sem('s_gl0'), sem('s_gl1')]

    block = es.enter_context(nc.Block())

    def SY(f):
        @block.sync
        def _(e):
            f(e)

    def VE(f):
        @block.vector
        def _(e):
            f(e)

    def SC(f):
        @block.scalar
        def _(e):
            f(e)

    def GP(f):
        @block.gpsimd
        def _(e):
            f(e)

    def TE(f):
        @block.tensor
        def _(e):
            f(e)


    def _early_exit(wait_list, stacks=()):
        def fin(e):
            for sh, vv in wait_list:
                e.wait_ge(sh, vv)
            if s_dbg.v:
                e.wait_ge(s_dbg.h, s_dbg.v)
        SY(fin)
        for st_ in reversed(stacks):
            st_.close()
        es.close()
        return nc

    def const_loads(e):
        for dst, src in [(idbf_sb, ident_bf), (idf_sb, ident_f)]:
            s_const.inc(e.dma_start(out=dst[:], in_=src), 16)
    SY(const_loads)
    LD_CONST = s_const.v

    VE_SETUP = [0]

    def ve_setup(v):
        v.memset(cvals[:, 0:1], 1e-38)
        v.memset(cvals[:, 1:2], EPS)
        v.memset(cvals[:, 2:3], 1.0 / D_IN)
        VE_SETUP[0] = s_ve.inc(v.drain())
    VE(ve_setup)

    SC_SETUP = [0]

    PHA = [None]
    esWb = ExitStack()
    cw_sb = esWb.enter_context(nc.sbuf_tensor('cw_sb', [128, 136], F32))
    cb_sb = esWb.enter_context(nc.sbuf_tensor('cb_sb', [128, 34], F32))
    dtb_sb = esWb.enter_context(nc.sbuf_tensor('dtb_sb', [64, 1], F32))
    alog_sb = esWb.enter_context(nc.sbuf_tensor('alog_sb', [64, 1], F32))
    nA_sb = esWb.enter_context(nc.sbuf_tensor('nA_sb', [64, 1], F32))
    zeros64 = esWb.enter_context(nc.sbuf_tensor('zeros64', [64, 128], F32))
    Wb = [esWb.enter_context(nc.sbuf_tensor(f'Wb{i}', [128, CC * 128], BF16))
          for i in range(2)]

    B_CONST = [0]

    def b_consts(e):
        for dst, srcap in [(cw_sb, conv_w_t), (cb_sb, conv_b_t),
                           (dtb_sb, dt_bias), (alog_sb, a_log)]:
            s_bc.inc(e.dma_start(out=dst[:], in_=srcap), 16)
        B_CONST[0] = s_bc.v
    SY(b_consts)

    def b_setup_ve(v):
        v.wait_ge(s_bc.h, B_CONST[0])
        v.memset(zeros64[:], 0.0)
        BZERO[0] = s_ve.inc(v.drain())
    BZERO = [0]
    VE(b_setup_ve)

    def b_setup_sc(sc):
        sc.wait_ge(s_bc.h, B_CONST[0])
        sc.activation(out=nA_sb[:], in_=alog_sb[:], func=AF.Exp)
        sc.drain()
        sc.activation(out=nA_sb[:], in_=nA_sb[:], func=AF.Copy, scale=-1.0)
        SC_SETUP[0] = s_sc.inc(sc.drain())
    SC(b_setup_sc)

    evB_w = {}

    def b_w01(gp):
        for g in range(2):
            evB_w[g] = (s_wb[g % 2].h, s_wb[g % 2].inc(
                gp.dma_start(out=Wb[g % 2][:],
                             in_=w_inT[g].rearrange('p c j -> p (c j)')), 16))
    GP(b_w01)

    esA = ExitStack()

    def sbA(name, shape, dtype):
        return esA.enter_context(nc.sbuf_tensor(name, shape, dtype))

    x_t = [sbA(f'x_t{i}', [128, DM], F32) for i in range(2)]
    ln_sq = sbA('ln_sq', [128, DM], F32)
    xn_t = [sbA(f'xn_t{i}', [128, DM], BF16) for i in range(2)]
    ln_m = [sbA(f'ln_m{i}', [128, 1], F32) for i in range(2)]
    ln_v = [sbA(f'ln_v{i}', [128, 1], F32) for i in range(2)]

    evA_load = {}
    evA_stats = {}
    evA_apply = {}
    evA_tq = {}
    evA_eq = {}

    for i in range(9):
        def a_load(e, i=i):
            if i >= 2:
                e.wait_ge(s_sc.h, evA_apply[i - 2])
            src = x_halo if i == 0 else x_main[(i - 1) * 128:i * 128, :]
            sx = s_x[i % 2]
            evA_load[i] = (sx.h, sx.inc(e.dma_start(out=x_t[i % 2][:], in_=src), 16))
        SY(a_load)

        def a_stats(v, i=i):
            b = i % 2
            v.wait_ge(*evA_load[i])
            v.reduce_sum(out=ln_m[b][:], in_=x_t[b][:], axis=AX.X)
            v.drain()
            v.tensor_scalar_mul(out=ln_m[b][:], in0=ln_m[b][:], scalar1=1.0 / DM)
            v.drain()
            v.tensor_scalar_sub(out=x_t[b][:], in0=x_t[b][:], scalar1=ln_m[b][:])
            v.drain()
            v.tensor_tensor(out=ln_sq[:], in0=x_t[b][:], in1=x_t[b][:], op=ALU.mult)
            v.drain()
            v.reduce_sum(out=ln_v[b][:], in_=ln_sq[:], axis=AX.X)
            v.drain()
            v.tensor_scalar(out=ln_v[b][:], in0=ln_v[b][:], scalar1=1.0 / DM,
                            scalar2=EPS, op0=ALU.mult, op1=ALU.add)
            evA_stats[i] = s_ve.inc(v.drain())
        VE(a_stats)

        def a_apply(sc, i=i):
            b = i % 2
            sc.wait_ge(s_ve.h, evA_stats[i])
            if i >= 2:
                sc.wait_ge(s_pe.h, evGB[(i - 2) * 4 + 3])
            sc.activation(out=ln_v[b][:], in_=ln_v[b][:], func=AF.Ln)
            sc.drain()
            sc.activation(out=ln_v[b][:], in_=ln_v[b][:], func=AF.Exp,
                          scale=-0.5)
            sc.drain()
            sc.activation(out=xn_t[b][:], in_=x_t[b][:], func=AF.Copy,
                          scale=ln_v[b][:])
            evA_apply[i] = s_sc.inc(sc.drain())
        SC(a_apply)

        for quad in range(4):
            def a_tr(te, i=i, quad=quad):
                k = gB[0]
                if i == 0 and quad == 0:
                    te.wait_ge(s_const.h, LD_CONST)
                if quad == 0:
                    te.wait_ge(s_sc.h, evA_apply[i])
                if k >= 2:
                    te.wait_ge(s_ve.h, evGBe[k - 2])
                bank = bfbank(k)
                for j in range(4):
                    cc = quad * 4 + j
                    te.transpose(bank[:, j * 128:j * 128 + 128],
                                 xn_t[i % 2][:, cc * 128:(cc + 1) * 128], idbf_sb[:])
                evGB[k] = s_pe.inc(te.drain())
                gB[0] += 1
            TE(a_tr)

            def a_ev(v, i=i, quad=quad):
                k = gB[0] - 1
                v.wait_ge(s_pe.h, evGB[k])
                bank = bfbank(k)
                cc0 = quad * 4
                if i == 0:
                    ins = v.tensor_copy(
                        out=xn_fm[:].rearrange('p (c w) -> p c w', c=CC)[
                            :, cc0:cc0 + 4, 0:3],
                        in_=bank[:, 0:512].rearrange('p (j w) -> p j w', j=4)[
                            :, :, 125:128])
                else:
                    off = 3 + (i - 1) * 128
                    ins = v.tensor_copy(
                        out=xn_fm[:].rearrange('p (c w) -> p c w', c=CC)[
                            :, cc0:cc0 + 4, off:off + 128],
                        in_=bank[:, 0:512].rearrange('p (j w) -> p j w', j=4))
                evGBe[k] = s_ve.inc(ins)
            VE(a_ev)

    A_XN = evGBe[35]
    if max_phase == 'A':
        return _early_exit([(s_ve.h, A_XN)], (esWb, esA))
    esA.close()

    if 'xn_fm' in dbg:
        def d_xn(e):
            e.wait_ge(s_ve.h, A_XN)
            s_dbg.inc(e.dma_start(out=dbg['xn_fm'], in_=xn_fm[:]), 16)
        SY(d_xn)

    esB2 = ExitStack()
    x_T = esB2.enter_context(nc.sbuf_tensor('x_T', [128, NT * D_IN], BF16))
    esB = ExitStack()

    def sbB(name, shape, dtype):
        return esB.enter_context(nc.sbuf_tensor(name, shape, dtype))

    gtile = [sbB(f'gtile{i}', [128, T], BF16) for i in range(2)]
    xbc32 = [sbB(f'xbc32{i}', [128, TW], F32) for i in range(2)]
    cacc = [sbB(f'cacc{i}', [128, T], F32) for i in range(2)]
    sil_t = [sbB(f'sil_t{i}', [128, T], BF16) for i in range(3)]
    dt_raw = sbB('dt_raw', [64, T], F32)
    sp_t1 = sbB('sp_t1', [64, T], F32)

    evB_gdone = {}
    evB_mm = {}
    evB_gsil = {}
    evB_gst = {}
    evB_cevac = {}
    evB_fma = {}
    evB_sil = {}
    evB_tq = {}
    evB_teq = {}
    evB_xtev = {}
    gate_cons = [None, None]

    def lhsT(g, cc, rows=128):
        return Wb[g % 2][:, cc * 128:cc * 128 + rows]

    def emit_wload(g):
        def gp_f(gp):
            gp.wait_ge(s_pe.h, evB_gdone[g - 2])
            evB_w[g] = (s_wb[g % 2].h, s_wb[g % 2].inc(
                gp.dma_start(out=Wb[g % 2][:],
                             in_=w_inT[g].rearrange('p c j -> p (c j)')), 16))
        GP(gp_f)

    def emit_xt_transposes(src_buf, cidx):
        for half in range(2):
            def te_f(te, half=half):
                if half == 0:
                    te.wait_ge(s_sc.h, evB_sil[cidx])
                k = gB[0]
                if k >= 2:
                    te.wait_ge(s_ve.h, evGBe[k - 2])
                bank = bfbank(k)
                for j in range(4):
                    z = half * 4 + j
                    te.transpose(bank[:, j * 128:j * 128 + 128],
                                 src_buf[:, z * 128:(z + 1) * 128], idbf_sb[:])
                evGB[k] = s_pe.inc(te.drain())
                gB[0] += 1
            TE(te_f)

            def ve_f(v, half=half):
                k = gB[0] - 1
                v.wait_ge(s_pe.h, evGB[k])
                bank = bfbank(k)
                if cidx < 32:
                    dst = x_T[:].rearrange('p (z w) -> p z w', z=NT)[
                        :, half * 4:half * 4 + 4,
                        cidx * 128:cidx * 128 + 128]
                else:
                    dst = B_T[:].rearrange('p (z w) -> p z w', z=NT)[
                        :, half * 4:half * 4 + 4, :]
                ins = v.tensor_copy(
                    out=dst, in_=bank[:, 0:512].rearrange('p (j w) -> p j w', j=4))
                evGBe[k] = s_ve.inc(ins)
                if half == 1:
                    evB_xtev[cidx] = evGBe[k]
            VE(ve_f)

    for g in range(NG):
        rc = g
        if g < 32:
            def te_f(te, g=g, rc=rc):
                te.wait_ge(*evB_w[g])
                if rc == 0:
                    te.wait_ge(s_ve.h, A_XN)
                for si, (t0, t1) in enumerate([(3, 515), (515, 1027)]):
                    if gate_cons[si] is not None:
                        te.wait_ge(s_sc.h, gate_cons[si])
                    dst = T1[:, si * 512:si * 512 + 512]
                    ins = None
                    for cc in range(CC):
                        ins = nc.tensor.matmul(
                            dst, lhsT(g, cc),
                            xn_fm[:, cc * TW + t0:cc * TW + t1],
                            start=(cc == 0), stop=(cc == CC - 1))
                    evB_mm[(rc, si)] = s_pe.inc(ins)
                evB_gdone[g] = s_pe.v
            TE(te_f)

            def sc_f(sc, rc=rc):
                for si in range(2):
                    sc.wait_ge(s_pe.h, evB_mm[(rc, si)])
                    if rc >= 2 and si == 0:
                        sc.wait_ge(*evB_gst[rc - 2])
                    src_ = T1[:, si * 512:si * 512 + 512]
                    ins = sc.activation(
                        out=gtile[rc % 2][:, si * 512:(si + 1) * 512],
                        in_=src_, func=AF.Silu)
                    gate_cons[si] = s_sc.inc(ins)
                evB_gsil[rc] = s_sc.v
            SC(sc_f)

            def sy_f(e, rc=rc):
                e.wait_ge(s_sc.h, evB_gsil[rc])
                sg_ = s_gst[rc % 2]
                evB_gst[rc] = (sg_.h, sg_.inc(
                    e.dma_start(out=g_stash.ap()[rc], in_=gtile[rc % 2][:]), 16))
            SY(sy_f)
        elif g < 66:
            cidx = g - 32
            par = cidx % 2
            spar = cidx % 3

            def te_f(te, g=g, cidx=cidx):
                te.wait_ge(*evB_w[g])
                if cidx == 0:
                    te.wait_ge(s_sc.h, gate_cons[0])
                    te.wait_ge(s_sc.h, gate_cons[1])
                if cidx > 0:
                    vv = evB_cevac[cidx - 1]
                    te.wait_ge(s_ve.h, vv[0])
                    te.wait_ge(s_sc.h, vv[1])
                ins = None
                for (t0, t1) in [(0, 512), (512, 1024), (1024, TW)]:
                    for cc in range(CC):
                        ins = nc.tensor.matmul(
                            T1[:, t0:t1], lhsT(g, cc),
                            xn_fm[:, cc * TW + t0:cc * TW + t1],
                            start=(cc == 0), stop=(cc == CC - 1))
                evB_mm[('c', cidx)] = s_pe.inc(ins)
                evB_gdone[g] = s_pe.v
            TE(te_f)

            def ve_e(v, cidx=cidx, par=par):
                v.wait_ge(s_pe.h, evB_mm[('c', cidx)])
                if cidx >= 2:
                    v.wait_ge(s_ve.h, evB_fma[cidx - 2])
                ins = v.tensor_copy(out=xbc32[par][:, 0:640], in_=T1[:, 0:640])
                evB_cevac.setdefault(cidx, [0, 0])[0] = s_ve.inc(ins)
            VE(ve_e)

            def sc_e(sc, cidx=cidx, par=par):
                sc.wait_ge(s_pe.h, evB_mm[('c', cidx)])
                if cidx >= 2:
                    sc.wait_ge(s_ve.h, evB_fma[cidx - 2])
                ins = sc.activation(out=xbc32[par][:, 640:TW],
                                    in_=T1[:, 640:TW], func=AF.Copy)
                evB_cevac[cidx][1] = s_sc.inc(ins)
            SC(sc_e)

            if cidx == 0 and 'xbcpre' in dbg:
                def d_xbc(e, cidx=cidx, par=par):
                    e.wait_ge(s_ve.h, evB_cevac[cidx][0])
                    e.wait_ge(s_sc.h, evB_cevac[cidx][1])
                    s_dbg.inc(e.dma_start(out=dbg['xbcpre'],
                                          in_=xbc32[par][:]), 16)
                SY(d_xbc)

            def ve_c(v, cidx=cidx, par=par):
                v.wait_ge(s_ve.h, evB_cevac[cidx][0])
                v.wait_ge(s_sc.h, evB_cevac[cidx][1])
                ins = None
                for colh in range(2):
                    c0 = colh * 512
                    for i in range(4):
                        if i == 0:
                            ins = v.tensor_scalar(
                                out=cacc[par][:, c0:c0 + 512],
                                in0=xbc32[par][:, c0:c0 + 512],
                                scalar1=cw_sb[:, cidx * 4:cidx * 4 + 1],
                                scalar2=None, op0=ALU.mult)
                        else:
                            ins = v.scalar_tensor_tensor(
                                out=cacc[par][:, c0:c0 + 512],
                                in0=xbc32[par][:, c0 + i:c0 + 512 + i],
                                scalar=cw_sb[:, cidx * 4 + i:cidx * 4 + i + 1],
                                in1=cacc[par][:, c0:c0 + 512],
                                op0=ALU.mult, op1=ALU.add)
                        v.drain()
                evB_fma[cidx] = s_ve.inc(v.drain())
            VE(ve_c)

            def sc_s(sc, cidx=cidx, par=par, spar=spar):
                sc.wait_ge(s_ve.h, evB_fma[cidx])
                if cidx < 32:
                    if cidx >= 3:
                        sc.wait_ge(s_ve.h, evB_xtev[cidx - 3])
                    dst = sil_t[spar][:]
                elif cidx == 32:
                    dst = B_fm[:]
                else:
                    dst = C_fm[:]
                ins = sc.activation(out=dst, in_=cacc[par][:, 0:T],
                                    func=AF.Silu, bias=cb_sb[:, cidx:cidx + 1])
                evB_sil[cidx] = s_sc.inc(ins)
            SC(sc_s)

            if cidx < 32:
                emit_xt_transposes(sil_t[spar], cidx)
            elif cidx == 32:
                emit_xt_transposes(B_fm, cidx)
        else:
            def te_f(te, g=g):
                te.wait_ge(*evB_w[g])
                vv = evB_cevac[33]
                te.wait_ge(s_ve.h, vv[0])
                te.wait_ge(s_sc.h, vv[1])
                for si, (t0, t1) in enumerate([(3, 515), (515, 1027)]):
                    if gate_cons[si] is not None:
                        te.wait_ge(s_sc.h, gate_cons[si])
                    dst = T1[0:64, si * 512:si * 512 + 512]
                    ins = None
                    for cc in range(CC):
                        ins = nc.tensor.matmul(
                            dst, lhsT(g, cc, rows=64),
                            xn_fm[:, cc * TW + t0:cc * TW + t1],
                            start=(cc == 0), stop=(cc == CC - 1))
                    evB_mm[('dt', si)] = s_pe.inc(ins)
                evB_gdone[g] = s_pe.v
            TE(te_f)

            def sc_f(sc):
                sc.wait_ge(s_pe.h, evB_mm[('dt', 0)])
                sc.wait_ge(s_pe.h, evB_mm[('dt', 1)])
                for si in range(2):
                    src_ = T1[0:64, si * 512:si * 512 + 512]
                    d0 = slice(si * 512, (si + 1) * 512)
                    sc.activation(out=sp_t1[:, d0], in_=src_, func=AF.Abs,
                                  bias=dtb_sb[:])
                    sc.drain()
                    sc.activation(out=sp_t1[:, d0], in_=sp_t1[:, d0],
                                  func=AF.Exp, scale=-1.0)
                    sc.drain()
                    sc.activation(out=sp_t1[:, d0], in_=sp_t1[:, d0],
                                  func=AF.Ln, bias=1.0)
                    sc.drain()
                    sc.activation(out=dt_raw[:, d0], in_=src_, func=AF.Relu,
                                  bias=dtb_sb[:])
                    sc.drain()
                evB_mm['dt_sc'] = s_sc.inc(sc.drain())
            SC(sc_f)

            def ve_f(v):
                v.wait_ge(s_sc.h, evB_mm['dt_sc'])
                v.wait_ge(s_sc.h, SC_SETUP[0])
                v.wait_ge(s_ve.h, BZERO[0])
                v.tensor_tensor(out=dt_fm[:], in0=dt_raw[:], in1=sp_t1[:],
                                op=ALU.add)
                v.drain()
                v.tensor_scalar_mul(out=est_or_dA[:], in0=dt_fm[:],
                                    scalar1=nA_sb[:])
                v.drain()
                for z in range(NT):
                    sl = slice(z * 128, (z + 1) * 128)
                    v.tensor_tensor_scan(out=grel[:, sl], data0=est_or_dA[:, sl],
                                         data1=zeros64[:], initial=0.0,
                                         op0=ALU.add, op1=ALU.add)
                    v.drain()
                evB_mm['dt_ve'] = s_ve.inc(v.drain())
            VE(ve_f)

            def sc_f2(sc):
                sc.wait_ge(s_ve.h, evB_mm['dt_ve'])
                sc.wait_ge(s_ve.h, VE_SETUP[0])
                sc.activation(out=ln_or_erel[:], in_=dt_fm[:], func=AF.Ln,
                              bias=cvals[0:64, 0:1])
                evB_mm['dt_sc2'] = s_sc.inc(sc.drain())
            SC(sc_f2)

            def ve_f2(v):
                v.wait_ge(s_sc.h, evB_mm['dt_sc2'])
                v.tensor_sub(out=sgrel[:], in0=grel[:], in1=ln_or_erel[:])
                v.drain()
                for z in range(NT):
                    sl = slice(z * 128, (z + 1) * 128)
                    v.tensor_scalar_sub(
                        out=est_or_dA[:, sl], in0=grel[:, sl],
                        scalar1=grel[:, z * 128 + 127:z * 128 + 128])
                    v.drain()
                v.reduce_sum(out=ctot_sum[:], in_=grel[:, 127::128], axis=AX.X)
                evB_mm['dt_ve2'] = s_ve.inc(v.drain())
            VE(ve_f2)

            def sc_f3(sc):
                sc.wait_ge(s_ve.h, evB_mm['dt_ve2'])
                sc.activation(out=est_or_dA[:], in_=est_or_dA[:], func=AF.Exp,
                              scale=-1.0)
                sc.drain()
                sc.activation(out=exp_ctot[:], in_=grel[:, 127::128],
                              func=AF.Exp)
                sc.drain()
                sc.activation(out=ln_or_erel[:], in_=grel[:], func=AF.Exp)
                evB_mm['dt_sc3'] = s_sc.inc(sc.drain())
            SC(sc_f3)

            def ve_f3(v):
                v.wait_ge(s_sc.h, evB_mm['dt_sc3'])
                v.tensor_tensor(out=dt_fm[:], in0=dt_fm[:], in1=est_or_dA[:],
                                op=ALU.mult)
                evB_mm['dt_ve3'] = s_ve.inc(v.drain())
            VE(ve_f3)

            for z in range(NT):
                def te_tt(te, z=z):
                    if z == 0:
                        te.wait_ge(s_ve.h, evB_mm['dt_ve3'])
                    k = gF[0]
                    if k >= 2:
                        te.wait_ge(s_ve.h, evGFe[k - 2])
                    bank = fpbank(k)
                    for jj, srcbuf in enumerate((grel, sgrel, dt_fm, ln_or_erel)):
                        te.transpose(
                            bank[:, jj * 128:jj * 128 + 64],
                            srcbuf[0:64, z * 128:(z + 1) * 128],
                            idf_sb[0:64, 0:64])
                    evGF[k] = s_pe.inc(te.drain())
                    gF[0] += 1
                TE(te_tt)

                def ve_tt(v, z=z):
                    k = gF[0] - 1
                    v.wait_ge(s_pe.h, evGF[k])
                    bank = fpbank(k)
                    ins = None
                    for jj, dstbuf in enumerate((grel_T, sgrel_T, dtE_T, erel_T)):
                        ins = v.tensor_copy(
                            out=dstbuf[:, z * 64:(z + 1) * 64],
                            in_=bank[:, jj * 128:jj * 128 + 64])
                        v.drain()
                    evGFe[k] = s_ve.inc(v.drain())
                    evB_mm['dect'] = s_ve.v
                VE(ve_tt)
        if g + 2 < NG:
            emit_wload(g + 2)

    B_XT = evB_xtev[32]
    B_DECT = evB_mm['dect']

    if 'bfm' in dbg:
        def d_bfm(e):
            e.wait_ge(s_sc.h, evB_sil[32])
            s_dbg.inc(e.dma_start(out=dbg['bfm'], in_=B_fm[:]), 16)
            e.wait_ge(s_sc.h, evB_sil[33])
            s_dbg.inc(e.dma_start(out=dbg['cfm'], in_=C_fm[:]), 16)
        SY(d_bfm)
    if 'dtf' in dbg:
        def d_dtf(e):
            e.wait_ge(s_ve.h, evB_mm['dt_ve'])
            s_dbg.inc(e.dma_start(out=dbg['dtf'], in_=dt_raw[:]), 16)
            s_dbg.inc(e.dma_start(out=dbg['grel'], in_=grel[:]), 16)
        SY(d_dtf)
    if 'xT' in dbg:
        def d_xt(e):
            e.wait_ge(s_ve.h, B_XT)
            for z in range(NT):
                s_dbg.inc(e.dma_start(out=dbg['xT'][z * 128:(z + 1) * 128, :],
                                      in_=x_T[:, z * D_IN:(z + 1) * D_IN]), 16)
        SY(d_xt)

    evB_xtst = {}

    def b_xt_stash(e):
        e.wait_ge(s_ve.h, B_XT)
        for z in range(NT):
            evB_xtst[z] = s_xtst.inc(
                e.dma_start(out=xT_stash.ap()[z],
                            in_=x_T[:, z * D_IN:(z + 1) * D_IN]), 16)
    SY(b_xt_stash)

    esB3 = ExitStack()
    xd_buf = esB3.enter_context(nc.sbuf_tensor('xd_buf', [128, D_IN], BF16))
    SW = {}

    def b_walk_init(v):
        v.wait_ge(s_ve.h, B_XT)
        v.wait_ge(s_ve.h, B_DECT)
        v.memset(S_run[:], 0.0)
        SW['init'] = s_ve.inc(v.drain())
    VE(b_walk_init)

    for z in range(NT):
        def ve_xd(v, z=z):
            if z == 0:
                v.wait_ge(s_ve.h, SW['init'])
            else:
                v.wait_ge(s_pe.h, SW[('mm', z - 1, 7)])
            ins = v.tensor_tensor(
                out=xd_buf[:].rearrange('p (h q) -> p h q', h=64),
                in0=x_T[:, z * D_IN:(z + 1) * D_IN].rearrange(
                    'p (h q) -> p h q', h=64),
                in1=dtE_T[:, z * 64:(z + 1) * 64].unsqueeze(2).broadcast_to(
                    [128, 64, 64]),
                op=ALU.mult)
            SW[('xd', z)] = s_ve.inc(ins)
        VE(ve_xd)

        def te_dec(te, z=z):
            te.wait_ge(s_ve.h, SW[('xd', z)])
            k = gF[0]
            if k >= 2:
                te.wait_ge(s_ve.h, evGFe[k - 2])
            te.transpose(fpbank(k)[:, 0:64],
                         exp_ctot[0:64, z:z + 1].broadcast_to([64, 128]),
                         idf_sb[0:64, 0:64])
            evGF[k] = s_pe.inc(te.drain())
            gF[0] += 1
        TE(te_dec)

        def ve_dec(v, z=z):
            k = gF[0] - 1
            v.wait_ge(s_pe.h, evGF[k])
            evGFe[k] = s_ve.inc(v.tensor_copy(
                out=dec_row[:, 0:64], in_=fpbank(k)[:, 0:64]))
            SW[('dec', z)] = evGFe[k]
        VE(ve_dec)

        for cg in range(8):
            def te_st(te, z=z, cg=cg):
                te.wait_ge(s_ve.h, SW[('xd', z)])
                if z == 0 and cg < 2:
                    te.wait_ge(s_sc.h, evB_mm['dt_sc'])
                if cg >= 2:
                    te.wait_ge(s_ve.h, SW[('upd', z, cg - 2)])
                dst = T1[:, (cg % 2) * 512:(cg % 2) * 512 + 512]
                ins = nc.tensor.matmul(
                    dst, B_T[:, z * 128:(z + 1) * 128],
                    xd_buf[:, cg * 512:(cg + 1) * 512], start=True, stop=True)
                SW[('mm', z, cg)] = s_pe.inc(ins)
            TE(te_st)

            def ve_upd(v, z=z, cg=cg):
                v.wait_ge(s_pe.h, SW[('mm', z, cg)])
                v.wait_ge(s_ve.h, SW[('dec', z)])
                sl = slice(cg * 512, (cg + 1) * 512)
                v.tensor_tensor(
                    out=S_run[:, sl].rearrange('p (h q) -> p h q', h=8),
                    in0=S_run[:, sl].rearrange('p (h q) -> p h q', h=8),
                    in1=dec_row[:, cg * 8:(cg + 1) * 8].unsqueeze(2).broadcast_to(
                        [128, 8, 64]),
                    op=ALU.mult)
                v.drain()
                ins = v.tensor_tensor(
                    out=S_run[:, sl], in0=S_run[:, sl],
                    in1=T1[:, (cg % 2) * 512:(cg % 2) * 512 + 512], op=ALU.add)
                SW[('upd', z, cg)] = s_ve.inc(ins)
            VE(ve_upd)

    B_SLOCAL = SW[('upd', NT - 1, 7)]

    if 'slocal' in dbg:
        def d_sl(e):
            e.wait_ge(s_ve.h, B_SLOCAL)
            s_dbg.inc(e.dma_start(out=dbg['slocal'], in_=S_run[:]), 16)
        SY(d_sl)

    evB_exp = {}

    def b_export(e):
        e.wait_ge(s_ve.h, B_SLOCAL)
        e.wait_ge(s_ve.h, BZERO[0])
        s_exp.inc(e.dma_start(out=cc_in.ap()[:, 0:D_IN], in_=S_run[:]), 16)
        s_exp.inc(
            e.dma_start(out=cc_in.ap()[0:64, D_IN:D_IN + 1], in_=ctot_sum[:],
                        allow_slow_non_contiguous=True), 16)
        evB_exp['done'] = s_exp.inc(
            e.dma_start(out=cc_in.ap()[64:128, D_IN:D_IN + 1],
                        in_=zeros64[0:64, 0:1],
                        allow_slow_non_contiguous=True), 16)
    SY(b_export)

    if max_phase == 'B':
        return _early_exit([(s_exp.h, evB_exp['done']), (s_xtst.h, s_xtst.v),
                            (s_gst[0].h, s_gst[0].v), (s_gst[1].h, s_gst[1].v)],
                           (esWb, esB2, esB, esB3))
    esB3.close()
    esB.close()
    esB2.close()
    esWb.close()

    esD = ExitStack()

    def sbD(name, shape, dtype):
        return esD.enter_context(nc.sbuf_tensor(name, shape, dtype))

    gs_sb = sbD('gs_sb', [128, IC], F32)
    us_sb = sbD('us_sb', [128, IC], F32)
    ds_sb = sbD('ds_sb', [128, DM], F32)
    Wg8 = [sbD(f'Wg8{i}', [128, CC * 128], FP8) for i in range(2)]
    Wu8 = [sbD(f'Wu8{i}', [128, CC * 128], FP8) for i in range(2)]
    gsil_t = sbD('gsil_t', [128, 512], F32)
    fuse_t = [sbD(f'fuse_t{i}', [128, T], BF16) for i in range(2)]

    evD_w = {}
    evD_done = {}
    evD_mm = {}
    evD_sil = {}
    evD_fuse = {}
    evD_fst = {}
    last_fuse = [None]

    D_CONST = [0]

    def d_consts(e):
        e.wait_ge(s_xtst.h, s_xtst.v)
        e.wait_ge(s_ve.h, B_SLOCAL)
        e.wait_ge(s_gst[0].h, s_gst[0].v)
        e.wait_ge(s_gst[1].h, s_gst[1].v)
        e.wait_ge(s_sc.h, evB_mm['dt_sc3'])
        e.wait_ge(s_exp.h, evB_exp['done'])
        for dst, srcap in [(gs_sb, gs_t), (us_sb, us_t), (ds_sb, ds_b)]:
            s_dc.inc(e.dma_start(out=dst[:], in_=srcap), 16)
        D_CONST[0] = s_dc.v
    SY(d_consts)

    def d_w01(gp):
        gp.wait_ge(s_xtst.h, s_xtst.v)
        gp.wait_ge(s_ve.h, B_SLOCAL)
        gp.wait_ge(s_exp.h, evB_exp['done'])
        for ic in range(2):
            sw = s_wgu[ic % 2]
            sw.inc(gp.dma_start(out=Wg8[ic % 2][:],
                                in_=w_g[ic].rearrange('p c j -> p (c j)')), 16)
            evD_w[ic] = (sw.h, sw.inc(
                gp.dma_start(out=Wu8[ic % 2][:],
                             in_=w_u[ic].rearrange('p c j -> p (c j)')), 16))
    GP(d_w01)

    for ic in range(IC):
        if ic >= 2:
            def gp_w(gp, ic=ic):
                gp.wait_ge(s_pe.h, evD_done[ic - 2])
                sw = s_wgu[ic % 2]
                sw.inc(gp.dma_start(
                    out=Wg8[ic % 2][:],
                    in_=w_g[ic].rearrange('p c j -> p (c j)')), 16)
                evD_w[ic] = (sw.h, sw.inc(gp.dma_start(
                    out=Wu8[ic % 2][:],
                    in_=w_u[ic].rearrange('p c j -> p (c j)')), 16))
            GP(gp_w)

        for th in range(2):
            def te_f(te, ic=ic, th=th):
                if ic == 0 and th == 0:
                    te.wait_ge(s_ve.h, A_XN)
                    te.wait_ge(s_sc.h, evB_mm['dt_sc'])
                    te.wait_ge(s_sc.h, evB_gsil[31])
                    te.wait_ge(s_ve.h, B_SLOCAL)
                if th == 0:
                    te.wait_ge(*evD_w[ic])
                if ic >= 1:
                    te.wait_ge(s_ve.h, evD_fuse[(ic - 1, th)])
                t0 = 3 + th * 512
                regs = ([(T1, 0), (T1, 512)] if th == 0
                        else [(T1, 1024), (T2, 0)])
                ins = None
                for Wt, (reg, off) in [(Wg8, regs[0]), (Wu8, regs[1])]:
                    dst = reg[:, off:off + 512]
                    for cc in range(CC):
                        ins = nc.tensor.matmul(
                            dst, Wt[ic % 2][:, cc * 128:(cc + 1) * 128],
                            xn_fm[:, cc * TW + t0:cc * TW + t0 + 512],
                            start=(cc == 0), stop=(cc == CC - 1))
                evD_mm[(ic, th)] = s_pe.inc(ins)
                if th == 1:
                    evD_done[ic] = s_pe.v
            TE(te_f)

            def sc_f(sc, ic=ic, th=th):
                sc.wait_ge(s_pe.h, evD_mm[(ic, th)])
                if ic == 0 and th == 0:
                    sc.wait_ge(s_dc.h, D_CONST[0])
                if last_fuse[0] is not None:
                    sc.wait_ge(s_ve.h, last_fuse[0])
                greg, goff = (T1, 0) if th == 0 else (T1, 1024)
                ins = sc.activation(out=gsil_t[:], in_=greg[:, goff:goff + 512],
                                    func=AF.Silu, scale=gs_sb[:, ic:ic + 1])
                evD_sil[(ic, th)] = s_sc.inc(ins)
            SC(sc_f)

            def ve_f(v, ic=ic, th=th):
                v.wait_ge(s_sc.h, evD_sil[(ic, th)])
                if ic >= 2:
                    v.wait_ge(*evD_fst[ic - 2])
                ureg, uoff = (T1, 512) if th == 0 else (T2, 0)
                ins = v.scalar_tensor_tensor(
                    out=fuse_t[ic % 2][:, th * 512:(th + 1) * 512],
                    in0=ureg[:, uoff:uoff + 512],
                    scalar=us_sb[:, ic:ic + 1],
                    in1=gsil_t[:], op0=ALU.mult, op1=ALU.mult)
                evD_fuse[(ic, th)] = s_ve.inc(ins)
                last_fuse[0] = evD_fuse[(ic, th)]
            VE(ve_f)

        def sy_f(e, ic=ic):
            e.wait_ge(s_ve.h, evD_fuse[(ic, 1)])
            sf = s_fst[ic % 2]
            evD_fst[ic] = (sf.h, sf.inc(
                e.dma_start(out=fuse_stash.ap()[ic], in_=fuse_t[ic % 2][:]), 16))
        SY(sy_f)

    Wd8 = [sbD(f'Wd8{i}', [128, 8 * 512], FP8) for i in range(2)]
    fuT = [sbD(f'fuT{i}', [128, 8 * 128], BF16) for i in range(2)]
    dn_t = sbD('dn_t', [128, 512], BF16)

    evD_dw = {}
    evD_dfu = {}
    evD_dmm = {}
    evD_dev = {}
    evD_dst = {}
    dn_pass = [(tbh, osq) for tbh in range(2) for osq in range(4)]
    fu_k = [0]
    dn_prev = [None]

    def dreg(tb4):
        return (T1[:, tb4 * 512:tb4 * 512 + 512] if tb4 < 3
                else T2[:, 0:512])

    def d_dw01(gp):
        for k12 in range(2):
            pi, sg = divmod(k12, 6)
            tbh, osq = dn_pass[pi]
            sw = s_wd[k12 % 2]
            evD_dw[k12] = (sw.h, sw.inc(gp.dma_start(
                out=Wd8[k12 % 2][:].rearrange('p (a b) -> p a b', a=8),
                in_=w_d[sg * 8:(sg + 1) * 8, :,
                        osq * 512:(osq + 1) * 512].transpose([1, 0, 2])), 16))
    GP(d_dw01)

    for pi, (tbh, osq) in enumerate(dn_pass):
        for sg in range(6):
            k12 = pi * 6 + sg
            if k12 >= 2:
                def gp_w(gp, k12=k12):
                    pi2, sg2 = divmod(k12, 6)
                    tbh2, osq2 = dn_pass[pi2]
                    gp.wait_ge(s_pe.h, evD_dmm[('sgdone', k12 - 2)])
                    sw = s_wd[k12 % 2]
                    evD_dw[k12] = (sw.h, sw.inc(gp.dma_start(
                        out=Wd8[k12 % 2][:].rearrange('p (a b) -> p a b', a=8),
                        in_=w_d[sg2 * 8:(sg2 + 1) * 8, :,
                                osq2 * 512:(osq2 + 1) * 512].transpose(
                            [1, 0, 2])), 16))
                GP(gp_w)

            for tb4 in range(4):
                tb = tbh * 4 + tb4
                k = fu_k[0]

                def sy_fu(e, pi=pi, sg=sg, tb4=tb4, tb=tb, k=k):
                    if k >= 2:
                        e.wait_ge(s_pe.h, evD_dmm[('fu', k - 2)])
                    if k == 0:
                        e.wait_ge(s_fst[0].h, s_fst[0].v)
                        e.wait_ge(s_fst[1].h, s_fst[1].v)
                    sf = s_fu[k % 2]
                    evD_dfu[k] = (sf.h, sf.inc(e.dma_start(
                        out=fuT[k % 2][:].rearrange('p (a b) -> p a b', a=8),
                        in_=fuse_stash.ap()[sg * 8:(sg + 1) * 8, :,
                                            tb * 128:(tb + 1) * 128].transpose(
                            [1, 0, 2])), 16))
                SY(sy_fu)

                def te_f(te, pi=pi, sg=sg, tb4=tb4, k=k, k12=k12):
                    te.wait_ge(*evD_dfu[k])
                    if tb4 == 0:
                        te.wait_ge(*evD_dw[k12])
                    if pi == 0 and sg == 0 and tb4 == 0:
                        te.wait_ge(s_ve.h, last_fuse[0])
                        te.wait_ge(s_ve.h, B_SLOCAL)
                    if pi >= 1 and sg == 0:
                        te.wait_ge(s_dst.h, evD_dst[(pi - 1, tb4)])
                    dst = dreg(tb4)
                    ins = None
                    for j in range(8):
                        ins = nc.tensor.matmul(
                            dst, fuT[k % 2][:, j * 128:(j + 1) * 128],
                            Wd8[k12 % 2][:, j * 512:(j + 1) * 512],
                            start=(sg == 0 and j == 0),
                            stop=(sg == 5 and j == 7))
                    evD_dmm[(pi, tb4)] = s_pe.inc(ins)
                    evD_dmm[('fu', k)] = s_pe.v
                    if tb4 == 3:
                        evD_dmm[('sgdone', k12)] = s_pe.v
                TE(te_f)
                fu_k[0] += 1

        for tb4 in range(4):
            tb = tbh * 4 + tb4

            def ve_f(v, pi=pi, osq=osq, tb4=tb4, tb=tb):
                v.wait_ge(s_pe.h, evD_dmm[(pi, tb4)])
                v.wait_ge(s_dc.h, D_CONST[0])
                if dn_prev[0] is not None:
                    v.wait_ge(s_dst.h, evD_dst[dn_prev[0]])
                o0 = osq * 512
                ins = v.tensor_tensor(out=dn_t[:], in0=dreg(tb4),
                                      in1=ds_sb[:, o0:o0 + 512], op=ALU.mult)
                evD_dev[(pi, tb4)] = s_ve.inc(ins)
            VE(ve_f)

            def sy_s(e, pi=pi, osq=osq, tb4=tb4, tb=tb):
                e.wait_ge(s_ve.h, evD_dev[(pi, tb4)])
                o0 = osq * 512
                evD_dst[(pi, tb4)] = s_dst.inc(e.dma_start(
                    out=ffn_stash.ap()[tb][:, o0:o0 + 512], in_=dn_t[:]), 16)
            SY(sy_s)
            dn_prev[0] = (pi, tb4)

    D_EV_ALL = [None]

    def d_fin(v):
        v.wait_ge(s_dst.h, evD_dst[(7, 3)])
        D_EV_ALL[0] = s_ve.inc(v.drain())
    VE(d_fin)

    def gp_cc(gp):
        gp.wait_ge(s_exp.h, evB_exp['done'])
        s_cc.inc(gp.collective_compute(
            'AllGather', ALU.bypass, replica_groups=[list(range(NCORES))],
            ins=[cc_in.ap().opt()], outs=[cc_out.ap().opt()]))
    GP(gp_cc)
    CC_DONE = s_cc.v

    if max_phase == 'D':
        return _early_exit([(s_ve.h, D_EV_ALL[0]), (s_cc.h, CC_DONE)], (esD,))
    esD.close()

    esE = ExitStack()

    def sbE(name, shape, dtype):
        return esE.enter_context(nc.sbuf_tensor(name, shape, dtype))

    mask_sb = sbE('mask_sb', [128, 128], BF16)
    drep_bf = sbE('drep_bf', [128, 64], BF16)
    nw_sb = sbE('nw_sb', [128, 32], F32)
    onebf_sb = sbE('onebf_sb', [128, 1], BF16)
    onef_sb = sbE('onef_sb', [1, 128], F32)
    mprev_sb = sbE('mprev_sb', [128, 8], F32)
    Sj = [sbE('Sj0', [128, D_IN], F32)] * 2
    dAj = sbE('dAj', [64, 1], F32)
    dj = sbE('dj', [64, 1], F32)
    S_run_bf = sbE('S_run_bf', [128, D_IN], BF16)

    evE = {}

    E_CONST = [0]

    def e_consts(e):
        e.wait_ge(s_ve.h, D_EV_ALL[0])
        for dst, srcap in [(mask_sb, mask_ge), (nw_sb, norm_w_t),
                           (onebf_sb, ones_bf), (onef_sb, ones_f),
                           (mprev_sb, mask_prev)]:
            s_ec.inc(e.dma_start(out=dst[:], in_=srcap), 16)
        E_CONST[0] = s_ec.v
    SY(e_consts)

    E_DREP = [0]

    def e_drep(gp):
        gp.wait_ge(s_ve.h, D_EV_ALL[0])
        E_DREP[0] = s_dr.inc(gp.dma_start(out=drep_bf[:], in_=d_rep), 16)
    GP(e_drep)

    def e_sin_zero(v):
        v.wait_ge(s_exp.h, evB_exp['done'])
        v.memset(S_run[:], 0.0)
        evE['zero'] = s_ve.inc(v.drain())
    VE(e_sin_zero)

    for j in range(NCORES):
        def e_loads(e, j=j):
            if j == 0:
                e.wait_ge(s_cc.h, CC_DONE)
                e.wait_ge(s_ve.h, D_EV_ALL[0])
            if j >= 1:
                e.wait_ge(s_ve.h, evE[('sacc', j - 1)])
            s_ej.inc(
                e.dma_start(out=Sj[j % 2][:],
                            in_=cc_out.ap()[j * 128:(j + 1) * 128, 0:D_IN]), 16)
            both = s_ej.inc(
                e.dma_start(out=dAj[:],
                            in_=cc_out.ap()[j * 128:j * 128 + 64,
                                            D_IN:D_IN + 1],
                            allow_slow_non_contiguous=True), 16)
            evE[('sj', j)] = (s_ej.h, both)
            evE[('dAj', j)] = (s_ej.h, both)
        SY(e_loads)

        def e_dj(sc, j=j):
            sc.wait_ge(*evE[('dAj', j)])
            if j == 0:
                sc.wait_ge(s_ec.h, E_CONST[0])
            if j >= 1:
                sc.wait_ge(s_ve.h, evE[('sacc', j - 1)])
            ins = sc.activation(out=dj[:], in_=dAj[:], func=AF.Exp,
                                scale=mprev_sb[0:64, j:j + 1])
            evE[('dj', j)] = s_sc.inc(ins)
        SC(e_dj)

        def te_dj(te, j=j):
            te.wait_ge(s_sc.h, evE[('dj', j)])
            if j == 0:
                te.wait_ge(s_ve.h, D_EV_ALL[0])
            k = gF[0]
            if k >= 2:
                te.wait_ge(s_ve.h, evGFe[k - 2])
            te.transpose(fpbank(k)[:, 0:64],
                         dj[0:64, 0:1].broadcast_to([64, 128]),
                         idf_sb[0:64, 0:64])
            evGF[k] = s_pe.inc(te.drain())
            gF[0] += 1
        TE(te_dj)

        def ve_dj(v, j=j):
            k = gF[0] - 1
            if j == 0:
                v.wait_ge(s_ec.h, E_CONST[0])
            v.wait_ge(s_pe.h, evGF[k])
            evGFe[k] = s_ve.inc(v.tensor_copy(
                out=dec_row[:, 0:64], in_=fpbank(k)[:, 0:64]))
            v.drain()
            v.wait_ge(*evE[('sj', j)])
            if j == 0:
                v.wait_ge(s_ve.h, evE['zero'])
            ins = None
            for cg in range(8):
                sl = slice(cg * 512, (cg + 1) * 512)
                v.tensor_tensor(
                    out=S_run[:, sl].rearrange('p (h q) -> p h q', h=8),
                    in0=S_run[:, sl].rearrange('p (h q) -> p h q', h=8),
                    in1=dec_row[:, cg * 8:(cg + 1) * 8].unsqueeze(2).broadcast_to(
                        [128, 8, 64]),
                    op=ALU.mult)
                v.drain()
                ins = v.scalar_tensor_tensor(
                    out=S_run[:, sl], in0=Sj[j % 2][:, sl],
                    scalar=mprev_sb[:, j:j + 1], in1=S_run[:, sl],
                    op0=ALU.mult, op1=ALU.add)
                v.drain()
            evE[('sacc', j)] = s_ve.inc(v.drain())
        VE(ve_dj)

    E_SIN = evE[('sacc', NCORES - 1)]

    if 'sin' in dbg:
        def d_sin(e):
            e.wait_ge(s_ve.h, E_SIN)
            s_dbg.inc(e.dma_start(out=dbg['sin'], in_=S_run[:]), 16)
        SY(d_sin)

    def e_sbf(v):
        v.wait_ge(s_ve.h, E_SIN)
        evE['sbf'] = s_ve.inc(v.tensor_copy(out=S_run_bf[:], in_=S_run[:]))
    VE(e_sbf)

    xTc = [sbE(f'xTc{i}', [128, D_IN], BF16) for i in range(2)]
    gsz = [sbE(f'gsz{i}', [128, DC * 128], BF16) for i in range(2)]
    G_m = sbE('G_m', [128, 128], BF16)
    diffc = sbE('diffc', [128, 8 * 128], F32)
    diffe = sbE('diffe', [128, 8 * 128], BF16)
    M_sb = sbE('M_sb', [128, HEADS * 128], BF16)
    ytmp = sbE('ytmp', [128, 512], F32)
    dterm = sbE('dterm', [128, D_IN], BF16)
    y_tok = sbE('y_tok', [128, D_IN], BF16)
    y_fm = sbE('y_fm', [128, DC * 128], BF16)
    hgn_c = sbE('hgn_c', [128, DC * 128], BF16)
    rstd = sbE('rstd', [1, 128], F32)
    xd_E = sbE('xd_E', [128, D_IN], BF16)

    evF = {}

    def e_pre_loads(e):
        e.wait_ge(s_ve.h, D_EV_ALL[0])
        e.wait_ge(s_xtst.h, s_xtst.v)
        e.wait_ge(s_gst[0].h, s_gst[0].v)
        e.wait_ge(s_gst[1].h, s_gst[1].v)
        s_ez[0].inc(e.dma_start(out=xTc[0][:], in_=xT_stash.ap()[0]), 16)
        both = s_ez[0].inc(
            e.dma_start(out=gsz[0][:].rearrange('p (a b) -> p a b', a=DC),
                        in_=g_stash.ap()[:, :, 0:128].transpose([1, 0, 2])), 16)
        evF[('xt', 0)] = (s_ez[0].h, both)
        evF[('gs', 0)] = (s_ez[0].h, both)
    SY(e_pre_loads)

    for z in range(NT):
        def e_next_loads(e, z=z):
            if z + 1 < NT:
                if z >= 1:
                    e.wait_ge(s_ve.h, evF[('hg', z - 1)])
                    e.wait_ge(s_pe.h, evF[('ydiag', z - 1, 7)])
                sz_ = s_ez[(z + 1) % 2]
                sz_.inc(e.dma_start(out=xTc[(z + 1) % 2][:],
                                    in_=xT_stash.ap()[z + 1]), 16)
                both = sz_.inc(
                    e.dma_start(
                        out=gsz[(z + 1) % 2][:].rearrange('p (a b) -> p a b', a=DC),
                        in_=g_stash.ap()[:, :, (z + 1) * 128:(z + 2) * 128].transpose(
                            [1, 0, 2])), 16)
                evF[('xt', z + 1)] = (sz_.h, both)
                evF[('gs', z + 1)] = (sz_.h, both)
        SY(e_next_loads)

        def te_g(te, z=z):
            if z == 0:
                te.wait_ge(s_ve.h, evE['sbf'])
                te.wait_ge(s_ve.h, D_EV_ALL[0])
            else:
                te.wait_ge(s_ve.h, evF[('gm', z - 1)])
                te.wait_ge(s_ve.h, evF[('yoffs', z - 1, 7)])
            ins = nc.tensor.matmul(
                T1[:, 0:128], B_fm[:, z * 128:(z + 1) * 128],
                C_fm[:, z * 128:(z + 1) * 128], start=True, stop=True)
            evF[('gmm', z)] = s_pe.inc(ins)
        TE(te_g)

        def ve_g(v, z=z):
            v.wait_ge(s_pe.h, evF[('gmm', z)])
            if z == 0:
                v.wait_ge(s_ec.h, E_CONST[0])
            if z >= 1:
                v.wait_ge(s_ve.h, evF[('m', z - 1, 7)])
            ins = v.tensor_tensor(out=G_m[:], in0=T1[:, 0:128],
                                  in1=mask_sb[:], op=ALU.mult)
            evF[('gm', z)] = s_ve.inc(ins)
        VE(ve_g)

        for hb in range(8):
            for sub in range(2):
                def te_d(te, z=z, hb=hb, sub=sub):
                    if z == 0 and hb == 0 and sub == 0:
                        te.wait_ge(s_ve.h, D_EV_ALL[0])
                    k = gF[0]
                    if k >= 2:
                        te.wait_ge(s_ve.h, evGFe[k - 2])
                    bank = fpbank(k)
                    for jj in range(4):
                        j = sub * 4 + jj
                        h = hb * 8 + j
                        te.transpose(
                            bank[:, jj * 128:jj * 128 + 128],
                            grel_T[:, z * 64 + h:z * 64 + h + 1].broadcast_to(
                                [128, 128]),
                            idf_sb[:])
                    evGF[k] = s_pe.inc(te.drain())
                    gF[0] += 1
                TE(te_d)

                def ve_d(v, z=z, hb=hb, sub=sub):
                    k = gF[0] - 1
                    if sub == 0:
                        prev = evF.get(('diffe', z, hb - 1)) if hb > 0 else (
                            evF.get(('diffe', z - 1, 7)) if z > 0 else None)
                        if prev is not None:
                            v.wait_ge(s_sc.h, prev)
                    v.wait_ge(s_pe.h, evGF[k])
                    bank = fpbank(k)
                    ins = None
                    for jj in range(4):
                        j = sub * 4 + jj
                        h = hb * 8 + j
                        ins = v.tensor_scalar(
                            out=diffc[:, j * 128:(j + 1) * 128],
                            in0=bank[:, jj * 128:jj * 128 + 128],
                            scalar1=sgrel_T[:, z * 64 + h:z * 64 + h + 1],
                            scalar2=30.0, op0=ALU.subtract, op1=ALU.min)
                        v.drain()
                    evGFe[k] = s_ve.inc(v.drain())
                    evF[('diffc', z, hb)] = s_ve.v
                VE(ve_d)

            def sc_d(sc, z=z, hb=hb):
                sc.wait_ge(s_ve.h, evF[('diffc', z, hb)])
                prevm = evF.get(('m', z, hb - 1)) if hb > 0 else (
                    evF.get(('m', z - 1, 7)) if z > 0 else None)
                if prevm is not None:
                    sc.wait_ge(s_ve.h, prevm)
                ins = sc.activation(out=diffe[:], in_=diffc[:], func=AF.Exp)
                evF[('diffe', z, hb)] = s_sc.inc(ins)
            SC(sc_d)

            def ve_m(v, z=z, hb=hb):
                v.wait_ge(s_sc.h, evF[('diffe', z, hb)])
                v.wait_ge(s_ve.h, evF[('gm', z)])
                if z >= 1 and hb == 0:
                    v.wait_ge(s_pe.h, evF[('ydiag', z - 1, 7)])
                ins = v.tensor_tensor(
                    out=M_sb[:, hb * 1024:(hb + 1) * 1024].rearrange(
                        'p (a b) -> p a b', a=8),
                    in0=diffe[:].rearrange('p (a b) -> p a b', a=8),
                    in1=G_m[:].unsqueeze(1).broadcast_to([128, 8, 128]),
                    op=ALU.mult)
                evF[('m', z, hb)] = s_ve.inc(ins)
            VE(ve_m)

        for cg in range(8):
            def te_y(te, z=z, cg=cg):
                offb, diagb = ybanks(cg)
                if cg == 0:
                    te.wait_ge(*evF[('xt', z)])
                    te.wait_ge(s_ve.h, evF[('gm', z)])
                    if z >= 1:
                        te.wait_ge(s_ve.h, evF[('supd', z - 1, 7)])
                        te.wait_ge(s_ve.h, evF[('hgn', z - 1)])
                if cg == 1 and z >= 1:
                    te.wait_ge(s_ve.h, evF[('supd', z - 1, 7)])
                te.wait_ge(s_ve.h, evF[('m', z, cg)])
                if cg >= 2:
                    te.wait_ge(s_ve.h, evF[('yoffs', z, cg - 2)])
                    te.wait_ge(s_ve.h, evF[('yfin', z, cg - 2)])
                ins = nc.tensor.matmul(
                    offb, C_fm[:, z * 128:(z + 1) * 128],
                    S_run_bf[:, cg * 512:(cg + 1) * 512], start=True, stop=True)
                evF[('yoff', z, cg)] = s_pe.inc(ins)
                ins = None
                for j in range(8):
                    h = cg * 8 + j
                    ins = nc.tensor.matmul(
                        diagb[:, j * 64:j * 64 + 64],
                        M_sb[:, h * 128:(h + 1) * 128],
                        xTc[z % 2][:, h * 64:(h + 1) * 64],
                        start=True, stop=True)
                    if j == 0 and cg <= 1:
                        pass
                evF[('ydiag', z, cg)] = s_pe.inc(ins)
            TE(te_y)

            def ve_y(v, z=z, cg=cg):
                offb, diagb = ybanks(cg)
                if cg >= 1:
                    v.wait_ge(s_ve.h, evF[('yfin', z, cg - 1)])
                v.wait_ge(s_pe.h, evF[('yoff', z, cg)])
                if z == 0 and cg == 0:
                    v.wait_ge(s_dr.h, E_DREP[0])
                v.tensor_tensor(
                    out=ytmp[:].rearrange('p (a b) -> p a b', a=8),
                    in0=offb.rearrange('p (a b) -> p a b', a=8),
                    in1=erel_T[:, z * 64 + cg * 8:z * 64 + (cg + 1) * 8].unsqueeze(
                        2).broadcast_to([128, 8, 64]),
                    op=ALU.mult)
                evF[('yoffs', z, cg)] = s_ve.inc(v.drain())
                v.tensor_tensor(
                    out=dterm[:, 0:512].rearrange('p (a b) -> p a b', a=8),
                    in0=xTc[z % 2][:, cg * 512:(cg + 1) * 512].rearrange(
                        'p (a b) -> p a b', a=8),
                    in1=drep_bf[:, cg * 8:(cg + 1) * 8].unsqueeze(2).broadcast_to(
                        [128, 8, 64]),
                    op=ALU.mult)
                v.drain()
                v.tensor_tensor(out=ytmp[:], in0=ytmp[:],
                                in1=dterm[:, 0:512], op=ALU.add)
                v.drain()
                v.wait_ge(s_pe.h, evF[('ydiag', z, cg)])
                ins = v.tensor_tensor(out=y_tok[:, cg * 512:(cg + 1) * 512],
                                      in0=diagb, in1=ytmp[:],
                                      op=ALU.add)
                evF[('yfin', z, cg)] = s_ve.inc(ins)
            VE(ve_y)

        def ve_xd2(v, z=z):
            v.wait_ge(s_ve.h, evF[('yfin', z, 7)])
            ins = v.tensor_tensor(
                out=xd_E[:].rearrange('p (h q) -> p h q', h=64),
                in0=xTc[z % 2][:].rearrange('p (h q) -> p h q', h=64),
                in1=dtE_T[:, z * 64:(z + 1) * 64].unsqueeze(2).broadcast_to(
                    [128, 64, 64]),
                op=ALU.mult)
            evF[('xd', z)] = s_ve.inc(ins)
        VE(ve_xd2)

        def te_dec2(te, z=z):
            te.wait_ge(s_ve.h, evF[('xd', z)])
            k = gF[0]
            if k >= 2:
                te.wait_ge(s_ve.h, evGFe[k - 2])
            te.transpose(fpbank(k)[:, 0:64],
                         exp_ctot[0:64, z:z + 1].broadcast_to([64, 128]),
                         idf_sb[0:64, 0:64])
            evGF[k] = s_pe.inc(te.drain())
            gF[0] += 1
        TE(te_dec2)

        def ve_dec2(v, z=z):
            k = gF[0] - 1
            v.wait_ge(s_pe.h, evGF[k])
            evGFe[k] = s_ve.inc(v.tensor_copy(
                out=dec_row[:, 0:64], in_=fpbank(k)[:, 0:64]))
            evF[('dec', z)] = evGFe[k]
        VE(ve_dec2)

        for cg in range(8):
            def te_st2(te, z=z, cg=cg):
                te.wait_ge(s_ve.h, evF[('xd', z)])
                if cg >= 2:
                    te.wait_ge(s_ve.h, evF[('supd', z, cg - 2)])
                dst = (T1[:, 1024:1536] if cg % 2 == 0 else T2[:, 0:512])
                ins = nc.tensor.matmul(
                    dst, B_T[:, z * 128:(z + 1) * 128],
                    xd_E[:, cg * 512:(cg + 1) * 512], start=True, stop=True)
                evF[('smm', z, cg)] = s_pe.inc(ins)
            TE(te_st2)

            def ve_upd2(v, z=z, cg=cg):
                v.wait_ge(s_pe.h, evF[('smm', z, cg)])
                v.wait_ge(s_ve.h, evF[('dec', z)])
                sl = slice(cg * 512, (cg + 1) * 512)
                v.tensor_tensor(
                    out=S_run[:, sl].rearrange('p (h q) -> p h q', h=8),
                    in0=S_run[:, sl].rearrange('p (h q) -> p h q', h=8),
                    in1=dec_row[:, cg * 8:(cg + 1) * 8].unsqueeze(2).broadcast_to(
                        [128, 8, 64]),
                    op=ALU.mult)
                v.drain()
                v.tensor_tensor(
                    out=S_run[:, sl], in0=S_run[:, sl],
                    in1=(T1[:, 1024:1536] if cg % 2 == 0 else T2[:, 0:512]),
                    op=ALU.add)
                v.drain()
                ins = v.tensor_copy(out=S_run_bf[:, sl], in_=S_run[:, sl])
                evF[('supd', z, cg)] = s_ve.inc(ins)
            VE(ve_upd2)

        for qd in range(8):
            def te_yt(te, z=z, qd=qd):
                if qd == 0:
                    te.wait_ge(s_ve.h, evF[('yfin', z, 7)])
                k = gB[0]
                if k >= 2:
                    te.wait_ge(s_ve.h, evGBe[k - 2])
                bank = bfbank(k)
                for j in range(4):
                    dc = qd * 4 + j
                    te.transpose(bank[:, j * 128:j * 128 + 128],
                                 y_tok[:, dc * 128:(dc + 1) * 128], idbf_sb[:])
                evGB[k] = s_pe.inc(te.drain())
                gB[0] += 1
            TE(te_yt)

            def ve_yt(v, z=z, qd=qd):
                k = gB[0] - 1
                v.wait_ge(s_pe.h, evGB[k])
                bank = bfbank(k)
                ins = v.tensor_copy(
                    out=y_fm[:].rearrange('p (d w) -> p d w', d=DC)[
                        :, qd * 4:qd * 4 + 4, :],
                    in_=bank[:, 0:512].rearrange('p (j w) -> p j w', j=4))
                evGBe[k] = s_ve.inc(ins)
                evF[('yt', z, qd)] = s_ve.v
            VE(ve_yt)
        evF[('ydone', z)] = evF[('yt', z, 7)]

        if 'ytok' in dbg:
            def d_yt(e, z=z):
                e.wait_ge(s_ve.h, evF[('ydone', z)])
                s_dbg.inc(e.dma_start(out=dbg['ytok'][z * 128:(z + 1) * 128, :],
                                      in_=y_tok[:]), 16)
            SY(d_yt)

        def ve_hg(v, z=z):
            v.wait_ge(s_ve.h, evF[('ydone', z)])
            v.wait_ge(*evF[('gs', z)])
            if z >= 1:
                v.wait_ge(s_pe.h, evF[('var', z - 1)])
                v.wait_ge(s_hst.h, evF[('hgnst', z - 1)])
            v.tensor_tensor(out=y_fm[:], in0=y_fm[:], in1=gsz[z % 2][:],
                            op=ALU.mult)
            v.drain()
            ins = v.tensor_tensor(out=hgn_c[:], in0=y_fm[:], in1=y_fm[:],
                                  op=ALU.mult)
            evF[('hg', z)] = s_ve.inc(ins)
        VE(ve_hg)

        def te_var(te, z=z):
            te.wait_ge(s_ve.h, evF[('hg', z)])
            if z >= 1:
                te.wait_ge(s_sc.h, evF[('sqv', z - 1)])
            ins = None
            for dc in range(DC):
                ins = nc.tensor.matmul(
                    T1[0:1, 512:640], onebf_sb[:],
                    hgn_c[:, dc * 128:(dc + 1) * 128],
                    start=(dc == 0), stop=(dc == DC - 1))
            evF[('var', z)] = s_pe.inc(ins)
        TE(te_var)

        def sc_rstd(sc, z=z):
            sc.wait_ge(s_pe.h, evF[('var', z)])
            sc.activation(out=rstd[:], in_=T1[0:1, 512:640],
                          func=AF.Ln, scale=cvals[0:1, 2:3],
                          bias=cvals[0:1, 1:2])
            sc.drain()
            ins = sc.activation(out=rstd[:], in_=rstd[:], func=AF.Exp,
                                scale=-0.5)
            evF[('sqv', z)] = s_sc.inc(ins)
        SC(sc_rstd)

        def ve_rstd(v, z=z):
            v.wait_ge(s_sc.h, evF[('sqv', z)])
            ins = v.tensor_copy(out=rstd[:], in_=rstd[:])
            evF[('rstd', z)] = s_ve.inc(ins)
        VE(ve_rstd)

        def te_rb(te, z=z):
            te.wait_ge(s_ve.h, evF[('rstd', z)])
            if z >= 1:
                te.wait_ge(s_ve.h, evF[('hgn', z - 1)])
            ins = nc.tensor.matmul(T1[:, 640:768], onef_sb[:], rstd[:],
                                   start=True, stop=True)
            evF[('rb', z)] = s_pe.inc(ins)
        TE(te_rb)

        def ve_hgn(v, z=z):
            v.wait_ge(s_pe.h, evF[('rb', z)])
            if z >= 1:
                v.wait_ge(s_hst.h, evF[('hgnst', z - 1)])
            ins = None
            for dc in range(DC):
                ins = v.scalar_tensor_tensor(
                    out=hgn_c[:, dc * 128:(dc + 1) * 128],
                    in0=y_fm[:, dc * 128:(dc + 1) * 128],
                    scalar=nw_sb[:, dc:dc + 1], in1=T1[:, 640:768],
                    op0=ALU.mult, op1=ALU.mult)
                v.drain()
            evF[('hgn', z)] = s_ve.inc(ins)
        VE(ve_hgn)

        def sy_hgn(e, z=z):
            e.wait_ge(s_ve.h, evF[('hgn', z)])
            evF[('hgnst', z)] = s_hst.inc(
                e.dma_start(out=hgn_stash.ap()[z], in_=hgn_c[:]), 16)
        SY(sy_hgn)

        if 'hgnz' in dbg:
            def d_hgn(e, z=z):
                e.wait_ge(s_ve.h, evF[('hgn', z)])
                s_dbg.inc(e.dma_start(out=dbg['hgnz'][z * 128:(z + 1) * 128, :],
                                      in_=hgn_c[:]), 16)
            SY(d_hgn)
        if 'gsilz' in dbg:
            def d_gsl(e, z=z):
                e.wait_ge(*evF[('gs', z)])
                s_dbg.inc(e.dma_start(out=dbg['gsilz'][z * 128:(z + 1) * 128, :],
                                      in_=gsz[z % 2][:]), 16)
            SY(d_gsl)

    if max_phase == 'F':
        return _early_exit([(s_hst.h, evF[('hgnst', NT - 1)]),
                            (s_ve.h, evF[('hgn', NT - 1)])], (esE,))
    esE.close()
    esG = ExitStack()

    def sbG(name, shape, dtype):
        return esG.enter_context(nc.sbuf_tensor(name, shape, dtype))

    Wo_bf = sbG('Wo_bf', [128, DC * 512], BF16)
    hgnT = [sbG(f'hgnT{i}', [128, DC * 128], BF16) for i in range(2)]
    xq_t = [sbG(f'xq_t{i}', [128, 512], F32) for i in range(2)]
    fq_t = [sbG(f'fq_t{i}', [128, 512], BF16) for i in range(2)]
    o_t = sbG('o_t', [128, 512], F32)

    evG = {}

    for q in range(4):
        def g_wload(gp, q=q):
            if q == 0:
                gp.wait_ge(s_ve.h, evF[('hgn', NT - 1)])
                gp.wait_ge(s_pe.h, evF[('rb', NT - 1)])
                gp.wait_ge(s_sc.h, evF[('sqv', NT - 1)])
                gp.wait_ge(s_hst.h, evF[('hgnst', NT - 1)])
            if q >= 1:
                gp.wait_ge(s_pe.h, evG[('mmlast', q - 1)])
            evG[('w', q)] = s_wo.inc(
                gp.dma_start(out=Wo_bf[:].rearrange('p (a b) -> p a b', a=DC),
                             in_=w_oT[q].transpose([1, 0, 2])), 16)
        GP(g_wload)

        for z in range(NT):
            def g_loads(e, q=q, z=z):
                kz = q * NT + z
                if kz >= 2:
                    e.wait_ge(s_ve.h, evG[('fin', (kz - 2) // NT, (kz - 2) % NT)])
                e.wait_ge(s_hst.h, evF[('hgnst', z)])
                if kz == 0:
                    e.wait_ge(s_dst.h, s_dst.v)
                    e.wait_ge(s_ve.h, evF[('hgn', NT - 1)])
                    e.wait_ge(s_pe.h, evF[('rb', NT - 1)])
                    e.wait_ge(s_sc.h, evF[('sqv', NT - 1)])
                sg2 = s_gl[kz % 2]
                sg2.inc(
                    e.dma_start(out=hgnT[kz % 2][:],
                                in_=hgn_stash.ap()[z]), 16)
                sg2.inc(
                    e.dma_start(out=xq_t[kz % 2][:],
                                in_=x_main[z * 128:(z + 1) * 128,
                                           q * 512:(q + 1) * 512]), 16)
                all3 = sg2.inc(
                    e.dma_start(out=fq_t[kz % 2][:],
                                in_=ffn_stash.ap()[z][:, q * 512:(q + 1) * 512]), 16)
                evG[('h', q, z)] = (sg2.h, all3)
                evG[('x', q, z)] = (sg2.h, all3)
                evG[('f', q, z)] = (sg2.h, all3)
            SY(g_loads)

            def g_mm(te, q=q, z=z):
                kz = q * NT + z
                te.wait_ge(*evG[('h', q, z)])
                if z == 0:
                    te.wait_ge(s_wo.h, evG[('w', q)])
                if kz == 0:
                    te.wait_ge(s_ve.h, evF[('yfin', NT - 1, 7)])
                if kz >= 2:
                    te.wait_ge(s_ve.h, evG[('ev', (kz - 2) // NT, (kz - 2) % NT)])
                dst = T1[:, (kz % 2) * 512:(kz % 2) * 512 + 512]
                ins = None
                for dc in range(DC):
                    ins = nc.tensor.matmul(
                        dst, hgnT[kz % 2][:, dc * 128:(dc + 1) * 128],
                        Wo_bf[:, dc * 512:(dc + 1) * 512],
                        start=(dc == 0), stop=(dc == DC - 1))
                evG[('mm', q, z)] = s_pe.inc(ins)
                if z == NT - 1:
                    evG[('mmlast', q)] = s_pe.v
            TE(g_mm)

            def g_fin(v, q=q, z=z):
                kz = q * NT + z
                v.wait_ge(s_pe.h, evG[('mm', q, z)])
                v.wait_ge(*evG[('x', q, z)])
                if kz >= 1:
                    pq, pz = (kz - 1) // NT, (kz - 1) % NT
                    v.wait_ge(s_out.h, evG[('st', pq, pz)])
                dst = T1[:, (kz % 2) * 512:(kz % 2) * 512 + 512]
                v.tensor_tensor(out=o_t[:], in0=dst, in1=xq_t[kz % 2][:],
                                op=ALU.add)
                evG[('ev', q, z)] = s_ve.inc(v.drain())
                ins = v.tensor_tensor(out=o_t[:], in0=o_t[:], in1=fq_t[kz % 2][:],
                                      op=ALU.add)
                evG[('fin', q, z)] = s_ve.inc(ins)
            VE(g_fin)

            def g_store(e, q=q, z=z):
                e.wait_ge(s_ve.h, evG[('fin', q, z)])
                evG[('st', q, z)] = s_out.inc(
                    e.dma_start(out=out_hbm[z * 128:(z + 1) * 128,
                                            q * 512:(q + 1) * 512],
                                in_=o_t[:]), 16)
            SY(g_store)

    def final_wait(e):
        e.wait_ge(s_out.h, s_out.v)
    SY(final_wait)

    return nc


def _prep_shared(inputs):
    f32 = np.float32
    W = np.ascontiguousarray(inputs['L0_attn_in_proj_w'], dtype=f32)
    Wpad = np.zeros((NG * 128, DM), f32)
    Wpad[:8512] = W
    w_inT = np.ascontiguousarray(
        Wpad.reshape(NG, 128, CC, 128).transpose(0, 3, 2, 1))
    gw = np.ascontiguousarray(inputs['L0_ffn_gate_w'], f32)
    uw = np.ascontiguousarray(inputs['L0_ffn_up_w'], f32)
    w_g = np.ascontiguousarray(gw.reshape(CC, 128, IC, 128).transpose(2, 1, 0, 3))
    w_u = np.ascontiguousarray(uw.reshape(CC, 128, IC, 128).transpose(2, 1, 0, 3))
    w_d = np.ascontiguousarray(inputs['L0_ffn_down_w'], f32).reshape(IC, 128, DM)
    Wo = np.ascontiguousarray(inputs['L0_attn_out_proj_w'], f32)
    w_oT = np.ascontiguousarray(Wo.reshape(4, 512, DC, 128).transpose(0, 2, 3, 1))
    conv_w = np.ascontiguousarray(inputs['L0_attn_conv_w'], f32)
    conv_w_t = np.ascontiguousarray(
        conv_w.reshape(34, 128, 4).transpose(1, 0, 2).reshape(128, 136))
    conv_b_t = np.ascontiguousarray(
        inputs['L0_attn_conv_b'].astype(f32).reshape(34, 128).T)
    dt_bias = inputs['L0_attn_dt_bias'].astype(f32).reshape(64, 1)
    a_log = inputs['L0_attn_A_log'].astype(f32).reshape(64, 1)
    d_rep = np.ascontiguousarray(
        np.broadcast_to(inputs['L0_attn_D'].astype(f32)[None, :], (128, 64)))
    norm_w_t = np.ascontiguousarray(
        inputs['L0_attn_norm_w'].astype(f32).reshape(32, 128).T)
    gs_t = np.ascontiguousarray(
        inputs['L0_ffn_gate_scale'].astype(f32).reshape(IC, 128).T)
    us_t = np.ascontiguousarray(
        inputs['L0_ffn_up_scale'].astype(f32).reshape(IC, 128).T)
    ds_b = np.ascontiguousarray(
        np.broadcast_to(inputs['L0_ffn_down_scale'].astype(f32)[None, :],
                        (128, DM)))
    ident_bf = np.eye(128, dtype=ml_dtypes.bfloat16)
    ident_f = np.eye(128, dtype=f32)
    s_idx = np.arange(128)[:, None]
    t_idx = np.arange(128)[None, :]
    mask_ge = (t_idx >= s_idx).astype(ml_dtypes.bfloat16)
    ones_bfv = np.ones((128, 1), ml_dtypes.bfloat16)
    ones_fv = np.ones((1, 128), f32)
    return dict(w_inT=w_inT, w_g=w_g, w_u=w_u, w_d=w_d, w_oT=w_oT,
                conv_w_t=conv_w_t, conv_b_t=conv_b_t, dt_bias=dt_bias,
                a_log=a_log, d_rep=d_rep, norm_w_t=norm_w_t, gs_t=gs_t,
                us_t=us_t, ds_b=ds_b, ident_bf=ident_bf, ident_f=ident_f,
                mask_ge=mask_ge, ones_bf=ones_bfv, ones_f=ones_fv)


def make_in_maps(inputs):
    shared = _prep_shared(inputs)
    x = np.ascontiguousarray(inputs['L0_attn_hidden_states'],
                             np.float32).reshape(L, DM)
    in_maps = []
    for k in range(NCORES):
        m = dict(shared)
        m['x_main'] = np.ascontiguousarray(x[k * T:(k + 1) * T])
        if k == 0:
            m['x_halo'] = np.zeros((128, DM), np.float32)
        else:
            m['x_halo'] = np.ascontiguousarray(x[k * T - 128:k * T])
        mp = np.zeros((128, 8), np.float32)
        mp[:, :k] = 1.0
        m['mask_prev'] = mp
        in_maps.append(m)
    return in_maps


def kernel(**inputs):
    nc = build_nc()
    in_maps = make_in_maps(inputs)
    res = run_bass_kernel_spmd(nc, in_maps, core_ids=list(range(NCORES)))
    out = np.concatenate([np.asarray(r['out']) for r in res.results], axis=0)
    return out.reshape(1, L, DM).astype(np.float32)


# version 09 (final)
# speedup vs baseline: 1.0639x; 1.0091x over previous
import numpy as np
import ml_dtypes
import concourse.bass as bass
import concourse.mybir as mybir
from concourse.bass_utils import run_bass_kernel_spmd
from contextlib import ExitStack

F32 = mybir.dt.float32
BF16 = mybir.dt.bfloat16
FP8 = mybir.dt.float8e4
AF = mybir.ActivationFunctionType
ALU = mybir.AluOpType
AX = mybir.AxisListType

NCORES = 8
L = 8192
DM = 2048
T = 1024
NT = 8
D_IN = 4096
HEADS = 64
NG = 67
CC = 16
IC = 48
DC = 32
EPS = 1e-5
TW = 1027


class Sem:
    __slots__ = ('h', 'v')

    def __init__(self, h):
        self.h = h
        self.v = 0

    def inc(self, instr, n=1):
        instr.then_inc(self.h, n)
        self.v += n
        return self.v


def build_nc(debug_taps=(), max_phase='G'):
    nc = bass.Bass('TRN2', target_bir_lowering=False)

    x_main = nc.dram_tensor('x_main', [T, DM], F32, kind='ExternalInput').ap()
    x_halo = nc.dram_tensor('x_halo', [128, DM], F32, kind='ExternalInput').ap()
    w_inT = nc.dram_tensor('w_inT', [NG, 128, CC, 128], F32, kind='ExternalInput').ap()
    w_g = nc.dram_tensor('w_g', [IC, 128, CC, 128], F32, kind='ExternalInput').ap()
    w_u = nc.dram_tensor('w_u', [IC, 128, CC, 128], F32, kind='ExternalInput').ap()
    w_d = nc.dram_tensor('w_d', [IC, 128, DM], F32, kind='ExternalInput').ap()
    w_oT = nc.dram_tensor('w_oT', [4, DC, 128, 512], F32, kind='ExternalInput').ap()
    conv_w_t = nc.dram_tensor('conv_w_t', [128, 136], F32, kind='ExternalInput').ap()
    conv_b_t = nc.dram_tensor('conv_b_t', [128, 34], F32, kind='ExternalInput').ap()
    dt_bias = nc.dram_tensor('dt_bias', [64, 1], F32, kind='ExternalInput').ap()
    a_log = nc.dram_tensor('a_log', [64, 1], F32, kind='ExternalInput').ap()
    d_rep = nc.dram_tensor('d_rep', [128, 64], F32, kind='ExternalInput').ap()
    norm_w_t = nc.dram_tensor('norm_w_t', [128, 32], F32, kind='ExternalInput').ap()
    gs_t = nc.dram_tensor('gs_t', [128, IC], F32, kind='ExternalInput').ap()
    us_t = nc.dram_tensor('us_t', [128, IC], F32, kind='ExternalInput').ap()
    ds_b = nc.dram_tensor('ds_b', [128, DM], F32, kind='ExternalInput').ap()
    ident_bf = nc.dram_tensor('ident_bf', [128, 128], BF16, kind='ExternalInput').ap()
    ident_f = nc.dram_tensor('ident_f', [128, 128], F32, kind='ExternalInput').ap()
    mask_ge = nc.dram_tensor('mask_ge', [128, 128], BF16, kind='ExternalInput').ap()
    ones_bf = nc.dram_tensor('ones_bf', [128, 1], BF16, kind='ExternalInput').ap()
    ones_f = nc.dram_tensor('ones_f', [1, 128], F32, kind='ExternalInput').ap()
    mask_prev = nc.dram_tensor('mask_prev', [128, 8], F32, kind='ExternalInput').ap()
    out_hbm = nc.dram_tensor('out', [T, DM], F32, kind='ExternalOutput').ap()

    g_stash = nc.dram_tensor('g_stash', [32, 128, T], BF16)
    xT_stash = nc.dram_tensor('xT_stash', [NT, 128, D_IN], BF16)
    hgn_stash = nc.dram_tensor('hgn_stash', [NT, 128, DC * 128], BF16)
    fuse_stash = nc.dram_tensor('fuse_stash', [NT, 128, IC * 128], BF16)
    ffn_stash = nc.dram_tensor('ffn_stash', [NT, 128, DM], BF16)
    cc_in = nc.dram_tensor('cc_in', [128, 4097], F32)
    cc_out = nc.dram_tensor('cc_out', [NCORES * 128, 4097], F32)

    dbg_shapes = {
        'xn_fm': [128, CC * TW], 'xbcpre': [128, TW],
        'bfm': [128, T], 'cfm': [128, T], 'dtf': [64, T],
        'grel': [64, T], 'xT': [NT * 128, D_IN],
        'slocal': [128, D_IN], 'sin': [128, D_IN],
        'ytok': [NT * 128, D_IN], 'hgnz': [NT * 128, DC * 128],
        'gsilz': [NT * 128, DC * 128],
    }
    dbg = {}
    for name in debug_taps:
        dbg[name] = nc.dram_tensor('dbg_' + name, dbg_shapes[name], F32,
                                   kind='ExternalOutput').ap()

    es = ExitStack()

    def sb(name, shape, dtype):
        return es.enter_context(nc.sbuf_tensor(name, shape, dtype))

    def sem(name):
        return Sem(es.enter_context(nc.semaphore(name)))

    xn_fm = sb('xn_fm', [128, CC * TW], BF16)
    B_fm = sb('B_fm', [128, T], BF16)
    C_fm = sb('C_fm', [128, T], BF16)
    B_T = sb('B_T', [128, NT * 128], BF16)
    dt_fm = sb('dt_fm', [64, T], F32)
    ln_or_erel = sb('ln_or_erel', [64, T], F32)
    grel = sb('grel', [64, T], F32)
    sgrel = sb('sgrel', [64, T], F32)
    est_or_dA = sb('est_or_dA', [64, T], F32)
    grel_T = sb('grel_T', [128, NT * 64], F32)
    sgrel_T = sb('sgrel_T', [128, NT * 64], F32)
    dtE_T = sb('dtE_T', [128, NT * 64], F32)
    erel_T = sb('erel_T', [128, NT * 64], F32)
    exp_ctot = sb('exp_ctot', [64, 8], F32)
    ctot_sum = sb('ctot_sum', [64, 1], F32)
    dec_row = sb('dec_row', [128, 64], F32)
    S_run = sb('S_run', [128, D_IN], F32)

    idbf_sb = sb('idbf_sb', [128, 128], BF16)
    idf_sb = sb('idf_sb', [128, 128], F32)
    cvals = sb('cvals', [128, 4], F32)

    T1 = es.enter_context(nc.psum_tensor('T1', [128, 1536], F32))
    T2 = es.enter_context(nc.psum_tensor('T2', [128, 1024], F32))
    T3 = es.enter_context(nc.psum_tensor('T3', [128, 1024], BF16))
    T4 = es.enter_context(nc.psum_tensor('T4', [128, 1024], BF16))
    T5 = es.enter_context(nc.psum_tensor('T5', [128, 512], F32))

    def bfbank(k):
        return T3 if k % 2 == 0 else T4

    def fpbank(k):
        return T2[:, 512:1024] if k % 2 == 0 else T5[:, 0:512]

    def ybanks(cg):
        if cg % 2 == 0:
            return T1[:, 0:512], T1[:, 512:1024]
        return T1[:, 1024:1536], T2[:, 0:512]

    gB = [0]
    gF = [0]
    evGB = {}
    evGBe = {}
    evGF = {}
    evGFe = {}


    s_const = sem('s_const')
    s_pe = sem('s_pe')
    s_ve = sem('s_ve')
    s_sc = sem('s_sc')
    s_gp = sem('s_gp')
    s_cc = sem('s_cc')
    s_out = sem('s_out')
    s_dbg = sem('s_dbg')
    s_x = [sem('s_x0'), sem('s_x1')]
    s_bc = sem('s_bc')
    s_wb = [sem('s_wb0'), sem('s_wb1')]
    s_gst = [sem('s_gst0'), sem('s_gst1')]
    s_xtst = sem('s_xtst')
    s_exp = sem('s_exp')
    s_dc = sem('s_dc')
    s_wgu = [sem('s_wgu0'), sem('s_wgu1')]
    s_fst = [sem('s_fst0'), sem('s_fst1')]
    s_wd = [sem('s_wd0'), sem('s_wd1')]
    s_fu = [sem('s_fu0'), sem('s_fu1')]
    s_dst = sem('s_dst')
    s_hst = sem('s_hst')
    s_ej = sem('s_ej')
    s_dr = sem('s_dr')
    s_ec = sem('s_ec')
    s_ez = [sem('s_ez0'), sem('s_ez1')]
    s_wo = sem('s_wo')
    s_gl = [sem('s_gl0'), sem('s_gl1')]

    block = es.enter_context(nc.Block())

    def SY(f):
        @block.sync
        def _(e):
            f(e)

    def VE(f):
        @block.vector
        def _(e):
            f(e)

    def SC(f):
        @block.scalar
        def _(e):
            f(e)

    def GP(f):
        @block.gpsimd
        def _(e):
            f(e)

    def TE(f):
        @block.tensor
        def _(e):
            f(e)


    def _early_exit(wait_list, stacks=()):
        def fin(e):
            for sh, vv in wait_list:
                e.wait_ge(sh, vv)
            if s_dbg.v:
                e.wait_ge(s_dbg.h, s_dbg.v)
        SY(fin)
        for st_ in reversed(stacks):
            st_.close()
        es.close()
        return nc

    def const_loads(e):
        for dst, src in [(idbf_sb, ident_bf), (idf_sb, ident_f)]:
            s_const.inc(e.dma_start(out=dst[:], in_=src), 16)
    SY(const_loads)
    LD_CONST = s_const.v

    VE_SETUP = [0]

    def ve_setup(v):
        v.memset(cvals[:, 0:1], 1e-38)
        v.memset(cvals[:, 1:2], EPS)
        v.memset(cvals[:, 2:3], 1.0 / D_IN)
        VE_SETUP[0] = s_ve.inc(v.drain())
    VE(ve_setup)

    SC_SETUP = [0]

    PHA = [None]
    esWb = ExitStack()
    cw_sb = esWb.enter_context(nc.sbuf_tensor('cw_sb', [128, 136], F32))
    cb_sb = esWb.enter_context(nc.sbuf_tensor('cb_sb', [128, 34], F32))
    dtb_sb = esWb.enter_context(nc.sbuf_tensor('dtb_sb', [64, 1], F32))
    alog_sb = esWb.enter_context(nc.sbuf_tensor('alog_sb', [64, 1], F32))
    nA_sb = esWb.enter_context(nc.sbuf_tensor('nA_sb', [64, 1], F32))
    zeros64 = esWb.enter_context(nc.sbuf_tensor('zeros64', [64, 128], F32))
    Wb = [esWb.enter_context(nc.sbuf_tensor(f'Wb{i}', [128, CC * 128], BF16))
          for i in range(2)]

    B_CONST = [0]

    def b_consts(e):
        for dst, srcap in [(cw_sb, conv_w_t), (cb_sb, conv_b_t),
                           (dtb_sb, dt_bias), (alog_sb, a_log)]:
            s_bc.inc(e.dma_start(out=dst[:], in_=srcap), 16)
        B_CONST[0] = s_bc.v
    SY(b_consts)

    def b_setup_ve(v):
        v.wait_ge(s_bc.h, B_CONST[0])
        v.memset(zeros64[:], 0.0)
        BZERO[0] = s_ve.inc(v.drain())
    BZERO = [0]
    VE(b_setup_ve)

    def b_setup_sc(sc):
        sc.wait_ge(s_bc.h, B_CONST[0])
        sc.activation(out=nA_sb[:], in_=alog_sb[:], func=AF.Exp)
        sc.drain()
        sc.activation(out=nA_sb[:], in_=nA_sb[:], func=AF.Copy, scale=-1.0)
        SC_SETUP[0] = s_sc.inc(sc.drain())
    SC(b_setup_sc)

    evB_w = {}

    def b_w01(gp):
        for g in range(2):
            evB_w[g] = (s_wb[g % 2].h, s_wb[g % 2].inc(
                gp.dma_start(out=Wb[g % 2][:],
                             in_=w_inT[g].rearrange('p c j -> p (c j)')), 16))
    GP(b_w01)

    esA = ExitStack()

    def sbA(name, shape, dtype):
        return esA.enter_context(nc.sbuf_tensor(name, shape, dtype))

    x_t = [sbA(f'x_t{i}', [128, DM], F32) for i in range(2)]
    ln_sq = sbA('ln_sq', [128, DM], F32)
    xn_t = [sbA(f'xn_t{i}', [128, DM], BF16) for i in range(2)]
    ln_m = [sbA(f'ln_m{i}', [128, 1], F32) for i in range(2)]
    ln_v = [sbA(f'ln_v{i}', [128, 1], F32) for i in range(2)]

    evA_load = {}
    evA_stats = {}
    evA_apply = {}
    evA_tq = {}
    evA_eq = {}

    for i in range(9):
        def a_load(e, i=i):
            if i >= 2:
                e.wait_ge(s_sc.h, evA_apply[i - 2])
            src = x_halo if i == 0 else x_main[(i - 1) * 128:i * 128, :]
            sx = s_x[i % 2]
            evA_load[i] = (sx.h, sx.inc(e.dma_start(out=x_t[i % 2][:], in_=src), 16))
        SY(a_load)

        def a_stats(v, i=i):
            b = i % 2
            v.wait_ge(*evA_load[i])
            v.reduce_sum(out=ln_m[b][:], in_=x_t[b][:], axis=AX.X)
            v.drain()
            v.tensor_scalar_mul(out=ln_m[b][:], in0=ln_m[b][:], scalar1=1.0 / DM)
            v.drain()
            v.tensor_scalar_sub(out=x_t[b][:], in0=x_t[b][:], scalar1=ln_m[b][:])
            v.drain()
            v.tensor_tensor(out=ln_sq[:], in0=x_t[b][:], in1=x_t[b][:], op=ALU.mult)
            v.drain()
            v.reduce_sum(out=ln_v[b][:], in_=ln_sq[:], axis=AX.X)
            v.drain()
            v.tensor_scalar(out=ln_v[b][:], in0=ln_v[b][:], scalar1=1.0 / DM,
                            scalar2=EPS, op0=ALU.mult, op1=ALU.add)
            evA_stats[i] = s_ve.inc(v.drain())
        VE(a_stats)

        def a_apply(sc, i=i):
            b = i % 2
            sc.wait_ge(s_ve.h, evA_stats[i])
            if i >= 2:
                sc.wait_ge(s_pe.h, evGB[(i - 2) * 4 + 3])
            sc.activation(out=ln_v[b][:], in_=ln_v[b][:], func=AF.Ln)
            sc.drain()
            sc.activation(out=ln_v[b][:], in_=ln_v[b][:], func=AF.Exp,
                          scale=-0.5)
            sc.drain()
            sc.activation(out=xn_t[b][:], in_=x_t[b][:], func=AF.Copy,
                          scale=ln_v[b][:])
            evA_apply[i] = s_sc.inc(sc.drain())
        SC(a_apply)

        for quad in range(4):
            def a_tr(te, i=i, quad=quad):
                k = gB[0]
                if i == 0 and quad == 0:
                    te.wait_ge(s_const.h, LD_CONST)
                if quad == 0:
                    te.wait_ge(s_sc.h, evA_apply[i])
                if k >= 2:
                    te.wait_ge(s_ve.h, evGBe[k - 2])
                bank = bfbank(k)
                for j in range(4):
                    cc = quad * 4 + j
                    te.transpose(bank[:, j * 128:j * 128 + 128],
                                 xn_t[i % 2][:, cc * 128:(cc + 1) * 128], idbf_sb[:])
                evGB[k] = s_pe.inc(te.drain())
                gB[0] += 1
            TE(a_tr)

            def a_ev(v, i=i, quad=quad):
                k = gB[0] - 1
                v.wait_ge(s_pe.h, evGB[k])
                bank = bfbank(k)
                cc0 = quad * 4
                if i == 0:
                    ins = v.tensor_copy(
                        out=xn_fm[:].rearrange('p (c w) -> p c w', c=CC)[
                            :, cc0:cc0 + 4, 0:3],
                        in_=bank[:, 0:512].rearrange('p (j w) -> p j w', j=4)[
                            :, :, 125:128])
                else:
                    off = 3 + (i - 1) * 128
                    ins = v.tensor_copy(
                        out=xn_fm[:].rearrange('p (c w) -> p c w', c=CC)[
                            :, cc0:cc0 + 4, off:off + 128],
                        in_=bank[:, 0:512].rearrange('p (j w) -> p j w', j=4))
                evGBe[k] = s_ve.inc(ins)
            VE(a_ev)

    A_XN = evGBe[35]
    if max_phase == 'A':
        return _early_exit([(s_ve.h, A_XN)], (esWb, esA))
    esA.close()

    if 'xn_fm' in dbg:
        def d_xn(e):
            e.wait_ge(s_ve.h, A_XN)
            s_dbg.inc(e.dma_start(out=dbg['xn_fm'], in_=xn_fm[:]), 16)
        SY(d_xn)

    esB2 = ExitStack()
    x_T = esB2.enter_context(nc.sbuf_tensor('x_T', [128, NT * D_IN], BF16))
    esB = ExitStack()

    def sbB(name, shape, dtype):
        return esB.enter_context(nc.sbuf_tensor(name, shape, dtype))

    gtile = [sbB(f'gtile{i}', [128, T], BF16) for i in range(2)]
    xbc32 = [sbB(f'xbc32{i}', [128, TW], F32) for i in range(2)]
    cacc = [sbB(f'cacc{i}', [128, T], F32) for i in range(2)]
    sil_t = [sbB(f'sil_t{i}', [128, T], BF16) for i in range(3)]
    dt_raw = sbB('dt_raw', [64, T], F32)
    sp_t1 = sbB('sp_t1', [64, T], F32)

    evB_gdone = {}
    evB_mm = {}
    evB_gsil = {}
    evB_gst = {}
    evB_cevac = {}
    evB_fma = {}
    evB_sil = {}
    evB_tq = {}
    evB_teq = {}
    evB_xtev = {}
    gate_cons = [None, None]

    def lhsT(g, cc, rows=128):
        return Wb[g % 2][:, cc * 128:cc * 128 + rows]

    def emit_wload(g):
        def gp_f(gp):
            gp.wait_ge(s_pe.h, evB_gdone[g - 2])
            evB_w[g] = (s_wb[g % 2].h, s_wb[g % 2].inc(
                gp.dma_start(out=Wb[g % 2][:],
                             in_=w_inT[g].rearrange('p c j -> p (c j)')), 16))
        GP(gp_f)

    def emit_xt_transposes(src_buf, cidx):
        for half in range(2):
            def te_f(te, half=half):
                if half == 0:
                    te.wait_ge(s_sc.h, evB_sil[cidx])
                k = gB[0]
                if k >= 2:
                    te.wait_ge(s_ve.h, evGBe[k - 2])
                bank = bfbank(k)
                for j in range(4):
                    z = half * 4 + j
                    te.transpose(bank[:, j * 128:j * 128 + 128],
                                 src_buf[:, z * 128:(z + 1) * 128], idbf_sb[:])
                evGB[k] = s_pe.inc(te.drain())
                gB[0] += 1
            TE(te_f)

            def ve_f(v, half=half):
                k = gB[0] - 1
                v.wait_ge(s_pe.h, evGB[k])
                bank = bfbank(k)
                if cidx < 32:
                    dst = x_T[:].rearrange('p (z w) -> p z w', z=NT)[
                        :, half * 4:half * 4 + 4,
                        cidx * 128:cidx * 128 + 128]
                else:
                    dst = B_T[:].rearrange('p (z w) -> p z w', z=NT)[
                        :, half * 4:half * 4 + 4, :]
                ins = v.tensor_copy(
                    out=dst, in_=bank[:, 0:512].rearrange('p (j w) -> p j w', j=4))
                evGBe[k] = s_ve.inc(ins)
                if half == 1:
                    evB_xtev[cidx] = evGBe[k]
            VE(ve_f)

    for g in range(NG):
        rc = g
        if g < 32:
            def te_f(te, g=g, rc=rc):
                te.wait_ge(*evB_w[g])
                if rc == 0:
                    te.wait_ge(s_ve.h, A_XN)
                for si, (t0, t1) in enumerate([(3, 515), (515, 1027)]):
                    if gate_cons[si] is not None:
                        te.wait_ge(s_sc.h, gate_cons[si])
                    dst = T1[:, si * 512:si * 512 + 512]
                    ins = None
                    for cc in range(CC):
                        ins = nc.tensor.matmul(
                            dst, lhsT(g, cc),
                            xn_fm[:, cc * TW + t0:cc * TW + t1],
                            start=(cc == 0), stop=(cc == CC - 1))
                    evB_mm[(rc, si)] = s_pe.inc(ins)
                evB_gdone[g] = s_pe.v
            TE(te_f)

            def sc_f(sc, rc=rc):
                for si in range(2):
                    sc.wait_ge(s_pe.h, evB_mm[(rc, si)])
                    if rc >= 2 and si == 0:
                        sc.wait_ge(*evB_gst[rc - 2])
                    src_ = T1[:, si * 512:si * 512 + 512]
                    ins = sc.activation(
                        out=gtile[rc % 2][:, si * 512:(si + 1) * 512],
                        in_=src_, func=AF.Silu)
                    gate_cons[si] = s_sc.inc(ins)
                evB_gsil[rc] = s_sc.v
            SC(sc_f)

            def sy_f(e, rc=rc):
                e.wait_ge(s_sc.h, evB_gsil[rc])
                sg_ = s_gst[rc % 2]
                evB_gst[rc] = (sg_.h, sg_.inc(
                    e.dma_start(out=g_stash.ap()[rc], in_=gtile[rc % 2][:]), 16))
            SY(sy_f)
        elif g < 66:
            cidx = g - 32
            par = cidx % 2
            spar = cidx % 3

            def te_f(te, g=g, cidx=cidx):
                te.wait_ge(*evB_w[g])
                if cidx == 0:
                    te.wait_ge(s_sc.h, gate_cons[0])
                    te.wait_ge(s_sc.h, gate_cons[1])
                if cidx > 0:
                    vv = evB_cevac[cidx - 1]
                    te.wait_ge(s_ve.h, vv[0])
                    te.wait_ge(s_sc.h, vv[1])
                ins = None
                for (t0, t1) in [(0, 512), (512, 1024), (1024, TW)]:
                    for cc in range(CC):
                        ins = nc.tensor.matmul(
                            T1[:, t0:t1], lhsT(g, cc),
                            xn_fm[:, cc * TW + t0:cc * TW + t1],
                            start=(cc == 0), stop=(cc == CC - 1))
                evB_mm[('c', cidx)] = s_pe.inc(ins)
                evB_gdone[g] = s_pe.v
            TE(te_f)

            def ve_e(v, cidx=cidx, par=par):
                v.wait_ge(s_pe.h, evB_mm[('c', cidx)])
                if cidx >= 2:
                    v.wait_ge(s_ve.h, evB_fma[cidx - 2])
                ins = v.tensor_copy(out=xbc32[par][:, 0:640], in_=T1[:, 0:640])
                evB_cevac.setdefault(cidx, [0, 0])[0] = s_ve.inc(ins)
            VE(ve_e)

            def sc_e(sc, cidx=cidx, par=par):
                sc.wait_ge(s_pe.h, evB_mm[('c', cidx)])
                if cidx >= 2:
                    sc.wait_ge(s_ve.h, evB_fma[cidx - 2])
                ins = sc.activation(out=xbc32[par][:, 640:TW],
                                    in_=T1[:, 640:TW], func=AF.Copy)
                evB_cevac[cidx][1] = s_sc.inc(ins)
            SC(sc_e)

            if cidx == 0 and 'xbcpre' in dbg:
                def d_xbc(e, cidx=cidx, par=par):
                    e.wait_ge(s_ve.h, evB_cevac[cidx][0])
                    e.wait_ge(s_sc.h, evB_cevac[cidx][1])
                    s_dbg.inc(e.dma_start(out=dbg['xbcpre'],
                                          in_=xbc32[par][:]), 16)
                SY(d_xbc)

            def ve_c(v, cidx=cidx, par=par):
                v.wait_ge(s_ve.h, evB_cevac[cidx][0])
                v.wait_ge(s_sc.h, evB_cevac[cidx][1])
                ins = None
                for colh in range(2):
                    c0 = colh * 512
                    for i in range(4):
                        if i == 0:
                            ins = v.tensor_scalar(
                                out=cacc[par][:, c0:c0 + 512],
                                in0=xbc32[par][:, c0:c0 + 512],
                                scalar1=cw_sb[:, cidx * 4:cidx * 4 + 1],
                                scalar2=None, op0=ALU.mult)
                        else:
                            ins = v.scalar_tensor_tensor(
                                out=cacc[par][:, c0:c0 + 512],
                                in0=xbc32[par][:, c0 + i:c0 + 512 + i],
                                scalar=cw_sb[:, cidx * 4 + i:cidx * 4 + i + 1],
                                in1=cacc[par][:, c0:c0 + 512],
                                op0=ALU.mult, op1=ALU.add)
                        v.drain()
                evB_fma[cidx] = s_ve.inc(v.drain())
            VE(ve_c)

            def sc_s(sc, cidx=cidx, par=par, spar=spar):
                sc.wait_ge(s_ve.h, evB_fma[cidx])
                if cidx < 32:
                    if cidx >= 3:
                        sc.wait_ge(s_ve.h, evB_xtev[cidx - 3])
                    dst = sil_t[spar][:]
                elif cidx == 32:
                    dst = B_fm[:]
                else:
                    dst = C_fm[:]
                ins = sc.activation(out=dst, in_=cacc[par][:, 0:T],
                                    func=AF.Silu, bias=cb_sb[:, cidx:cidx + 1])
                evB_sil[cidx] = s_sc.inc(ins)
            SC(sc_s)

            if cidx < 32:
                emit_xt_transposes(sil_t[spar], cidx)
            elif cidx == 32:
                emit_xt_transposes(B_fm, cidx)
        else:
            def te_f(te, g=g):
                te.wait_ge(*evB_w[g])
                vv = evB_cevac[33]
                te.wait_ge(s_ve.h, vv[0])
                te.wait_ge(s_sc.h, vv[1])
                for si, (t0, t1) in enumerate([(3, 515), (515, 1027)]):
                    if gate_cons[si] is not None:
                        te.wait_ge(s_sc.h, gate_cons[si])
                    dst = T1[0:64, si * 512:si * 512 + 512]
                    ins = None
                    for cc in range(CC):
                        ins = nc.tensor.matmul(
                            dst, lhsT(g, cc, rows=64),
                            xn_fm[:, cc * TW + t0:cc * TW + t1],
                            start=(cc == 0), stop=(cc == CC - 1))
                    evB_mm[('dt', si)] = s_pe.inc(ins)
                evB_gdone[g] = s_pe.v
            TE(te_f)

            def sc_f(sc):
                sc.wait_ge(s_pe.h, evB_mm[('dt', 0)])
                sc.wait_ge(s_pe.h, evB_mm[('dt', 1)])
                for si in range(2):
                    src_ = T1[0:64, si * 512:si * 512 + 512]
                    d0 = slice(si * 512, (si + 1) * 512)
                    sc.activation(out=sp_t1[:, d0], in_=src_, func=AF.Abs,
                                  bias=dtb_sb[:])
                    sc.drain()
                    sc.activation(out=sp_t1[:, d0], in_=sp_t1[:, d0],
                                  func=AF.Exp, scale=-1.0)
                    sc.drain()
                    sc.activation(out=sp_t1[:, d0], in_=sp_t1[:, d0],
                                  func=AF.Ln, bias=1.0)
                    sc.drain()
                    sc.activation(out=dt_raw[:, d0], in_=src_, func=AF.Relu,
                                  bias=dtb_sb[:])
                    sc.drain()
                evB_mm['dt_sc'] = s_sc.inc(sc.drain())
            SC(sc_f)

            def ve_f(v):
                v.wait_ge(s_sc.h, evB_mm['dt_sc'])
                v.wait_ge(s_sc.h, SC_SETUP[0])
                v.wait_ge(s_ve.h, BZERO[0])
                v.tensor_tensor(out=dt_fm[:], in0=dt_raw[:], in1=sp_t1[:],
                                op=ALU.add)
                v.drain()
                v.tensor_scalar_mul(out=est_or_dA[:], in0=dt_fm[:],
                                    scalar1=nA_sb[:])
                v.drain()
                for z in range(NT):
                    sl = slice(z * 128, (z + 1) * 128)
                    v.tensor_tensor_scan(out=grel[:, sl], data0=est_or_dA[:, sl],
                                         data1=zeros64[:], initial=0.0,
                                         op0=ALU.add, op1=ALU.add)
                    v.drain()
                evB_mm['dt_ve'] = s_ve.inc(v.drain())
            VE(ve_f)

            def sc_f2(sc):
                sc.wait_ge(s_ve.h, evB_mm['dt_ve'])
                sc.wait_ge(s_ve.h, VE_SETUP[0])
                sc.activation(out=ln_or_erel[:], in_=dt_fm[:], func=AF.Ln,
                              bias=cvals[0:64, 0:1])
                evB_mm['dt_sc2'] = s_sc.inc(sc.drain())
            SC(sc_f2)

            def ve_f2(v):
                v.wait_ge(s_sc.h, evB_mm['dt_sc2'])
                v.tensor_sub(out=sgrel[:], in0=grel[:], in1=ln_or_erel[:])
                v.drain()
                for z in range(NT):
                    sl = slice(z * 128, (z + 1) * 128)
                    v.tensor_scalar_sub(
                        out=est_or_dA[:, sl], in0=grel[:, sl],
                        scalar1=grel[:, z * 128 + 127:z * 128 + 128])
                    v.drain()
                v.reduce_sum(out=ctot_sum[:], in_=grel[:, 127::128], axis=AX.X)
                evB_mm['dt_ve2'] = s_ve.inc(v.drain())
            VE(ve_f2)

            def sc_f3(sc):
                sc.wait_ge(s_ve.h, evB_mm['dt_ve2'])
                sc.activation(out=est_or_dA[:], in_=est_or_dA[:], func=AF.Exp,
                              scale=-1.0)
                sc.drain()
                sc.activation(out=exp_ctot[:], in_=grel[:, 127::128],
                              func=AF.Exp)
                sc.drain()
                sc.activation(out=ln_or_erel[:], in_=grel[:], func=AF.Exp)
                evB_mm['dt_sc3'] = s_sc.inc(sc.drain())
            SC(sc_f3)

            def ve_f3(v):
                v.wait_ge(s_sc.h, evB_mm['dt_sc3'])
                v.tensor_tensor(out=dt_fm[:], in0=dt_fm[:], in1=est_or_dA[:],
                                op=ALU.mult)
                evB_mm['dt_ve3'] = s_ve.inc(v.drain())
            VE(ve_f3)

            for z in range(NT):
                def te_tt(te, z=z):
                    if z == 0:
                        te.wait_ge(s_ve.h, evB_mm['dt_ve3'])
                    k = gF[0]
                    if k >= 2:
                        te.wait_ge(s_ve.h, evGFe[k - 2])
                    bank = fpbank(k)
                    for jj, srcbuf in enumerate((grel, sgrel, dt_fm, ln_or_erel)):
                        te.transpose(
                            bank[:, jj * 128:jj * 128 + 64],
                            srcbuf[0:64, z * 128:(z + 1) * 128],
                            idf_sb[0:64, 0:64])
                    evGF[k] = s_pe.inc(te.drain())
                    gF[0] += 1
                TE(te_tt)

                def ve_tt(v, z=z):
                    k = gF[0] - 1
                    v.wait_ge(s_pe.h, evGF[k])
                    bank = fpbank(k)
                    ins = None
                    for jj, dstbuf in enumerate((grel_T, sgrel_T, dtE_T, erel_T)):
                        ins = v.tensor_copy(
                            out=dstbuf[:, z * 64:(z + 1) * 64],
                            in_=bank[:, jj * 128:jj * 128 + 64])
                        v.drain()
                    evGFe[k] = s_ve.inc(v.drain())
                    evB_mm['dect'] = s_ve.v
                VE(ve_tt)
        if g + 2 < NG:
            emit_wload(g + 2)

    B_XT = evB_xtev[32]
    B_DECT = evB_mm['dect']

    if 'bfm' in dbg:
        def d_bfm(e):
            e.wait_ge(s_sc.h, evB_sil[32])
            s_dbg.inc(e.dma_start(out=dbg['bfm'], in_=B_fm[:]), 16)
            e.wait_ge(s_sc.h, evB_sil[33])
            s_dbg.inc(e.dma_start(out=dbg['cfm'], in_=C_fm[:]), 16)
        SY(d_bfm)
    if 'dtf' in dbg:
        def d_dtf(e):
            e.wait_ge(s_ve.h, evB_mm['dt_ve'])
            s_dbg.inc(e.dma_start(out=dbg['dtf'], in_=dt_raw[:]), 16)
            s_dbg.inc(e.dma_start(out=dbg['grel'], in_=grel[:]), 16)
        SY(d_dtf)
    if 'xT' in dbg:
        def d_xt(e):
            e.wait_ge(s_ve.h, B_XT)
            for z in range(NT):
                s_dbg.inc(e.dma_start(out=dbg['xT'][z * 128:(z + 1) * 128, :],
                                      in_=x_T[:, z * D_IN:(z + 1) * D_IN]), 16)
        SY(d_xt)

    evB_xtst = {}

    def b_xt_stash(e):
        e.wait_ge(s_ve.h, B_XT)
        for z in range(NT):
            evB_xtst[z] = s_xtst.inc(
                e.dma_start(out=xT_stash.ap()[z],
                            in_=x_T[:, z * D_IN:(z + 1) * D_IN]), 16)
    SY(b_xt_stash)

    esB3 = ExitStack()
    xd_buf = esB3.enter_context(nc.sbuf_tensor('xd_buf', [128, D_IN], BF16))
    SW = {}

    def b_walk_init(v):
        v.wait_ge(s_ve.h, B_XT)
        v.wait_ge(s_ve.h, B_DECT)
        v.memset(S_run[:], 0.0)
        SW['init'] = s_ve.inc(v.drain())
    VE(b_walk_init)

    for z in range(NT):
        def ve_xd(v, z=z):
            if z == 0:
                v.wait_ge(s_ve.h, SW['init'])
            else:
                v.wait_ge(s_pe.h, SW[('mm', z - 1, 7)])
            ins = v.tensor_tensor(
                out=xd_buf[:].rearrange('p (h q) -> p h q', h=64),
                in0=x_T[:, z * D_IN:(z + 1) * D_IN].rearrange(
                    'p (h q) -> p h q', h=64),
                in1=dtE_T[:, z * 64:(z + 1) * 64].unsqueeze(2).broadcast_to(
                    [128, 64, 64]),
                op=ALU.mult)
            SW[('xd', z)] = s_ve.inc(ins)
        VE(ve_xd)

        def te_dec(te, z=z):
            te.wait_ge(s_ve.h, SW[('xd', z)])
            k = gF[0]
            if k >= 2:
                te.wait_ge(s_ve.h, evGFe[k - 2])
            te.transpose(fpbank(k)[:, 0:64],
                         exp_ctot[0:64, z:z + 1].broadcast_to([64, 128]),
                         idf_sb[0:64, 0:64])
            evGF[k] = s_pe.inc(te.drain())
            gF[0] += 1
        TE(te_dec)

        def ve_dec(v, z=z):
            k = gF[0] - 1
            v.wait_ge(s_pe.h, evGF[k])
            evGFe[k] = s_ve.inc(v.tensor_copy(
                out=dec_row[:, 0:64], in_=fpbank(k)[:, 0:64]))
            SW[('dec', z)] = evGFe[k]
        VE(ve_dec)

        for cg in range(8):
            def te_st(te, z=z, cg=cg):
                te.wait_ge(s_ve.h, SW[('xd', z)])
                if z == 0 and cg < 2:
                    te.wait_ge(s_sc.h, evB_mm['dt_sc'])
                if cg >= 2:
                    te.wait_ge(s_ve.h, SW[('upd', z, cg - 2)])
                dst = T1[:, (cg % 2) * 512:(cg % 2) * 512 + 512]
                ins = nc.tensor.matmul(
                    dst, B_T[:, z * 128:(z + 1) * 128],
                    xd_buf[:, cg * 512:(cg + 1) * 512], start=True, stop=True)
                SW[('mm', z, cg)] = s_pe.inc(ins)
            TE(te_st)

            def ve_upd(v, z=z, cg=cg):
                v.wait_ge(s_pe.h, SW[('mm', z, cg)])
                v.wait_ge(s_ve.h, SW[('dec', z)])
                sl = slice(cg * 512, (cg + 1) * 512)
                v.tensor_tensor(
                    out=S_run[:, sl].rearrange('p (h q) -> p h q', h=8),
                    in0=S_run[:, sl].rearrange('p (h q) -> p h q', h=8),
                    in1=dec_row[:, cg * 8:(cg + 1) * 8].unsqueeze(2).broadcast_to(
                        [128, 8, 64]),
                    op=ALU.mult)
                v.drain()
                ins = v.tensor_tensor(
                    out=S_run[:, sl], in0=S_run[:, sl],
                    in1=T1[:, (cg % 2) * 512:(cg % 2) * 512 + 512], op=ALU.add)
                SW[('upd', z, cg)] = s_ve.inc(ins)
            VE(ve_upd)

    B_SLOCAL = SW[('upd', NT - 1, 7)]

    if 'slocal' in dbg:
        def d_sl(e):
            e.wait_ge(s_ve.h, B_SLOCAL)
            s_dbg.inc(e.dma_start(out=dbg['slocal'], in_=S_run[:]), 16)
        SY(d_sl)

    evB_exp = {}

    def b_export(e):
        e.wait_ge(s_ve.h, B_SLOCAL)
        e.wait_ge(s_ve.h, BZERO[0])
        s_exp.inc(e.dma_start(out=cc_in.ap()[:, 0:D_IN], in_=S_run[:]), 16)
        s_exp.inc(
            e.dma_start(out=cc_in.ap()[0:64, D_IN:D_IN + 1], in_=ctot_sum[:],
                        allow_slow_non_contiguous=True), 16)
        evB_exp['done'] = s_exp.inc(
            e.dma_start(out=cc_in.ap()[64:128, D_IN:D_IN + 1],
                        in_=zeros64[0:64, 0:1],
                        allow_slow_non_contiguous=True), 16)
    SY(b_export)

    if max_phase == 'B':
        return _early_exit([(s_exp.h, evB_exp['done']), (s_xtst.h, s_xtst.v),
                            (s_gst[0].h, s_gst[0].v), (s_gst[1].h, s_gst[1].v)],
                           (esWb, esB2, esB, esB3))
    esB3.close()
    esB.close()
    esB2.close()
    esWb.close()

    esD = ExitStack()

    def sbD(name, shape, dtype):
        return esD.enter_context(nc.sbuf_tensor(name, shape, dtype))

    gs_sb = sbD('gs_sb', [128, IC], F32)
    us_sb = sbD('us_sb', [128, IC], F32)
    ds_sb = sbD('ds_sb', [128, DM], F32)
    Wg8 = [sbD(f'Wg8{i}', [128, CC * 128], FP8) for i in range(2)]
    Wu8 = [sbD(f'Wu8{i}', [128, CC * 128], FP8) for i in range(2)]
    gsil_t = sbD('gsil_t', [128, 512], F32)
    fuse_t = [sbD(f'fuse_t{i}', [128, T], BF16) for i in range(2)]

    evD_w = {}
    evD_done = {}
    evD_mm = {}
    evD_sil = {}
    evD_fuse = {}
    evD_fst = {}
    last_fuse = [None]

    D_CONST = [0]

    def d_consts(e):
        e.wait_ge(s_xtst.h, s_xtst.v)
        e.wait_ge(s_ve.h, B_SLOCAL)
        e.wait_ge(s_gst[0].h, s_gst[0].v)
        e.wait_ge(s_gst[1].h, s_gst[1].v)
        e.wait_ge(s_sc.h, evB_mm['dt_sc3'])
        e.wait_ge(s_exp.h, evB_exp['done'])
        for dst, srcap in [(gs_sb, gs_t), (us_sb, us_t), (ds_sb, ds_b)]:
            s_dc.inc(e.dma_start(out=dst[:], in_=srcap), 16)
        D_CONST[0] = s_dc.v
    SY(d_consts)

    def d_w01(gp):
        gp.wait_ge(s_xtst.h, s_xtst.v)
        gp.wait_ge(s_ve.h, B_SLOCAL)
        gp.wait_ge(s_exp.h, evB_exp['done'])
        for ic in range(2):
            sw = s_wgu[ic % 2]
            sw.inc(gp.dma_start(out=Wg8[ic % 2][:],
                                in_=w_g[ic].rearrange('p c j -> p (c j)')), 16)
            evD_w[ic] = (sw.h, sw.inc(
                gp.dma_start(out=Wu8[ic % 2][:],
                             in_=w_u[ic].rearrange('p c j -> p (c j)')), 16))
    GP(d_w01)

    for ic in range(IC):
        if ic >= 2:
            def gp_w(gp, ic=ic):
                gp.wait_ge(s_pe.h, evD_done[ic - 2])
                sw = s_wgu[ic % 2]
                sw.inc(gp.dma_start(
                    out=Wg8[ic % 2][:],
                    in_=w_g[ic].rearrange('p c j -> p (c j)')), 16)
                evD_w[ic] = (sw.h, sw.inc(gp.dma_start(
                    out=Wu8[ic % 2][:],
                    in_=w_u[ic].rearrange('p c j -> p (c j)')), 16))
            GP(gp_w)

        for th in range(2):
            def te_f(te, ic=ic, th=th):
                if ic == 0 and th == 0:
                    te.wait_ge(s_ve.h, A_XN)
                    te.wait_ge(s_sc.h, evB_mm['dt_sc'])
                    te.wait_ge(s_sc.h, evB_gsil[31])
                    te.wait_ge(s_ve.h, B_SLOCAL)
                if th == 0:
                    te.wait_ge(*evD_w[ic])
                if ic >= 1:
                    te.wait_ge(s_ve.h, evD_fuse[(ic - 1, th)])
                t0 = 3 + th * 512
                regs = ([(T1, 0), (T1, 512)] if th == 0
                        else [(T1, 1024), (T2, 0)])
                ins = None
                for Wt, (reg, off) in [(Wg8, regs[0]), (Wu8, regs[1])]:
                    dst = reg[:, off:off + 512]
                    for cc in range(CC):
                        ins = nc.tensor.matmul(
                            dst, Wt[ic % 2][:, cc * 128:(cc + 1) * 128],
                            xn_fm[:, cc * TW + t0:cc * TW + t0 + 512],
                            start=(cc == 0), stop=(cc == CC - 1))
                evD_mm[(ic, th)] = s_pe.inc(ins)
                if th == 1:
                    evD_done[ic] = s_pe.v
            TE(te_f)

            def sc_f(sc, ic=ic, th=th):
                sc.wait_ge(s_pe.h, evD_mm[(ic, th)])
                if ic == 0 and th == 0:
                    sc.wait_ge(s_dc.h, D_CONST[0])
                if last_fuse[0] is not None:
                    sc.wait_ge(s_ve.h, last_fuse[0])
                greg, goff = (T1, 0) if th == 0 else (T1, 1024)
                ins = sc.activation(out=gsil_t[:], in_=greg[:, goff:goff + 512],
                                    func=AF.Silu, scale=gs_sb[:, ic:ic + 1])
                evD_sil[(ic, th)] = s_sc.inc(ins)
            SC(sc_f)

            def ve_f(v, ic=ic, th=th):
                v.wait_ge(s_sc.h, evD_sil[(ic, th)])
                if ic >= 2:
                    v.wait_ge(*evD_fst[ic - 2])
                ureg, uoff = (T1, 512) if th == 0 else (T2, 0)
                ins = v.scalar_tensor_tensor(
                    out=fuse_t[ic % 2][:, th * 512:(th + 1) * 512],
                    in0=ureg[:, uoff:uoff + 512],
                    scalar=us_sb[:, ic:ic + 1],
                    in1=gsil_t[:], op0=ALU.mult, op1=ALU.mult)
                evD_fuse[(ic, th)] = s_ve.inc(ins)
                last_fuse[0] = evD_fuse[(ic, th)]
            VE(ve_f)

        def sy_f(e, ic=ic):
            e.wait_ge(s_ve.h, evD_fuse[(ic, 1)])
            sf = s_fst[ic % 2]
            evD_fst[ic] = (sf.h, sf.inc(
                e.dma_start(
                    out=fuse_stash.ap()[:, :, ic * 128:(ic + 1) * 128].transpose(
                        [1, 0, 2]),
                    in_=fuse_t[ic % 2][:].rearrange('p (z w) -> p z w', z=NT)), 16))
        SY(sy_f)

    Wd8 = [sbD(f'Wd8{i}', [128, 8 * 512], FP8) for i in range(2)]
    fuT = [sbD(f'fuT{i}', [128, 8 * 128], BF16) for i in range(2)]
    dn_t = sbD('dn_t', [128, 512], BF16)

    evD_dw = {}
    evD_dfu = {}
    evD_dmm = {}
    evD_dev = {}
    evD_dst = {}
    dn_pass = [(tbh, osq) for tbh in range(2) for osq in range(4)]
    fu_k = [0]
    dn_prev = [None]

    def dreg(tb4):
        return (T1[:, tb4 * 512:tb4 * 512 + 512] if tb4 < 3
                else T2[:, 0:512])

    def d_dw01(gp):
        for k12 in range(2):
            pi, sg = divmod(k12, 6)
            tbh, osq = dn_pass[pi]
            sw = s_wd[k12 % 2]
            evD_dw[k12] = (sw.h, sw.inc(gp.dma_start(
                out=Wd8[k12 % 2][:].rearrange('p (a b) -> p a b', a=8),
                in_=w_d[sg * 8:(sg + 1) * 8, :,
                        osq * 512:(osq + 1) * 512].transpose([1, 0, 2])), 16))
    GP(d_dw01)

    for pi, (tbh, osq) in enumerate(dn_pass):
        for sg in range(6):
            k12 = pi * 6 + sg
            if k12 >= 2:
                def gp_w(gp, k12=k12):
                    pi2, sg2 = divmod(k12, 6)
                    tbh2, osq2 = dn_pass[pi2]
                    gp.wait_ge(s_pe.h, evD_dmm[('sgdone', k12 - 2)])
                    sw = s_wd[k12 % 2]
                    evD_dw[k12] = (sw.h, sw.inc(gp.dma_start(
                        out=Wd8[k12 % 2][:].rearrange('p (a b) -> p a b', a=8),
                        in_=w_d[sg2 * 8:(sg2 + 1) * 8, :,
                                osq2 * 512:(osq2 + 1) * 512].transpose(
                            [1, 0, 2])), 16))
                GP(gp_w)

            for tb4 in range(4):
                tb = tbh * 4 + tb4
                k = fu_k[0]

                def sy_fu(e, pi=pi, sg=sg, tb4=tb4, tb=tb, k=k):
                    if k >= 2:
                        e.wait_ge(s_pe.h, evD_dmm[('fu', k - 2)])
                    if k == 0:
                        e.wait_ge(s_fst[0].h, s_fst[0].v)
                        e.wait_ge(s_fst[1].h, s_fst[1].v)
                    sf = s_fu[k % 2]
                    evD_dfu[k] = (sf.h, sf.inc(e.dma_start(
                        out=fuT[k % 2][:],
                        in_=fuse_stash.ap()[tb][:, sg * 1024:(sg + 1) * 1024]), 16))
                SY(sy_fu)

                def te_f(te, pi=pi, sg=sg, tb4=tb4, k=k, k12=k12):
                    te.wait_ge(*evD_dfu[k])
                    if tb4 == 0:
                        te.wait_ge(*evD_dw[k12])
                    if pi == 0 and sg == 0 and tb4 == 0:
                        te.wait_ge(s_ve.h, last_fuse[0])
                        te.wait_ge(s_ve.h, B_SLOCAL)
                    if pi >= 1 and sg == 0:
                        te.wait_ge(s_dst.h, evD_dst[(pi - 1, tb4)])
                    dst = dreg(tb4)
                    ins = None
                    for j in range(8):
                        ins = nc.tensor.matmul(
                            dst, fuT[k % 2][:, j * 128:(j + 1) * 128],
                            Wd8[k12 % 2][:, j * 512:(j + 1) * 512],
                            start=(sg == 0 and j == 0),
                            stop=(sg == 5 and j == 7))
                    evD_dmm[(pi, tb4)] = s_pe.inc(ins)
                    evD_dmm[('fu', k)] = s_pe.v
                    if tb4 == 3:
                        evD_dmm[('sgdone', k12)] = s_pe.v
                TE(te_f)
                fu_k[0] += 1

        for tb4 in range(4):
            tb = tbh * 4 + tb4

            def ve_f(v, pi=pi, osq=osq, tb4=tb4, tb=tb):
                v.wait_ge(s_pe.h, evD_dmm[(pi, tb4)])
                v.wait_ge(s_dc.h, D_CONST[0])
                if dn_prev[0] is not None:
                    v.wait_ge(s_dst.h, evD_dst[dn_prev[0]])
                o0 = osq * 512
                ins = v.tensor_tensor(out=dn_t[:], in0=dreg(tb4),
                                      in1=ds_sb[:, o0:o0 + 512], op=ALU.mult)
                evD_dev[(pi, tb4)] = s_ve.inc(ins)
            VE(ve_f)

            def sy_s(e, pi=pi, osq=osq, tb4=tb4, tb=tb):
                e.wait_ge(s_ve.h, evD_dev[(pi, tb4)])
                o0 = osq * 512
                evD_dst[(pi, tb4)] = s_dst.inc(e.dma_start(
                    out=ffn_stash.ap()[tb][:, o0:o0 + 512], in_=dn_t[:]), 16)
            SY(sy_s)
            dn_prev[0] = (pi, tb4)

    D_EV_ALL = [None]

    def d_fin(v):
        v.wait_ge(s_dst.h, evD_dst[(7, 3)])
        D_EV_ALL[0] = s_ve.inc(v.drain())
    VE(d_fin)

    def gp_cc(gp):
        gp.wait_ge(s_exp.h, evB_exp['done'])
        s_cc.inc(gp.collective_compute(
            'AllGather', ALU.bypass, replica_groups=[list(range(NCORES))],
            ins=[cc_in.ap().opt()], outs=[cc_out.ap().opt()]))
    GP(gp_cc)
    CC_DONE = s_cc.v

    if max_phase == 'D':
        return _early_exit([(s_ve.h, D_EV_ALL[0]), (s_cc.h, CC_DONE)], (esD,))
    esD.close()

    esE = ExitStack()

    def sbE(name, shape, dtype):
        return esE.enter_context(nc.sbuf_tensor(name, shape, dtype))

    mask_sb = sbE('mask_sb', [128, 128], BF16)
    drep_bf = sbE('drep_bf', [128, 64], BF16)
    nw_sb = sbE('nw_sb', [128, 32], F32)
    onebf_sb = sbE('onebf_sb', [128, 1], BF16)
    onef_sb = sbE('onef_sb', [1, 128], F32)
    mprev_sb = sbE('mprev_sb', [128, 8], F32)
    Sj = [sbE('Sj0', [128, D_IN], F32)] * 2
    dAj = sbE('dAj', [64, 1], F32)
    dj = sbE('dj', [64, 1], F32)
    S_run_bf = sbE('S_run_bf', [128, D_IN], BF16)

    evE = {}

    E_CONST = [0]

    def e_consts(e):
        e.wait_ge(s_ve.h, D_EV_ALL[0])
        for dst, srcap in [(mask_sb, mask_ge), (nw_sb, norm_w_t),
                           (onebf_sb, ones_bf), (onef_sb, ones_f),
                           (mprev_sb, mask_prev)]:
            s_ec.inc(e.dma_start(out=dst[:], in_=srcap), 16)
        E_CONST[0] = s_ec.v
    SY(e_consts)

    E_DREP = [0]

    def e_drep(gp):
        gp.wait_ge(s_ve.h, D_EV_ALL[0])
        E_DREP[0] = s_dr.inc(gp.dma_start(out=drep_bf[:], in_=d_rep), 16)
    GP(e_drep)

    def e_sin_zero(v):
        v.wait_ge(s_exp.h, evB_exp['done'])
        v.memset(S_run[:], 0.0)
        evE['zero'] = s_ve.inc(v.drain())
    VE(e_sin_zero)

    for j in range(NCORES):
        def e_loads(e, j=j):
            if j == 0:
                e.wait_ge(s_cc.h, CC_DONE)
                e.wait_ge(s_ve.h, D_EV_ALL[0])
            if j >= 1:
                e.wait_ge(s_ve.h, evE[('sacc', j - 1)])
            s_ej.inc(
                e.dma_start(out=Sj[j % 2][:],
                            in_=cc_out.ap()[j * 128:(j + 1) * 128, 0:D_IN]), 16)
            both = s_ej.inc(
                e.dma_start(out=dAj[:],
                            in_=cc_out.ap()[j * 128:j * 128 + 64,
                                            D_IN:D_IN + 1],
                            allow_slow_non_contiguous=True), 16)
            evE[('sj', j)] = (s_ej.h, both)
            evE[('dAj', j)] = (s_ej.h, both)
        SY(e_loads)

        def e_dj(sc, j=j):
            sc.wait_ge(*evE[('dAj', j)])
            if j == 0:
                sc.wait_ge(s_ec.h, E_CONST[0])
            if j >= 1:
                sc.wait_ge(s_ve.h, evE[('sacc', j - 1)])
            ins = sc.activation(out=dj[:], in_=dAj[:], func=AF.Exp,
                                scale=mprev_sb[0:64, j:j + 1])
            evE[('dj', j)] = s_sc.inc(ins)
        SC(e_dj)

        def te_dj(te, j=j):
            te.wait_ge(s_sc.h, evE[('dj', j)])
            if j == 0:
                te.wait_ge(s_ve.h, D_EV_ALL[0])
            k = gF[0]
            if k >= 2:
                te.wait_ge(s_ve.h, evGFe[k - 2])
            te.transpose(fpbank(k)[:, 0:64],
                         dj[0:64, 0:1].broadcast_to([64, 128]),
                         idf_sb[0:64, 0:64])
            evGF[k] = s_pe.inc(te.drain())
            gF[0] += 1
        TE(te_dj)

        def ve_dj(v, j=j):
            k = gF[0] - 1
            if j == 0:
                v.wait_ge(s_ec.h, E_CONST[0])
            v.wait_ge(s_pe.h, evGF[k])
            evGFe[k] = s_ve.inc(v.tensor_copy(
                out=dec_row[:, 0:64], in_=fpbank(k)[:, 0:64]))
            v.drain()
            v.wait_ge(*evE[('sj', j)])
            if j == 0:
                v.wait_ge(s_ve.h, evE['zero'])
            ins = None
            for cg in range(8):
                sl = slice(cg * 512, (cg + 1) * 512)
                v.tensor_tensor(
                    out=S_run[:, sl].rearrange('p (h q) -> p h q', h=8),
                    in0=S_run[:, sl].rearrange('p (h q) -> p h q', h=8),
                    in1=dec_row[:, cg * 8:(cg + 1) * 8].unsqueeze(2).broadcast_to(
                        [128, 8, 64]),
                    op=ALU.mult)
                v.drain()
                ins = v.scalar_tensor_tensor(
                    out=S_run[:, sl], in0=Sj[j % 2][:, sl],
                    scalar=mprev_sb[:, j:j + 1], in1=S_run[:, sl],
                    op0=ALU.mult, op1=ALU.add)
                v.drain()
            evE[('sacc', j)] = s_ve.inc(v.drain())
        VE(ve_dj)

    E_SIN = evE[('sacc', NCORES - 1)]

    if 'sin' in dbg:
        def d_sin(e):
            e.wait_ge(s_ve.h, E_SIN)
            s_dbg.inc(e.dma_start(out=dbg['sin'], in_=S_run[:]), 16)
        SY(d_sin)

    def e_sbf(v):
        v.wait_ge(s_ve.h, E_SIN)
        evE['sbf'] = s_ve.inc(v.tensor_copy(out=S_run_bf[:], in_=S_run[:]))
    VE(e_sbf)

    xTc = [sbE(f'xTc{i}', [128, D_IN], BF16) for i in range(2)]
    gsz = [sbE(f'gsz{i}', [128, DC * 128], BF16) for i in range(2)]
    G_m = sbE('G_m', [128, 128], BF16)
    diffc = sbE('diffc', [128, 8 * 128], F32)
    diffe = sbE('diffe', [128, 8 * 128], BF16)
    M_sb = sbE('M_sb', [128, HEADS * 128], BF16)
    ytmp = sbE('ytmp', [128, 512], F32)
    dterm = sbE('dterm', [128, D_IN], BF16)
    y_tok = sbE('y_tok', [128, D_IN], BF16)
    y_fm = sbE('y_fm', [128, DC * 128], BF16)
    hgn_c = sbE('hgn_c', [128, DC * 128], BF16)
    rstd = sbE('rstd', [1, 128], F32)
    xd_E = sbE('xd_E', [128, D_IN], BF16)

    evF = {}

    def e_pre_loads(e):
        e.wait_ge(s_ve.h, D_EV_ALL[0])
        e.wait_ge(s_xtst.h, s_xtst.v)
        e.wait_ge(s_gst[0].h, s_gst[0].v)
        e.wait_ge(s_gst[1].h, s_gst[1].v)
        s_ez[0].inc(e.dma_start(out=xTc[0][:], in_=xT_stash.ap()[0]), 16)
        both = s_ez[0].inc(
            e.dma_start(out=gsz[0][:].rearrange('p (a b) -> p a b', a=DC),
                        in_=g_stash.ap()[:, :, 0:128].transpose([1, 0, 2])), 16)
        evF[('xt', 0)] = (s_ez[0].h, both)
        evF[('gs', 0)] = (s_ez[0].h, both)
    SY(e_pre_loads)

    for z in range(NT):
        def e_next_loads(e, z=z):
            if z + 1 < NT:
                if z >= 1:
                    e.wait_ge(s_ve.h, evF[('hg', z - 1)])
                    e.wait_ge(s_pe.h, evF[('ydiag', z - 1, 7)])
                sz_ = s_ez[(z + 1) % 2]
                sz_.inc(e.dma_start(out=xTc[(z + 1) % 2][:],
                                    in_=xT_stash.ap()[z + 1]), 16)
                both = sz_.inc(
                    e.dma_start(
                        out=gsz[(z + 1) % 2][:].rearrange('p (a b) -> p a b', a=DC),
                        in_=g_stash.ap()[:, :, (z + 1) * 128:(z + 2) * 128].transpose(
                            [1, 0, 2])), 16)
                evF[('xt', z + 1)] = (sz_.h, both)
                evF[('gs', z + 1)] = (sz_.h, both)
        SY(e_next_loads)

        def te_g(te, z=z):
            if z == 0:
                te.wait_ge(s_ve.h, evE['sbf'])
                te.wait_ge(s_ve.h, D_EV_ALL[0])
            else:
                te.wait_ge(s_ve.h, evF[('gm', z - 1)])
                te.wait_ge(s_ve.h, evF[('yoffs', z - 1, 7)])
            ins = nc.tensor.matmul(
                T1[:, 0:128], B_fm[:, z * 128:(z + 1) * 128],
                C_fm[:, z * 128:(z + 1) * 128], start=True, stop=True)
            evF[('gmm', z)] = s_pe.inc(ins)
        TE(te_g)

        def ve_g(v, z=z):
            v.wait_ge(s_pe.h, evF[('gmm', z)])
            if z == 0:
                v.wait_ge(s_ec.h, E_CONST[0])
            if z >= 1:
                v.wait_ge(s_ve.h, evF[('m', z - 1, 7)])
            ins = v.tensor_tensor(out=G_m[:], in0=T1[:, 0:128],
                                  in1=mask_sb[:], op=ALU.mult)
            evF[('gm', z)] = s_ve.inc(ins)
        VE(ve_g)

        for hb in range(8):
            for sub in range(2):
                def te_d(te, z=z, hb=hb, sub=sub):
                    if z == 0 and hb == 0 and sub == 0:
                        te.wait_ge(s_ve.h, D_EV_ALL[0])
                    k = gF[0]
                    if k >= 2:
                        te.wait_ge(s_ve.h, evGFe[k - 2])
                    bank = fpbank(k)
                    for jj in range(4):
                        j = sub * 4 + jj
                        h = hb * 8 + j
                        te.transpose(
                            bank[:, jj * 128:jj * 128 + 128],
                            grel_T[:, z * 64 + h:z * 64 + h + 1].broadcast_to(
                                [128, 128]),
                            idf_sb[:])
                    evGF[k] = s_pe.inc(te.drain())
                    gF[0] += 1
                TE(te_d)

                def ve_d(v, z=z, hb=hb, sub=sub):
                    k = gF[0] - 1
                    if sub == 0:
                        prev = evF.get(('diffe', z, hb - 1)) if hb > 0 else (
                            evF.get(('diffe', z - 1, 7)) if z > 0 else None)
                        if prev is not None:
                            v.wait_ge(s_sc.h, prev)
                    v.wait_ge(s_pe.h, evGF[k])
                    bank = fpbank(k)
                    ins = None
                    for jj in range(4):
                        j = sub * 4 + jj
                        h = hb * 8 + j
                        ins = v.tensor_scalar(
                            out=diffc[:, j * 128:(j + 1) * 128],
                            in0=bank[:, jj * 128:jj * 128 + 128],
                            scalar1=sgrel_T[:, z * 64 + h:z * 64 + h + 1],
                            scalar2=30.0, op0=ALU.subtract, op1=ALU.min)
                        v.drain()
                    evGFe[k] = s_ve.inc(v.drain())
                    evF[('diffc', z, hb)] = s_ve.v
                VE(ve_d)

            def sc_d(sc, z=z, hb=hb):
                sc.wait_ge(s_ve.h, evF[('diffc', z, hb)])
                prevm = evF.get(('m', z, hb - 1)) if hb > 0 else (
                    evF.get(('m', z - 1, 7)) if z > 0 else None)
                if prevm is not None:
                    sc.wait_ge(s_ve.h, prevm)
                ins = sc.activation(out=diffe[:], in_=diffc[:], func=AF.Exp)
                evF[('diffe', z, hb)] = s_sc.inc(ins)
            SC(sc_d)

            def ve_m(v, z=z, hb=hb):
                v.wait_ge(s_sc.h, evF[('diffe', z, hb)])
                v.wait_ge(s_ve.h, evF[('gm', z)])
                if z >= 1 and hb == 0:
                    v.wait_ge(s_pe.h, evF[('ydiag', z - 1, 7)])
                ins = v.tensor_tensor(
                    out=M_sb[:, hb * 1024:(hb + 1) * 1024].rearrange(
                        'p (a b) -> p a b', a=8),
                    in0=diffe[:].rearrange('p (a b) -> p a b', a=8),
                    in1=G_m[:].unsqueeze(1).broadcast_to([128, 8, 128]),
                    op=ALU.mult)
                evF[('m', z, hb)] = s_ve.inc(ins)
            VE(ve_m)

        for cg in range(8):
            def te_y(te, z=z, cg=cg):
                offb, diagb = ybanks(cg)
                if cg == 0:
                    te.wait_ge(*evF[('xt', z)])
                    te.wait_ge(s_ve.h, evF[('gm', z)])
                    if z >= 1:
                        te.wait_ge(s_ve.h, evF[('supd', z - 1, 7)])
                        te.wait_ge(s_ve.h, evF[('hgn', z - 1)])
                if cg == 1 and z >= 1:
                    te.wait_ge(s_ve.h, evF[('supd', z - 1, 7)])
                te.wait_ge(s_ve.h, evF[('m', z, cg)])
                if cg >= 2:
                    te.wait_ge(s_ve.h, evF[('yoffs', z, cg - 2)])
                    te.wait_ge(s_ve.h, evF[('yfin', z, cg - 2)])
                ins = nc.tensor.matmul(
                    offb, C_fm[:, z * 128:(z + 1) * 128],
                    S_run_bf[:, cg * 512:(cg + 1) * 512], start=True, stop=True)
                evF[('yoff', z, cg)] = s_pe.inc(ins)
                ins = None
                for j in range(8):
                    h = cg * 8 + j
                    ins = nc.tensor.matmul(
                        diagb[:, j * 64:j * 64 + 64],
                        M_sb[:, h * 128:(h + 1) * 128],
                        xTc[z % 2][:, h * 64:(h + 1) * 64],
                        start=True, stop=True)
                    if j == 0 and cg <= 1:
                        pass
                evF[('ydiag', z, cg)] = s_pe.inc(ins)
            TE(te_y)

            def ve_y(v, z=z, cg=cg):
                offb, diagb = ybanks(cg)
                if cg >= 1:
                    v.wait_ge(s_ve.h, evF[('yfin', z, cg - 1)])
                v.wait_ge(s_pe.h, evF[('yoff', z, cg)])
                if z == 0 and cg == 0:
                    v.wait_ge(s_dr.h, E_DREP[0])
                v.tensor_tensor(
                    out=ytmp[:].rearrange('p (a b) -> p a b', a=8),
                    in0=offb.rearrange('p (a b) -> p a b', a=8),
                    in1=erel_T[:, z * 64 + cg * 8:z * 64 + (cg + 1) * 8].unsqueeze(
                        2).broadcast_to([128, 8, 64]),
                    op=ALU.mult)
                evF[('yoffs', z, cg)] = s_ve.inc(v.drain())
                v.tensor_tensor(
                    out=dterm[:, 0:512].rearrange('p (a b) -> p a b', a=8),
                    in0=xTc[z % 2][:, cg * 512:(cg + 1) * 512].rearrange(
                        'p (a b) -> p a b', a=8),
                    in1=drep_bf[:, cg * 8:(cg + 1) * 8].unsqueeze(2).broadcast_to(
                        [128, 8, 64]),
                    op=ALU.mult)
                v.drain()
                v.tensor_tensor(out=ytmp[:], in0=ytmp[:],
                                in1=dterm[:, 0:512], op=ALU.add)
                v.drain()
                v.wait_ge(s_pe.h, evF[('ydiag', z, cg)])
                ins = v.tensor_tensor(out=y_tok[:, cg * 512:(cg + 1) * 512],
                                      in0=diagb, in1=ytmp[:],
                                      op=ALU.add)
                evF[('yfin', z, cg)] = s_ve.inc(ins)
            VE(ve_y)

        def ve_xd2(v, z=z):
            v.wait_ge(s_ve.h, evF[('yfin', z, 7)])
            ins = v.tensor_tensor(
                out=xd_E[:].rearrange('p (h q) -> p h q', h=64),
                in0=xTc[z % 2][:].rearrange('p (h q) -> p h q', h=64),
                in1=dtE_T[:, z * 64:(z + 1) * 64].unsqueeze(2).broadcast_to(
                    [128, 64, 64]),
                op=ALU.mult)
            evF[('xd', z)] = s_ve.inc(ins)
        VE(ve_xd2)

        def te_dec2(te, z=z):
            te.wait_ge(s_ve.h, evF[('xd', z)])
            k = gF[0]
            if k >= 2:
                te.wait_ge(s_ve.h, evGFe[k - 2])
            te.transpose(fpbank(k)[:, 0:64],
                         exp_ctot[0:64, z:z + 1].broadcast_to([64, 128]),
                         idf_sb[0:64, 0:64])
            evGF[k] = s_pe.inc(te.drain())
            gF[0] += 1
        TE(te_dec2)

        def ve_dec2(v, z=z):
            k = gF[0] - 1
            v.wait_ge(s_pe.h, evGF[k])
            evGFe[k] = s_ve.inc(v.tensor_copy(
                out=dec_row[:, 0:64], in_=fpbank(k)[:, 0:64]))
            evF[('dec', z)] = evGFe[k]
        VE(ve_dec2)

        for cg in range(8):
            def te_st2(te, z=z, cg=cg):
                te.wait_ge(s_ve.h, evF[('xd', z)])
                if cg >= 2:
                    te.wait_ge(s_ve.h, evF[('supd', z, cg - 2)])
                dst = (T1[:, 1024:1536] if cg % 2 == 0 else T2[:, 0:512])
                ins = nc.tensor.matmul(
                    dst, B_T[:, z * 128:(z + 1) * 128],
                    xd_E[:, cg * 512:(cg + 1) * 512], start=True, stop=True)
                evF[('smm', z, cg)] = s_pe.inc(ins)
            TE(te_st2)

            def ve_upd2(v, z=z, cg=cg):
                v.wait_ge(s_pe.h, evF[('smm', z, cg)])
                v.wait_ge(s_ve.h, evF[('dec', z)])
                sl = slice(cg * 512, (cg + 1) * 512)
                v.tensor_tensor(
                    out=S_run[:, sl].rearrange('p (h q) -> p h q', h=8),
                    in0=S_run[:, sl].rearrange('p (h q) -> p h q', h=8),
                    in1=dec_row[:, cg * 8:(cg + 1) * 8].unsqueeze(2).broadcast_to(
                        [128, 8, 64]),
                    op=ALU.mult)
                v.drain()
                v.tensor_tensor(
                    out=S_run[:, sl], in0=S_run[:, sl],
                    in1=(T1[:, 1024:1536] if cg % 2 == 0 else T2[:, 0:512]),
                    op=ALU.add)
                v.drain()
                ins = v.tensor_copy(out=S_run_bf[:, sl], in_=S_run[:, sl])
                evF[('supd', z, cg)] = s_ve.inc(ins)
            VE(ve_upd2)

        for qd in range(8):
            def te_yt(te, z=z, qd=qd):
                if qd == 0:
                    te.wait_ge(s_ve.h, evF[('yfin', z, 7)])
                k = gB[0]
                if k >= 2:
                    te.wait_ge(s_ve.h, evGBe[k - 2])
                bank = bfbank(k)
                for j in range(4):
                    dc = qd * 4 + j
                    te.transpose(bank[:, j * 128:j * 128 + 128],
                                 y_tok[:, dc * 128:(dc + 1) * 128], idbf_sb[:])
                evGB[k] = s_pe.inc(te.drain())
                gB[0] += 1
            TE(te_yt)

            def ve_yt(v, z=z, qd=qd):
                k = gB[0] - 1
                v.wait_ge(s_pe.h, evGB[k])
                bank = bfbank(k)
                ins = v.tensor_copy(
                    out=y_fm[:].rearrange('p (d w) -> p d w', d=DC)[
                        :, qd * 4:qd * 4 + 4, :],
                    in_=bank[:, 0:512].rearrange('p (j w) -> p j w', j=4))
                evGBe[k] = s_ve.inc(ins)
                evF[('yt', z, qd)] = s_ve.v
            VE(ve_yt)
        evF[('ydone', z)] = evF[('yt', z, 7)]

        if 'ytok' in dbg:
            def d_yt(e, z=z):
                e.wait_ge(s_ve.h, evF[('ydone', z)])
                s_dbg.inc(e.dma_start(out=dbg['ytok'][z * 128:(z + 1) * 128, :],
                                      in_=y_tok[:]), 16)
            SY(d_yt)

        def ve_hg(v, z=z):
            v.wait_ge(s_ve.h, evF[('ydone', z)])
            v.wait_ge(*evF[('gs', z)])
            if z >= 1:
                v.wait_ge(s_pe.h, evF[('var', z - 1)])
                v.wait_ge(s_hst.h, evF[('hgnst', z - 1)])
            v.tensor_tensor(out=y_fm[:], in0=y_fm[:], in1=gsz[z % 2][:],
                            op=ALU.mult)
            v.drain()
            ins = v.tensor_tensor(out=hgn_c[:], in0=y_fm[:], in1=y_fm[:],
                                  op=ALU.mult)
            evF[('hg', z)] = s_ve.inc(ins)
        VE(ve_hg)

        def te_var(te, z=z):
            te.wait_ge(s_ve.h, evF[('hg', z)])
            if z >= 1:
                te.wait_ge(s_sc.h, evF[('sqv', z - 1)])
            ins = None
            for dc in range(DC):
                ins = nc.tensor.matmul(
                    T1[0:1, 512:640], onebf_sb[:],
                    hgn_c[:, dc * 128:(dc + 1) * 128],
                    start=(dc == 0), stop=(dc == DC - 1))
            evF[('var', z)] = s_pe.inc(ins)
        TE(te_var)

        def sc_rstd(sc, z=z):
            sc.wait_ge(s_pe.h, evF[('var', z)])
            sc.activation(out=rstd[:], in_=T1[0:1, 512:640],
                          func=AF.Ln, scale=cvals[0:1, 2:3],
                          bias=cvals[0:1, 1:2])
            sc.drain()
            ins = sc.activation(out=rstd[:], in_=rstd[:], func=AF.Exp,
                                scale=-0.5)
            evF[('sqv', z)] = s_sc.inc(ins)
        SC(sc_rstd)

        def ve_rstd(v, z=z):
            v.wait_ge(s_sc.h, evF[('sqv', z)])
            ins = v.tensor_copy(out=rstd[:], in_=rstd[:])
            evF[('rstd', z)] = s_ve.inc(ins)
        VE(ve_rstd)

        def te_rb(te, z=z):
            te.wait_ge(s_ve.h, evF[('rstd', z)])
            if z >= 1:
                te.wait_ge(s_ve.h, evF[('hgn', z - 1)])
            ins = nc.tensor.matmul(T1[:, 640:768], onef_sb[:], rstd[:],
                                   start=True, stop=True)
            evF[('rb', z)] = s_pe.inc(ins)
        TE(te_rb)

        def ve_hgn(v, z=z):
            v.wait_ge(s_pe.h, evF[('rb', z)])
            if z >= 1:
                v.wait_ge(s_hst.h, evF[('hgnst', z - 1)])
            ins = None
            for dc in range(DC):
                ins = v.scalar_tensor_tensor(
                    out=hgn_c[:, dc * 128:(dc + 1) * 128],
                    in0=y_fm[:, dc * 128:(dc + 1) * 128],
                    scalar=nw_sb[:, dc:dc + 1], in1=T1[:, 640:768],
                    op0=ALU.mult, op1=ALU.mult)
                v.drain()
            evF[('hgn', z)] = s_ve.inc(ins)
        VE(ve_hgn)

        def sy_hgn(e, z=z):
            e.wait_ge(s_ve.h, evF[('hgn', z)])
            evF[('hgnst', z)] = s_hst.inc(
                e.dma_start(out=hgn_stash.ap()[z], in_=hgn_c[:]), 16)
        SY(sy_hgn)

        if 'hgnz' in dbg:
            def d_hgn(e, z=z):
                e.wait_ge(s_ve.h, evF[('hgn', z)])
                s_dbg.inc(e.dma_start(out=dbg['hgnz'][z * 128:(z + 1) * 128, :],
                                      in_=hgn_c[:]), 16)
            SY(d_hgn)
        if 'gsilz' in dbg:
            def d_gsl(e, z=z):
                e.wait_ge(*evF[('gs', z)])
                s_dbg.inc(e.dma_start(out=dbg['gsilz'][z * 128:(z + 1) * 128, :],
                                      in_=gsz[z % 2][:]), 16)
            SY(d_gsl)

    if max_phase == 'F':
        return _early_exit([(s_hst.h, evF[('hgnst', NT - 1)]),
                            (s_ve.h, evF[('hgn', NT - 1)])], (esE,))
    esE.close()
    esG = ExitStack()

    def sbG(name, shape, dtype):
        return esG.enter_context(nc.sbuf_tensor(name, shape, dtype))

    Wo_bf = sbG('Wo_bf', [128, DC * 512], BF16)
    hgnT = [sbG(f'hgnT{i}', [128, DC * 128], BF16) for i in range(2)]
    xq_t = [sbG(f'xq_t{i}', [128, 512], F32) for i in range(2)]
    fq_t = [sbG(f'fq_t{i}', [128, 512], BF16) for i in range(2)]
    o_t = sbG('o_t', [128, 512], F32)

    evG = {}

    for q in range(4):
        def g_wload(gp, q=q):
            if q == 0:
                gp.wait_ge(s_ve.h, evF[('hgn', NT - 1)])
                gp.wait_ge(s_pe.h, evF[('rb', NT - 1)])
                gp.wait_ge(s_sc.h, evF[('sqv', NT - 1)])
                gp.wait_ge(s_hst.h, evF[('hgnst', NT - 1)])
            if q >= 1:
                gp.wait_ge(s_pe.h, evG[('mmlast', q - 1)])
            evG[('w', q)] = s_wo.inc(
                gp.dma_start(out=Wo_bf[:].rearrange('p (a b) -> p a b', a=DC),
                             in_=w_oT[q].transpose([1, 0, 2])), 16)
        GP(g_wload)

        for z in range(NT):
            def g_loads(e, q=q, z=z):
                kz = q * NT + z
                if kz >= 2:
                    e.wait_ge(s_ve.h, evG[('fin', (kz - 2) // NT, (kz - 2) % NT)])
                e.wait_ge(s_hst.h, evF[('hgnst', z)])
                if kz == 0:
                    e.wait_ge(s_dst.h, s_dst.v)
                    e.wait_ge(s_ve.h, evF[('hgn', NT - 1)])
                    e.wait_ge(s_pe.h, evF[('rb', NT - 1)])
                    e.wait_ge(s_sc.h, evF[('sqv', NT - 1)])
                sg2 = s_gl[kz % 2]
                sg2.inc(
                    e.dma_start(out=hgnT[kz % 2][:],
                                in_=hgn_stash.ap()[z]), 16)
                sg2.inc(
                    e.dma_start(out=xq_t[kz % 2][:],
                                in_=x_main[z * 128:(z + 1) * 128,
                                           q * 512:(q + 1) * 512]), 16)
                all3 = sg2.inc(
                    e.dma_start(out=fq_t[kz % 2][:],
                                in_=ffn_stash.ap()[z][:, q * 512:(q + 1) * 512]), 16)
                evG[('h', q, z)] = (sg2.h, all3)
                evG[('x', q, z)] = (sg2.h, all3)
                evG[('f', q, z)] = (sg2.h, all3)
            SY(g_loads)

            def g_mm(te, q=q, z=z):
                kz = q * NT + z
                te.wait_ge(*evG[('h', q, z)])
                if z == 0:
                    te.wait_ge(s_wo.h, evG[('w', q)])
                if kz == 0:
                    te.wait_ge(s_ve.h, evF[('yfin', NT - 1, 7)])
                if kz >= 2:
                    te.wait_ge(s_ve.h, evG[('ev', (kz - 2) // NT, (kz - 2) % NT)])
                dst = T1[:, (kz % 2) * 512:(kz % 2) * 512 + 512]
                ins = None
                for dc in range(DC):
                    ins = nc.tensor.matmul(
                        dst, hgnT[kz % 2][:, dc * 128:(dc + 1) * 128],
                        Wo_bf[:, dc * 512:(dc + 1) * 512],
                        start=(dc == 0), stop=(dc == DC - 1))
                evG[('mm', q, z)] = s_pe.inc(ins)
                if z == NT - 1:
                    evG[('mmlast', q)] = s_pe.v
            TE(g_mm)

            def g_fin(v, q=q, z=z):
                kz = q * NT + z
                v.wait_ge(s_pe.h, evG[('mm', q, z)])
                v.wait_ge(*evG[('x', q, z)])
                if kz >= 1:
                    pq, pz = (kz - 1) // NT, (kz - 1) % NT
                    v.wait_ge(s_out.h, evG[('st', pq, pz)])
                dst = T1[:, (kz % 2) * 512:(kz % 2) * 512 + 512]
                v.tensor_tensor(out=o_t[:], in0=dst, in1=xq_t[kz % 2][:],
                                op=ALU.add)
                evG[('ev', q, z)] = s_ve.inc(v.drain())
                ins = v.tensor_tensor(out=o_t[:], in0=o_t[:], in1=fq_t[kz % 2][:],
                                      op=ALU.add)
                evG[('fin', q, z)] = s_ve.inc(ins)
            VE(g_fin)

            def g_store(e, q=q, z=z):
                e.wait_ge(s_ve.h, evG[('fin', q, z)])
                evG[('st', q, z)] = s_out.inc(
                    e.dma_start(out=out_hbm[z * 128:(z + 1) * 128,
                                            q * 512:(q + 1) * 512],
                                in_=o_t[:]), 16)
            SY(g_store)

    def final_wait(e):
        e.wait_ge(s_out.h, s_out.v)
    SY(final_wait)

    return nc


def _prep_shared(inputs):
    f32 = np.float32
    W = np.ascontiguousarray(inputs['L0_attn_in_proj_w'], dtype=f32)
    Wpad = np.zeros((NG * 128, DM), f32)
    Wpad[:8512] = W
    w_inT = np.ascontiguousarray(
        Wpad.reshape(NG, 128, CC, 128).transpose(0, 3, 2, 1))
    gw = np.ascontiguousarray(inputs['L0_ffn_gate_w'], f32)
    uw = np.ascontiguousarray(inputs['L0_ffn_up_w'], f32)
    w_g = np.ascontiguousarray(gw.reshape(CC, 128, IC, 128).transpose(2, 1, 0, 3))
    w_u = np.ascontiguousarray(uw.reshape(CC, 128, IC, 128).transpose(2, 1, 0, 3))
    w_d = np.ascontiguousarray(inputs['L0_ffn_down_w'], f32).reshape(IC, 128, DM)
    Wo = np.ascontiguousarray(inputs['L0_attn_out_proj_w'], f32)
    w_oT = np.ascontiguousarray(Wo.reshape(4, 512, DC, 128).transpose(0, 2, 3, 1))
    conv_w = np.ascontiguousarray(inputs['L0_attn_conv_w'], f32)
    conv_w_t = np.ascontiguousarray(
        conv_w.reshape(34, 128, 4).transpose(1, 0, 2).reshape(128, 136))
    conv_b_t = np.ascontiguousarray(
        inputs['L0_attn_conv_b'].astype(f32).reshape(34, 128).T)
    dt_bias = inputs['L0_attn_dt_bias'].astype(f32).reshape(64, 1)
    a_log = inputs['L0_attn_A_log'].astype(f32).reshape(64, 1)
    d_rep = np.ascontiguousarray(
        np.broadcast_to(inputs['L0_attn_D'].astype(f32)[None, :], (128, 64)))
    norm_w_t = np.ascontiguousarray(
        inputs['L0_attn_norm_w'].astype(f32).reshape(32, 128).T)
    gs_t = np.ascontiguousarray(
        inputs['L0_ffn_gate_scale'].astype(f32).reshape(IC, 128).T)
    us_t = np.ascontiguousarray(
        inputs['L0_ffn_up_scale'].astype(f32).reshape(IC, 128).T)
    ds_b = np.ascontiguousarray(
        np.broadcast_to(inputs['L0_ffn_down_scale'].astype(f32)[None, :],
                        (128, DM)))
    ident_bf = np.eye(128, dtype=ml_dtypes.bfloat16)
    ident_f = np.eye(128, dtype=f32)
    s_idx = np.arange(128)[:, None]
    t_idx = np.arange(128)[None, :]
    mask_ge = (t_idx >= s_idx).astype(ml_dtypes.bfloat16)
    ones_bfv = np.ones((128, 1), ml_dtypes.bfloat16)
    ones_fv = np.ones((1, 128), f32)
    return dict(w_inT=w_inT, w_g=w_g, w_u=w_u, w_d=w_d, w_oT=w_oT,
                conv_w_t=conv_w_t, conv_b_t=conv_b_t, dt_bias=dt_bias,
                a_log=a_log, d_rep=d_rep, norm_w_t=norm_w_t, gs_t=gs_t,
                us_t=us_t, ds_b=ds_b, ident_bf=ident_bf, ident_f=ident_f,
                mask_ge=mask_ge, ones_bf=ones_bfv, ones_f=ones_fv)


def make_in_maps(inputs):
    shared = _prep_shared(inputs)
    x = np.ascontiguousarray(inputs['L0_attn_hidden_states'],
                             np.float32).reshape(L, DM)
    in_maps = []
    for k in range(NCORES):
        m = dict(shared)
        m['x_main'] = np.ascontiguousarray(x[k * T:(k + 1) * T])
        if k == 0:
            m['x_halo'] = np.zeros((128, DM), np.float32)
        else:
            m['x_halo'] = np.ascontiguousarray(x[k * T - 128:k * T])
        mp = np.zeros((128, 8), np.float32)
        mp[:, :k] = 1.0
        m['mask_prev'] = mp
        in_maps.append(m)
    return in_maps


def kernel(**inputs):
    nc = build_nc()
    in_maps = make_in_maps(inputs)
    res = run_bass_kernel_spmd(nc, in_maps, core_ids=list(range(NCORES)))
    out = np.concatenate([np.asarray(r['out']) for r in res.results], axis=0)
    return out.reshape(1, L, DM).astype(np.float32)
